# version 07
# baseline (speedup 1.0000x reference)
import contextlib
import numpy as np
import ml_dtypes
import concourse.bass as bass
import concourse.mybir as mybir
from concourse.bass_utils import run_bass_kernel_spmd
from concourse.ap import AP as APc

F32 = mybir.dt.float32
BF16 = mybir.dt.bfloat16
I32 = mybir.dt.int32
AF = mybir.ActivationFunctionType
ALU = mybir.AluOpType

P = 128
D = 1024
T = 2048
NB = 2
NCC = 8
CS = T + 4
XOFF = 4


class Q:
    def __init__(self, ctx, eng, name):
        self.ctx = ctx
        self.eng = eng
        self.sem = ctx.stack.enter_context(ctx.nc.semaphore(name))
        self.n = 0

    def inc(self, inst, k=1):
        inst.then_inc(self.sem, k)
        self.n += k
        return self.n

    def tick(self, k=1):
        return self.inc(self.eng.nop(), k)

    def dma(self, out, in_, **kw):
        return self.inc(self.eng.dma_start(out=out, in_=in_, **kw), 16)

    def wait(self, other, n):
        if n > 0:
            self.eng.wait_ge(other.sem, n)

    def drain(self):
        self.eng.drain()


class Stream:

    def __init__(self, ctx, eng, name):
        self.ctx = ctx
        self.eng = eng
        self.sem = ctx.stack.enter_context(ctx.nc.semaphore(name))
        self.n = 0

    def dma(self, out, in_, **kw):
        self.eng.dma_start(out=out, in_=in_, **kw).then_inc(self.sem, 16)
        self.n += 16
        return self.n


class MK:
    def __init__(self):
        self.nc = bass.Bass("TRN2", target_bir_lowering=False)
        self.stack = contextlib.ExitStack()
        nc = self.nc
        self.T = Q(self, nc.tensor, "sT")
        self.V = Q(self, nc.vector, "sV")
        self.S = Q(self, nc.scalar, "sS")
        self.G = Q(self, nc.gpsimd, "sG")
        self.D = Q(self, nc.sync, "sD")
        self.st_setup = Stream(self, nc.sync, "st_setup")
        self.ss_tab = Stream(self, nc.sync, "ss_tab")
        self.ss_bcol = Stream(self, nc.sync, "ss_bcol")
        self.ss_rw = Stream(self, nc.sync, "ss_rw")
        self.ss_b2 = Stream(self, nc.sync, "ss_b2")
        self.st_al = Stream(self, nc.sync, "st_al")
        self.st_mx = Stream(self, nc.sync, "st_mx")
        self.st_wa = Stream(self, nc.sync, "st_wa")
        self.st_wb = Stream(self, nc.sync, "st_wb")
        self.st_x = Stream(self, nc.sync, "st_x")
        self.st_tmp = Stream(self, nc.sync, "st_tmp")
        self.st_tmp2 = Stream(self, nc.sync, "st_tmp2")
        self.st_ao = Stream(self, nc.sync, "st_ao")
        self.st_out = Stream(self, nc.sync, "st_out")
        self.sg_xn = Stream(self, nc.gpsimd, "sg_xn")
        self.sg_w = Stream(self, nc.gpsimd, "sg_w")
        self.ss_wt = Stream(self, nc.sync, "ss_wt")
        self.sg_band = [Stream(self, nc.sync, "sg_band%d" % i) for i in range(8)]
        self.ss_ep = Stream(self, nc.sync, "ss_ep")
        self.sg_stg = Stream(self, nc.gpsimd, "sg_stg")
        self.guard = {}
        self.stgw = {}
        self.dbg_ap = None

    def swait(self, q, stream, n=None):
        n = stream.n if n is None else n
        if n > 0:
            q.eng.wait_ge(stream.sem, n)

    def din(self, name, shape, dtype=F32):
        return self.nc.dram_tensor(name, list(shape), dtype, kind="ExternalInput").ap()

    def dout(self, name, shape, dtype=F32):
        return self.nc.dram_tensor(name, list(shape), dtype, kind="ExternalOutput").ap()

    def dint(self, name, shape, dtype=F32):
        return self.nc.dram_tensor(name, list(shape), dtype, kind="Internal").ap()

    def sbuf(self, name, shape, dtype):
        return self.stack.enter_context(self.nc.sbuf_tensor(name, shape, dtype))

    def psumt(self, name, shape, dtype=F32):
        return self.stack.enter_context(self.nc.psum_tensor(name, shape, dtype))

    def alloc(self):
        self.XMID = self.sbuf("XMID", [P, NCC * CS], F32)
        self.XN = self.sbuf("XN", [P, NCC * CS], BF16)
        self.SCR = self.sbuf("SCR", [P, 35072], BF16)
        self.WCOL = self.sbuf("WCOL", [P, 4096], BF16)
        self.TAB = self.sbuf("TAB", [P, 2 * T], BF16)
        self.RW = self.sbuf("RW", [1, 4096], F32)
        self.BCOL = self.sbuf("BCOL", [P, 16], F32)
        self.TMP = self.sbuf("TMP", [P, 1024], F32)
        self.X2 = self.sbuf("X2", [P, 1024], BF16)
        self.IDN = self.sbuf("IDN", [P, P], F32)
        self.IDNB = self.sbuf("IDNB", [P, P], BF16)
        self.MSK = self.sbuf("MSK", [P, 2 * P], BF16)
        self.OCOL = self.sbuf("OCOL", [P, 2], F32)
        self.ONER = self.sbuf("ONER", [P, 256], BF16)
        self.MT = self.sbuf("MT", [P, 80], BF16)
        self.BLK2 = self.sbuf("BLK2", [P, 2], BF16)
        self.BLK2T = self.sbuf("BLK2T", [2, P], F32)
        self.BCOL2 = self.sbuf("BCOL2", [P, 96], F32)
        self.OCOLB = self.sbuf("OCOLB", [P, 2], BF16)
        self.P0 = self.psumt("P0", [P, 512])
        self.P1 = self.psumt("P1", [P, 512])
        self.P2 = self.psumt("P2", [P, 512])
        self.P3 = self.psumt("P3", [P, 512])
        self.P4 = self.psumt("P4", [66, 512])
        self.P5 = self.psumt("P5", [66, 512])
        self.P6 = self.psumt("P6", [66, 512])

    def xmid(self, cc, lo=0, hi=T):
        return self.XMID[:, cc * CS + XOFF + lo: cc * CS + XOFF + hi]

    def xn(self, cc, lo=0, hi=T):
        return self.XN[:, cc * CS + XOFF + lo: cc * CS + XOFF + hi]

    def scr(self, off, lo, hi, p0=0, p1=P):
        return self.SCR[p0:p1, off + lo: off + hi]

    def row(self, i, w=512):
        return self.RW[0:1, i * 512: i * 512 + w]

    def use(self, waiter, *names):
        for name in names:
            if name in self.guard:
                q, n = self.guard[name]
                waiter.wait(q, n)

    def done(self, name, q, n=None):
        self.guard[name] = (q, q.n if n is None else n)

    def barrier(self):
        qs = [self.T, self.V, self.S, self.G, self.D]
        sts = [self.st_setup, self.ss_tab, self.ss_bcol, self.ss_rw, self.ss_b2,
               self.st_al, self.st_mx, self.st_wa, self.st_wb,
               self.st_x, self.st_tmp, self.st_tmp2, self.st_ao, self.st_out,
               self.sg_xn, self.sg_w, self.ss_wt, self.ss_ep,
               self.sg_stg] + self.sg_band
        for a in qs:
            for b_ in qs:
                if a is not b_:
                    a.wait(b_, b_.n)
            for st in sts:
                self.swait(a, st)

    def load_consts(self, hc_shapes=True):
        self.hc_ident = self.din("hc_ident", [P, P], F32)
        self.hc_tri_ge = self.din("hc_tri_ge", [P, P], BF16)
        self.hc_tri_le = self.din("hc_tri_le", [P, P], BF16)
        self.hc_identb = self.din("hc_identb", [P, P], BF16)
        self.hc_blk2 = self.din("hc_blk2", [P, 2], BF16)
        self.hc_blk2t = self.din("hc_blk2t", [2, P], F32)
        self.st_setup.dma(self.IDN[:], self.hc_ident)
        self.st_setup.dma(self.MSK[:, 0:P], self.hc_tri_ge)
        self.st_setup.dma(self.MSK[:, P:2 * P], self.hc_tri_le)
        self.st_setup.dma(self.IDNB[:], self.hc_identb)
        self.st_setup.dma(self.BLK2[:], self.hc_blk2)
        self.st_setup.dma(self.BLK2T[:], self.hc_blk2t)
        nc = self.nc
        nc.vector.memset(self.RW[0:1, 0:1024], 1.0)
        nc.vector.memset(self.OCOL[:], 1.0)
        nc.vector.memset(self.OCOLB[:], 1.0)
        nc.vector.memset(self.XMID[:], 0.0)
        nc.vector.memset(self.ONER[:, 0:128], 1.0)
        nc.vector.memset(self.ONER[:, 128:256], -1.0)
        self.V.inc(nc.vector.memset(self.XN[:], 0.0))
        self.consts_v = self.V.n
        self.swait(self.T, self.st_setup)
        self.T.wait(self.V, self.consts_v)
        self.S.wait(self.V, self.consts_v)
        self.G.wait(self.V, self.consts_v)
        self.D.wait(self.V, self.consts_v)

    @property
    def ones_row(self):
        return self.RW[0:1, 0:512]

    def emit_addx(self, xfm, b):
        nc = self.nc
        for cc in range(NCC):
            for tt in range(4):
                half = (cc * 4 + tt) % 2
                th = self.TMP[:, half * 512:(half + 1) * 512]
                st = (self.st_tmp, self.st_tmp2)[half]
                self.use(self.D, "TMP" + str(half))
                ld = st.dma(th, xfm[b, cc, :, tt * 512:(tt + 1) * 512])
                self.swait(self.V, st, ld)
                self.use(self.V, "TMP")
                vn = self.V.inc(nc.vector.tensor_tensor(
                    self.xmid(cc, tt * 512, (tt + 1) * 512), th,
                    self.xmid(cc, tt * 512, (tt + 1) * 512), ALU.add))
                self.done("TMP" + str(half), self.V, vn)
        self.barrier()
        return

    def dump_dbg(self, tag, b, src="xmid"):
        import os
        if os.environ.get("DBGPHASE") != tag or b != 0 or self.dbg_ap is None:
            return
        self.barrier()
        for cc in range(NCC):
            if src == "xmid":
                self.st_out.dma(self.dbg_ap[cc], self.xmid(cc))
            else:
                self.st_out.dma(self.dbg_ap[cc].bitcast(BF16)[:, 0:T], self.xn(cc))
        self.barrier()

    def load_tab(self, *aps):
        off = 0
        for ap in aps:
            self.use(self.D, "TAB")
            self.ss_tab.dma(self.TAB[:, off:off + ap.shape[-1]], ap)
            off += ap.shape[-1]
        self.tab_d = self.ss_tab.n
        return self.tab_d

    def emit_ingest(self, xin, xfm):
        nc = self.nc
        for b in range(NB):
            self.swait(self.V, self.st_x)
            for tc in range(T // P):
                self.use(self.D, "TMP")
                ld = self.st_tmp.dma(self.TMP[:, 0:1024], xin[b, tc * P:(tc + 1) * P, :])
                self.swait(self.T, self.st_tmp, ld)
                for half in range(2):
                    pst = (self.P0, self.P1)[half]
                    self.use(self.T, "P" + str(half))
                    tn = 0
                    for q in range(4):
                        cc = half * 4 + q
                        mm = nc.tensor.matmul(pst[:, q * P:(q + 1) * P],
                                              self.TMP[:, cc * P:(cc + 1) * P],
                                              self.IDN[:], start=True, stop=True)
                        if q == 3:
                            tn = self.T.inc(mm)
                    self.V.wait(self.T, tn)
                    vn = 0
                    for q in range(4):
                        cc = half * 4 + q
                        cp = nc.vector.tensor_copy(
                            out=self.xmid(cc, tc * P, (tc + 1) * P),
                            in_=pst[:, q * P:(q + 1) * P])
                        if q == 3:
                            vn = self.V.inc(cp)
                    self.done("P" + str(half), self.V, vn)
                self.done("TMP", self.V)
            self.D.wait(self.V, self.V.n)
            for cc in range(NCC):
                self.st_x.dma(xfm[b, cc], self.xmid(cc))
        self.done("XMID", self.st_x)
        return self.st_x.n

    def emit_egress(self, b, xout):
        nc = self.nc
        for tc in range(T // P):
            for half in range(2):
                pst = (self.P0, self.P1)[half]
                self.use(self.T, "P" + str(half))
                tn = 0
                for q in range(4):
                    cc = half * 4 + q
                    mm = nc.tensor.matmul(pst[:, q * P:(q + 1) * P],
                                          self.xmid(cc, tc * P, (tc + 1) * P),
                                          self.IDN[:], start=True, stop=True)
                    if q == 3:
                        tn = self.T.inc(mm)
                self.V.wait(self.T, tn)
                self.use(self.V, "TMP")
                vn = self.V.inc(nc.vector.tensor_copy(
                    out=self.TMP[:, half * 512:(half + 1) * 512], in_=pst[:]))
                self.done("P" + str(half), self.V, vn)
            self.D.wait(self.V, self.V.n)
            self.st_out.dma(xout[b, tc * P:(tc + 1) * P, :], self.TMP[:, 0:1024])
            self.done("TMP", self.st_out)
        return self.st_out.n

    def load_x(self, xfm, b):
        self.use(self.D, "XMID")
        self.use(self.G, "XN")
        self.swait(self.D, self.st_x)
        for cc in range(NCC):
            self.st_x.dma(self.xmid(cc), xfm[b, cc])
            self.sg_xn.dma(self.xn(cc), xfm[b, cc])
        self.x_d = self.st_x.n
        self.x_g = self.sg_xn.n
        return self.x_d, self.x_g

    def emit_stage_wcol(self, name, wap, ncc, ow, n_oc, col0=0, rows=None,
                        w_last=None):
        rows = ncc * P if rows is None else rows
        colw = ncc * ow
        h = self.nc.dram_tensor(name, [n_oc * P * colw], BF16, kind="Internal")
        base = wap.offset
        pitch = wap.ap[0][0]
        for oc in range(n_oc):
            w_oc = ow if (w_last is None or oc < n_oc - 1) else w_last
            c_full = min(ncc, rows // P)
            if c_full:
                src = APc(wap.tensor, base + col0 + oc * ow,
                          [[pitch, P], [P * pitch, c_full], [1, w_oc]])
                dst = APc(h, oc * P * colw, [[colw, P], [ow, c_full], [1, w_oc]])
                self.sg_stg.dma(dst, src)
            hh = rows - c_full * P
            if hh > 0:
                src = APc(wap.tensor, base + c_full * P * pitch + col0 + oc * ow,
                          [[pitch, hh], [1, w_oc]])
                dst = APc(h, oc * P * colw + c_full * ow, [[colw, hh], [1, w_oc]])
                self.sg_stg.dma(dst, src)
        return h, colw

    def load_wcol2(self, hcw, oc, dst, stream):
        h, colw = hcw
        return stream.dma(dst[:, 0:colw],
                          APc(h, oc * P * colw, [[colw, P], [1, colw]]))

    def load_wcol2_part(self, hcw, oc, coff, wid, dst, stream):
        h, colw = hcw
        return stream.dma(dst[:, 0:wid],
                          APc(h, oc * P * colw + coff, [[colw, P], [1, wid]]))

    def emit_stage_l0a(self, wq, wk, wv, wp):
        self.stgw["a_wv"] = self.emit_stage_wcol("stg_a_wv", wv, 8, 512, 2)
        self.stgw["a_wp"] = self.emit_stage_wcol("stg_a_wp", wp, 8, 128, 8)
        self.stgw["a_wq"] = self.emit_stage_wcol("stg_a_wq", wq, 8, 128, 8)
        self.stgw["a_wk"] = self.emit_stage_wcol("stg_a_wk", wk, 8, 128, 8)

    def emit_stage_l0f(self, wqkv, wo, w1, w2):
        for l in range(2):
            t = str(l)
            self.stgw["b_wqk" + t] = self.emit_stage_wcol(
                "stg_b_wqk" + t, wqkv[l], 8, 128, 8)
            self.stgw["b_wv" + t] = self.emit_stage_wcol(
                "stg_b_wv" + t, wqkv[l], 8, 512, 1, col0=1024)
            self.stgw["b_wo" + t] = self.emit_stage_wcol(
                "stg_b_wo" + t, wo[l], 4, 128, 8, rows=512)
            self.stgw["b_w1a" + t] = self.emit_stage_wcol(
                "stg_b_w1a" + t, w1[l], 8, 128, 22, w_last=42)
            self.stgw["b_w1g" + t] = self.emit_stage_wcol(
                "stg_b_w1g" + t, w1[l], 8, 128, 22, col0=2730, w_last=42)
            self.stgw["b_w2" + t] = self.emit_stage_wcol(
                "stg_b_w2" + t, w2[l], 22, 128, 8, rows=2730)

    def emit_stage_deint(self):
        nc = self.nc
        vv = nc.vector
        self.swait(self.D, self.sg_stg)
        for key in ("a_wq", "a_wk"):
            h = nc.dram_tensor("stg_" + key + "d", [8 * P * 1024], BF16,
                               kind="Internal")
            for oc in range(8):
                buf = oc % 2
                st = (self.st_wa, self.st_wb)[buf]
                wn = self.WCOL[:, buf * 2048: buf * 2048 + 1024]
                wd = self.WCOL[:, buf * 2048 + 1024: buf * 2048 + 2048]
                self.use(self.D, "WC" + str(buf))
                wld = self.load_wcol2(self.stgw[key], oc, wn, st)
                self.swait(self.V, st, wld)
                self.use(self.V, "WCd" + str(buf))
                dei = 0
                for h2 in range(2):
                    for eo in range(2):
                        src = wn.rearrange("p (c x) -> p c x", c=8)[
                            :, :, h2 * 64 + eo: h2 * 64 + 64: 2]
                        dst = wd.rearrange("p (c x) -> p c x", c=8)[
                            :, :, h2 * 64 + eo * 32: h2 * 64 + eo * 32 + 32]
                        dei = self.V.inc(vv.tensor_copy(out=dst, in_=src))
                vv.drain()
                self.done("WC" + str(buf), self.V, dei)
                self.D.wait(self.V, dei)
                sst = (self.st_tmp, self.st_tmp2)[buf]
                sn = sst.dma(APc(h, oc * P * 1024, [[1024, P], [1, 1024]]), wd)
                self.done("WCd" + str(buf), sst, sn)
            self.stgw[key + "d"] = (h, 1024)
        self.barrier()

    def load_wcol(self, wdram, oc, ncc, ow, dst, w=None, col0=None, rows=None):
        w = w or ow
        c0 = oc * ow if col0 is None else col0
        n = 0
        nrow = ncc * P if rows is None else rows
        for cc in range(ncc):
            h = min(P, nrow - cc * P)
            if h <= 0:
                break
            src = wdram[cc * P: cc * P + h, c0: c0 + w]
            n = self.sg_w.dma(dst[0:h, cc * ow: cc * ow + w], src)
        return n

    def _ln_rows(self, ps_sum, ps_sq, eps, nfeat):
        nc = self.nc
        v = nc.vector
        m_row = self.row(1)
        r_row = self.row(2)
        v_row = self.row(3)
        mr_row = self.row(4)
        v.tensor_scalar(m_row, ps_sum, 1.0 / nfeat, None, ALU.mult)
        v.drain()
        v.tensor_scalar(v_row, ps_sq, 1.0 / nfeat, None, ALU.mult)
        v.drain()
        v.tensor_tensor(mr_row, m_row, m_row, ALU.mult)
        v.drain()
        vn = self.V.inc(v.scalar_tensor_tensor(
            v_row, v_row, float(eps), mr_row, ALU.add, ALU.subtract))
        self.S.wait(self.V, vn)
        nc.scalar.activation(out=r_row, in_=v_row, func=AF.Ln)
        self.S.drain()
        sn = self.S.inc(nc.scalar.activation(out=r_row, in_=r_row, func=AF.Exp,
                                             scale=-0.5))
        self.V.wait(self.S, sn)
        return self.V.inc(v.tensor_tensor(mr_row, m_row, r_row, ALU.mult))

    def emit_ln(self, srcf=None, dstf=None, dstb=None, gains=None, eps=1e-5,
                dram_out=None, b=None):
        nc = self.nc
        if srcf is None:
            srcf = self.xmid
        ps_sum = self.P6[64:65, :]
        ps_sq = self.P4[64:65, :]
        r_row = self.row(2)
        mr_row = self.row(4)
        for tt in range(T // 512):
            lo, hi = tt * 512, (tt + 1) * 512
            self.use(self.T, "P6", "P4")
            for cc in range(NCC):
                mm = nc.tensor.matmul(ps_sum, self.OCOL[:, 0:1], srcf(cc, lo, hi),
                                      start=(cc == 0), stop=(cc == NCC - 1))
                if cc == NCC - 1:
                    self.T.inc(mm)
            for cc in range(NCC):
                xh = self.X2[:, (cc % 2) * 512:(cc % 2) * 512 + 512]
                self.use(self.S, "X2" + str(cc % 2))
                sn = self.S.inc(nc.scalar.activation(
                    out=xh, in_=srcf(cc, lo, hi), func=AF.Square))
                self.T.wait(self.S, sn)
                tn = self.T.inc(nc.tensor.matmul(
                    ps_sq, self.OCOLB[:, 0:1], xh,
                    start=(cc == 0), stop=(cc == NCC - 1)))
                self.done("X2" + str(cc % 2), self.T, tn)
            self.V.wait(self.T, self.T.n)
            self.use(self.V, "LNrows")
            vrows = self._ln_rows(ps_sum, ps_sq, eps, D)
            self.done("P6", self.V, vrows)
            self.done("P4", self.V, vrows)
            self.use(self.T, "P0", "P1")
            self.T.wait(self.V, vrows)
            nc.tensor.matmul(self.P0[:], self.RW[0:1, 0:P], r_row,
                             start=True, stop=True)
            t_bc = self.T.inc(nc.tensor.matmul(
                self.P1[:], self.RW[0:1, 0:P], mr_row, start=True, stop=True))
            self.done("LNrows", self.T, t_bc)
            self.V.wait(self.T, t_bc)
            for cc in range(NCC):
                v = nc.vector
                self.use(self.V, "TMP")
                v.tensor_tensor(self.TMP[:, 0:512], srcf(cc, lo, hi), self.P0[:],
                                ALU.mult)
                v.drain()
                if gains is None:
                    tgt = dstf(cc, lo, hi) if dstf is not None else dstb(cc, lo, hi)
                    vn = self.V.inc(v.tensor_tensor(tgt, self.TMP[:, 0:512],
                                                    self.P1[:], ALU.subtract))
                    v.drain()
                    if dstf is not None and dstb is not None:
                        vn = self.V.inc(v.tensor_copy(out=dstb(cc, lo, hi),
                                                      in_=dstf(cc, lo, hi)))
                        v.drain()
                    self.done("TMP", self.V, vn)
                else:
                    vn = self.V.inc(v.tensor_tensor(
                        self.TMP[:, 512:1024], self.TMP[:, 0:512],
                        self.P1[:], ALU.subtract))
                    v.drain()
                    self.done("TMP", self.V, vn)
                    gcol, bcol = gains
                    self.S.wait(self.V, vn)
                    sn = self.S.inc(nc.scalar.activation(
                        out=dstb(cc, lo, hi), in_=self.TMP[:, 512:1024],
                        func=AF.Identity, bias=bcol(cc), scale=gcol(cc)))
                    self.done("TMP", self.S, sn)
            self.done("P0", self.V if gains is None else self.S)
            self.done("P1", self.V if gains is None else self.S)
        if dram_out is not None:
            self.D.wait(self.V, self.V.n)
            self.D.wait(self.S, self.S.n)
            for cc in range(NCC):
                self.st_x.dma(dram_out[b, cc], self.xmid(cc))
            self.done("XMID", self.st_x)
        return

    def emit_l0_attn(self, b, wq, bq, wk, bk, wv, bv, wp, bp, attb):
        nc = self.nc
        off_v = 0
        off_q = 16640
        off_k = 18688
        off_sw = 20736
        off_t2 = 22784
        off_pt = 24832
        off_ao = 33024
        off_al = 0
        EXPSC = 0.125

        bvrow = self.row(6, 1024)
        self.use(self.D, "RW8")
        self.ss_rw.dma(bvrow, bv.unsqueeze(0))
        bv_d = self.ss_rw.n
        self.use(self.V, "SCRv")
        v_ones = self.V.inc(nc.vector.memset(self.SCR[:, off_v:off_v + 16640], 1.0))
        for half in range(2):
            wsl = self.WCOL[:, 0:4096]
            self.use(self.D, "WCOL")
            wld = self.load_wcol2(self.stgw["a_wv"], half, wsl, self.st_wa)
            self.swait(self.T, self.st_wa, wld)
            self.swait(self.T, self.sg_xn, self.x_g)
            self.swait(self.T, self.ss_rw, bv_d)
            for tk in range(T // P):
                pst = (self.P0, self.P1)[tk % 2]
                self.use(self.T, "P" + str(tk % 2))
                for cc in range(NCC):
                    nc.tensor.matmul(pst[:], self.xn(cc, tk * P, (tk + 1) * P),
                                     wsl[:, cc * 512:(cc + 1) * 512],
                                     start=(cc == 0), stop=False)
                tn = self.T.inc(nc.tensor.matmul(
                    pst[:], self.ones_row[0:1, 0:P],
                    bvrow[0:1, half * 512:(half + 1) * 512],
                    start=False, stop=True))
                self.V.wait(self.T, tn)
                if tk == 0 and half == 0:
                    self.V.wait(self.V, v_ones)
                vn = 0
                for j in range(8):
                    h = half * 8 + j
                    slot = off_v + tk * 1040 + h * 65
                    cp = nc.vector.tensor_copy(out=self.SCR[:, slot:slot + 64],
                                               in_=pst[:, j * 64:(j + 1) * 64])
                    if j == 7:
                        vn = self.V.inc(cp)
                self.done("P" + str(tk % 2), self.V, vn)
            self.done("WCOL", self.T)
        v_ready = self.V.n

        self.use(self.D, "WCOL", "WCa", "WCb")
        wsts = (self.st_wa, self.st_wb)
        wlds = [0, 0]
        self.load_wcol2(self.stgw["a_wqd"], 0, self.WCOL[:, 0:1024], wsts[0])
        wlds[0] = self.load_wcol2(self.stgw["a_wkd"], 0,
                                  self.WCOL[:, 1024:2048], wsts[0])
        for hp in range(NCC):
            buf = hp % 2
            wq_d = self.WCOL[:, buf * 2048: buf * 2048 + 1024]
            wk_d = self.WCOL[:, buf * 2048 + 1024: buf * 2048 + 2048]
            if hp + 1 < NCC:
                nb = 1 - buf
                self.use(self.D, "WC" + "ab"[nb])
                self.load_wcol2(self.stgw["a_wqd"], hp + 1,
                                self.WCOL[:, nb * 2048: nb * 2048 + 1024], wsts[nb])
                wlds[nb] = self.load_wcol2(
                    self.stgw["a_wkd"], hp + 1,
                    self.WCOL[:, nb * 2048 + 1024: nb * 2048 + 2048], wsts[nb])
            bq_c = self.BCOL[:, 0:1]
            bk_c = self.BCOL[:, 1:2]
            self.use(self.D, "BCOL01")
            for (bsrc, bdst) in ((bq, bq_c), (bk, bk_c)):
                for h2 in range(2):
                    for eo in range(2):
                        src = APc(bsrc.tensor, bsrc.offset + 128 * hp + 64 * h2 + eo,
                                  [[2, 32], [1, 1]])
                        self.ss_bcol.dma(bdst[h2 * 64 + eo * 32: h2 * 64 + eo * 32 + 32, :],
                                         src, allow_slow_non_contiguous=True)
            bias_d = self.ss_bcol.n

            for (wdsl, bcol, offd) in ((wq_d, bq_c, off_q), (wk_d, bk_c, off_k)):
                for tt in range(4):
                    pst = (self.P0, self.P1)[tt % 2]
                    self.use(self.T, "P" + str(tt % 2))
                    self.swait(self.T, wsts[buf], wlds[buf])
                    tn = 0
                    for cc in range(NCC):
                        mm = nc.tensor.matmul(
                            pst[:], wdsl[:, cc * 128:(cc + 1) * 128],
                            self.xn(cc, tt * 512, (tt + 1) * 512),
                            start=(cc == 0), stop=(cc == NCC - 1))
                        if cc == NCC - 1:
                            tn = self.T.inc(mm)
                    self.S.wait(self.T, tn)
                    self.swait(self.S, self.ss_bcol, bias_d)
                    self.use(self.S, "SCRq")
                    sn = self.S.inc(nc.scalar.activation(
                        out=self.scr(offd, tt * 512, (tt + 1) * 512),
                        in_=pst[:], func=AF.Identity, bias=bcol, scale=1.0))
                    self.done("P" + str(tt % 2), self.S, sn)
            s_qk = self.S.n
            self.done("WC" + "ab"[buf], self.S, s_qk)
            self.done("BCOL01", self.S, s_qk)

            self.V.wait(self.S, s_qk)
            self.swait(self.V, self.ss_tab, getattr(self, "tab_d", 0))
            for offd in (off_q, off_k):
                vv = nc.vector
                self.use(self.V, "SCRsw", "SCRt2")
                for (d0, s0) in ((0, 32), (32, 0), (64, 96), (96, 64)):
                    vv.tensor_copy(out=self.scr(off_sw, 0, T, d0, d0 + 32),
                                   in_=self.scr(offd, 0, T, s0, s0 + 32))
                vv.drain()
                vv.tensor_tensor(self.scr(off_t2, 0, T), self.scr(offd, 0, T),
                                 self.TAB[:, 0:T], ALU.mult)
                vv.drain()
                vv.tensor_tensor(self.scr(off_sw, 0, T), self.scr(off_sw, 0, T),
                                 self.TAB[:, T:2 * T], ALU.mult)
                vv.drain()
                rope_v = self.V.inc(vv.tensor_tensor(
                    self.scr(offd, 0, T), self.scr(off_t2, 0, T),
                    self.scr(off_sw, 0, T), ALU.add))
                vv.drain()
            self.done("SCRsw", self.V, rope_v)
            self.done("SCRt2", self.V, rope_v)
            self.done("SCRq", self.V, rope_v)

            for h2 in range(2):
                h = hp * 2 + h2
                pbase = h2 * 64
                ps_o = (self.P4, self.P5)[h2]
                for qt in range(4):
                    kmax = 4 * qt + 4
                    for kc in range(kmax):
                        dlt = 128 * kc - 512 * qt
                        lo = max(0, dlt)
                        pss = (self.P0, self.P1)[kc % 2]
                        self.use(self.T, "P" + str(kc % 2), "SCRpt")
                        self.T.wait(self.V, rope_v)
                        tn = self.T.inc(nc.tensor.matmul(
                            pss[:, lo:512],
                            self.scr(off_k, kc * P, (kc + 1) * P, pbase, pbase + 64),
                            self.scr(off_q, qt * 512 + lo, (qt + 1) * 512,
                                     pbase, pbase + 64),
                            start=True, stop=True))
                        self.S.wait(self.T, tn)
                        sn = self.S.inc(nc.scalar.activation(
                            out=self.scr(off_pt, kc * 512 + lo, (kc + 1) * 512),
                            in_=pss[:, lo:512], func=AF.Exp, scale=EXPSC))
                        self.done("P" + str(kc % 2), self.S, sn)
                        if dlt >= 0:
                            self.V.wait(self.S, sn)
                            vv = nc.vector
                            self.V.inc(vv.tensor_tensor(
                                self.scr(off_pt, kc * 512 + dlt, kc * 512 + dlt + P),
                                self.scr(off_pt, kc * 512 + dlt, kc * 512 + dlt + P),
                                self.MSK[:, 0:P], ALU.mult))
                    s_exp = self.S.n
                    v_msk = self.V.n
                    self.T.wait(self.S, s_exp)
                    self.T.wait(self.V, v_msk)
                    self.use(self.T, ("P4", "P5")[h2])
                    if hp == 0 and h2 == 0 and qt == 0:
                        self.T.wait(self.V, v_ready)
                    t_pv = 0
                    for kc in range(kmax):
                        lo = max(0, 128 * kc - 512 * qt)
                        slot = off_v + kc * 1040 + h * 65
                        mm = nc.tensor.matmul(
                            ps_o[0:65, lo:512], self.SCR[:, slot:slot + 65],
                            self.scr(off_pt, kc * 512 + lo, (kc + 1) * 512),
                            start=(kc == 0), stop=(kc == kmax - 1))
                        if kc == kmax - 1:
                            t_pv = self.T.inc(mm)
                    self.done("SCRpt", self.T, t_pv)
                    self.V.wait(self.T, t_pv)
                    rl_row = self.row(5)
                    self.use(self.V, "RW6")
                    vn = self.V.inc(nc.vector.reciprocal(rl_row, ps_o[64:65, :]))
                    self.use(self.T, "P6")
                    self.T.wait(self.V, vn)
                    t_bc = self.T.inc(nc.tensor.matmul(
                        self.P6[0:64, :], self.RW[0:1, 0:64], rl_row,
                        start=True, stop=True))
                    self.done("RW6", self.T, t_bc)
                    self.use(self.S, "SCRt2")
                    self.S.wait(self.T, t_bc)
                    sn = self.S.inc(nc.scalar.copy(
                        out=self.scr(off_t2, 0, 512, 0, 64), in_=ps_o[0:64, :]))
                    self.done(("P4", "P5")[h2], self.S, sn)
                    self.V.wait(self.S, sn)
                    self.use(self.V, "SCRao")
                    vv = nc.vector
                    vn = self.V.inc(vv.tensor_tensor(
                        self.scr(off_ao, qt * 512, (qt + 1) * 512, pbase, pbase + 64),
                        self.scr(off_t2, 0, 512, 0, 64), self.P6[0:64, :], ALU.mult))
                    self.done("P6", self.V, vn)
                    self.done("SCRt2", self.V, vn)
            self.D.wait(self.V, self.V.n)
            dn = self.st_ao.dma(attb[b, hp], self.scr(off_ao, 0, T))
            self.done("SCRao", self.st_ao, dn)
            self.done("SCRq", self.st_ao, dn)

        self.use(self.D, "SCRv_w")
        t_after_pv = self.T.n
        self.D.wait(self.T, t_after_pv)
        self.swait(self.D, self.st_ao)
        al = 0
        for cc in range(NCC):
            al = self.st_al.dma(self.SCR[:, off_al + cc * T: off_al + (cc + 1) * T],
                                attb[b, cc])
        bp_c = self.BCOL[:, 2:3]
        self.use(self.D, "WCOL", "WCa", "WCb")
        wsts = (self.st_wa, self.st_wb)
        wlds = [0, 0]
        wlds[0] = self.load_wcol2(self.stgw["a_wp"], 0, self.WCOL[:, 0:1024],
                                  wsts[0])
        for oc in range(NCC):
            buf = oc % 2
            wsl = self.WCOL[:, buf * 1024: buf * 1024 + 1024]
            if oc + 1 < NCC:
                nb = 1 - buf
                self.use(self.D, "WC" + "ab"[nb])
                wlds[nb] = self.load_wcol2(
                    self.stgw["a_wp"], oc + 1,
                    self.WCOL[:, nb * 1024: nb * 1024 + 1024], wsts[nb])
            wld = wlds[buf]
            self.use(self.D, "BCOL2")
            self.ss_b2.dma(bp_c, bp[oc * P:(oc + 1) * P].unsqueeze(1))
            bp_d = self.ss_b2.n
            for tt in range(4):
                pst = (self.P0, self.P1)[tt % 2]
                self.use(self.T, "P" + str(tt % 2))
                self.swait(self.T, wsts[buf], wld)
                self.swait(self.T, self.st_al, al)
                tn = 0
                for cc in range(NCC):
                    mm = nc.tensor.matmul(
                        pst[:], wsl[:, cc * P:(cc + 1) * P],
                        self.SCR[:, off_al + cc * T + tt * 512:
                                 off_al + cc * T + (tt + 1) * 512],
                        start=(cc == 0), stop=(cc == NCC - 1))
                    if cc == NCC - 1:
                        tn = self.T.inc(mm)
                self.V.wait(self.T, tn)
                self.swait(self.V, self.ss_b2, bp_d)
                vv = nc.vector
                vn = self.V.inc(vv.scalar_tensor_tensor(
                    out=self.xmid(oc, tt * 512, (tt + 1) * 512),
                    in0=pst[:], scalar=bp_c,
                    in1=self.xmid(oc, tt * 512, (tt + 1) * 512),
                    op0=ALU.add, op1=ALU.add))
                self.done("P" + str(tt % 2), self.V, vn)
            self.done("WC" + "ab"[buf], self.V)
            self.done("BCOL2", self.V)
        self.done("WCOL", self.V)
        self.done("SCRv", self.V)
        self.done("SCRv_w", self.V)
        self.done("RW8", self.T)
        self.done("XN", self.T)
        self.barrier()
        return

    def emit_stage_bf16(self, pairs):
        n = 0
        for (src, dst) in pairs:
            n = self.sg_stg.dma(dst, src)
        self.stg_n = n
        return n

    def emit_l1_emb(self, emb):
        nc = self.nc
        for r in range(65):
            self.sg_stg.dma(self.X2[r:r + 1, 0:64],
                            emb[64 - r].unsqueeze(0))
        self.swait(self.T, self.sg_stg)
        nc.vector.tensor_copy(out=self.MT[0:65, 0:65], in_=self.IDNB[0:65, 0:65])
        nc.vector.drain()
        nc.vector.memset(self.MT[64:65, 0:64], -1.0)
        nc.vector.drain()
        mtv = self.V.inc(nc.vector.memset(self.MT[64:65, 64:65], 0.0))
        self.T.wait(self.V, mtv)
        self.use(self.T, "P2")
        mm = nc.tensor.matmul(self.P2[0:64, 0:65],
                              self.X2[0:65, 0:64],
                              self.MT[0:65, 0:65], start=True, stop=True)
        tn = self.T.inc(mm)
        self.V.wait(self.T, tn)
        nc.vector.tensor_copy(out=self.TAB[0:64, 0:65], in_=self.P2[0:64, 0:65])
        nc.vector.drain()
        vn = self.V.inc(nc.vector.tensor_copy(out=self.TAB[64:128, 0:65],
                                              in_=self.P2[0:64, 0:65]))
        self.done("P2", self.V, vn)
        self.done("X20", self.V, vn)
        self.emb_v = vn
        return vn

    def emit_l1_attn(self, b, wq_bf, wo_bf, epad_h, attb):
        import os
        noband = bool(os.environ.get("NOBAND"))
        self._bandcap = int(os.environ.get("BANDCAP", "999999"))
        noep = bool(os.environ.get("NOEP"))
        nc = self.nc
        off_v = 0
        off_q = 16640
        off_e = 18688
        off_ep = 20736
        off_bt = 21312
        off_t2 = 22336
        off_pt = 24832
        off_ao = 33024
        off_al = 0
        EXPSC = 0.125
        EPW = 576

        self.use(self.V, "SCRv")
        v_ones = self.V.inc(nc.vector.memset(self.SCR[:, off_v:off_v + 16640], 1.0))
        for half in range(2):
            wsl = self.WCOL[:, 0:4096]
            self.use(self.D, "WCOL")
            for cc in range(NCC):
                wld = self.ss_wt.dma(
                    wsl[:, cc * 512:(cc + 1) * 512],
                    wq_bf[half * 512:(half + 1) * 512, cc * P:(cc + 1) * P],
                    transpose=True)
            self.swait(self.T, self.ss_wt, wld)
            self.swait(self.T, self.sg_xn, self.x_g)
            for tk in range(T // P):
                pst = (self.P0, self.P1)[tk % 2]
                self.use(self.T, "P" + str(tk % 2))
                tn = 0
                for cc in range(NCC):
                    mm = nc.tensor.matmul(pst[:], self.xn(cc, tk * P, (tk + 1) * P),
                                          wsl[:, cc * 512:(cc + 1) * 512],
                                          start=(cc == 0), stop=(cc == NCC - 1))
                    if cc == NCC - 1:
                        tn = self.T.inc(mm)
                self.V.wait(self.T, tn)
                if tk == 0 and half == 0:
                    self.V.wait(self.V, v_ones)
                vn = 0
                for j in range(8):
                    h = half * 8 + j
                    slot = off_v + tk * 1040 + h * 65
                    cp = nc.vector.tensor_copy(out=self.SCR[:, slot:slot + 64],
                                               in_=pst[:, j * 64:(j + 1) * 64])
                    if j == 7:
                        vn = self.V.inc(cp)
                self.done("P" + str(tk % 2), self.V, vn)
            self.done("WCOL", self.T)
        v_ready = self.V.n

        if not hasattr(self, "_ep_zeroed"):
            self._ep_zeroed = True
            self.use(self.V, "SCRep")
            self.V.inc(nc.vector.memset(self.scr(off_ep, 321, 544), 0.0))

        for hp in range(NCC):
            wsl = self.WCOL[:, 0:1024]
            self.use(self.D, "WCOL", "WCOLd")
            self.D.wait(self.T, self.T.n)
            for st_ in self.sg_band:
                self.swait(self.D, st_)
            for cc in range(NCC):
                wld = self.ss_wt.dma(
                    wsl[:, cc * P:(cc + 1) * P],
                    wq_bf[hp * P:(hp + 1) * P, cc * P:(cc + 1) * P],
                    transpose=True)
            for tt in range(4):
                pst = (self.P0, self.P1)[tt % 2]
                self.use(self.T, "P" + str(tt % 2))
                self.swait(self.T, self.ss_wt, wld)
                tn = 0
                for cc in range(NCC):
                    mm = nc.tensor.matmul(
                        pst[:], wsl[:, cc * P:(cc + 1) * P],
                        self.xn(cc, tt * 512, (tt + 1) * 512),
                        start=(cc == 0), stop=(cc == NCC - 1))
                    if cc == NCC - 1:
                        tn = self.T.inc(mm)
                self.S.wait(self.T, tn)
                self.use(self.S, "SCRq")
                sn = self.S.inc(nc.scalar.copy(
                    out=self.scr(off_q, tt * 512, (tt + 1) * 512), in_=pst[:]))
                self.done("P" + str(tt % 2), self.S, sn)
            s_q = self.S.n
            self.done("WCOLd", self.S, s_q)

            for h2 in range(2):
                h = hp * 2 + h2
                pbase = h2 * 64
                ps_o = (self.P4, self.P5)[h2]
                self.T.wait(self.V, self.emb_v)
                self.T.wait(self.S, s_q)
                for tt in range(4):
                    self.use(self.T, "P6")
                    tn = self.T.inc(nc.tensor.matmul(
                        self.P6[0:65, :], self.TAB[pbase:pbase + 64, 0:65],
                        self.scr(off_q, tt * 512, (tt + 1) * 512, pbase, pbase + 64),
                        start=True, stop=True))
                    self.V.wait(self.T, tn)
                    self.use(self.V, "SCRe")
                    vn = self.V.inc(nc.vector.tensor_copy(
                        out=self.scr(off_e, tt * 512, (tt + 1) * 512, 0, 65),
                        in_=self.P6[0:65, :]))
                    self.done("P6", self.V, vn)
                e_v = self.V.n
                for tk in range(0 if not noep else T // P, T // P):
                    self.use(self.T, "P2")
                    self.T.wait(self.V, e_v)
                    tn = self.T.inc(nc.tensor.matmul(
                        self.P2[:, 0:65], self.scr(off_e, tk * P, (tk + 1) * P, 0, 65),
                        self.IDNB[0:65, 0:65], start=True, stop=True))
                    self.V.wait(self.T, tn)
                    self.use(self.V, "SCRep")
                    vn = self.V.inc(nc.vector.tensor_copy(
                        out=self.scr(off_ep, 256, 321), in_=self.P2[:, 0:65]))
                    self.done("P2", self.V, vn)
                    self.S.wait(self.V, vn)
                    sn = self.S.inc(nc.scalar.activation(
                        out=self.scr(off_ep, 33, 256), in_=self.scr(off_ep, 33, 256),
                        func=AF.Identity, bias=self.scr(off_ep, 256, 257),
                        scale=0.0))
                    self.D.wait(self.S, sn)
                    base = ((b * 16 + h) * T + tk * P) * EPW
                    dst = APc(epad_h, base + 33, [[EPW, P], [1, 511]])
                    en = self.ss_ep.dma(dst, self.scr(off_ep, 33, 544))
                    self.done("SCRep", self.ss_ep, en)
                ep_done = self.ss_ep.n

                for qt in range(4):
                    pend = {}

                    def _issue(kci):
                        bl = [it for it in range(4 * qt, 4 * qt + 4)
                              if abs(it - kci) <= 1] if (not noband
                                                         and kci < 16) else []
                        if self._bandcap <= 0:
                            bl = []
                        self._bandcap -= len(bl)
                        lst = []
                        for bi, it in enumerate(bl):
                            si = (kci % 2) * 4 + bi
                            slot = off_bt + si * P
                            self.use(self.D, "BT" + str(si))
                            self.swait(self.D, self.ss_ep, ep_done)
                            base = ((b * 16 + h) * T + it * P) * EPW
                            step = EPW if os.environ.get("SKEW576") else EPW - 1
                            src = APc(epad_h, base + 288 - 128 * (it - kci),
                                      [[step, P], [1, P]])
                            bn = self.sg_band[si].dma(
                                self.SCR[:, slot:slot + P], src)
                            lst.append((it, si, bn))
                        pend[kci] = lst

                    _issue(0)
                    for kc in range(16):
                        _issue(kc + 1)
                        pss = (self.P0, self.P1)[kc % 2]
                        self.use(self.T, "P" + str(kc % 2))
                        bands = pend.pop(kc)
                        locol = max(0, (kc + 2) * P - 512 * qt) if not noband else 512
                        n_extra = len(bands) + (1 if locol < 512 else 0)
                        ei = 0
                        tn = self.T.inc(nc.tensor.matmul(
                            pss[:],
                            self.scr(off_q, kc * P, (kc + 1) * P, pbase, pbase + 64),
                            self.scr(off_q, qt * 512, (qt + 1) * 512, pbase, pbase + 64),
                            start=True, stop=(n_extra == 0),
                            skip_group_check=(n_extra > 0)))
                        for (it, si, bn) in bands:
                            slot = off_bt + si * P
                            self.swait(self.T, self.sg_band[si], bn)
                            ei += 1
                            tn = self.T.inc(nc.tensor.matmul(
                                pss[:, (it - 4 * qt) * P:(it - 4 * qt + 1) * P],
                                self.SCR[:, slot:slot + P], self.IDNB[:],
                                start=False, stop=(ei == n_extra),
                                skip_group_check=True))
                            self.done("BT" + str(si), self.T, tn)
                        if locol < 512:
                            ei += 1
                            tn = self.T.inc(nc.tensor.matmul(
                                pss[:, locol:512], self.ONER[0:1, 0:128],
                                self.scr(off_e, qt * 512 + locol, (qt + 1) * 512, 0, 1),
                                start=False, stop=(ei == n_extra),
                                skip_group_check=True))
                        self.S.wait(self.T, tn)
                        sn = self.S.inc(nc.scalar.activation(
                            out=self.scr(off_pt, kc * 512, (kc + 1) * 512),
                            in_=pss[:], func=AF.Exp, scale=EXPSC))
                        self.done("P" + str(kc % 2), self.S, sn)
                    s_exp = self.S.n
                    self.T.wait(self.S, s_exp)
                    self.use(self.T, ("P4", "P5")[h2])
                    if hp == 0 and h2 == 0 and qt == 0:
                        self.T.wait(self.V, v_ready)
                    t_pv = 0
                    for kc in range(16):
                        slot = off_v + kc * 1040 + h * 65
                        mm = nc.tensor.matmul(
                            ps_o[0:65, :], self.SCR[:, slot:slot + 65],
                            self.scr(off_pt, kc * 512, (kc + 1) * 512),
                            start=(kc == 0), stop=(kc == 15))
                        if kc == 15:
                            t_pv = self.T.inc(mm)
                    self.done("SCRpt", self.T, t_pv)
                    self.V.wait(self.T, t_pv)
                    rl_row = self.row(5)
                    self.use(self.V, "RW6")
                    vn = self.V.inc(nc.vector.reciprocal(rl_row, ps_o[64:65, :]))
                    self.use(self.T, "P6")
                    self.T.wait(self.V, vn)
                    t_bc = self.T.inc(nc.tensor.matmul(
                        self.P6[0:64, :], self.RW[0:1, 0:64], rl_row,
                        start=True, stop=True))
                    self.done("RW6", self.T, t_bc)
                    self.use(self.S, "SCRt2")
                    self.S.wait(self.T, t_bc)
                    sn = self.S.inc(nc.scalar.copy(
                        out=self.scr(off_t2, 0, 512, 0, 64), in_=ps_o[0:64, :]))
                    self.done(("P4", "P5")[h2], self.S, sn)
                    self.V.wait(self.S, sn)
                    self.use(self.V, "SCRao")
                    vn = self.V.inc(nc.vector.tensor_tensor(
                        self.scr(off_ao, qt * 512, (qt + 1) * 512, pbase, pbase + 64),
                        self.scr(off_t2, 0, 512, 0, 64), self.P6[0:64, :], ALU.mult))
                    self.done("P6", self.V, vn)
                    self.done("SCRt2", self.V, vn)
                self.done("SCRe", self.T)
            self.D.wait(self.V, self.V.n)
            dn = self.st_ao.dma(attb[b, hp], self.scr(off_ao, 0, T))
            self.done("SCRao", self.st_ao, dn)
            self.done("SCRq", self.st_ao, dn)

        self.swait(self.D, self.st_ao)
        self.use(self.D, "SCRv_w")
        self.D.wait(self.T, self.T.n)
        al = 0
        for cc in range(NCC):
            al = self.st_al.dma(self.SCR[:, off_al + cc * T: off_al + (cc + 1) * T],
                                attb[b, cc])
        for oc in range(NCC):
            wsl = self.WCOL[:, 0:1024]
            self.use(self.D, "WCOL", "WCOLd")
            for cc in range(NCC):
                wld = self.ss_wt.dma(
                    wsl[:, cc * P:(cc + 1) * P],
                    wo_bf[oc * P:(oc + 1) * P, cc * P:(cc + 1) * P],
                    transpose=True)
            for tt in range(4):
                pst = (self.P0, self.P1)[tt % 2]
                self.use(self.T, "P" + str(tt % 2))
                self.swait(self.T, self.ss_wt, wld)
                self.swait(self.T, self.st_al, al)
                tn = 0
                for cc in range(NCC):
                    mm = nc.tensor.matmul(
                        pst[:], wsl[:, cc * P:(cc + 1) * P],
                        self.SCR[:, off_al + cc * T + tt * 512:
                                 off_al + cc * T + (tt + 1) * 512],
                        start=(cc == 0), stop=(cc == NCC - 1))
                    if cc == NCC - 1:
                        tn = self.T.inc(mm)
                self.V.wait(self.T, tn)
                vn = self.V.inc(nc.vector.tensor_tensor(
                    self.xmid(oc, tt * 512, (tt + 1) * 512), pst[:],
                    self.xmid(oc, tt * 512, (tt + 1) * 512), ALU.add))
                self.done("P" + str(tt % 2), self.V, vn)
            self.done("WCOL", self.V)
            self.done("WCOLd", self.V)
        self.done("SCRv", self.V)
        self.done("SCRv_w", self.V)
        self.done("XN", self.T)
        self.barrier()
        return

    def emit_l0_ffn(self, b, wqkv, q_scale, k_scale, wo, ln_g, ln_b, w1, b1,
                    w2, b2, attb, xtabs=None):
        self.xtabs = xtabs or self.xtabs
        nc = self.nc
        HID = 2730
        NJ = 22
        off_v = 0
        off_q = 8320
        off_k = 10368
        off_sw = 12416
        off_t2 = 14464
        off_pt = 16512
        off_ao = 20608
        off_tab = 22656
        off_hf = 0
        off_al = 12416

        def xsh(cc, lo, hi, base=XOFF):
            d = 1 if cc >= 4 else 0
            return self.XN[:, cc * CS + base + lo - d: cc * CS + base + hi - d]

        def xshm(cc, lo, hi):
            d = 1 if cc >= 4 else 0
            return self.XMID[:, cc * CS + XOFF + lo - d: cc * CS + XOFF + hi - d]

        for l in range(2):
            wqkv_l = wqkv[l]
            wo_l = wo[l]
            w1_l = w1[l]
            w2_l = w2[l]
            qs_c = self.BCOL[:, 4:5]
            ks_c = self.BCOL[:, 5:6]
            self.use(self.D, "BCOL45")
            for (src, dstc) in ((q_scale, qs_c), (k_scale, ks_c)):
                self.ss_bcol.dma(dstc[0:64, :], src[l].unsqueeze(1))
                self.ss_bcol.dma(dstc[64:128, :], src[l].unsqueeze(1))
            qs_s = self.BCOL[:, 6:7]
            ks_s = self.BCOL[:, 7:8]
            for (src, dstc) in ((q_scale, qs_s), (k_scale, ks_s)):
                sl = src[l]
                self.ss_bcol.dma(dstc[0:32, :], sl[32:64].unsqueeze(1))
                self.ss_bcol.dma(dstc[32:64, :], sl[0:32].unsqueeze(1))
                self.ss_bcol.dma(dstc[64:96, :], sl[32:64].unsqueeze(1))
                self.ss_bcol.dma(dstc[96:128, :], sl[0:32].unsqueeze(1))
            bc_d = self.ss_bcol.n
            self.use(self.D, "SCRtab")
            tl = 0
            for (ti, hcap) in enumerate(self.xtabs):
                tl = self.ss_tab.dma(self.scr(off_tab + ti * T, 0, T), hcap)
            self.swait(self.V, self.ss_tab, tl)
            self.swait(self.V, self.ss_bcol, bc_d)
            vv = nc.vector
            vn = 0
            for (ti, col) in enumerate((qs_c, qs_s, ks_c, ks_s)):
                vn = self.V.inc(vv.tensor_scalar(
                    self.scr(off_tab + ti * T, 0, T),
                    self.scr(off_tab + ti * T, 0, T),
                    col, None, ALU.mult))
            tab_v = vn

            self.use(self.V, "SCRv")
            v_ones = self.V.inc(nc.vector.memset(self.SCR[:, off_v:off_v + 8320], 1.0))
            wsl = self.WCOL[:, 0:4096]
            self.use(self.D, "WCOL")
            wld = self.load_wcol2(self.stgw["b_wv" + str(l)], 0, wsl, self.st_wa)
            self.swait(self.T, self.st_wa, wld)
            self.swait(self.T, self.sg_xn, self.x_g)
            for tk in range(T // P):
                pst = (self.P0, self.P1)[tk % 2]
                self.use(self.T, "P" + str(tk % 2))
                tn = 0
                for cc in range(NCC):
                    mm = nc.tensor.matmul(pst[:], xsh(cc, tk * P, (tk + 1) * P),
                                          wsl[:, cc * 512:(cc + 1) * 512],
                                          start=(cc == 0), stop=(cc == NCC - 1))
                    if cc == NCC - 1:
                        tn = self.T.inc(mm)
                self.V.wait(self.T, tn)
                if tk == 0:
                    self.V.wait(self.V, v_ones)
                vn = 0
                for h in range(8):
                    slot = off_v + tk * 520 + h * 65
                    cp = nc.vector.tensor_copy(out=self.SCR[:, slot:slot + 64],
                                               in_=pst[:, h * 64:(h + 1) * 64])
                    if h == 7:
                        vn = self.V.inc(cp)
                self.done("P" + str(tk % 2), self.V, vn)
            self.done("WCOL", self.T)
            v_ready = self.V.n

            self.use(self.D, "WCOL", "WCa", "WCb")
            wsts = (self.st_wa, self.st_wb)
            wlds = [0, 0]
            self.load_wcol2(self.stgw["b_wqk" + str(l)], 0,
                            self.WCOL[:, 0:1024], wsts[0])
            wlds[0] = self.load_wcol2(self.stgw["b_wqk" + str(l)], 4,
                                      self.WCOL[:, 1024:2048], wsts[0])
            for hp in range(4):
                buf = hp % 2
                wq_c = self.WCOL[:, buf * 2048: buf * 2048 + 1024]
                wk_c = self.WCOL[:, buf * 2048 + 1024: buf * 2048 + 2048]
                if hp + 1 < 4:
                    nb = 1 - buf
                    self.use(self.D, "WC" + "ab"[nb])
                    self.load_wcol2(self.stgw["b_wqk" + str(l)], hp + 1,
                                    self.WCOL[:, nb * 2048: nb * 2048 + 1024],
                                    wsts[nb])
                    wlds[nb] = self.load_wcol2(
                        self.stgw["b_wqk" + str(l)], 5 + hp,
                        self.WCOL[:, nb * 2048 + 1024: nb * 2048 + 2048],
                        wsts[nb])
                wld = wlds[buf]
                vv = nc.vector
                for (wc, offd, colc, cols, tabC, tabS) in (
                        (wq_c, off_q, qs_c, qs_s, 0, 1),
                        (wk_c, off_k, ks_c, ks_s, 2, 3)):
                    for tt in range(4):
                        lo, hi = tt * 512, (tt + 1) * 512
                        pst = (self.P0, self.P1)[tt % 2]
                        self.use(self.T, "P" + str(tt % 2))
                        self.swait(self.T, wsts[buf], wld)
                        tn = 0
                        for cc in range(NCC):
                            mm = nc.tensor.matmul(
                                pst[:], wc[:, cc * P:(cc + 1) * P],
                                xsh(cc, lo, hi),
                                start=(cc == 0), stop=(cc == NCC - 1))
                            if cc == NCC - 1:
                                tn = self.T.inc(mm)
                        self.S.wait(self.T, tn)
                        self.use(self.S, "X2" + str(tt % 2))
                        x2h = self.X2[:, (tt % 2) * 512:(tt % 2) * 512 + 512]
                        sn = self.S.inc(nc.scalar.activation(
                            out=x2h, in_=pst[:], func=AF.Square))
                        self.use(self.S, "SCRq" if offd == off_q else "SCRk")
                        sn2 = self.S.inc(nc.scalar.copy(
                            out=self.scr(offd, lo, hi), in_=pst[:]))
                        self.done("P" + str(tt % 2), self.S, sn2)
                        self.T.wait(self.S, sn)
                        self.use(self.T, "P6")
                        tn = self.T.inc(nc.tensor.matmul(
                            self.P6[0:2, :], self.BLK2[:, 0:2], x2h,
                            start=True, stop=True))
                        self.done("X2" + str(tt % 2), self.T, tn)
                        self.V.wait(self.T, tn)
                        self.use(self.V, "RW6")
                        r2 = self.P6[0:2, :]
                        vn = self.V.inc(nc.vector.tensor_scalar(
                            self.TMP[0:2, 0:512], r2, 1e-24, None, ALU.max))
                        self.S.wait(self.V, vn)
                        nc.scalar.activation(out=self.TMP[0:2, 0:512],
                                             in_=self.TMP[0:2, 0:512], func=AF.Ln)
                        self.S.drain()
                        sn = self.S.inc(nc.scalar.activation(
                            out=self.TMP[0:2, 0:512], in_=self.TMP[0:2, 0:512],
                            func=AF.Exp, scale=-0.5))
                        self.T.wait(self.S, sn)
                        self.use(self.T, "P2")
                        t_bc = self.T.inc(nc.tensor.matmul(
                            self.P2[:], self.BLK2T[0:2, :],
                            self.TMP[0:2, 0:512], start=True, stop=True))
                        self.done("RW6", self.T, t_bc)
                        self.V.wait(self.T, t_bc)
                        vn = self.V.inc(vv.tensor_tensor(
                            self.scr(offd, lo, hi), self.scr(offd, lo, hi),
                            self.P2[:], ALU.mult))
                        self.done("P6", self.V, vn)
                        self.done("P2", self.V, vn)
                    self.V.wait(self.V, tab_v)
                    self.use(self.V, "SCRsw")
                    for (d0, s0) in ((0, 32), (32, 0), (64, 96), (96, 64)):
                        vv.tensor_copy(out=self.scr(off_sw, 0, T, d0, d0 + 32),
                                       in_=self.scr(offd, 0, T, s0, s0 + 32))
                    vv.drain()
                    vv.tensor_tensor(self.scr(offd, 0, T), self.scr(offd, 0, T),
                                     self.scr(off_tab + tabC * T, 0, T),
                                     ALU.mult)
                    vv.tensor_tensor(self.scr(off_sw, 0, T),
                                     self.scr(off_sw, 0, T),
                                     self.scr(off_tab + tabS * T, 0, T),
                                     ALU.mult)
                    vv.drain()
                    vn = self.V.inc(vv.tensor_tensor(
                        self.scr(offd, 0, T), self.scr(offd, 0, T),
                        self.scr(off_sw, 0, T), ALU.add))
                    vv.drain()
                    self.done("SCRsw", self.V, vn)
                rope_v = self.V.n
                self.done("WC" + "ab"[buf], self.V, rope_v)

                for h2 in range(2):
                    h = hp * 2 + h2
                    pbase = h2 * 64
                    ps_o = (self.P4, self.P5)[h2]
                    for qt in range(4):
                        kcs = [kc for kc in range(max(0, 4 * (qt - 1)), 4 * qt + 4)]
                        for ki, kc in enumerate(kcs):
                            dlt = 128 * kc - 512 * qt
                            lo = max(0, dlt)
                            hi = min(512, dlt + 640)
                            pss = (self.P0, self.P1)[ki % 2]
                            self.use(self.T, "P" + str(ki % 2))
                            self.T.wait(self.V, rope_v)
                            tn = self.T.inc(nc.tensor.matmul(
                                pss[:, lo:hi],
                                self.scr(off_k, kc * P, (kc + 1) * P, pbase, pbase + 64),
                                self.scr(off_q, qt * 512 + lo, qt * 512 + hi,
                                         pbase, pbase + 64),
                                start=True, stop=True))
                            self.S.wait(self.T, tn)
                            sn = self.S.inc(nc.scalar.activation(
                                out=self.scr(off_pt, ki * 512 + lo, ki * 512 + hi),
                                in_=pss[:, lo:hi], func=AF.Exp, scale=8.0))
                            self.done("P" + str(ki % 2), self.S, sn)
                            self.V.wait(self.S, sn)
                            vv = nc.vector
                            if dlt >= 0:
                                self.V.inc(vv.tensor_tensor(
                                    self.scr(off_pt, ki * 512 + dlt, ki * 512 + dlt + P),
                                    self.scr(off_pt, ki * 512 + dlt, ki * 512 + dlt + P),
                                    self.MSK[:, 0:P], ALU.mult))
                            else:
                                d2 = dlt + 512
                                self.V.inc(vv.tensor_tensor(
                                    self.scr(off_pt, ki * 512 + d2, ki * 512 + d2 + P),
                                    self.scr(off_pt, ki * 512 + d2, ki * 512 + d2 + P),
                                    self.MSK[:, P:2 * P], ALU.mult))
                        s_exp = self.S.n
                        v_msk = self.V.n
                        self.T.wait(self.S, s_exp)
                        self.T.wait(self.V, v_msk)
                        self.use(self.T, ("P4", "P5")[h2])
                        if hp == 0 and h2 == 0 and qt == 0:
                            self.T.wait(self.V, v_ready)
                        t_pv = 0
                        kio = sorted(range(len(kcs)),
                                     key=lambda ki_: (not (
                                         max(0, 128 * kcs[ki_] - 512 * qt) == 0
                                         and min(512, 128 * kcs[ki_] - 512 * qt + 640) == 512), ki_))
                        for oi, ki in enumerate(kio):
                            kc = kcs[ki]
                            dlt = 128 * kc - 512 * qt
                            lo = max(0, dlt)
                            hi = min(512, dlt + 640)
                            slot = off_v + kc * 520 + h * 65
                            mm = nc.tensor.matmul(
                                ps_o[0:65, lo:hi], self.SCR[:, slot:slot + 65],
                                self.scr(off_pt, ki * 512 + lo, ki * 512 + hi),
                                start=(oi == 0), stop=(oi == len(kcs) - 1),
                                skip_group_check=True)
                            if oi == len(kcs) - 1:
                                t_pv = self.T.inc(mm)
                        self.done("SCRpt", self.T, t_pv)
                        self.V.wait(self.T, t_pv)
                        rl_row = self.RW[0:1, 2560:3072]
                        self.use(self.V, "RW6")
                        vn = self.V.inc(nc.vector.reciprocal(rl_row, ps_o[64:65, :]))
                        self.use(self.T, "P6")
                        self.T.wait(self.V, vn)
                        t_bc = self.T.inc(nc.tensor.matmul(
                            self.P6[0:64, :], self.RW[0:1, 0:64], rl_row,
                            start=True, stop=True))
                        self.done("RW6", self.T, t_bc)
                        self.use(self.S, "SCRt2")
                        self.S.wait(self.T, t_bc)
                        sn = self.S.inc(nc.scalar.copy(
                            out=self.scr(off_t2, 0, 512, 0, 64), in_=ps_o[0:64, :]))
                        self.done(("P4", "P5")[h2], self.S, sn)
                        self.V.wait(self.S, sn)
                        self.use(self.V, "SCRao")
                        vn = self.V.inc(nc.vector.tensor_tensor(
                            self.scr(off_ao, qt * 512, (qt + 1) * 512, pbase, pbase + 64),
                            self.scr(off_t2, 0, 512, 0, 64), self.P6[0:64, :], ALU.mult))
                        self.done("P6", self.V, vn)
                        self.done("SCRt2", self.V, vn)
                self.D.wait(self.V, self.V.n)
                dn = self.st_ao.dma(attb[b, hp], self.scr(off_ao, 0, T))
                self.done("SCRao", self.st_ao, dn)
                self.done("SCRq", self.st_ao, dn)
                self.done("SCRk", self.st_ao, dn)

            self.done("WCOL", self.V)
            self.swait(self.D, self.st_ao)
            self.use(self.D, "SCRal")
            self.D.wait(self.T, self.T.n)
            al = 0
            for cc in range(4):
                al = self.st_al.dma(
                    self.SCR[:, off_al + cc * T: off_al + (cc + 1) * T], attb[b, cc])
            self.use(self.D, "WCOL", "WCa", "WCb")
            wsts = (self.st_wa, self.st_wb)
            wlds = [0, 0]
            wlds[0] = self.load_wcol2(self.stgw["b_wo" + str(l)], 0,
                                      self.WCOL[:, 0:512], wsts[0])
            for oc in range(NCC):
                buf = oc % 2
                wsl = self.WCOL[:, buf * 512: buf * 512 + 512]
                if oc + 1 < NCC:
                    nb = 1 - buf
                    self.use(self.D, "WC" + "ab"[nb])
                    wlds[nb] = self.load_wcol2(
                        self.stgw["b_wo" + str(l)], oc + 1,
                        self.WCOL[:, nb * 512: nb * 512 + 512], wsts[nb])
                wld = wlds[buf]
                for tt in range(4):
                    pst = (self.P0, self.P1)[tt % 2]
                    self.use(self.T, "P" + str(tt % 2))
                    self.swait(self.T, wsts[buf], wld)
                    self.swait(self.T, self.st_al, al)
                    tn = 0
                    for cc in range(4):
                        mm = nc.tensor.matmul(
                            pst[:], wsl[:, cc * P:(cc + 1) * P],
                            self.SCR[:, off_al + cc * T + tt * 512:
                                     off_al + cc * T + (tt + 1) * 512],
                            start=(cc == 0), stop=(cc == 3))
                        if cc == 3:
                            tn = self.T.inc(mm)
                    self.V.wait(self.T, tn)
                    vn = self.V.inc(nc.vector.tensor_tensor(
                        self.xmid(oc, tt * 512, (tt + 1) * 512), pst[:],
                        self.xmid(oc, tt * 512, (tt + 1) * 512), ALU.add))
                    self.done("P" + str(tt % 2), self.V, vn)
                self.done("WC" + "ab"[buf], self.V)
            self.done("WCOL", self.V)
            self.done("SCRal", self.V)
            self.done("SCRv", self.V)
            self.dump_dbg("xa%d" % l, b)

            self.use(self.D, "BCOLg")
            for cc in range(NCC):
                self.ss_bcol.dma(self.BCOL[:, 8 + cc:9 + cc],
                                 ln_g[l, cc * P:(cc + 1) * P].unsqueeze(1))
                self.ss_bcol.dma(self.BCOL2[:, cc:cc + 1],
                                 ln_b[l, cc * P:(cc + 1) * P].unsqueeze(1))
            g_d = self.ss_bcol.n
            self.swait(self.S, self.ss_bcol, g_d)
            self.barrier()
            self.emit_ln(srcf=xshm, dstf=None,
                         dstb=self.xn,
                         gains=(lambda cc: self.BCOL[:, 8 + cc:9 + cc],
                                lambda cc: self.BCOL2[:, cc:cc + 1]))
            self.done("BCOLg", self.S)
            self.barrier()
            self.dump_dbg("hn%d" % l, b, src="xn")

            b1a_c = self.BCOL[:, 0:1]
            b1g_c = self.BCOL[:, 1:2]
            b2_c = self.BCOL[:, 2:3]
            for tt in range(4):
                lo, hi = tt * 512, (tt + 1) * 512
                self.use(self.D, "WCOL", "WCa", "WCb")
                wsts = (self.st_wa, self.st_wb)
                wlds = [0, 0]
                self.load_wcol2(self.stgw["b_w1a" + str(l)], 0,
                                self.WCOL[:, 0:1024], wsts[0])
                wlds[0] = self.load_wcol2(self.stgw["b_w1g" + str(l)], 0,
                                          self.WCOL[:, 1024:2048], wsts[0])
                for j in range(NJ):
                    w = min(P, HID - j * P)
                    buf = j % 2
                    wa = self.WCOL[:, buf * 2048: buf * 2048 + 1024]
                    wg = self.WCOL[:, buf * 2048 + 1024: buf * 2048 + 2048]
                    if j + 1 < NJ:
                        nb = 1 - buf
                        self.use(self.D, "WC" + "ab"[nb])
                        self.load_wcol2(self.stgw["b_w1a" + str(l)], j + 1,
                                        self.WCOL[:, nb * 2048: nb * 2048 + 1024],
                                        wsts[nb])
                        wlds[nb] = self.load_wcol2(
                            self.stgw["b_w1g" + str(l)], j + 1,
                            self.WCOL[:, nb * 2048 + 1024: nb * 2048 + 2048],
                            wsts[nb])
                    wld = wlds[buf]
                    self.use(self.D, "BCOL01")
                    self.ss_bcol.dma(b1a_c[0:w, :], b1[l, j * P: j * P + w].unsqueeze(1))
                    self.ss_bcol.dma(b1g_c[0:w, :],
                                     b1[l, HID + j * P: HID + j * P + w].unsqueeze(1))
                    bd = self.ss_bcol.n
                    for (wc, bcol, half) in ((wa, b1a_c, 0), (wg, b1g_c, 1)):
                        pst = (self.P0, self.P1)[half]
                        self.use(self.T, "P" + str(half))
                        self.swait(self.T, wsts[buf], wld)
                        tn = 0
                        for cc in range(NCC):
                            mm = nc.tensor.matmul(
                                pst[0:w, :], wc[:, cc * P: cc * P + w],
                                self.xn(cc, lo, hi),
                                start=(cc == 0), stop=(cc == NCC - 1))
                            if cc == NCC - 1:
                                tn = self.T.inc(mm)
                        self.S.wait(self.T, tn)
                        self.swait(self.S, self.ss_bcol, bd)
                        self.use(self.S, "X2" + str(half))
                        sn = self.S.inc(nc.scalar.activation(
                            out=self.X2[0:w, half * 512:half * 512 + 512],
                            in_=pst[0:w, :],
                            func=(AF.Identity if half == 0 else AF.Gelu),
                            bias=bcol[0:w, :], scale=1.0))
                        self.done("P" + str(half), self.S, sn)
                    s_ag = self.S.n
                    self.done("WC" + "ab"[buf], self.S, s_ag)
                    self.done("BCOL01", self.S, s_ag)
                    self.V.wait(self.S, s_ag)
                    self.use(self.V, "SCRhf")
                    vn = self.V.inc(nc.vector.tensor_tensor(
                        self.scr(off_hf, j * 512, j * 512 + 512, 0, w),
                        self.X2[0:w, 0:512], self.X2[0:w, 512:1024], ALU.mult))
                    self.done("X20", self.V, vn)
                    self.done("X21", self.V, vn)
                hf_v = self.V.n
                HW2 = 11 * P
                self.use(self.D, "WCOL", "WCa", "WCb")
                wsts = (self.st_wa, self.st_wb)
                wlds = [0, 0]
                wlds[0] = self.load_wcol2_part(
                    self.stgw["b_w2" + str(l)], 0, 0, HW2,
                    self.WCOL[:, 0:HW2], wsts[0])
                for oc in range(NCC):
                    self.use(self.D, "BCOL2")
                    self.ss_b2.dma(b2_c, b2[l, oc * P:(oc + 1) * P].unsqueeze(1))
                    b2d = self.ss_b2.n
                    pst = (self.P0, self.P1)[oc % 2]
                    self.use(self.T, "P" + str(oc % 2))
                    self.T.wait(self.V, hf_v)
                    tn = 0
                    for hv in range(2):
                        idx = oc * 2 + hv
                        buf = idx % 2
                        if idx + 1 < 16:
                            nb = 1 - buf
                            noc, nhv = divmod(idx + 1, 2)
                            self.use(self.D, "WC" + "ab"[nb])
                            wlds[nb] = self.load_wcol2_part(
                                self.stgw["b_w2" + str(l)], noc, nhv * HW2, HW2,
                                self.WCOL[:, nb * HW2: nb * HW2 + HW2], wsts[nb])
                        wsl = self.WCOL[:, buf * HW2: buf * HW2 + HW2]
                        self.swait(self.T, wsts[buf], wlds[buf])
                        tg = 0
                        for jl in range(11):
                            j = hv * 11 + jl
                            w = min(P, HID - j * P)
                            mm = nc.tensor.matmul(
                                pst[:], wsl[0:w, jl * P:(jl + 1) * P],
                                self.scr(off_hf, j * 512, j * 512 + 512, 0, w),
                                start=(j == 0), stop=(j == NJ - 1))
                            if jl == 10:
                                tg = self.T.inc(mm)
                        self.done("WC" + "ab"[buf], self.T, tg)
                        if hv == 1:
                            tn = tg
                    self.V.wait(self.T, tn)
                    self.swait(self.V, self.ss_b2, b2d)
                    vn = self.V.inc(nc.vector.scalar_tensor_tensor(
                        out=self.xmid(oc, lo, hi), in0=pst[:], scalar=b2_c,
                        in1=self.xmid(oc, lo, hi), op0=ALU.add, op1=ALU.add))
                    self.done("P" + str(oc % 2), self.V, vn)
                    self.done("BCOL2", self.V)
                self.done("WCOL", self.T)
                self.done("SCRhf", self.T)
            self.barrier()
            vn = 0
            for cc in range(NCC):
                vn = self.V.inc(nc.vector.tensor_copy(out=self.xn(cc), in_=self.xmid(cc)))
            self.done("XN", self.V, vn)
            self.done("SCRtab", self.V, vn)
            self.done("BCOL45", self.V, vn)
            self.barrier()
            self.dump_dbg("x%d" % l, b)
        return

    def emit_l1_ffn_prep(self, fc_logits, pj_logits, fc_s1, fc_s2, pj_s1,
                         pj_s2, fc_mix, pj_mix, iota8, ssel, mixt, mixtp):
        nc = self.nc
        import concourse.mybir as mybir_
        r = lambda i, w=8: self.TMP[0:1, 64 + i * 16: 64 + i * 16 + w]
        self.barrier()
        fl_i32 = self.sbuf("FLI", [1, 320], I32)
        self.FLI = fl_i32
        pos = 0
        self.swait(self.D, self.st_setup)
        iot = self.TMP[0:1, 48:56]
        self.st_tmp.dma(iot, iota8)
        eqrows = {}
        for (lg, tens) in (("fc", fc_logits), ("pj", pj_logits)):
            lrow = r(0)
            self.D.wait(self.V, self.V.n)
            self.D.wait(self.T, self.T.n)
            self.st_tmp.dma(lrow, tens.unsqueeze(0))
            self.swait(self.V, self.st_tmp)
            v = nc.vector
            m1 = r(1, 1)
            v.tensor_reduce(m1, lrow, mybir_.AxisListType.X, ALU.max)
            v.drain()
            eq1 = r(2) if lg == "fc" else r(16)
            v.tensor_scalar(eq1, lrow, m1, None, ALU.is_equal)
            v.drain()
            msk = r(3)
            v.scalar_tensor_tensor(msk, eq1, -1e30, lrow, ALU.mult, ALU.add)
            v.drain()
            m2 = r(4, 1)
            v.tensor_reduce(m2, msk, mybir_.AxisListType.X, ALU.max)
            v.drain()
            eq2 = r(5) if lg == "fc" else r(17)
            v.tensor_scalar(eq2, msk, m2, None, ALU.is_equal)
            v.drain()
            eqrows[lg] = (eq1, eq2)
            d_ = r(6, 1)
            vn = self.V.inc(v.tensor_tensor(d_, m2, m1, ALU.subtract))
            v.drain()
            self.S.wait(self.V, vn)
            e_ = r(8, 1)
            sn = self.S.inc(nc.scalar.activation(out=e_, in_=d_, func=AF.Exp))
            self.S.drain()
            self.V.wait(self.S, sn)
            se = r(9, 1)
            v.tensor_scalar(se, e_, 1.0, None, ALU.add)
            v.drain()
            c1_ = r(10, 1)
            v.reciprocal(c1_, se)
            v.drain()
            c2_ = r(11, 1)
            vn = self.V.inc(v.tensor_scalar(c2_, c1_, -1.0, 1.0, ALU.mult, ALU.add))
            v.drain()
            self.T.wait(self.V, vn)
            self.use(self.T, "P2")
            col0 = 8 if lg == "fc" else 10
            nc.tensor.matmul(self.P2[:, 0:1], self.RW[0:1, 0:P], c1_,
                             start=True, stop=True)
            tn = self.T.inc(nc.tensor.matmul(self.P2[:, 1:2], self.RW[0:1, 0:P],
                                             c2_, start=True, stop=True))
            self.V.wait(self.T, tn)
            v.tensor_copy(out=self.BCOL2[:, col0:col0 + 1], in_=self.P2[:, 0:1])
            v.drain()
            vn = self.V.inc(v.tensor_copy(out=self.BCOL2[:, col0 + 1:col0 + 2],
                                          in_=self.P2[:, 1:2]))
            self.done("P2", self.V, vn)
            pos += 160
        for ti, (lg, ki) in enumerate((("fc", 0), ("fc", 1), ("pj", 0), ("pj", 1))):
            eq = eqrows[lg][ki]
            for sh in range(8):
                self.use(self.T, "P2")
                tn = self.T.inc(nc.tensor.matmul(
                    self.P2[:, 2:3], self.RW[0:1, 0:P], eq[0:1, sh:sh + 1],
                    start=True, stop=True))
                self.V.wait(self.T, tn)
                vn = self.V.inc(nc.vector.tensor_copy(
                    out=self.BCOL2[:, 12 + ti * 8 + sh: 13 + ti * 8 + sh],
                    in_=self.P2[:, 2:3]))
                nc.vector.drain()
                self.done("P2", self.V, vn)
        for ti in range(4):
            shc = self.BCOL2[:, 44 + ti: 45 + ti]
            nc.vector.tensor_scalar(
                shc, self.BCOL2[:, 13 + ti * 8: 14 + ti * 8], 1.0, None,
                ALU.mult)
            nc.vector.drain()
            for bb in range(2, 8):
                op = nc.vector.scalar_tensor_tensor(
                    shc, self.BCOL2[:, 12 + ti * 8 + bb: 13 + ti * 8 + bb],
                    float(bb), shc, ALU.mult, ALU.add)
                if bb == 7:
                    self.V.inc(op)
                nc.vector.drain()
        self.barrier()

        pcol_h = self.din("hc_pcols", [P, 8])
        ocol_h = self.din("hc_ocols", [P, 64])
        self.IDX = self.sbuf("IDXS", [P, 96], I32)
        self.IDI = self.sbuf("IDI", [P, 32], I32)
        self.PCt = self.sbuf("PCt", [P, 8], F32)
        self.OCt = self.sbuf("OCt", [P, 64], F32)
        self.st_setup.dma(self.PCt[:], pcol_h)
        self.st_setup.dma(self.OCt[:], ocol_h)
        self.swait(self.V, self.st_setup)
        v = nc.vector
        iota = self.PCt[:, 0:1]
        fcs = self.TMP[:, 512:513]
        for (ci, ti, R) in ((0, 0, 1024), (1, 1, 1024), (2, 0, 4096),
                            (3, 1, 4096), (4, 2, 4096), (5, 3, 4096),
                            (6, 2, 1024), (7, 3, 1024)):
            shc = self.BCOL2[:, 44 + ti: 45 + ti]
            v.tensor_scalar(fcs, shc, float(R), None, ALU.mult)
            v.drain()
            v.tensor_tensor(fcs, fcs, iota, ALU.add)
            v.drain()
            self.V.inc(v.tensor_copy(out=self.IDX[:, ci:ci + 1], in_=fcs))
            v.drain()
        for (ti_base, n, ocn, OCd, cb, kk, pmc, pdc) in (
                (0, 16, 8, 8, 8, 8.0, 2, 3), (2, 64, 32, 32, 24, 2.0, 4, 1)):
            for ki in range(2):
                ti = ti_base + ki
                shc = self.BCOL2[:, 44 + ti: 45 + ti]
                Ac = self.TMP[:, 512:513]
                Bc = self.TMP[:, 513:514]
                pmod = self.PCt[:, pmc:pmc + 1]
                pdiv = self.PCt[:, pdc:pdc + 1]
                v.tensor_scalar(Ac, pmod, shc, float(n), ALU.subtract,
                                ALU.add)
                v.drain()
                v.tensor_scalar(Bc, Ac, 1.0 / n, -(n - 1) / (2.0 * n),
                                ALU.mult, ALU.add)
                v.drain()
                v.tensor_copy(out=self.IDI[:, 0:1], in_=Bc)
                v.drain()
                v.tensor_copy(out=Bc, in_=self.IDI[:, 0:1])
                v.drain()
                v.scalar_tensor_tensor(Ac, Bc, -float(n), Ac, ALU.mult,
                                       ALU.add)
                v.drain()
                v.tensor_scalar(Ac, Ac, 64.0, None, ALU.mult)
                v.drain()
                v.tensor_scalar(Ac, Ac, pdiv, None, ALU.add)
                v.drain()
                A = self.TMP[:, 576:576 + ocn]
                B = self.TMP[:, 640:640 + ocn]
                v.tensor_scalar(A, self.OCt[:, 0:ocn], kk / 2.0, None,
                                ALU.mult)
                v.drain()
                v.tensor_scalar(A, A, Ac, None, ALU.add)
                v.drain()
                v.tensor_scalar(B, A, 1.0 / 128, -127.0 / 256.0,
                                ALU.mult, ALU.add)
                v.drain()
                v.tensor_copy(out=self.IDI[:, 0:ocn], in_=B)
                v.drain()
                v.tensor_copy(out=B, in_=self.IDI[:, 0:ocn])
                v.drain()
                v.scalar_tensor_tensor(A, B, -128.0, A, ALU.mult, ALU.add)
                v.drain()
                v.tensor_scalar(A, A, float(OCd), None, ALU.mult)
                v.drain()
                v.tensor_tensor(A, A, B, ALU.add)
                v.drain()
                self.V.inc(v.tensor_copy(out=self.IDX[:, cb + ki * ocn:
                                                      cb + ki * ocn + ocn],
                                         in_=A))
                v.drain()
        self.barrier()

        IOA = lambda col: bass.IndirectOffsetOnAxis(ap=col, axis=0)
        sgq = self.sg_w
        for (key, tens, R, ti_base, scaled, colR) in (
                ("s1f", fc_s1, 1024, 0, False, 0),
                ("s2f", fc_s2, 4096, 0, True, 2),
                ("s1p", pj_s1, 4096, 2, False, 4),
                ("s2p", pj_s2, 1024, 2, True, 6)):
            srcv = APc(tens.tensor, tens.offset, [[64, 8 * R], [1, 64]])
            for ki in range(2):
                icol = self.IDX[:, colR + ki: colR + ki + 1]
                ccol = self.BCOL2[:, (8 if ti_base == 0 else 10) + ki:
                                  (9 if ti_base == 0 else 11) + ki]
                for cc in range(R // P):
                    buf = cc % 2
                    xsl = self.XMID[:, buf * 64: buf * 64 + 64]
                    bsl = self.SCR[:, buf * 64: buf * 64 + 64]
                    self.use(self.G, "PXA" + str(buf))
                    g = nc.gpsimd.indirect_dma_start(
                        out=xsl, out_offset=None, in_=srcv,
                        in_offset=IOA(icol), element_offset=cc * P * 64)
                    g.then_inc(sgq.sem, 16)
                    sgq.n += 16
                    self.swait(self.V, sgq)
                    self.use(self.V, "PSB" + str(buf))
                    if scaled:
                        vn = self.V.inc(v.tensor_scalar(bsl, xsl, ccol, None,
                                                        ALU.mult))
                    else:
                        vn = self.V.inc(v.tensor_copy(out=bsl, in_=xsl))
                    self.done("PXA" + str(buf), self.V, vn)
                    self.D.wait(self.V, vn)
                    st = self.st_x.dma(ssel[key][ki, cc * P:(cc + 1) * P, :],
                                       bsl)
                    self.done("PSB" + str(buf), self.st_x, st)
        self.barrier()

        for (lg, mix, NR, W, grp, OCs, scol, cb) in (
                ("fc", fc_mix, 4096, 1024, 4, 8, 2, 8),
                ("pj", pj_mix, 1024, 4096, 1, 32, 6, 24)):
            srcv = APc(mix.tensor, mix.offset, [[W, 8 * NR], [1, W]])
            rowW = grp * P
            for ki in range(2):
                mtp = mixtp[lg + str(ki)]
                nrows = NR * W // rowW
                outv = APc(mtp, 0, [[rowW, nrows], [1, rowW]])
                gi = self.IDX[:, scol + ki: scol + ki + 1]
                for mg in range(NR // P // grp):
                    cvn = 0
                    for u in range(grp):
                        mcc = mg * grp + u
                        slot = u if grp > 1 else (mg % 2)
                        xsl = self.XMID[:, slot * W: (slot + 1) * W]
                        csl = self.SCR[:, slot * W: (slot + 1) * W]
                        self.use(self.G, "MXA%d" % slot)
                        g = nc.gpsimd.indirect_dma_start(
                            out=xsl, out_offset=None, in_=srcv,
                            in_offset=IOA(gi), element_offset=mcc * P * W)
                        g.then_inc(sgq.sem, 16)
                        sgq.n += 16
                        self.swait(self.V, sgq)
                        self.use(self.V, "MCB%d" % slot)
                        cvn = self.V.inc(v.tensor_copy(out=csl, in_=xsl))
                        self.done("MXA%d" % slot, self.V, cvn)
                    for o_ in range(OCs):
                        ps = (self.P2, self.P3)[o_ % 2]
                        self.use(self.T, "P" + str(2 + o_ % 2))
                        self.T.wait(self.V, cvn)
                        tn = 0
                        for u in range(grp):
                            slot = u if grp > 1 else (mg % 2)
                            csl = self.SCR[:, slot * W: (slot + 1) * W]
                            mm = nc.tensor.matmul(
                                ps[:, u * P:(u + 1) * P],
                                csl[:, o_ * P:(o_ + 1) * P], self.IDNB[:],
                                start=True, stop=True)
                            if u == grp - 1:
                                tn = self.T.inc(mm)
                        self.V.wait(self.T, tn)
                        sslot = o_ % 2
                        stg = self.SCR[:, 8192 + sslot * 512:
                                       8192 + sslot * 512 + rowW]
                        self.use(self.V, "MST%d" % sslot)
                        svn = self.V.inc(v.tensor_copy(out=stg,
                                                       in_=ps[:, 0:rowW]))
                        self.done("P" + str(2 + o_ % 2), self.V, svn)
                        self.G.wait(self.V, svn)
                        sc = nc.gpsimd.indirect_dma_start(
                            out=outv,
                            out_offset=IOA(self.IDX[:, cb + ki * OCs + o_:
                                                    cb + ki * OCs + o_ + 1]),
                            in_=stg, in_offset=None,
                            element_offset=mg * P * OCs * rowW)
                        sc.then_inc(sgq.sem, 16)
                        sgq.n += 16
                        self.done("MST%d" % sslot, sgq)
                    for u in range(grp):
                        slot = u if grp > 1 else (mg % 2)
                        self.done("MCB%d" % slot, self.T, tn)
        self.barrier()
        self.V.inc(nc.vector.memset(self.XMID[:], 0.0))
        self.barrier()
        return

    def load_flags(self, tag):
        import concourse.mybir as mybir__
        gidx = 0 if tag.startswith("fc") else 1
        ki = int(tag[-1])
        base = gidx * 160
        vals = [self.nc.values_load(
            self.FLI[0:1, base + ki * 8 + bb: base + ki * 8 + bb + 1],
            min_val=0, max_val=1, skip_runtime_bounds_check=True)
            for bb in range(8)]
        return vals

    def load_idx(self, tag):
        gidx = 0 if tag.startswith("fc") else 1
        ki = int(tag[-1])
        base = gidx * 160
        return self.nc.values_load(self.FLI[0:1, base + 16 + ki: base + 17 + ki],
                                   min_val=0, max_val=7,
                                   skip_runtime_bounds_check=True)

    def free_flags(self, vals):
        for v in vals:
            for h in v.val.handles:
                en = str(h.engine).split(".")[-1]
                eng = {"Pool": self.nc.gpsimd, "SP": self.nc.sync,
                       "PE": self.nc.tensor, "DVE": self.nc.vector,
                       "Activation": self.nc.scalar}.get(en)
                if eng is None:
                    continue
                try:
                    eng.free_register(h)
                except Exception:
                    pass

    def emit_rms(self, eps=1e-6):
        nc = self.nc
        ps_sq = self.P4[64:65, :]
        r_row = self.row(2)
        for tt in range(T // 512):
            lo, hi = tt * 512, (tt + 1) * 512
            self.use(self.T, "P4")
            for cc in range(NCC):
                xh = self.X2[:, (cc % 2) * 512:(cc % 2) * 512 + 512]
                self.use(self.S, "X2" + str(cc % 2))
                sn = self.S.inc(nc.scalar.activation(
                    out=xh, in_=self.xmid(cc, lo, hi), func=AF.Square))
                self.T.wait(self.S, sn)
                tn = self.T.inc(nc.tensor.matmul(
                    ps_sq, self.OCOLB[:, 0:1], xh,
                    start=(cc == 0), stop=(cc == NCC - 1)))
                self.done("X2" + str(cc % 2), self.T, tn)
            self.V.wait(self.T, self.T.n)
            self.use(self.V, "LNrows")
            v = nc.vector
            vn = self.V.inc(v.tensor_scalar(r_row, ps_sq, 1.0 / D, float(eps),
                                            ALU.mult, ALU.add))
            self.done("P4", self.V, vn)
            self.S.wait(self.V, vn)
            nc.scalar.activation(out=r_row, in_=r_row, func=AF.Ln)
            self.S.drain()
            sn = self.S.inc(nc.scalar.activation(out=r_row, in_=r_row,
                                                 func=AF.Exp, scale=-0.5))
            self.use(self.T, "P0")
            self.T.wait(self.S, sn)
            t_bc = self.T.inc(nc.tensor.matmul(
                self.P0[:], self.RW[0:1, 0:P], r_row, start=True, stop=True))
            self.done("LNrows", self.T, t_bc)
            self.V.wait(self.T, t_bc)
            vn = 0
            for cc in range(NCC):
                vn = self.V.inc(v.tensor_tensor(
                    self.xn(cc, lo, hi), self.xmid(cc, lo, hi), self.P0[:],
                    ALU.mult))
            self.done("P0", self.V, vn)
        self.done("XN", self.V)
        return

    def _l1f_normrope(self, offd, gcol, eps=1e-6):
        nc = self.nc
        for tt in range(4):
            lo, hi = tt * 512, (tt + 1) * 512
            self.use(self.S, "X20")
            sn = self.S.inc(nc.scalar.activation(
                out=self.X2[:, 0:512], in_=self.scr(offd, lo, hi), func=AF.Square))
            self.T.wait(self.S, sn)
            self.use(self.T, "P6")
            tn = self.T.inc(nc.tensor.matmul(
                self.P6[0:2, :], self.BLK2[:, 0:2], self.X2[:, 0:512],
                start=True, stop=True))
            self.done("X20", self.T, tn)
            self.V.wait(self.T, tn)
            self.use(self.V, "RW6")
            v = nc.vector
            vn = self.V.inc(v.tensor_scalar(self.TMP[0:2, 0:512], self.P6[0:2, :],
                                            1.0 / 64, float(eps), ALU.mult,
                                            ALU.add))
            self.S.wait(self.V, vn)
            nc.scalar.activation(out=self.TMP[0:2, 0:512], in_=self.TMP[0:2, 0:512],
                                 func=AF.Ln)
            self.S.drain()
            sn = self.S.inc(nc.scalar.activation(
                out=self.TMP[0:2, 0:512], in_=self.TMP[0:2, 0:512],
                func=AF.Exp, scale=-0.5))
            self.V.wait(self.S, sn)
            vn = self.V.inc(v.tensor_scalar(self.TMP[0:2, 0:512],
                                            self.TMP[0:2, 0:512], gcol, None,
                                            ALU.mult))
            self.use(self.T, "P2")
            self.T.wait(self.V, vn)
            t_bc = self.T.inc(nc.tensor.matmul(
                self.P2[:], self.BLK2T[0:2, :], self.TMP[0:2, 0:512],
                start=True, stop=True))
            self.done("RW6", self.T, t_bc)
            self.done("P6", self.T, t_bc)
            self.V.wait(self.T, t_bc)
            self.swait(self.V, self.ss_tab, getattr(self, "tab_d", 0))
            self.use(self.V, "SCRsw2", "SCRt22")
            off_sw = self._l1f_offsw
            off_t2 = self._l1f_offt2
            v.tensor_tensor(self.scr(off_t2, 0, 512), self.scr(offd, lo, hi),
                            self.P2[:], ALU.mult)
            v.drain()
            for (d0, s0) in ((0, 32), (32, 0), (64, 96), (96, 64)):
                v.tensor_copy(out=self.scr(off_sw, 0, 512, d0, d0 + 32),
                              in_=self.scr(off_t2, 0, 512, s0, s0 + 32))
            v.drain()
            v.tensor_tensor(self.scr(off_t2, 0, 512), self.scr(off_t2, 0, 512),
                            self.TAB[:, lo:hi], ALU.mult)
            v.drain()
            v.tensor_tensor(self.scr(off_sw, 0, 512), self.scr(off_sw, 0, 512),
                            self.TAB[:, T + lo: T + hi], ALU.mult)
            v.drain()
            vn = self.V.inc(v.tensor_tensor(
                self.scr(offd, lo, hi), self.scr(off_t2, 0, 512),
                self.scr(off_sw, 0, 512), ALU.add))
            v.drain()
            self.done("P2", self.V, vn)
            self.done("SCRsw2", self.V, vn)
            self.done("SCRt22", self.V, vn)
        return self.V.n

    def emit_l1_ffn(self, b, wq_bf, wk_bf, wv_bf, wo_bf, q_gain, k_gain,
                    ssel, mixtp, attb, hfc_d):
        nc = self.nc
        off_v = 0
        off_k = 4160
        off_kd = 8256
        off_q = 12352
        off_sw = 14400
        off_t2 = 16448
        off_pt = 18496
        off_ao = 26688
        self._l1f_offsw = off_sw
        self._l1f_offt2 = off_t2
        EXPSC = 0.125

        self.emit_rms()
        self.barrier()

        gq_c = self.BCOL[0:2, 12:13]
        for ck in range(2):
            wsl = self.WCOL[:, 0:1024]
            self.use(self.D, "WCOL")
            self.D.wait(self.T, self.T.n)
            for st_ in self.sg_band:
                self.swait(self.D, st_)
            for cc in range(NCC):
                wld = self.ss_wt.dma(
                    wsl[:, cc * P:(cc + 1) * P],
                    wk_bf[ck * P:(ck + 1) * P, cc * P:(cc + 1) * P],
                    transpose=True)
            for tt in range(4):
                pst = (self.P0, self.P1)[tt % 2]
                self.use(self.T, "P" + str(tt % 2))
                self.swait(self.T, self.ss_wt, wld)
                self.swait(self.T, self.sg_xn, self.x_g)
                tn = 0
                for cc in range(NCC):
                    mm = nc.tensor.matmul(
                        pst[:], wsl[:, cc * P:(cc + 1) * P],
                        self.xn(cc, tt * 512, (tt + 1) * 512),
                        start=(cc == 0), stop=(cc == NCC - 1))
                    if cc == NCC - 1:
                        tn = self.T.inc(mm)
                self.S.wait(self.T, tn)
                self.use(self.S, "SCRk")
                sn = self.S.inc(nc.scalar.copy(
                    out=self.scr(off_k + ck * T, tt * 512, (tt + 1) * 512),
                    in_=pst[:]))
                self.S.drain()
                self.done("P" + str(tt % 2), self.S, sn)
            self.use(self.D, "BCOLg2")
            gl = self.ss_bcol.dma(gq_c, k_gain[2 * ck: 2 * ck + 2].unsqueeze(1))
            self.swait(self.V, self.ss_bcol, gl)
            self._l1f_normrope(off_k + ck * T, gq_c)
            self.done("BCOLg2", self.V)
            self.done("SCRk", self.V)
        self.done("WCOL", self.T)
        self.G.wait(self.V, self.V.n)
        kd = 0
        for ck in range(2):
            for (d0, s0) in ((0, 64), (64, 0)):
                kd = self.sg_w.dma(
                    self.scr(off_kd + ck * T, 0, T, d0, d0 + 64),
                    self.scr(off_k + ck * T, 0, T, s0, s0 + 64))
        self.kd_n = kd
        self.use(self.V, "SCRv")
        v_ones = self.V.inc(nc.vector.memset(self.SCR[:, off_v:off_v + 4160], 1.0))
        wsl = self.WCOL[:, 1024 + 0:1024 + 2048]
        self.use(self.D, "WCOLv")
        self.D.wait(self.T, self.T.n)
        for cc in range(NCC):
            wld = self.ss_wt.dma(
                wsl[:, cc * 256:(cc + 1) * 256],
                wv_bf[0:256, cc * P:(cc + 1) * P], transpose=True)
        for tk in range(T // P):
            pst = (self.P0, self.P1)[tk % 2]
            self.use(self.T, "P" + str(tk % 2))
            self.swait(self.T, self.ss_wt, wld)
            tn = 0
            for cc in range(NCC):
                mm = nc.tensor.matmul(
                    pst[:, 0:256], self.xn(cc, tk * P, (tk + 1) * P),
                    wsl[:, cc * 256:(cc + 1) * 256],
                    start=(cc == 0), stop=(cc == NCC - 1))
                if cc == NCC - 1:
                    tn = self.T.inc(mm)
            self.V.wait(self.T, tn)
            if tk == 0:
                self.V.wait(self.V, v_ones)
            vn = 0
            for kvh in range(4):
                slot = off_v + tk * 260 + kvh * 65
                cp = nc.vector.tensor_copy(out=self.SCR[:, slot:slot + 64],
                                           in_=pst[:, kvh * 64:(kvh + 1) * 64])
                if kvh == 3:
                    vn = self.V.inc(cp)
            self.done("P" + str(tk % 2), self.V, vn)
        self.done("WCOLv", self.T)
        v_ready = self.V.n

        wq_bufs = (self.WCOL[:, 0:1024], self.WCOL[:, 3072:4096])
        wq_lds = [0, 0]

        def _load_wq(hp_):
            if hp_ >= NCC:
                return
            buf_ = hp_ % 2
            wsl_ = wq_bufs[buf_]
            self.use(self.D, "WQT" + str(buf_))
            if hp_ == 0:
                self.use(self.D, "WCOL")
            for cc in range(NCC):
                wq_lds[buf_] = self.ss_wt.dma(
                    wsl_[:, cc * P:(cc + 1) * P],
                    wq_bf[hp_ * P:(hp_ + 1) * P, cc * P:(cc + 1) * P],
                    transpose=True)

        _load_wq(0)
        for hp in range(NCC):
            buf = hp % 2
            wsl = wq_bufs[buf]
            tn = 0
            for tt in range(4):
                pst = (self.P0, self.P1)[tt % 2]
                self.use(self.T, "P" + str(tt % 2))
                self.swait(self.T, self.ss_wt, wq_lds[buf])
                for cc in range(NCC):
                    mm = nc.tensor.matmul(
                        pst[:], wsl[:, cc * P:(cc + 1) * P],
                        self.xn(cc, tt * 512, (tt + 1) * 512),
                        start=(cc == 0), stop=(cc == NCC - 1))
                    if cc == NCC - 1:
                        tn = self.T.inc(mm)
                self.S.wait(self.T, tn)
                self.use(self.S, "SCRq")
                sn = self.S.inc(nc.scalar.copy(
                    out=self.scr(off_q, tt * 512, (tt + 1) * 512), in_=pst[:]))
                self.S.drain()
                self.done("P" + str(tt % 2), self.S, sn)
            self.done("WQT" + str(buf), self.T, tn)
            _load_wq(hp + 1)
            self.use(self.D, "BCOLg2")
            gl = self.ss_bcol.dma(gq_c, q_gain[2 * hp: 2 * hp + 2].unsqueeze(1))
            self.swait(self.V, self.ss_bcol, gl)
            rope_v = self._l1f_normrope(off_q, gq_c)
            self.done("BCOLg2", self.V)

            for h2 in range(2):
                h = hp * 2 + h2
                pbase = h2 * 64
                kvh = h // 4
                ktile = off_k if (kvh % 2) == (h % 2) else off_kd
                kcol = ktile + (kvh // 2) * T
                ps_o = (self.P4, self.P5)[h2]
                for qt in range(4):
                    kmax = 4 * qt + 4
                    for kc in range(kmax):
                        dlt = 128 * kc - 512 * qt
                        lo = max(0, dlt)
                        pss = (self.P0, self.P1)[kc % 2]
                        self.use(self.T, "P" + str(kc % 2))
                        self.T.wait(self.V, rope_v)
                        self.swait(self.T, self.sg_w, self.kd_n)
                        tn = self.T.inc(nc.tensor.matmul(
                            pss[:, lo:512],
                            self.scr(kcol, kc * P, (kc + 1) * P, pbase, pbase + 64),
                            self.scr(off_q, qt * 512 + lo, (qt + 1) * 512,
                                     pbase, pbase + 64),
                            start=True, stop=True))
                        self.S.wait(self.T, tn)
                        sn = self.S.inc(nc.scalar.activation(
                            out=self.scr(off_pt, kc * 512 + lo, (kc + 1) * 512),
                            in_=pss[:, lo:512], func=AF.Exp, scale=EXPSC))
                        self.done("P" + str(kc % 2), self.S, sn)
                        if dlt >= 0:
                            self.V.wait(self.S, sn)
                            self.V.inc(nc.vector.tensor_tensor(
                                self.scr(off_pt, kc * 512 + dlt, kc * 512 + dlt + P),
                                self.scr(off_pt, kc * 512 + dlt, kc * 512 + dlt + P),
                                self.MSK[:, 0:P], ALU.mult))
                    s_exp = self.S.n
                    v_msk = self.V.n
                    self.T.wait(self.S, s_exp)
                    self.T.wait(self.V, v_msk)
                    self.use(self.T, ("P4", "P5")[h2])
                    if hp == 0 and h2 == 0 and qt == 0:
                        self.T.wait(self.V, v_ready)
                    t_pv = 0
                    for kc in range(kmax):
                        lo = max(0, 128 * kc - 512 * qt)
                        slot = off_v + kc * 260 + kvh * 65
                        mm = nc.tensor.matmul(
                            ps_o[0:65, lo:512], self.SCR[:, slot:slot + 65],
                            self.scr(off_pt, kc * 512 + lo, (kc + 1) * 512),
                            start=(kc == 0), stop=(kc == kmax - 1))
                        if kc == kmax - 1:
                            t_pv = self.T.inc(mm)
                    self.done("SCRpt", self.T, t_pv)
                    self.V.wait(self.T, t_pv)
                    rl_row = self.row(5)
                    self.use(self.V, "RW6")
                    vn = self.V.inc(nc.vector.reciprocal(rl_row, ps_o[64:65, :]))
                    self.use(self.T, "P6")
                    self.T.wait(self.V, vn)
                    t_bc = self.T.inc(nc.tensor.matmul(
                        self.P6[0:64, :], self.RW[0:1, 0:64], rl_row,
                        start=True, stop=True))
                    self.done("RW6", self.T, t_bc)
                    self.use(self.S, "SCRt22")
                    self.S.wait(self.T, t_bc)
                    sn = self.S.inc(nc.scalar.copy(
                        out=self.scr(off_t2, 0, 512, 0, 64), in_=ps_o[0:64, :]))
                    self.done(("P4", "P5")[h2], self.S, sn)
                    self.V.wait(self.S, sn)
                    self.use(self.V, "SCRao")
                    vn = self.V.inc(nc.vector.tensor_tensor(
                        self.scr(off_ao, qt * 512, (qt + 1) * 512, pbase, pbase + 64),
                        self.scr(off_t2, 0, 512, 0, 64), self.P6[0:64, :], ALU.mult))
                    self.done("P6", self.V, vn)
                    self.done("SCRt22", self.V, vn)
            self.D.wait(self.V, self.V.n)
            dn = self.st_ao.dma(attb[b, hp], self.scr(off_ao, 0, T))
            self.done("SCRao", self.st_ao, dn)
            self.done("SCRq", self.st_ao, dn)

        self.swait(self.D, self.st_ao)
        self.use(self.D, "SCRal")
        self.D.wait(self.T, self.T.n)
        al = 0
        for cc in range(NCC):
            al = self.st_al.dma(self.SCR[:, 4160 + cc * T: 4160 + (cc + 1) * T],
                                attb[b, cc])
        wo_bufs = (self.WCOL[:, 0:1024], self.WCOL[:, 1024:2048])
        wo_lds = [0, 0]

        def _load_wo(oc_):
            if oc_ >= NCC:
                return
            buf_ = oc_ % 2
            wsl_ = wo_bufs[buf_]
            self.use(self.D, "WOT" + str(buf_))
            if oc_ == 0:
                self.use(self.D, "WQT0")
            if oc_ == 1:
                self.use(self.D, "WCOLv")
            for cc in range(NCC):
                wo_lds[buf_] = self.ss_wt.dma(
                    wsl_[:, cc * P:(cc + 1) * P],
                    wo_bf[oc_ * P:(oc_ + 1) * P, cc * P:(cc + 1) * P],
                    transpose=True)

        _load_wo(0)
        _load_wo(1)
        for oc in range(NCC):
            buf = oc % 2
            wsl = wo_bufs[buf]
            tn = 0
            for tt in range(4):
                pst = (self.P0, self.P1)[tt % 2]
                self.use(self.T, "P" + str(tt % 2))
                self.swait(self.T, self.ss_wt, wo_lds[buf])
                self.swait(self.T, self.st_al, al)
                for cc in range(NCC):
                    mm = nc.tensor.matmul(
                        pst[:], wsl[:, cc * P:(cc + 1) * P],
                        self.SCR[:, 4160 + cc * T + tt * 512:
                                 4160 + cc * T + (tt + 1) * 512],
                        start=(cc == 0), stop=(cc == NCC - 1))
                    if cc == NCC - 1:
                        tn = self.T.inc(mm)
                self.V.wait(self.T, tn)
                vn = self.V.inc(nc.vector.tensor_tensor(
                    self.xmid(oc, tt * 512, (tt + 1) * 512), pst[:],
                    self.xmid(oc, tt * 512, (tt + 1) * 512), ALU.add))
                self.done("P" + str(tt % 2), self.V, vn)
            self.done("WOT" + str(buf), self.T, tn)
            _load_wo(oc + 2)
        self.done("SCRal", self.V)
        self.done("SCRk", self.V)
        self.done("SCRv", self.V)
        self.barrier()
        self.dump_dbg("l1fxa", b)

        self.emit_rms()
        self.barrier()

        o_s1f = 0
        o_s2f = 1024
        o_s1p = 5120
        o_s2p = 9216
        o_hb1 = 10240
        o_hm = 26688
        o_hst = 27200
        ldn = 0
        for (key, off0, ncc_) in (("s1f", o_s1f, 8), ("s2f", o_s2f, 32),
                                  ("s1p", o_s1p, 32), ("s2p", o_s2p, 8)):
            for ki in range(2):
                base = off0 + ki * ncc_ * 64
                for cc in range(ncc_):
                    ldn = self.sg_w.dma(
                        self.SCR[:, base + cc * 64: base + (cc + 1) * 64],
                        ssel[key][ki, cc * P:(cc + 1) * P, :])
        self.G.wait(self.V, 0)
        bd_v = self.V.n
        self.swait(self.T, self.sg_w, ldn)

        for tt in range(4):
            lo, hi = tt * 512, (tt + 1) * 512
            for ki in range(2):
                for p_ in range(NCC):
                    pst = (self.P0, self.P1)[p_ % 2]
                    self.use(self.T, "P" + str(p_ % 2))
                    self.T.wait(self.V, bd_v)
                    base = o_s1f + ki * 512 + p_ * 64
                    nc.tensor.matmul(
                        pst[0:64, :], self.SCR[0:64, base: base + 64],
                        self.xn(p_, lo, hi, )[0:64, :],
                        start=True, stop=True, tile_position=(0, 0))
                    tn = self.T.inc(nc.tensor.matmul(
                        pst[64:128, :], self.SCR[64:128, base: base + 64],
                        self.xn(p_, lo, hi)[64:128, :],
                        start=True, stop=True, tile_position=(64, 64)))
                    self.S.wait(self.T, tn)
                    self.use(self.S, "SCRhb")
                    sn = self.S.inc(nc.scalar.copy(
                        out=self.scr(o_hb1 + ki * 4096, p_ * 512, (p_ + 1) * 512),
                        in_=pst[:]))
                    self.done("P" + str(p_ % 2), self.S, sn)
            hb_s = self.S.n
            wlds_fc = [0, 0]
            for m in range(32):
                g4, mi = divmod(m, 4)
                for ki in range(2):
                    psm = (self.P2, self.P3)[ki]
                    wsl = (self.WCOL[:, 0:4096],
                           self.SCR[:, 18432:22528])[ki]
                    stm = (self.st_al, self.st_mx)[ki]
                    if mi == 0:
                        self.use(self.D, "WCOLm" + str(ki))
                        wlds_fc[ki] = stm.dma(
                            wsl, APc(mixtp["fc" + str(ki)], g4 * P * 8 * 512,
                                     [[8 * 512, P], [1, 8 * 512]]))
                    self.use(self.T, "P" + str(2 + ki))
                    self.swait(self.T, stm, wlds_fc[ki])
                    self.T.wait(self.S, hb_s)
                    tn = 0
                    for o_ in range(NCC):
                        mm = nc.tensor.matmul(
                            psm[:], wsl[:, o_ * 512 + mi * P:
                                         o_ * 512 + (mi + 1) * P],
                            self.scr(o_hb1 + ki * 4096, o_ * 512, (o_ + 1) * 512),
                            start=(o_ == 0), stop=(o_ == NCC - 1))
                        if o_ == NCC - 1:
                            tn = self.T.inc(mm)
                    if mi == 3:
                        self.done("WCOLm" + str(ki), self.T, tn)
                    self.V.wait(self.T, tn)
                    self.use(self.V, "X2" + str(ki))
                    vn = self.V.inc(nc.vector.tensor_copy(
                        out=self.X2[:, ki * 512:(ki + 1) * 512], in_=psm[:]))
                    self.done("P" + str(2 + ki), self.V, vn)
                hm_v = self.V.n
                self.use(self.T, "P0")
                self.T.wait(self.V, hm_v)
                tn = 0
                for ki in range(2):
                    base = o_s2f + ki * 2048 + m * 64
                    nc.tensor.matmul(
                        self.P0[0:64, :], self.SCR[0:64, base: base + 64],
                        self.X2[0:64, ki * 512:(ki + 1) * 512],
                        start=(ki == 0), stop=False, tile_position=(0, 0),
                        skip_group_check=True)
                    tn = self.T.inc(nc.tensor.matmul(
                        self.P0[64:128, :], self.SCR[64:128, base: base + 64],
                        self.X2[64:128, ki * 512:(ki + 1) * 512],
                        start=(ki == 0), stop=(ki == 1), tile_position=(64, 64),
                        skip_group_check=True))
                self.done("X20", self.T, tn)
                self.done("X21", self.T, tn)
                import os as _os3
                if _os3.environ.get("DBGBD2"):
                    self.S.wait(self.T, tn)
                    self.use(self.S, "SCRhm")
                    sn = self.S.inc(nc.scalar.copy(
                        out=self.scr(o_hm, 0, 512), in_=self.P0[:]))
                    self.S.drain()
                    vn = self.V.n
                    self.V.wait(self.S, sn)
                    vn = self.V.tick()
                    self.done("P0", self.V, vn)
                    self.D.wait(self.S, sn)
                    hn_ = self.st_x.dma(hfc_d[tt, m], self.scr(o_hm, 0, 512))
                    self.done("SCRhm", self.st_x, hn_)
                    continue
                if _os3.environ.get("DBGHM"):
                    self.S.wait(self.T, tn)
                    self.use(self.S, "SCRhm")
                    _half = 512 if _os3.environ.get("DBGHM2") else 0
                    sn = self.S.inc(nc.scalar.copy(
                        out=self.scr(o_hm, 0, 512), in_=self.X2[:, _half:_half + 512]))
                    self.S.drain()
                    self.V.wait(self.S, sn)
                    vn = self.V.tick()
                    self.done("P0", self.V, vn)
                    self.D.wait(self.S, sn)
                    hn_ = self.st_x.dma(hfc_d[tt, m], self.scr(o_hm, 0, 512))
                    self.done("SCRhm", self.st_x, hn_)
                    continue
                self.S.wait(self.T, tn)
                self.use(self.S, "SCRhm")
                sn = self.S.inc(nc.scalar.activation(
                    out=self.scr(o_hm, 0, 512), in_=self.P0[:], func=AF.Relu))
                self.V.wait(self.S, sn)
                vn = self.V.inc(nc.vector.tensor_tensor(
                    self.scr(o_hm, 0, 512), self.P0[:], self.scr(o_hm, 0, 512),
                    ALU.mult))
                self.done("P0", self.V, vn)
                self.D.wait(self.V, vn)
                hn_ = self.st_x.dma(hfc_d[tt, m], self.scr(o_hm, 0, 512))
                self.done("SCRhm", self.st_x, hn_)
            self.done("SCRhb", self.T)
        hfc_done = self.st_x.n
        self.barrier()

        import os as _os2
        for ki in range(0 if not _os2.environ.get("SKIP_PJ") else 2, 2):
            for tt in range(4):
                lo, hi = tt * 512, (tt + 1) * 512
                for p_ in range(32):
                    hslot = o_hst + (p_ % 2) * 512
                    self.use(self.D, "HST" + str(p_ % 2))
                    hld = (self.st_tmp, self.st_tmp2)[p_ % 2].dma(
                        self.scr(hslot, 0, 512), hfc_d[tt, p_])
                    pst = (self.P0, self.P1)[p_ % 2]
                    self.use(self.T, "P" + str(p_ % 2))
                    self.swait(self.T, (self.st_tmp, self.st_tmp2)[p_ % 2], hld)
                    self.T.wait(self.V, bd_v)
                    base = o_s1p + ki * 2048 + p_ * 64
                    nc.tensor.matmul(
                        pst[0:64, :], self.SCR[0:64, base: base + 64],
                        self.scr(hslot, 0, 512, 0, 64),
                        start=True, stop=True, tile_position=(0, 0))
                    tn = self.T.inc(nc.tensor.matmul(
                        pst[64:128, :], self.SCR[64:128, base: base + 64],
                        self.scr(hslot, 0, 512, 64, 128),
                        start=True, stop=True, tile_position=(64, 64)))
                    self.done("HST" + str(p_ % 2), self.T, tn)
                    self.S.wait(self.T, tn)
                    self.use(self.S, "SCRhb")
                    sn = self.S.inc(nc.scalar.copy(
                        out=self.scr(o_hb1, p_ * 512, (p_ + 1) * 512), in_=pst[:]))
                    self.done("P" + str(p_ % 2), self.S, sn)
                hb_s = self.S.n
                for m in range(NCC):
                    psm = self.P2
                    wsl = self.WCOL[:, 0:4096]
                    self.use(self.D, "WCOLm0")
                    wld = 0
                    for half in range(2):
                        wld = self.st_al.dma(
                            wsl[:, half * 2048: half * 2048 + 2048],
                            APc(mixtp["pj" + str(ki)],
                                m * P * 32 * P + half * 16 * P,
                                [[32 * P, P], [1, 16 * P]]))
                    self.use(self.T, "P2")
                    self.swait(self.T, self.st_al, wld)
                    self.T.wait(self.S, hb_s)
                    tn = 0
                    for o_ in range(32):
                        mm = nc.tensor.matmul(
                            psm[:], wsl[:, o_ * P:(o_ + 1) * P],
                            self.scr(o_hb1, o_ * 512, (o_ + 1) * 512),
                            start=(o_ == 0), stop=(o_ == 31))
                        if o_ == 31:
                            tn = self.T.inc(mm)
                    self.done("WCOLm0", self.T, tn)
                    self.V.wait(self.T, tn)
                    self.use(self.V, "X20")
                    vn = self.V.inc(nc.vector.tensor_copy(
                        out=self.X2[:, 0:512], in_=psm[:]))
                    self.done("P2", self.V, vn)
                    self.use(self.T, "P3")
                    self.T.wait(self.V, vn)
                    base = o_s2p + ki * 512 + m * 64
                    nc.tensor.matmul(
                        self.P3[0:64, :], self.SCR[0:64, base: base + 64],
                        self.X2[0:64, 0:512],
                        start=True, stop=True, tile_position=(0, 0))
                    tn = self.T.inc(nc.tensor.matmul(
                        self.P3[64:128, :], self.SCR[64:128, base: base + 64],
                        self.X2[64:128, 0:512],
                        start=True, stop=True, tile_position=(64, 64)))
                    self.done("X20", self.T, tn)
                    self.V.wait(self.T, tn)
                    vn = self.V.inc(nc.vector.tensor_tensor(
                        self.xmid(m, lo, hi), self.P3[:],
                        self.xmid(m, lo, hi), ALU.add))
                    self.done("P3", self.V, vn)
                self.done("SCRhb", self.T)
        self.done("XN", self.T)
        self.barrier()
        return


def make_host_consts():
    c = {}
    c["hc_ident"] = np.eye(P, dtype=np.float32)
    i = np.arange(P)[:, None]
    j = np.arange(P)[None, :]
    c["hc_tri_ge"] = (j >= i).astype(ml_dtypes.bfloat16)
    c["hc_tri_le"] = (j <= i).astype(ml_dtypes.bfloat16)
    c["hc_identb"] = np.eye(P).astype(ml_dtypes.bfloat16)
    HS = 64
    half = HS // 2
    inv_freq = 1.0 / 10000.0 ** (np.arange(half, dtype=np.float64) * 2.0 / HS)
    t = np.arange(T, dtype=np.float64)
    ang = t[:, None] * inv_freq[None, :]
    cos = np.cos(ang).T
    sin = np.sin(ang).T
    C = np.concatenate([cos, cos], 0)
    Sp = np.concatenate([-sin, sin], 0)
    c["hc_l0C"] = np.tile(C, (2, 1)).astype(ml_dtypes.bfloat16)
    c["hc_l0S"] = np.tile(Sp, (2, 1)).astype(ml_dtypes.bfloat16)

    HD = 64
    half_idx = np.arange(0, HD, 2, dtype=np.float64)
    inv_freq = 1.0 / 10000.0 ** (half_idx / HD)
    tt = np.arange(T, dtype=np.float64)
    fr = tt[:, None] * inv_freq[None, :]
    base = (half_idx + 0.4 * HD) / (1.4 * HD)
    power = (tt - T // 2) / float(512 // 2)
    scl = base[None, :] ** power[:, None]
    cosf = np.cos(fr)
    sinf = np.sin(fr)
    Cq = (cosf * scl).T
    Sq = (sinf * scl).T
    Ck = (cosf / scl).T
    Sk = (sinf / scl).T
    def half2(a_first, a_second):
        blk = np.concatenate([a_first, a_second], 0)
        return np.tile(blk, (2, 1)).astype(ml_dtypes.bfloat16)
    c["hc_xCq"] = half2(Cq, Cq)
    c["hc_xSq"] = half2(-Sq, Sq)
    c["hc_xCk"] = half2(Ck, Ck)
    c["hc_xSk"] = half2(-Sk, Sk)
    blk2t = np.zeros((2, P), dtype=np.float32)
    blk2t[0, 0:64] = 1.0
    blk2t[1, 64:128] = 1.0
    c["hc_blk2t"] = blk2t
    blk2 = np.zeros((P, 2), dtype=ml_dtypes.bfloat16)
    blk2[0:64, 0] = 1.0
    blk2[64:128, 1] = 1.0
    c["hc_blk2"] = blk2
    c["hc_iota8"] = np.arange(8, dtype=np.float32)[None, :]
    pp = np.arange(P)
    pc = np.zeros((P, 8), dtype=np.float32)
    pc[:, 0] = pp
    pc[:, 1] = pp // 64
    pc[:, 2] = pp % 16
    pc[:, 3] = pp // 16
    pc[:, 4] = pp % 64
    c["hc_pcols"] = pc
    ocm = np.zeros((P, 64), dtype=np.float32)
    ocm[:, 0:32] = (2.0 * np.arange(32, dtype=np.float32))[None, :]
    c["hc_ocols"] = ocm
    inv1 = 1.0 / 10000.0 ** (np.arange(0, 64, 2, dtype=np.float64) / 64)
    fr1 = np.arange(T, dtype=np.float64)[:, None] * inv1[None, :]
    c1 = np.cos(fr1).T
    s1_ = np.sin(fr1).T
    C1 = np.concatenate([c1, c1], 0)
    S1 = np.concatenate([s1_, -s1_], 0)
    c["hc_l1C"] = np.tile(C1, (2, 1)).astype(ml_dtypes.bfloat16)
    c["hc_l1S"] = np.tile(S1, (2, 1)).astype(ml_dtypes.bfloat16)
    return c
_PROGRAMS = {}


def _build_a():
    m = MK()
    xin = m.din("xin", [NB, T, D])
    wq = m.din("L0_attn_Wq", [D, D]); bq = m.din("L0_attn_bq", [D])
    wk = m.din("L0_attn_Wk", [D, D]); bk = m.din("L0_attn_bk", [D])
    wv = m.din("L0_attn_Wv", [D, D]); bv = m.din("L0_attn_bv", [D])
    wp = m.din("L0_attn_Wp", [D, D]); bp = m.din("L0_attn_bp", [D])
    out = m.dout("out", [NB, T, D])
    xfm = m.dint("xfm", [NB, NCC, P, T])
    attb = m.dint("attb", [NB, NCC, P, T], BF16)
    m.alloc()
    m.load_consts()
    t1 = m.din("hc_l0C", [P, T], BF16)
    t2 = m.din("hc_l0S", [P, T], BF16)
    m.load_tab(t1, t2)
    m.emit_ingest(xin, xfm)
    m.barrier()
    for b in range(NB):
        m.load_x(xfm, b)
        m.barrier()
        m.emit_l0_attn(b, wq, bq, wk, bk, wv, bv, wp, bp, attb)
        m.emit_ln(dstf=m.xmid, dstb=m.xn)
        m.barrier()
        m.emit_egress(b, out)
        m.done("XMID", m.T)
        m.barrier()
    m.nc.sync.wait_ge(m.st_out.sem, m.st_out.n)
    return m


def _build_b():
    m = MK()
    xin = m.din("xin", [NB, T, D])
    wqkv = m.din("L0_ffn_wqkv", [2, D, 1536])
    qs = m.din("L0_ffn_q_scale", [2, 64]); ks = m.din("L0_ffn_k_scale", [2, 64])
    wo = m.din("L0_ffn_wo", [2, 512, D])
    lg = m.din("L0_ffn_ln_g", [2, D]); lb = m.din("L0_ffn_ln_b", [2, D])
    w1 = m.din("L0_ffn_w1", [2, D, 5460]); b1 = m.din("L0_ffn_b1", [2, 5460])
    w2 = m.din("L0_ffn_w2", [2, 2730, D]); b2 = m.din("L0_ffn_b2", [2, D])
    out = m.dout("out", [NB, T, D])
    xfm = m.dint("xfm", [NB, NCC, P, T])
    attb = m.dint("attb", [NB, NCC, P, T], BF16)
    m.alloc()
    m.load_consts()
    xtabs = [m.din("hc_xCq", [P, T], BF16), m.din("hc_xSq", [P, T], BF16),
             m.din("hc_xCk", [P, T], BF16), m.din("hc_xSk", [P, T], BF16)]
    m.emit_ingest(xin, xfm)
    m.barrier()
    for b in range(NB):
        m.load_x(xfm, b)
        m.barrier()
        m.emit_l0_ffn(b, wqkv, qs, ks, wo, lg, lb, w1, b1, w2, b2, attb,
                      xtabs=xtabs)
        m.emit_addx(xfm, b)
        m.emit_ln(dstf=m.xmid, dstb=m.xn)
        m.barrier()
        m.emit_egress(b, out)
        m.done("XMID", m.T)
        m.barrier()
    m.nc.sync.wait_ge(m.st_out.sem, m.st_out.n)
    return m


def _build_c():
    m = MK()
    xin = m.din("xin", [NB, T, D])
    wq = m.din("L1_attn_W_q", [D, D])
    wo = m.din("L1_attn_W_o", [D, D])
    emb = m.din("L1_attn_rel_k_emb", [65, 64])
    out = m.dout("out", [NB, T, D])
    xfm = m.dint("xfm", [NB, NCC, P, T])
    attb = m.dint("attb", [NB, NCC, P, T], BF16)
    wq_bf = m.dint("wq_bf", [D, D], BF16)
    wo_bf = m.dint("wo_bf", [D, D], BF16)
    epad_h = m.nc.dram_tensor("epad", [NB * 16 * T * 576], BF16, kind="Internal")
    m.alloc()
    m.load_consts()
    m.emit_stage_bf16([(wq, wq_bf), (wo, wo_bf)])
    m.emit_l1_emb(emb)
    m.emit_ingest(xin, xfm)
    m.barrier()
    for b in range(NB):
        m.load_x(xfm, b)
        m.barrier()
        m.emit_l1_attn(b, wq_bf, wo_bf, epad_h, attb)
        m.emit_ln(dstf=m.xmid, dstb=m.xn)
        m.barrier()
        m.emit_egress(b, out)
        m.done("XMID", m.T)
        m.barrier()
    m.nc.sync.wait_ge(m.st_out.sem, m.st_out.n)
    return m


def _build_d():
    m = MK()
    xin = m.din("xin", [NB, T, D])
    wq = m.din("L1_ffn_wq", [D, D])
    wk = m.din("L1_ffn_wk", [256, D])
    wv = m.din("L1_ffn_wv", [256, D])
    wo = m.din("L1_ffn_wo", [D, D])
    qg = m.din("L1_ffn_q_gain", [16]); kg = m.din("L1_ffn_k_gain", [4])
    fc_s1 = m.din("L1_ffn_fc_s1", [8, D, 64])
    fc_s2 = m.din("L1_ffn_fc_s2", [8, 4096, 64])
    fc_mix = m.din("L1_ffn_fc_mix", [8, 4096, D])
    fc_logits = m.din("L1_ffn_fc_logits", [8])
    pj_s1 = m.din("L1_ffn_pj_s1", [8, 4096, 64])
    pj_s2 = m.din("L1_ffn_pj_s2", [8, D, 64])
    pj_mix = m.din("L1_ffn_pj_mix", [8, D, 4096])
    pj_logits = m.din("L1_ffn_pj_logits", [8])
    iota8 = m.din("hc_iota8", [1, 8])
    out = m.dout("out", [NB, T, D])
    xfm = m.dint("xfm", [NB, NCC, P, T])
    attb = m.dint("attb", [NB, NCC, P, T], BF16)
    wq_bf = m.dint("wq_bf", [D, D], BF16)
    wk_bf = m.dint("wk_bf", [256, D], BF16)
    wv_bf = m.dint("wv_bf", [256, D], BF16)
    wo_bf = m.dint("wo_bf", [D, D], BF16)
    hfc_d = m.dint("hfc_d", [4, 32, P, 512], BF16)
    mixt, mixtp = {}, {}
    for lg_, Md, Od in (("fc", 4096, 1024), ("pj", 1024, 4096)):
        for ki in range(2):
            tag = lg_ + str(ki)
            mixt[tag] = m.nc.dram_tensor("mt_" + tag, [Od * Md], BF16, kind="Internal")
            mixtp[tag] = m.nc.dram_tensor("mtp_" + tag, [Od * Md], BF16, kind="Internal")
    ssel = {"s1f": m.dint("ssel_s1f", [2, D, 64], BF16),
            "s2f": m.dint("ssel_s2f", [2, 4096, 64], BF16),
            "s1p": m.dint("ssel_s1p", [2, 4096, 64], BF16),
            "s2p": m.dint("ssel_s2p", [2, D, 64], BF16)}
    m.alloc()
    m.load_consts()
    t1 = m.din("hc_l1C", [P, T], BF16)
    t2 = m.din("hc_l1S", [P, T], BF16)
    m.load_tab(t1, t2)
    m.emit_stage_bf16([(wq, wq_bf), (wk, wk_bf), (wv, wv_bf), (wo, wo_bf)])
    m.emit_l1_ffn_prep(fc_logits, pj_logits, fc_s1, fc_s2, pj_s1, pj_s2,
                       fc_mix, pj_mix, iota8, ssel, mixt, mixtp)
    m.emit_ingest(xin, xfm)
    m.barrier()
    for b in range(NB):
        m.load_x(xfm, b)
        m.barrier()
        m.emit_l1_ffn(b, wq_bf, wk_bf, wv_bf, wo_bf, qg, kg,
                      ssel, mixtp, attb, hfc_d)
        m.emit_addx(xfm, b)
        m.emit_ln(dstf=m.xmid, dstb=m.xn)
        m.barrier()
        m.emit_egress(b, out)
        m.done("XMID", m.T)
        m.barrier()
    m.nc.sync.wait_ge(m.st_out.sem, m.st_out.n)
    return m


_WN = {
    "a": ["L0_attn_Wq", "L0_attn_bq", "L0_attn_Wk", "L0_attn_bk",
          "L0_attn_Wv", "L0_attn_bv", "L0_attn_Wp", "L0_attn_bp"],
    "b": ["L0_ffn_wqkv", "L0_ffn_q_scale", "L0_ffn_k_scale", "L0_ffn_wo",
          "L0_ffn_ln_g", "L0_ffn_ln_b", "L0_ffn_w1", "L0_ffn_b1",
          "L0_ffn_w2", "L0_ffn_b2"],
    "c": ["L1_attn_W_q", "L1_attn_W_o", "L1_attn_rel_k_emb"],
    "d": ["L1_ffn_wq", "L1_ffn_wk", "L1_ffn_wv", "L1_ffn_wo",
          "L1_ffn_q_gain", "L1_ffn_k_gain", "L1_ffn_fc_s1", "L1_ffn_fc_s2",
          "L1_ffn_fc_mix", "L1_ffn_fc_logits", "L1_ffn_pj_s1", "L1_ffn_pj_s2",
          "L1_ffn_pj_mix", "L1_ffn_pj_logits"],
}


def _input_names(m):
    return {a.memorylocations[0].name for a in m.nc.m.functions[0].allocations
            if hasattr(a, "kind") and a.kind == "ExternalInput"}


def kernel(**inputs):
    import numpy as _np
    hc = make_host_consts()
    if not _PROGRAMS:
        _PROGRAMS["a"] = _build_a()
        _PROGRAMS["b"] = _build_b()
        _PROGRAMS["c"] = _build_c()
        _PROGRAMS["d"] = _build_d()
    x = _np.ascontiguousarray(_np.asarray(inputs["L0_attn_x"], dtype=_np.float32))
    for stage in ("a", "b", "c", "d"):
        m = _PROGRAMS[stage]
        names = _input_names(m)
        in_maps = []
        for c in range(8):
            dmap = {"xin": _np.ascontiguousarray(x[c * NB:(c + 1) * NB])}
            if stage == "a":
                dmap["xin"] = dmap.pop("xin")
            for w in _WN[stage]:
                dmap[w] = _np.ascontiguousarray(_np.asarray(inputs[w],
                                                            dtype=_np.float32))
            dmap.update({k: v for k, v in hc.items() if k in names})
            in_maps.append(dmap)
        res = run_bass_kernel_spmd(m.nc, in_maps, core_ids=list(range(8)))
        x = _np.concatenate([_np.asarray(r["out"]) for r in res.results], 0)
    return x.astype(_np.float32)


def _build_fused():
    m = MK()
    xin = m.din("L0_attn_x", [NB, T, D])
    a_wq = m.din("L0_attn_Wq", [D, D]); a_bq = m.din("L0_attn_bq", [D])
    a_wk = m.din("L0_attn_Wk", [D, D]); a_bk = m.din("L0_attn_bk", [D])
    a_wv = m.din("L0_attn_Wv", [D, D]); a_bv = m.din("L0_attn_bv", [D])
    a_wp = m.din("L0_attn_Wp", [D, D]); a_bp = m.din("L0_attn_bp", [D])
    wqkv = m.din("L0_ffn_wqkv", [2, D, 1536])
    qs = m.din("L0_ffn_q_scale", [2, 64]); ks = m.din("L0_ffn_k_scale", [2, 64])
    b_wo = m.din("L0_ffn_wo", [2, 512, D])
    lg = m.din("L0_ffn_ln_g", [2, D]); lb = m.din("L0_ffn_ln_b", [2, D])
    w1 = m.din("L0_ffn_w1", [2, D, 5460]); b1 = m.din("L0_ffn_b1", [2, 5460])
    w2 = m.din("L0_ffn_w2", [2, 2730, D]); b2 = m.din("L0_ffn_b2", [2, D])
    c_wq = m.din("L1_attn_W_q", [D, D])
    c_wo = m.din("L1_attn_W_o", [D, D])
    emb = m.din("L1_attn_rel_k_emb", [65, 64])
    d_wq = m.din("L1_ffn_wq", [D, D])
    d_wk = m.din("L1_ffn_wk", [256, D])
    d_wv = m.din("L1_ffn_wv", [256, D])
    d_wo = m.din("L1_ffn_wo", [D, D])
    qg = m.din("L1_ffn_q_gain", [16]); kg = m.din("L1_ffn_k_gain", [4])
    fc_s1 = m.din("L1_ffn_fc_s1", [8, D, 64])
    fc_s2 = m.din("L1_ffn_fc_s2", [8, 4096, 64])
    fc_mix = m.din("L1_ffn_fc_mix", [8, 4096, D])
    fc_logits = m.din("L1_ffn_fc_logits", [8])
    pj_s1 = m.din("L1_ffn_pj_s1", [8, 4096, 64])
    pj_s2 = m.din("L1_ffn_pj_s2", [8, D, 64])
    pj_mix = m.din("L1_ffn_pj_mix", [8, D, 4096])
    pj_logits = m.din("L1_ffn_pj_logits", [8])
    iota8 = m.din("hc_iota8", [1, 8])
    out = m.dout("out", [NB, T, D])
    xfm = m.dint("xfm", [NB, NCC, P, T])
    xf1 = m.dint("xf1", [NB, NCC, P, T])
    xf3 = m.dint("xf3", [NB, NCC, P, T])
    attb = m.dint("attb", [NB, NCC, P, T], BF16)
    cwq_bf = m.dint("cwq_bf", [D, D], BF16)
    cwo_bf = m.dint("cwo_bf", [D, D], BF16)
    dwq_bf = m.dint("dwq_bf", [D, D], BF16)
    dwk_bf = m.dint("dwk_bf", [256, D], BF16)
    dwv_bf = m.dint("dwv_bf", [256, D], BF16)
    dwo_bf = m.dint("dwo_bf", [D, D], BF16)
    hfc_d = m.dint("hfc_d", [4, 32, P, 512], BF16)
    epad_h = m.nc.dram_tensor("epad", [NB * 16 * T * 576], BF16, kind="Internal")
    mixt, mixtp = {}, {}
    for lg_, Md, Od in (("fc", 4096, 1024), ("pj", 1024, 4096)):
        for ki in range(2):
            tag = lg_ + str(ki)
            mixt[tag] = m.nc.dram_tensor("mt_" + tag, [Od * Md], BF16, kind="Internal")
            mixtp[tag] = m.nc.dram_tensor("mtp_" + tag, [Od * Md], BF16, kind="Internal")
    ssel = {"s1f": m.dint("ssel_s1f", [2, D, 64], BF16),
            "s2f": m.dint("ssel_s2f", [2, 4096, 64], BF16),
            "s1p": m.dint("ssel_s1p", [2, 4096, 64], BF16),
            "s2p": m.dint("ssel_s2p", [2, D, 64], BF16)}
    m.alloc()
    m.load_consts()
    t_l0C = m.din("hc_l0C", [P, T], BF16)
    t_l0S = m.din("hc_l0S", [P, T], BF16)
    t_l1C = m.din("hc_l1C", [P, T], BF16)
    t_l1S = m.din("hc_l1S", [P, T], BF16)
    xtabs = [m.din("hc_xCq", [P, T], BF16), m.din("hc_xSq", [P, T], BF16),
             m.din("hc_xCk", [P, T], BF16), m.din("hc_xSk", [P, T], BF16)]
    m.emit_stage_bf16([(c_wq, cwq_bf), (c_wo, cwo_bf), (d_wq, dwq_bf),
                       (d_wk, dwk_bf), (d_wv, dwv_bf), (d_wo, dwo_bf)])
    m.emit_stage_l0a(a_wq, a_wk, a_wv, a_wp)
    m.emit_stage_l0f(wqkv, b_wo, w1, w2)
    m.emit_stage_deint()
    m.emit_l1_ffn_prep(fc_logits, pj_logits, fc_s1, fc_s2, pj_s1, pj_s2,
                       fc_mix, pj_mix, iota8, ssel, mixt, mixtp)
    m.emit_ingest(xin, xfm)
    m.barrier()
    for b in range(NB):
        m.load_x(xfm, b)
        m.barrier()
        m.load_tab(t_l0C, t_l0S)
        m.emit_l0_attn(b, a_wq, a_bq, a_wk, a_bk, a_wv, a_bv, a_wp, a_bp, attb)
        m.emit_ln(dstf=m.xmid, dstb=m.xn, dram_out=xf1, b=b)
        m.barrier()
        m.emit_l0_ffn(b, wqkv, qs, ks, b_wo, lg, lb, w1, b1, w2, b2, attb,
                      xtabs=xtabs)
        m.emit_addx(xf1, b)
        m.emit_ln(dstf=m.xmid, dstb=m.xn)
        m.barrier()
        m.emit_l1_emb(emb)
        m.barrier()
        m.emit_l1_attn(b, cwq_bf, cwo_bf, epad_h, attb)
        m.emit_ln(dstf=m.xmid, dstb=m.xn, dram_out=xf3, b=b)
        m.barrier()
        m.load_tab(t_l1C, t_l1S)
        m.emit_l1_ffn(b, dwq_bf, dwk_bf, dwv_bf, dwo_bf, qg, kg,
                      ssel, mixtp, attb, hfc_d)
        m.emit_addx(xf3, b)
        m.emit_ln(dstf=m.xmid, dstb=m.xn)
        m.barrier()
        m.emit_egress(b, out)
        m.done("XMID", m.T)
        m.done("XN", m.T)
        m.barrier()
    m.nc.sync.wait_ge(m.st_out.sem, m.st_out.n)
    return m


def kernel(**inputs):
    import numpy as _np
    hc = make_host_consts()
    if "fused" not in _PROGRAMS:
        _PROGRAMS["fused"] = _build_fused()
    m = _PROGRAMS["fused"]
    names = _input_names(m)
    x = _np.ascontiguousarray(_np.asarray(inputs["L0_attn_x"], dtype=_np.float32))
    wn = [k for k in inputs.keys() if k != "L0_attn_x"]
    in_maps = []
    for c in range(8):
        dmap = {"L0_attn_x": _np.ascontiguousarray(x[c * NB:(c + 1) * NB])}
        for w in wn:
            dmap[w] = _np.ascontiguousarray(_np.asarray(inputs[w], dtype=_np.float32))
        dmap.update({k: v for k, v in hc.items() if k in names})
        in_maps.append(dmap)
    res = run_bass_kernel_spmd(m.nc, in_maps, core_ids=list(range(8)))
    return _np.concatenate([_np.asarray(r["out"]) for r in res.results],
                           0).astype(_np.float32)


# version 08
# speedup vs baseline: 1.0182x; 1.0182x over previous
import contextlib
import numpy as np
import ml_dtypes
import concourse.bass as bass
import concourse.mybir as mybir
from concourse.bass_utils import run_bass_kernel_spmd
from concourse.ap import AP as APc

F32 = mybir.dt.float32
BF16 = mybir.dt.bfloat16
I32 = mybir.dt.int32
AF = mybir.ActivationFunctionType
ALU = mybir.AluOpType

P = 128
D = 1024
T = 2048
NB = 2
NCC = 8
CS = T + 4
XOFF = 4


class Q:
    def __init__(self, ctx, eng, name):
        self.ctx = ctx
        self.eng = eng
        self.sem = ctx.stack.enter_context(ctx.nc.semaphore(name))
        self.n = 0

    def inc(self, inst, k=1):
        inst.then_inc(self.sem, k)
        self.n += k
        return self.n

    def tick(self, k=1):
        return self.inc(self.eng.nop(), k)

    def dma(self, out, in_, **kw):
        return self.inc(self.eng.dma_start(out=out, in_=in_, **kw), 16)

    def wait(self, other, n):
        if n > 0:
            self.eng.wait_ge(other.sem, n)

    def drain(self):
        self.eng.drain()


class Stream:

    def __init__(self, ctx, eng, name):
        self.ctx = ctx
        self.eng = eng
        self.sem = ctx.stack.enter_context(ctx.nc.semaphore(name))
        self.n = 0

    def dma(self, out, in_, **kw):
        self.eng.dma_start(out=out, in_=in_, **kw).then_inc(self.sem, 16)
        self.n += 16
        return self.n


class MK:
    def __init__(self):
        self.nc = bass.Bass("TRN2", target_bir_lowering=False)
        self.stack = contextlib.ExitStack()
        nc = self.nc
        self.T = Q(self, nc.tensor, "sT")
        self.V = Q(self, nc.vector, "sV")
        self.S = Q(self, nc.scalar, "sS")
        self.G = Q(self, nc.gpsimd, "sG")
        self.D = Q(self, nc.sync, "sD")
        self.st_setup = Stream(self, nc.sync, "st_setup")
        self.ss_tab = Stream(self, nc.sync, "ss_tab")
        self.ss_bcol = Stream(self, nc.sync, "ss_bcol")
        self.ss_rw = Stream(self, nc.sync, "ss_rw")
        self.ss_b2 = Stream(self, nc.sync, "ss_b2")
        self.st_al = Stream(self, nc.sync, "st_al")
        self.st_mx = Stream(self, nc.sync, "st_mx")
        self.st_wa = Stream(self, nc.sync, "st_wa")
        self.st_wb = Stream(self, nc.sync, "st_wb")
        self.st_x = Stream(self, nc.sync, "st_x")
        self.st_tmp = Stream(self, nc.sync, "st_tmp")
        self.st_tmp2 = Stream(self, nc.sync, "st_tmp2")
        self.st_ao = Stream(self, nc.sync, "st_ao")
        self.st_out = Stream(self, nc.sync, "st_out")
        self.sg_xn = Stream(self, nc.gpsimd, "sg_xn")
        self.sg_w = Stream(self, nc.gpsimd, "sg_w")
        self.ss_wt = Stream(self, nc.sync, "ss_wt")
        self.sg_band = [Stream(self, nc.sync, "sg_band%d" % i) for i in range(8)]
        self.ss_ep = Stream(self, nc.sync, "ss_ep")
        self.sg_stg = Stream(self, nc.gpsimd, "sg_stg")
        self.guard = {}
        self.stgw = {}
        self.dbg_ap = None

    def swait(self, q, stream, n=None):
        n = stream.n if n is None else n
        if n > 0:
            q.eng.wait_ge(stream.sem, n)

    def din(self, name, shape, dtype=F32):
        return self.nc.dram_tensor(name, list(shape), dtype, kind="ExternalInput").ap()

    def dout(self, name, shape, dtype=F32):
        return self.nc.dram_tensor(name, list(shape), dtype, kind="ExternalOutput").ap()

    def dint(self, name, shape, dtype=F32):
        return self.nc.dram_tensor(name, list(shape), dtype, kind="Internal").ap()

    def sbuf(self, name, shape, dtype):
        return self.stack.enter_context(self.nc.sbuf_tensor(name, shape, dtype))

    def psumt(self, name, shape, dtype=F32):
        return self.stack.enter_context(self.nc.psum_tensor(name, shape, dtype))

    def alloc(self):
        self.XMID = self.sbuf("XMID", [P, NCC * CS], F32)
        self.XN = self.sbuf("XN", [P, NCC * CS], BF16)
        self.SCR = self.sbuf("SCR", [P, 35072], BF16)
        self.WCOL = self.sbuf("WCOL", [P, 4096], BF16)
        self.TAB = self.sbuf("TAB", [P, 2 * T], BF16)
        self.RW = self.sbuf("RW", [1, 4096], F32)
        self.BCOL = self.sbuf("BCOL", [P, 16], F32)
        self.TMP = self.sbuf("TMP", [P, 1024], F32)
        self.X2 = self.sbuf("X2", [P, 1024], BF16)
        self.IDN = self.sbuf("IDN", [P, P], F32)
        self.IDNB = self.sbuf("IDNB", [P, P], BF16)
        self.MSK = self.sbuf("MSK", [P, 2 * P], BF16)
        self.OCOL = self.sbuf("OCOL", [P, 2], F32)
        self.ONER = self.sbuf("ONER", [P, 256], BF16)
        self.MT = self.sbuf("MT", [P, 80], BF16)
        self.BLK2 = self.sbuf("BLK2", [P, 2], BF16)
        self.BLK2T = self.sbuf("BLK2T", [2, P], F32)
        self.BCOL2 = self.sbuf("BCOL2", [P, 96], F32)
        self.OCOLB = self.sbuf("OCOLB", [P, 2], BF16)
        self.P0 = self.psumt("P0", [P, 512])
        self.P1 = self.psumt("P1", [P, 512])
        self.P2 = self.psumt("P2", [P, 512])
        self.P3 = self.psumt("P3", [P, 512])
        self.P4 = self.psumt("P4", [66, 512])
        self.P5 = self.psumt("P5", [66, 512])
        self.P6 = self.psumt("P6", [66, 512])

    def xmid(self, cc, lo=0, hi=T):
        return self.XMID[:, cc * CS + XOFF + lo: cc * CS + XOFF + hi]

    def xn(self, cc, lo=0, hi=T):
        return self.XN[:, cc * CS + XOFF + lo: cc * CS + XOFF + hi]

    def scr(self, off, lo, hi, p0=0, p1=P):
        return self.SCR[p0:p1, off + lo: off + hi]

    def row(self, i, w=512):
        return self.RW[0:1, i * 512: i * 512 + w]

    def use(self, waiter, *names):
        for name in names:
            if name in self.guard:
                q, n = self.guard[name]
                waiter.wait(q, n)

    def done(self, name, q, n=None):
        self.guard[name] = (q, q.n if n is None else n)

    def barrier(self):
        qs = [self.T, self.V, self.S, self.G, self.D]
        sts = [self.st_setup, self.ss_tab, self.ss_bcol, self.ss_rw, self.ss_b2,
               self.st_al, self.st_mx, self.st_wa, self.st_wb,
               self.st_x, self.st_tmp, self.st_tmp2, self.st_ao, self.st_out,
               self.sg_xn, self.sg_w, self.ss_wt, self.ss_ep,
               self.sg_stg] + self.sg_band
        for a in qs:
            for b_ in qs:
                if a is not b_:
                    a.wait(b_, b_.n)
            for st in sts:
                self.swait(a, st)

    def load_consts(self, hc_shapes=True):
        self.hc_ident = self.din("hc_ident", [P, P], F32)
        self.hc_tri_ge = self.din("hc_tri_ge", [P, P], BF16)
        self.hc_tri_le = self.din("hc_tri_le", [P, P], BF16)
        self.hc_identb = self.din("hc_identb", [P, P], BF16)
        self.hc_blk2 = self.din("hc_blk2", [P, 2], BF16)
        self.hc_blk2t = self.din("hc_blk2t", [2, P], F32)
        self.st_setup.dma(self.IDN[:], self.hc_ident)
        self.st_setup.dma(self.MSK[:, 0:P], self.hc_tri_ge)
        self.st_setup.dma(self.MSK[:, P:2 * P], self.hc_tri_le)
        self.st_setup.dma(self.IDNB[:], self.hc_identb)
        self.st_setup.dma(self.BLK2[:], self.hc_blk2)
        self.st_setup.dma(self.BLK2T[:], self.hc_blk2t)
        nc = self.nc
        nc.vector.memset(self.RW[0:1, 0:1024], 1.0)
        nc.vector.memset(self.OCOL[:], 1.0)
        nc.vector.memset(self.OCOLB[:], 1.0)
        nc.vector.memset(self.XMID[:], 0.0)
        nc.vector.memset(self.ONER[:, 0:128], 1.0)
        nc.vector.memset(self.ONER[:, 128:256], -1.0)
        self.V.inc(nc.vector.memset(self.XN[:], 0.0))
        self.consts_v = self.V.n
        self.swait(self.T, self.st_setup)
        self.T.wait(self.V, self.consts_v)
        self.S.wait(self.V, self.consts_v)
        self.G.wait(self.V, self.consts_v)
        self.D.wait(self.V, self.consts_v)

    @property
    def ones_row(self):
        return self.RW[0:1, 0:512]

    def emit_addx(self, xfm, b):
        nc = self.nc
        for cc in range(NCC):
            for tt in range(4):
                half = (cc * 4 + tt) % 2
                th = self.TMP[:, half * 512:(half + 1) * 512]
                st = (self.st_tmp, self.st_tmp2)[half]
                self.use(self.D, "TMP" + str(half))
                ld = st.dma(th, xfm[b, cc, :, tt * 512:(tt + 1) * 512])
                self.swait(self.V, st, ld)
                self.use(self.V, "TMP")
                vn = self.V.inc(nc.vector.tensor_tensor(
                    self.xmid(cc, tt * 512, (tt + 1) * 512), th,
                    self.xmid(cc, tt * 512, (tt + 1) * 512), ALU.add))
                self.done("TMP" + str(half), self.V, vn)
        self.barrier()
        return

    def dump_dbg(self, tag, b, src="xmid"):
        import os
        if os.environ.get("DBGPHASE") != tag or b != 0 or self.dbg_ap is None:
            return
        self.barrier()
        for cc in range(NCC):
            if src == "xmid":
                self.st_out.dma(self.dbg_ap[cc], self.xmid(cc))
            else:
                self.st_out.dma(self.dbg_ap[cc].bitcast(BF16)[:, 0:T], self.xn(cc))
        self.barrier()

    def load_tab(self, *aps):
        off = 0
        for ap in aps:
            self.use(self.D, "TAB")
            self.ss_tab.dma(self.TAB[:, off:off + ap.shape[-1]], ap)
            off += ap.shape[-1]
        self.tab_d = self.ss_tab.n
        return self.tab_d

    def emit_ingest(self, xin, xfm):
        nc = self.nc
        for b in range(NB):
            self.swait(self.V, self.st_x)
            for tc in range(T // P):
                self.use(self.D, "TMP")
                ld = self.st_tmp.dma(self.TMP[:, 0:1024], xin[b, tc * P:(tc + 1) * P, :])
                self.swait(self.T, self.st_tmp, ld)
                for half in range(2):
                    pst = (self.P0, self.P1)[half]
                    self.use(self.T, "P" + str(half))
                    tn = 0
                    for q in range(4):
                        cc = half * 4 + q
                        mm = nc.tensor.matmul(pst[:, q * P:(q + 1) * P],
                                              self.TMP[:, cc * P:(cc + 1) * P],
                                              self.IDN[:], start=True, stop=True)
                        if q == 3:
                            tn = self.T.inc(mm)
                    self.V.wait(self.T, tn)
                    vn = 0
                    for q in range(4):
                        cc = half * 4 + q
                        cp = nc.vector.tensor_copy(
                            out=self.xmid(cc, tc * P, (tc + 1) * P),
                            in_=pst[:, q * P:(q + 1) * P])
                        if q == 3:
                            vn = self.V.inc(cp)
                    self.done("P" + str(half), self.V, vn)
                self.done("TMP", self.V)
            self.D.wait(self.V, self.V.n)
            for cc in range(NCC):
                self.st_x.dma(xfm[b, cc], self.xmid(cc))
        self.done("XMID", self.st_x)
        return self.st_x.n

    def emit_egress(self, b, xout):
        nc = self.nc
        for tc in range(T // P):
            for half in range(2):
                pst = (self.P0, self.P1)[half]
                self.use(self.T, "P" + str(half))
                tn = 0
                for q in range(4):
                    cc = half * 4 + q
                    mm = nc.tensor.matmul(pst[:, q * P:(q + 1) * P],
                                          self.xmid(cc, tc * P, (tc + 1) * P),
                                          self.IDN[:], start=True, stop=True)
                    if q == 3:
                        tn = self.T.inc(mm)
                self.V.wait(self.T, tn)
                self.use(self.V, "TMP")
                vn = self.V.inc(nc.vector.tensor_copy(
                    out=self.TMP[:, half * 512:(half + 1) * 512], in_=pst[:]))
                self.done("P" + str(half), self.V, vn)
            self.D.wait(self.V, self.V.n)
            self.st_out.dma(xout[b, tc * P:(tc + 1) * P, :], self.TMP[:, 0:1024])
            self.done("TMP", self.st_out)
        return self.st_out.n

    def load_x(self, xfm, b):
        self.use(self.D, "XMID")
        self.use(self.G, "XN")
        self.swait(self.D, self.st_x)
        for cc in range(NCC):
            self.st_x.dma(self.xmid(cc), xfm[b, cc])
            self.sg_xn.dma(self.xn(cc), xfm[b, cc])
        self.x_d = self.st_x.n
        self.x_g = self.sg_xn.n
        return self.x_d, self.x_g

    def emit_stage_wcol(self, name, wap, ncc, ow, n_oc, col0=0, rows=None,
                        w_last=None):
        rows = ncc * P if rows is None else rows
        colw = ncc * ow
        h = self.nc.dram_tensor(name, [n_oc * P * colw], BF16, kind="Internal")
        base = wap.offset
        pitch = wap.ap[0][0]
        for oc in range(n_oc):
            w_oc = ow if (w_last is None or oc < n_oc - 1) else w_last
            c_full = min(ncc, rows // P)
            if c_full:
                src = APc(wap.tensor, base + col0 + oc * ow,
                          [[pitch, P], [P * pitch, c_full], [1, w_oc]])
                dst = APc(h, oc * P * colw, [[colw, P], [ow, c_full], [1, w_oc]])
                self.sg_stg.dma(dst, src)
            hh = rows - c_full * P
            if hh > 0:
                src = APc(wap.tensor, base + c_full * P * pitch + col0 + oc * ow,
                          [[pitch, hh], [1, w_oc]])
                dst = APc(h, oc * P * colw + c_full * ow, [[colw, hh], [1, w_oc]])
                self.sg_stg.dma(dst, src)
        return h, colw

    def load_wcol2(self, hcw, oc, dst, stream):
        h, colw = hcw
        return stream.dma(dst[:, 0:colw],
                          APc(h, oc * P * colw, [[colw, P], [1, colw]]))

    def load_wcol2_part(self, hcw, oc, coff, wid, dst, stream):
        h, colw = hcw
        return stream.dma(dst[:, 0:wid],
                          APc(h, oc * P * colw + coff, [[colw, P], [1, wid]]))

    def emit_stage_l0a(self, wq, wk, wv, wp):
        self.stgw["a_wv"] = self.emit_stage_wcol("stg_a_wv", wv, 8, 512, 2)
        self.stgw["a_wp"] = self.emit_stage_wcol("stg_a_wp", wp, 8, 128, 8)
        self.stgw["a_wq"] = self.emit_stage_wcol("stg_a_wq", wq, 8, 128, 8)
        self.stgw["a_wk"] = self.emit_stage_wcol("stg_a_wk", wk, 8, 128, 8)

    def emit_stage_l0f(self, wqkv, wo, w1, w2):
        for l in range(2):
            t = str(l)
            self.stgw["b_wqk" + t] = self.emit_stage_wcol(
                "stg_b_wqk" + t, wqkv[l], 8, 128, 8)
            self.stgw["b_wv" + t] = self.emit_stage_wcol(
                "stg_b_wv" + t, wqkv[l], 8, 512, 1, col0=1024)
            self.stgw["b_wo" + t] = self.emit_stage_wcol(
                "stg_b_wo" + t, wo[l], 4, 128, 8, rows=512)
            self.stgw["b_w1a" + t] = self.emit_stage_wcol(
                "stg_b_w1a" + t, w1[l], 8, 128, 22, w_last=42)
            self.stgw["b_w1g" + t] = self.emit_stage_wcol(
                "stg_b_w1g" + t, w1[l], 8, 128, 22, col0=2730, w_last=42)
            self.stgw["b_w2" + t] = self.emit_stage_wcol(
                "stg_b_w2" + t, w2[l], 22, 128, 8, rows=2730)

    def emit_stage_deint(self):
        nc = self.nc
        vv = nc.vector
        self.swait(self.D, self.sg_stg)
        for key in ("a_wq", "a_wk"):
            h = nc.dram_tensor("stg_" + key + "d", [8 * P * 1024], BF16,
                               kind="Internal")
            for oc in range(8):
                buf = oc % 2
                st = (self.st_wa, self.st_wb)[buf]
                wn = self.WCOL[:, buf * 2048: buf * 2048 + 1024]
                wd = self.WCOL[:, buf * 2048 + 1024: buf * 2048 + 2048]
                self.use(self.D, "WC" + str(buf))
                wld = self.load_wcol2(self.stgw[key], oc, wn, st)
                self.swait(self.V, st, wld)
                self.use(self.V, "WCd" + str(buf))
                dei = 0
                for h2 in range(2):
                    for eo in range(2):
                        src = wn.rearrange("p (c x) -> p c x", c=8)[
                            :, :, h2 * 64 + eo: h2 * 64 + 64: 2]
                        dst = wd.rearrange("p (c x) -> p c x", c=8)[
                            :, :, h2 * 64 + eo * 32: h2 * 64 + eo * 32 + 32]
                        dei = self.V.inc(vv.tensor_copy(out=dst, in_=src))
                vv.drain()
                self.done("WC" + str(buf), self.V, dei)
                self.D.wait(self.V, dei)
                sst = (self.st_tmp, self.st_tmp2)[buf]
                sn = sst.dma(APc(h, oc * P * 1024, [[1024, P], [1, 1024]]), wd)
                self.done("WCd" + str(buf), sst, sn)
            self.stgw[key + "d"] = (h, 1024)
        self.barrier()

    def load_wcol(self, wdram, oc, ncc, ow, dst, w=None, col0=None, rows=None):
        w = w or ow
        c0 = oc * ow if col0 is None else col0
        n = 0
        nrow = ncc * P if rows is None else rows
        for cc in range(ncc):
            h = min(P, nrow - cc * P)
            if h <= 0:
                break
            src = wdram[cc * P: cc * P + h, c0: c0 + w]
            n = self.sg_w.dma(dst[0:h, cc * ow: cc * ow + w], src)
        return n

    def _ln_rows(self, ps_sum, ps_sq, eps, nfeat):
        nc = self.nc
        v = nc.vector
        m_row = self.row(1)
        r_row = self.row(2)
        v_row = self.row(3)
        mr_row = self.row(4)
        v.tensor_scalar(m_row, ps_sum, 1.0 / nfeat, None, ALU.mult)
        v.drain()
        v.tensor_scalar(v_row, ps_sq, 1.0 / nfeat, None, ALU.mult)
        v.drain()
        v.tensor_tensor(mr_row, m_row, m_row, ALU.mult)
        v.drain()
        vn = self.V.inc(v.scalar_tensor_tensor(
            v_row, v_row, float(eps), mr_row, ALU.add, ALU.subtract))
        self.S.wait(self.V, vn)
        nc.scalar.activation(out=r_row, in_=v_row, func=AF.Ln)
        self.S.drain()
        sn = self.S.inc(nc.scalar.activation(out=r_row, in_=r_row, func=AF.Exp,
                                             scale=-0.5))
        self.V.wait(self.S, sn)
        return self.V.inc(v.tensor_tensor(mr_row, m_row, r_row, ALU.mult))

    def emit_ln(self, srcf=None, dstf=None, dstb=None, gains=None, eps=1e-5,
                dram_out=None, b=None):
        nc = self.nc
        if srcf is None:
            srcf = self.xmid
        ps_sum = self.P6[64:65, :]
        ps_sq = self.P4[64:65, :]
        r_row = self.row(2)
        mr_row = self.row(4)
        for tt in range(T // 512):
            lo, hi = tt * 512, (tt + 1) * 512
            self.use(self.T, "P6", "P4")
            for cc in range(NCC):
                mm = nc.tensor.matmul(ps_sum, self.OCOL[:, 0:1], srcf(cc, lo, hi),
                                      start=(cc == 0), stop=(cc == NCC - 1))
                if cc == NCC - 1:
                    self.T.inc(mm)
            for cc in range(NCC):
                xh = self.X2[:, (cc % 2) * 512:(cc % 2) * 512 + 512]
                self.use(self.S, "X2" + str(cc % 2))
                sn = self.S.inc(nc.scalar.activation(
                    out=xh, in_=srcf(cc, lo, hi), func=AF.Square))
                self.T.wait(self.S, sn)
                tn = self.T.inc(nc.tensor.matmul(
                    ps_sq, self.OCOLB[:, 0:1], xh,
                    start=(cc == 0), stop=(cc == NCC - 1)))
                self.done("X2" + str(cc % 2), self.T, tn)
            self.V.wait(self.T, self.T.n)
            self.use(self.V, "LNrows")
            vrows = self._ln_rows(ps_sum, ps_sq, eps, D)
            self.done("P6", self.V, vrows)
            self.done("P4", self.V, vrows)
            self.use(self.T, "P0", "P1")
            self.T.wait(self.V, vrows)
            nc.tensor.matmul(self.P0[:], self.RW[0:1, 0:P], r_row,
                             start=True, stop=True)
            t_bc = self.T.inc(nc.tensor.matmul(
                self.P1[:], self.RW[0:1, 0:P], mr_row, start=True, stop=True))
            self.done("LNrows", self.T, t_bc)
            self.V.wait(self.T, t_bc)
            for cc in range(NCC):
                v = nc.vector
                self.use(self.V, "TMP")
                v.tensor_tensor(self.TMP[:, 0:512], srcf(cc, lo, hi), self.P0[:],
                                ALU.mult)
                v.drain()
                if gains is None:
                    tgt = dstf(cc, lo, hi) if dstf is not None else dstb(cc, lo, hi)
                    vn = self.V.inc(v.tensor_tensor(tgt, self.TMP[:, 0:512],
                                                    self.P1[:], ALU.subtract))
                    v.drain()
                    if dstf is not None and dstb is not None:
                        vn = self.V.inc(v.tensor_copy(out=dstb(cc, lo, hi),
                                                      in_=dstf(cc, lo, hi)))
                        v.drain()
                    self.done("TMP", self.V, vn)
                else:
                    vn = self.V.inc(v.tensor_tensor(
                        self.TMP[:, 512:1024], self.TMP[:, 0:512],
                        self.P1[:], ALU.subtract))
                    v.drain()
                    self.done("TMP", self.V, vn)
                    gcol, bcol = gains
                    self.S.wait(self.V, vn)
                    sn = self.S.inc(nc.scalar.activation(
                        out=dstb(cc, lo, hi), in_=self.TMP[:, 512:1024],
                        func=AF.Identity, bias=bcol(cc), scale=gcol(cc)))
                    self.done("TMP", self.S, sn)
            self.done("P0", self.V if gains is None else self.S)
            self.done("P1", self.V if gains is None else self.S)
        if dram_out is not None:
            self.D.wait(self.V, self.V.n)
            self.D.wait(self.S, self.S.n)
            for cc in range(NCC):
                self.st_x.dma(dram_out[b, cc], self.xmid(cc))
            self.done("XMID", self.st_x)
        return

    def emit_l0_attn(self, b, wq, bq, wk, bk, wv, bv, wp, bp, attb):
        nc = self.nc
        off_v = 0
        off_q = 16640
        off_k = 18688
        off_sw = 20736
        off_t2 = 22784
        off_pt = 24832
        off_ao = 33024
        off_al = 0
        EXPSC = 0.125

        bvrow = self.row(6, 1024)
        self.use(self.D, "RW8")
        self.ss_rw.dma(bvrow, bv.unsqueeze(0))
        bv_d = self.ss_rw.n
        self.use(self.V, "SCRv")
        v_ones = self.V.inc(nc.vector.memset(self.SCR[:, off_v:off_v + 16640], 1.0))
        for half in range(2):
            wsl = self.WCOL[:, 0:4096]
            self.use(self.D, "WCOL")
            wld = self.load_wcol2(self.stgw["a_wv"], half, wsl, self.st_wa)
            self.swait(self.T, self.st_wa, wld)
            self.swait(self.T, self.sg_xn, self.x_g)
            self.swait(self.T, self.ss_rw, bv_d)
            for tk in range(T // P):
                pst = (self.P0, self.P1)[tk % 2]
                self.use(self.T, "P" + str(tk % 2))
                for cc in range(NCC):
                    nc.tensor.matmul(pst[:], self.xn(cc, tk * P, (tk + 1) * P),
                                     wsl[:, cc * 512:(cc + 1) * 512],
                                     start=(cc == 0), stop=False)
                tn = self.T.inc(nc.tensor.matmul(
                    pst[:], self.ones_row[0:1, 0:P],
                    bvrow[0:1, half * 512:(half + 1) * 512],
                    start=False, stop=True))
                self.V.wait(self.T, tn)
                if tk == 0 and half == 0:
                    self.V.wait(self.V, v_ones)
                vn = 0
                for j in range(8):
                    h = half * 8 + j
                    slot = off_v + tk * 1040 + h * 65
                    cp = nc.vector.tensor_copy(out=self.SCR[:, slot:slot + 64],
                                               in_=pst[:, j * 64:(j + 1) * 64])
                    if j == 7:
                        vn = self.V.inc(cp)
                self.done("P" + str(tk % 2), self.V, vn)
            self.done("WCOL", self.T)
        v_ready = self.V.n

        self.use(self.D, "WCOL", "WCa", "WCb")
        wsts = (self.st_wa, self.st_wb)
        wlds = [0, 0]
        self.load_wcol2(self.stgw["a_wqd"], 0, self.WCOL[:, 0:1024], wsts[0])
        wlds[0] = self.load_wcol2(self.stgw["a_wkd"], 0,
                                  self.WCOL[:, 1024:2048], wsts[0])
        for hp in range(NCC):
            buf = hp % 2
            wq_d = self.WCOL[:, buf * 2048: buf * 2048 + 1024]
            wk_d = self.WCOL[:, buf * 2048 + 1024: buf * 2048 + 2048]
            if hp + 1 < NCC:
                nb = 1 - buf
                self.use(self.D, "WC" + "ab"[nb])
                self.load_wcol2(self.stgw["a_wqd"], hp + 1,
                                self.WCOL[:, nb * 2048: nb * 2048 + 1024], wsts[nb])
                wlds[nb] = self.load_wcol2(
                    self.stgw["a_wkd"], hp + 1,
                    self.WCOL[:, nb * 2048 + 1024: nb * 2048 + 2048], wsts[nb])
            bq_c = self.BCOL[:, 0:1]
            bk_c = self.BCOL[:, 1:2]
            self.use(self.D, "BCOL01")
            for (bsrc, bdst) in ((bq, bq_c), (bk, bk_c)):
                for h2 in range(2):
                    for eo in range(2):
                        src = APc(bsrc.tensor, bsrc.offset + 128 * hp + 64 * h2 + eo,
                                  [[2, 32], [1, 1]])
                        self.ss_bcol.dma(bdst[h2 * 64 + eo * 32: h2 * 64 + eo * 32 + 32, :],
                                         src, allow_slow_non_contiguous=True)
            bias_d = self.ss_bcol.n

            for (wdsl, bcol, offd) in ((wq_d, bq_c, off_q), (wk_d, bk_c, off_k)):
                for tt in range(4):
                    pst = (self.P0, self.P1)[tt % 2]
                    self.use(self.T, "P" + str(tt % 2))
                    self.swait(self.T, wsts[buf], wlds[buf])
                    tn = 0
                    for cc in range(NCC):
                        mm = nc.tensor.matmul(
                            pst[:], wdsl[:, cc * 128:(cc + 1) * 128],
                            self.xn(cc, tt * 512, (tt + 1) * 512),
                            start=(cc == 0), stop=(cc == NCC - 1))
                        if cc == NCC - 1:
                            tn = self.T.inc(mm)
                    self.S.wait(self.T, tn)
                    self.swait(self.S, self.ss_bcol, bias_d)
                    self.use(self.S, "SCRq")
                    sn = self.S.inc(nc.scalar.activation(
                        out=self.scr(offd, tt * 512, (tt + 1) * 512),
                        in_=pst[:], func=AF.Identity, bias=bcol, scale=1.0))
                    self.done("P" + str(tt % 2), self.S, sn)
            s_qk = self.S.n
            self.done("WC" + "ab"[buf], self.S, s_qk)
            self.done("BCOL01", self.S, s_qk)

            self.V.wait(self.S, s_qk)
            self.swait(self.V, self.ss_tab, getattr(self, "tab_d", 0))
            for offd in (off_q, off_k):
                vv = nc.vector
                self.use(self.V, "SCRsw", "SCRt2")
                for (d0, s0) in ((0, 32), (32, 0), (64, 96), (96, 64)):
                    vv.tensor_copy(out=self.scr(off_sw, 0, T, d0, d0 + 32),
                                   in_=self.scr(offd, 0, T, s0, s0 + 32))
                vv.drain()
                vv.tensor_tensor(self.scr(off_t2, 0, T), self.scr(offd, 0, T),
                                 self.TAB[:, 0:T], ALU.mult)
                vv.drain()
                vv.tensor_tensor(self.scr(off_sw, 0, T), self.scr(off_sw, 0, T),
                                 self.TAB[:, T:2 * T], ALU.mult)
                vv.drain()
                rope_v = self.V.inc(vv.tensor_tensor(
                    self.scr(offd, 0, T), self.scr(off_t2, 0, T),
                    self.scr(off_sw, 0, T), ALU.add))
                vv.drain()
            self.done("SCRsw", self.V, rope_v)
            self.done("SCRt2", self.V, rope_v)
            self.done("SCRq", self.V, rope_v)

            for h2 in range(2):
                h = hp * 2 + h2
                pbase = h2 * 64
                ps_o = (self.P4, self.P5)[h2]
                for qt in range(4):
                    kmax = 4 * qt + 4
                    for kc in range(kmax):
                        dlt = 128 * kc - 512 * qt
                        lo = max(0, dlt)
                        pss = (self.P0, self.P1)[kc % 2]
                        self.use(self.T, "P" + str(kc % 2), "SCRpt")
                        self.T.wait(self.V, rope_v)
                        tn = self.T.inc(nc.tensor.matmul(
                            pss[:, lo:512],
                            self.scr(off_k, kc * P, (kc + 1) * P, pbase, pbase + 64),
                            self.scr(off_q, qt * 512 + lo, (qt + 1) * 512,
                                     pbase, pbase + 64),
                            start=True, stop=True))
                        self.S.wait(self.T, tn)
                        sn = self.S.inc(nc.scalar.activation(
                            out=self.scr(off_pt, kc * 512 + lo, (kc + 1) * 512),
                            in_=pss[:, lo:512], func=AF.Exp, scale=EXPSC))
                        self.done("P" + str(kc % 2), self.S, sn)
                        if dlt >= 0:
                            self.V.wait(self.S, sn)
                            vv = nc.vector
                            self.V.inc(vv.tensor_tensor(
                                self.scr(off_pt, kc * 512 + dlt, kc * 512 + dlt + P),
                                self.scr(off_pt, kc * 512 + dlt, kc * 512 + dlt + P),
                                self.MSK[:, 0:P], ALU.mult))
                    s_exp = self.S.n
                    v_msk = self.V.n
                    self.T.wait(self.S, s_exp)
                    self.T.wait(self.V, v_msk)
                    self.use(self.T, ("P4", "P5")[h2])
                    if hp == 0 and h2 == 0 and qt == 0:
                        self.T.wait(self.V, v_ready)
                    t_pv = 0
                    for kc in range(kmax):
                        lo = max(0, 128 * kc - 512 * qt)
                        slot = off_v + kc * 1040 + h * 65
                        mm = nc.tensor.matmul(
                            ps_o[0:65, lo:512], self.SCR[:, slot:slot + 65],
                            self.scr(off_pt, kc * 512 + lo, (kc + 1) * 512),
                            start=(kc == 0), stop=(kc == kmax - 1))
                        if kc == kmax - 1:
                            t_pv = self.T.inc(mm)
                    self.done("SCRpt", self.T, t_pv)
                    self.V.wait(self.T, t_pv)
                    rl_row = self.row(5)
                    self.use(self.V, "RW6")
                    vn = self.V.inc(nc.vector.reciprocal(rl_row, ps_o[64:65, :]))
                    self.use(self.T, "P6")
                    self.T.wait(self.V, vn)
                    t_bc = self.T.inc(nc.tensor.matmul(
                        self.P6[0:64, :], self.RW[0:1, 0:64], rl_row,
                        start=True, stop=True))
                    self.done("RW6", self.T, t_bc)
                    self.use(self.S, "SCRt2")
                    self.S.wait(self.T, t_bc)
                    sn = self.S.inc(nc.scalar.copy(
                        out=self.scr(off_t2, 0, 512, 0, 64), in_=ps_o[0:64, :]))
                    self.done(("P4", "P5")[h2], self.S, sn)
                    self.V.wait(self.S, sn)
                    self.use(self.V, "SCRao")
                    vv = nc.vector
                    vn = self.V.inc(vv.tensor_tensor(
                        self.scr(off_ao, qt * 512, (qt + 1) * 512, pbase, pbase + 64),
                        self.scr(off_t2, 0, 512, 0, 64), self.P6[0:64, :], ALU.mult))
                    self.done("P6", self.V, vn)
                    self.done("SCRt2", self.V, vn)
            self.D.wait(self.V, self.V.n)
            dn = self.st_ao.dma(attb[b, hp], self.scr(off_ao, 0, T))
            self.done("SCRao", self.st_ao, dn)
            self.done("SCRq", self.st_ao, dn)

        self.use(self.D, "SCRv_w")
        t_after_pv = self.T.n
        self.D.wait(self.T, t_after_pv)
        self.swait(self.D, self.st_ao)
        al = 0
        for cc in range(NCC):
            al = self.st_al.dma(self.SCR[:, off_al + cc * T: off_al + (cc + 1) * T],
                                attb[b, cc])
        bp_c = self.BCOL[:, 2:3]
        self.use(self.D, "WCOL", "WCa", "WCb")
        wsts = (self.st_wa, self.st_wb)
        wlds = [0, 0]
        wlds[0] = self.load_wcol2(self.stgw["a_wp"], 0, self.WCOL[:, 0:1024],
                                  wsts[0])
        for oc in range(NCC):
            buf = oc % 2
            wsl = self.WCOL[:, buf * 1024: buf * 1024 + 1024]
            if oc + 1 < NCC:
                nb = 1 - buf
                self.use(self.D, "WC" + "ab"[nb])
                wlds[nb] = self.load_wcol2(
                    self.stgw["a_wp"], oc + 1,
                    self.WCOL[:, nb * 1024: nb * 1024 + 1024], wsts[nb])
            wld = wlds[buf]
            self.use(self.D, "BCOL2")
            self.ss_b2.dma(bp_c, bp[oc * P:(oc + 1) * P].unsqueeze(1))
            bp_d = self.ss_b2.n
            for tt in range(4):
                pst = (self.P0, self.P1)[tt % 2]
                self.use(self.T, "P" + str(tt % 2))
                self.swait(self.T, wsts[buf], wld)
                self.swait(self.T, self.st_al, al)
                tn = 0
                for cc in range(NCC):
                    mm = nc.tensor.matmul(
                        pst[:], wsl[:, cc * P:(cc + 1) * P],
                        self.SCR[:, off_al + cc * T + tt * 512:
                                 off_al + cc * T + (tt + 1) * 512],
                        start=(cc == 0), stop=(cc == NCC - 1))
                    if cc == NCC - 1:
                        tn = self.T.inc(mm)
                self.V.wait(self.T, tn)
                self.swait(self.V, self.ss_b2, bp_d)
                vv = nc.vector
                vn = self.V.inc(vv.scalar_tensor_tensor(
                    out=self.xmid(oc, tt * 512, (tt + 1) * 512),
                    in0=pst[:], scalar=bp_c,
                    in1=self.xmid(oc, tt * 512, (tt + 1) * 512),
                    op0=ALU.add, op1=ALU.add))
                self.done("P" + str(tt % 2), self.V, vn)
            self.done("WC" + "ab"[buf], self.V)
            self.done("BCOL2", self.V)
        self.done("WCOL", self.V)
        self.done("SCRv", self.V)
        self.done("SCRv_w", self.V)
        self.done("RW8", self.T)
        self.done("XN", self.T)
        self.barrier()
        return

    def emit_stage_bf16(self, pairs):
        n = 0
        for (src, dst) in pairs:
            n = self.sg_stg.dma(dst, src)
        self.stg_n = n
        return n

    def emit_l1_emb(self, emb):
        nc = self.nc
        for r in range(65):
            self.sg_stg.dma(self.X2[r:r + 1, 0:64],
                            emb[64 - r].unsqueeze(0))
        self.swait(self.T, self.sg_stg)
        nc.vector.tensor_copy(out=self.MT[0:65, 0:65], in_=self.IDNB[0:65, 0:65])
        nc.vector.drain()
        nc.vector.memset(self.MT[64:65, 0:64], -1.0)
        nc.vector.drain()
        mtv = self.V.inc(nc.vector.memset(self.MT[64:65, 64:65], 0.0))
        self.T.wait(self.V, mtv)
        self.use(self.T, "P2")
        mm = nc.tensor.matmul(self.P2[0:64, 0:65],
                              self.X2[0:65, 0:64],
                              self.MT[0:65, 0:65], start=True, stop=True)
        tn = self.T.inc(mm)
        self.V.wait(self.T, tn)
        nc.vector.tensor_copy(out=self.TAB[0:64, 0:65], in_=self.P2[0:64, 0:65])
        nc.vector.drain()
        vn = self.V.inc(nc.vector.tensor_copy(out=self.TAB[64:128, 0:65],
                                              in_=self.P2[0:64, 0:65]))
        self.done("P2", self.V, vn)
        self.done("X20", self.V, vn)
        self.emb_v = vn
        return vn

    def emit_l1_attn(self, b, wq_bf, wo_bf, epad_h, attb):
        import os
        noband = bool(os.environ.get("NOBAND"))
        self._bandcap = int(os.environ.get("BANDCAP", "999999"))
        noep = bool(os.environ.get("NOEP"))
        nc = self.nc
        off_v = 0
        off_q = 16640
        off_e = 18688
        off_ep = 20736
        off_bt = 21312
        off_t2 = 22336
        off_pt = 24832
        off_ao = 33024
        off_al = 0
        EXPSC = 0.125
        EPW = 576

        self.use(self.V, "SCRv")
        v_ones = self.V.inc(nc.vector.memset(self.SCR[:, off_v:off_v + 16640], 1.0))
        for half in range(2):
            wsl = self.WCOL[:, 0:4096]
            self.use(self.D, "WCOL")
            for cc in range(NCC):
                wld = self.ss_wt.dma(
                    wsl[:, cc * 512:(cc + 1) * 512],
                    wq_bf[half * 512:(half + 1) * 512, cc * P:(cc + 1) * P],
                    transpose=True)
            self.swait(self.T, self.ss_wt, wld)
            self.swait(self.T, self.sg_xn, self.x_g)
            for tk in range(T // P):
                pst = (self.P0, self.P1)[tk % 2]
                self.use(self.T, "P" + str(tk % 2))
                tn = 0
                for cc in range(NCC):
                    mm = nc.tensor.matmul(pst[:], self.xn(cc, tk * P, (tk + 1) * P),
                                          wsl[:, cc * 512:(cc + 1) * 512],
                                          start=(cc == 0), stop=(cc == NCC - 1))
                    if cc == NCC - 1:
                        tn = self.T.inc(mm)
                self.V.wait(self.T, tn)
                if tk == 0 and half == 0:
                    self.V.wait(self.V, v_ones)
                vn = 0
                for j in range(8):
                    h = half * 8 + j
                    slot = off_v + tk * 1040 + h * 65
                    cp = nc.vector.tensor_copy(out=self.SCR[:, slot:slot + 64],
                                               in_=pst[:, j * 64:(j + 1) * 64])
                    if j == 7:
                        vn = self.V.inc(cp)
                self.done("P" + str(tk % 2), self.V, vn)
            self.done("WCOL", self.T)
        v_ready = self.V.n

        if not hasattr(self, "_ep_zeroed"):
            self._ep_zeroed = True
            self.use(self.V, "SCRep")
            self.V.inc(nc.vector.memset(self.scr(off_ep, 321, 544), 0.0))

        for hp in range(NCC):
            wsl = self.WCOL[:, 0:1024]
            self.use(self.D, "WCOL", "WCOLd")
            self.D.wait(self.T, self.T.n)
            for st_ in self.sg_band:
                self.swait(self.D, st_)
            for cc in range(NCC):
                wld = self.ss_wt.dma(
                    wsl[:, cc * P:(cc + 1) * P],
                    wq_bf[hp * P:(hp + 1) * P, cc * P:(cc + 1) * P],
                    transpose=True)
            for tt in range(4):
                pst = (self.P0, self.P1)[tt % 2]
                self.use(self.T, "P" + str(tt % 2))
                self.swait(self.T, self.ss_wt, wld)
                tn = 0
                for cc in range(NCC):
                    mm = nc.tensor.matmul(
                        pst[:], wsl[:, cc * P:(cc + 1) * P],
                        self.xn(cc, tt * 512, (tt + 1) * 512),
                        start=(cc == 0), stop=(cc == NCC - 1))
                    if cc == NCC - 1:
                        tn = self.T.inc(mm)
                self.S.wait(self.T, tn)
                self.use(self.S, "SCRq")
                sn = self.S.inc(nc.scalar.copy(
                    out=self.scr(off_q, tt * 512, (tt + 1) * 512), in_=pst[:]))
                self.done("P" + str(tt % 2), self.S, sn)
            s_q = self.S.n
            self.done("WCOLd", self.S, s_q)

            for h2 in range(2):
                h = hp * 2 + h2
                pbase = h2 * 64
                ps_o = (self.P4, self.P5)[h2]
                self.T.wait(self.V, self.emb_v)
                self.T.wait(self.S, s_q)
                for tt in range(4):
                    self.use(self.T, "P6")
                    tn = self.T.inc(nc.tensor.matmul(
                        self.P6[0:65, :], self.TAB[pbase:pbase + 64, 0:65],
                        self.scr(off_q, tt * 512, (tt + 1) * 512, pbase, pbase + 64),
                        start=True, stop=True))
                    self.V.wait(self.T, tn)
                    self.use(self.V, "SCRe")
                    vn = self.V.inc(nc.vector.tensor_copy(
                        out=self.scr(off_e, tt * 512, (tt + 1) * 512, 0, 65),
                        in_=self.P6[0:65, :]))
                    self.done("P6", self.V, vn)
                e_v = self.V.n
                for tk in range(0 if not noep else T // P, T // P):
                    self.use(self.T, "P2")
                    self.T.wait(self.V, e_v)
                    tn = self.T.inc(nc.tensor.matmul(
                        self.P2[:, 0:65], self.scr(off_e, tk * P, (tk + 1) * P, 0, 65),
                        self.IDNB[0:65, 0:65], start=True, stop=True))
                    self.V.wait(self.T, tn)
                    self.use(self.V, "SCRep")
                    vn = self.V.inc(nc.vector.tensor_copy(
                        out=self.scr(off_ep, 256, 321), in_=self.P2[:, 0:65]))
                    self.done("P2", self.V, vn)
                    self.S.wait(self.V, vn)
                    sn = self.S.inc(nc.scalar.activation(
                        out=self.scr(off_ep, 33, 256), in_=self.scr(off_ep, 33, 256),
                        func=AF.Identity, bias=self.scr(off_ep, 256, 257),
                        scale=0.0))
                    self.D.wait(self.S, sn)
                    base = ((b * 16 + h) * T + tk * P) * EPW
                    dst = APc(epad_h, base + 33, [[EPW, P], [1, 511]])
                    en = self.ss_ep.dma(dst, self.scr(off_ep, 33, 544))
                    self.done("SCRep", self.ss_ep, en)
                ep_done = self.ss_ep.n

                for qt in range(4):
                    pend = {}

                    def _issue(kci):
                        bl = [it for it in range(4 * qt, 4 * qt + 4)
                              if abs(it - kci) <= 1] if (not noband
                                                         and kci < 16) else []
                        if self._bandcap <= 0:
                            bl = []
                        self._bandcap -= len(bl)
                        lst = []
                        for bi, it in enumerate(bl):
                            si = (kci % 2) * 4 + bi
                            slot = off_bt + si * P
                            self.use(self.D, "BT" + str(si))
                            self.swait(self.D, self.ss_ep, ep_done)
                            base = ((b * 16 + h) * T + it * P) * EPW
                            step = EPW if os.environ.get("SKEW576") else EPW - 1
                            src = APc(epad_h, base + 288 - 128 * (it - kci),
                                      [[step, P], [1, P]])
                            bn = self.sg_band[si].dma(
                                self.SCR[:, slot:slot + P], src)
                            lst.append((it, si, bn))
                        pend[kci] = lst

                    _issue(0)
                    for kc in range(16):
                        _issue(kc + 1)
                        pss = (self.P0, self.P1)[kc % 2]
                        self.use(self.T, "P" + str(kc % 2))
                        bands = pend.pop(kc)
                        locol = max(0, (kc + 2) * P - 512 * qt) if not noband else 512
                        n_extra = len(bands) + (1 if locol < 512 else 0)
                        ei = 0
                        tn = self.T.inc(nc.tensor.matmul(
                            pss[:],
                            self.scr(off_q, kc * P, (kc + 1) * P, pbase, pbase + 64),
                            self.scr(off_q, qt * 512, (qt + 1) * 512, pbase, pbase + 64),
                            start=True, stop=(n_extra == 0),
                            skip_group_check=(n_extra > 0)))
                        for (it, si, bn) in bands:
                            slot = off_bt + si * P
                            self.swait(self.T, self.sg_band[si], bn)
                            ei += 1
                            tn = self.T.inc(nc.tensor.matmul(
                                pss[:, (it - 4 * qt) * P:(it - 4 * qt + 1) * P],
                                self.SCR[:, slot:slot + P], self.IDNB[:],
                                start=False, stop=(ei == n_extra),
                                skip_group_check=True))
                            self.done("BT" + str(si), self.T, tn)
                        if locol < 512:
                            ei += 1
                            tn = self.T.inc(nc.tensor.matmul(
                                pss[:, locol:512], self.ONER[0:1, 0:128],
                                self.scr(off_e, qt * 512 + locol, (qt + 1) * 512, 0, 1),
                                start=False, stop=(ei == n_extra),
                                skip_group_check=True))
                        self.S.wait(self.T, tn)
                        sn = self.S.inc(nc.scalar.activation(
                            out=self.scr(off_pt, kc * 512, (kc + 1) * 512),
                            in_=pss[:], func=AF.Exp, scale=EXPSC))
                        self.done("P" + str(kc % 2), self.S, sn)
                    s_exp = self.S.n
                    self.T.wait(self.S, s_exp)
                    self.use(self.T, ("P4", "P5")[h2])
                    if hp == 0 and h2 == 0 and qt == 0:
                        self.T.wait(self.V, v_ready)
                    t_pv = 0
                    for kc in range(16):
                        slot = off_v + kc * 1040 + h * 65
                        mm = nc.tensor.matmul(
                            ps_o[0:65, :], self.SCR[:, slot:slot + 65],
                            self.scr(off_pt, kc * 512, (kc + 1) * 512),
                            start=(kc == 0), stop=(kc == 15))
                        if kc == 15:
                            t_pv = self.T.inc(mm)
                    self.done("SCRpt", self.T, t_pv)
                    self.V.wait(self.T, t_pv)
                    rl_row = self.row(5)
                    self.use(self.V, "RW6")
                    vn = self.V.inc(nc.vector.reciprocal(rl_row, ps_o[64:65, :]))
                    self.use(self.T, "P6")
                    self.T.wait(self.V, vn)
                    t_bc = self.T.inc(nc.tensor.matmul(
                        self.P6[0:64, :], self.RW[0:1, 0:64], rl_row,
                        start=True, stop=True))
                    self.done("RW6", self.T, t_bc)
                    self.use(self.S, "SCRt2")
                    self.S.wait(self.T, t_bc)
                    sn = self.S.inc(nc.scalar.copy(
                        out=self.scr(off_t2, 0, 512, 0, 64), in_=ps_o[0:64, :]))
                    self.done(("P4", "P5")[h2], self.S, sn)
                    self.V.wait(self.S, sn)
                    self.use(self.V, "SCRao")
                    vn = self.V.inc(nc.vector.tensor_tensor(
                        self.scr(off_ao, qt * 512, (qt + 1) * 512, pbase, pbase + 64),
                        self.scr(off_t2, 0, 512, 0, 64), self.P6[0:64, :], ALU.mult))
                    self.done("P6", self.V, vn)
                    self.done("SCRt2", self.V, vn)
                self.done("SCRe", self.T)
            self.D.wait(self.V, self.V.n)
            dn = self.st_ao.dma(attb[b, hp], self.scr(off_ao, 0, T))
            self.done("SCRao", self.st_ao, dn)
            self.done("SCRq", self.st_ao, dn)

        self.swait(self.D, self.st_ao)
        self.use(self.D, "SCRv_w")
        self.D.wait(self.T, self.T.n)
        al = 0
        for cc in range(NCC):
            al = self.st_al.dma(self.SCR[:, off_al + cc * T: off_al + (cc + 1) * T],
                                attb[b, cc])
        for oc in range(NCC):
            wsl = self.WCOL[:, 0:1024]
            self.use(self.D, "WCOL", "WCOLd")
            for cc in range(NCC):
                wld = self.ss_wt.dma(
                    wsl[:, cc * P:(cc + 1) * P],
                    wo_bf[oc * P:(oc + 1) * P, cc * P:(cc + 1) * P],
                    transpose=True)
            for tt in range(4):
                pst = (self.P0, self.P1)[tt % 2]
                self.use(self.T, "P" + str(tt % 2))
                self.swait(self.T, self.ss_wt, wld)
                self.swait(self.T, self.st_al, al)
                tn = 0
                for cc in range(NCC):
                    mm = nc.tensor.matmul(
                        pst[:], wsl[:, cc * P:(cc + 1) * P],
                        self.SCR[:, off_al + cc * T + tt * 512:
                                 off_al + cc * T + (tt + 1) * 512],
                        start=(cc == 0), stop=(cc == NCC - 1))
                    if cc == NCC - 1:
                        tn = self.T.inc(mm)
                self.V.wait(self.T, tn)
                vn = self.V.inc(nc.vector.tensor_tensor(
                    self.xmid(oc, tt * 512, (tt + 1) * 512), pst[:],
                    self.xmid(oc, tt * 512, (tt + 1) * 512), ALU.add))
                self.done("P" + str(tt % 2), self.V, vn)
            self.done("WCOL", self.V)
            self.done("WCOLd", self.V)
        self.done("SCRv", self.V)
        self.done("SCRv_w", self.V)
        self.done("XN", self.T)
        self.barrier()
        return

    def emit_l0_ffn(self, b, wqkv, q_scale, k_scale, wo, ln_g, ln_b, w1, b1,
                    w2, b2, attb, xtabs=None):
        self.xtabs = xtabs or self.xtabs
        nc = self.nc
        HID = 2730
        NJ = 22
        off_v = 0
        off_q = 8320
        off_k = 10368
        off_sw = 12416
        off_t2 = 14464
        off_pt = 16512
        off_ao = 20608
        off_tab = 22656
        off_hf = 0
        off_al = 12416

        def xsh(cc, lo, hi, base=XOFF):
            d = 1 if cc >= 4 else 0
            return self.XN[:, cc * CS + base + lo - d: cc * CS + base + hi - d]

        def xshm(cc, lo, hi):
            d = 1 if cc >= 4 else 0
            return self.XMID[:, cc * CS + XOFF + lo - d: cc * CS + XOFF + hi - d]

        for l in range(2):
            wqkv_l = wqkv[l]
            wo_l = wo[l]
            w1_l = w1[l]
            w2_l = w2[l]
            qs_c = self.BCOL[:, 4:5]
            ks_c = self.BCOL[:, 5:6]
            self.use(self.D, "BCOL45")
            for (src, dstc) in ((q_scale, qs_c), (k_scale, ks_c)):
                self.ss_bcol.dma(dstc[0:64, :], src[l].unsqueeze(1))
                self.ss_bcol.dma(dstc[64:128, :], src[l].unsqueeze(1))
            qs_s = self.BCOL[:, 6:7]
            ks_s = self.BCOL[:, 7:8]
            for (src, dstc) in ((q_scale, qs_s), (k_scale, ks_s)):
                sl = src[l]
                self.ss_bcol.dma(dstc[0:32, :], sl[32:64].unsqueeze(1))
                self.ss_bcol.dma(dstc[32:64, :], sl[0:32].unsqueeze(1))
                self.ss_bcol.dma(dstc[64:96, :], sl[32:64].unsqueeze(1))
                self.ss_bcol.dma(dstc[96:128, :], sl[0:32].unsqueeze(1))
            bc_d = self.ss_bcol.n
            self.use(self.D, "SCRtab")
            tl = 0
            for (ti, hcap) in enumerate(self.xtabs):
                tl = self.ss_tab.dma(self.scr(off_tab + ti * T, 0, T), hcap)
            self.swait(self.V, self.ss_tab, tl)
            self.swait(self.V, self.ss_bcol, bc_d)
            vv = nc.vector
            vn = 0
            for (ti, col) in enumerate((qs_c, qs_s, ks_c, ks_s)):
                vn = self.V.inc(vv.tensor_scalar(
                    self.scr(off_tab + ti * T, 0, T),
                    self.scr(off_tab + ti * T, 0, T),
                    col, None, ALU.mult))
            tab_v = vn

            self.use(self.V, "SCRv")
            v_ones = self.V.inc(nc.vector.memset(self.SCR[:, off_v:off_v + 8320], 1.0))
            wsl = self.WCOL[:, 0:4096]
            self.use(self.D, "WCOL")
            wld = self.load_wcol2(self.stgw["b_wv" + str(l)], 0, wsl, self.st_wa)
            self.swait(self.T, self.st_wa, wld)
            self.swait(self.T, self.sg_xn, self.x_g)
            for tk in range(T // P):
                pst = (self.P0, self.P1)[tk % 2]
                self.use(self.T, "P" + str(tk % 2))
                tn = 0
                for cc in range(NCC):
                    mm = nc.tensor.matmul(pst[:], xsh(cc, tk * P, (tk + 1) * P),
                                          wsl[:, cc * 512:(cc + 1) * 512],
                                          start=(cc == 0), stop=(cc == NCC - 1))
                    if cc == NCC - 1:
                        tn = self.T.inc(mm)
                self.V.wait(self.T, tn)
                if tk == 0:
                    self.V.wait(self.V, v_ones)
                vn = 0
                for h in range(8):
                    slot = off_v + tk * 520 + h * 65
                    cp = nc.vector.tensor_copy(out=self.SCR[:, slot:slot + 64],
                                               in_=pst[:, h * 64:(h + 1) * 64])
                    if h == 7:
                        vn = self.V.inc(cp)
                self.done("P" + str(tk % 2), self.V, vn)
            self.done("WCOL", self.T)
            v_ready = self.V.n

            self.use(self.D, "WCOL", "WCa", "WCb")
            wsts = (self.st_wa, self.st_wb)
            wlds = [0, 0]
            self.load_wcol2(self.stgw["b_wqk" + str(l)], 0,
                            self.WCOL[:, 0:1024], wsts[0])
            wlds[0] = self.load_wcol2(self.stgw["b_wqk" + str(l)], 4,
                                      self.WCOL[:, 1024:2048], wsts[0])
            for hp in range(4):
                buf = hp % 2
                wq_c = self.WCOL[:, buf * 2048: buf * 2048 + 1024]
                wk_c = self.WCOL[:, buf * 2048 + 1024: buf * 2048 + 2048]
                if hp + 1 < 4:
                    nb = 1 - buf
                    self.use(self.D, "WC" + "ab"[nb])
                    self.load_wcol2(self.stgw["b_wqk" + str(l)], hp + 1,
                                    self.WCOL[:, nb * 2048: nb * 2048 + 1024],
                                    wsts[nb])
                    wlds[nb] = self.load_wcol2(
                        self.stgw["b_wqk" + str(l)], 5 + hp,
                        self.WCOL[:, nb * 2048 + 1024: nb * 2048 + 2048],
                        wsts[nb])
                wld = wlds[buf]
                vv = nc.vector
                for (wc, offd, colc, cols, tabC, tabS) in (
                        (wq_c, off_q, qs_c, qs_s, 0, 1),
                        (wk_c, off_k, ks_c, ks_s, 2, 3)):
                    for tt in range(4):
                        lo, hi = tt * 512, (tt + 1) * 512
                        pst = (self.P0, self.P1)[tt % 2]
                        self.use(self.T, "P" + str(tt % 2))
                        self.swait(self.T, wsts[buf], wld)
                        tn = 0
                        for cc in range(NCC):
                            mm = nc.tensor.matmul(
                                pst[:], wc[:, cc * P:(cc + 1) * P],
                                xsh(cc, lo, hi),
                                start=(cc == 0), stop=(cc == NCC - 1))
                            if cc == NCC - 1:
                                tn = self.T.inc(mm)
                        self.S.wait(self.T, tn)
                        self.use(self.S, "X2" + str(tt % 2))
                        x2h = self.X2[:, (tt % 2) * 512:(tt % 2) * 512 + 512]
                        sn = self.S.inc(nc.scalar.activation(
                            out=x2h, in_=pst[:], func=AF.Square))
                        self.use(self.S, "SCRq" if offd == off_q else "SCRk")
                        sn2 = self.S.inc(nc.scalar.copy(
                            out=self.scr(offd, lo, hi), in_=pst[:]))
                        self.done("P" + str(tt % 2), self.S, sn2)
                        self.T.wait(self.S, sn)
                        self.use(self.T, "P6")
                        tn = self.T.inc(nc.tensor.matmul(
                            self.P6[0:2, :], self.BLK2[:, 0:2], x2h,
                            start=True, stop=True))
                        self.done("X2" + str(tt % 2), self.T, tn)
                        self.V.wait(self.T, tn)
                        self.use(self.V, "RW6")
                        r2 = self.P6[0:2, :]
                        vn = self.V.inc(nc.vector.tensor_scalar(
                            self.TMP[0:2, 0:512], r2, 1e-24, None, ALU.max))
                        self.S.wait(self.V, vn)
                        nc.scalar.activation(out=self.TMP[0:2, 0:512],
                                             in_=self.TMP[0:2, 0:512], func=AF.Ln)
                        self.S.drain()
                        sn = self.S.inc(nc.scalar.activation(
                            out=self.TMP[0:2, 0:512], in_=self.TMP[0:2, 0:512],
                            func=AF.Exp, scale=-0.5))
                        self.T.wait(self.S, sn)
                        self.use(self.T, "P2")
                        t_bc = self.T.inc(nc.tensor.matmul(
                            self.P2[:], self.BLK2T[0:2, :],
                            self.TMP[0:2, 0:512], start=True, stop=True))
                        self.done("RW6", self.T, t_bc)
                        self.V.wait(self.T, t_bc)
                        vn = self.V.inc(vv.tensor_tensor(
                            self.scr(offd, lo, hi), self.scr(offd, lo, hi),
                            self.P2[:], ALU.mult))
                        self.done("P6", self.V, vn)
                        self.done("P2", self.V, vn)
                    self.V.wait(self.V, tab_v)
                    self.use(self.V, "SCRsw")
                    for (d0, s0) in ((0, 32), (32, 0), (64, 96), (96, 64)):
                        vv.tensor_copy(out=self.scr(off_sw, 0, T, d0, d0 + 32),
                                       in_=self.scr(offd, 0, T, s0, s0 + 32))
                    vv.drain()
                    vv.tensor_tensor(self.scr(offd, 0, T), self.scr(offd, 0, T),
                                     self.scr(off_tab + tabC * T, 0, T),
                                     ALU.mult)
                    vv.tensor_tensor(self.scr(off_sw, 0, T),
                                     self.scr(off_sw, 0, T),
                                     self.scr(off_tab + tabS * T, 0, T),
                                     ALU.mult)
                    vv.drain()
                    vn = self.V.inc(vv.tensor_tensor(
                        self.scr(offd, 0, T), self.scr(offd, 0, T),
                        self.scr(off_sw, 0, T), ALU.add))
                    vv.drain()
                    self.done("SCRsw", self.V, vn)
                rope_v = self.V.n
                self.done("WC" + "ab"[buf], self.V, rope_v)

                for h2 in range(2):
                    h = hp * 2 + h2
                    pbase = h2 * 64
                    ps_o = (self.P4, self.P5)[h2]
                    for qt in range(4):
                        kcs = [kc for kc in range(max(0, 4 * (qt - 1)), 4 * qt + 4)]
                        for ki, kc in enumerate(kcs):
                            dlt = 128 * kc - 512 * qt
                            lo = max(0, dlt)
                            hi = min(512, dlt + 640)
                            pss = (self.P0, self.P1)[ki % 2]
                            self.use(self.T, "P" + str(ki % 2))
                            self.T.wait(self.V, rope_v)
                            tn = self.T.inc(nc.tensor.matmul(
                                pss[:, lo:hi],
                                self.scr(off_k, kc * P, (kc + 1) * P, pbase, pbase + 64),
                                self.scr(off_q, qt * 512 + lo, qt * 512 + hi,
                                         pbase, pbase + 64),
                                start=True, stop=True))
                            self.S.wait(self.T, tn)
                            sn = self.S.inc(nc.scalar.activation(
                                out=self.scr(off_pt, ki * 512 + lo, ki * 512 + hi),
                                in_=pss[:, lo:hi], func=AF.Exp, scale=8.0))
                            self.done("P" + str(ki % 2), self.S, sn)
                            self.V.wait(self.S, sn)
                            vv = nc.vector
                            if dlt >= 0:
                                self.V.inc(vv.tensor_tensor(
                                    self.scr(off_pt, ki * 512 + dlt, ki * 512 + dlt + P),
                                    self.scr(off_pt, ki * 512 + dlt, ki * 512 + dlt + P),
                                    self.MSK[:, 0:P], ALU.mult))
                            else:
                                d2 = dlt + 512
                                self.V.inc(vv.tensor_tensor(
                                    self.scr(off_pt, ki * 512 + d2, ki * 512 + d2 + P),
                                    self.scr(off_pt, ki * 512 + d2, ki * 512 + d2 + P),
                                    self.MSK[:, P:2 * P], ALU.mult))
                        s_exp = self.S.n
                        v_msk = self.V.n
                        self.T.wait(self.S, s_exp)
                        self.T.wait(self.V, v_msk)
                        self.use(self.T, ("P4", "P5")[h2])
                        if hp == 0 and h2 == 0 and qt == 0:
                            self.T.wait(self.V, v_ready)
                        t_pv = 0
                        kio = sorted(range(len(kcs)),
                                     key=lambda ki_: (not (
                                         max(0, 128 * kcs[ki_] - 512 * qt) == 0
                                         and min(512, 128 * kcs[ki_] - 512 * qt + 640) == 512), ki_))
                        for oi, ki in enumerate(kio):
                            kc = kcs[ki]
                            dlt = 128 * kc - 512 * qt
                            lo = max(0, dlt)
                            hi = min(512, dlt + 640)
                            slot = off_v + kc * 520 + h * 65
                            mm = nc.tensor.matmul(
                                ps_o[0:65, lo:hi], self.SCR[:, slot:slot + 65],
                                self.scr(off_pt, ki * 512 + lo, ki * 512 + hi),
                                start=(oi == 0), stop=(oi == len(kcs) - 1),
                                skip_group_check=True)
                            if oi == len(kcs) - 1:
                                t_pv = self.T.inc(mm)
                        self.done("SCRpt", self.T, t_pv)
                        self.V.wait(self.T, t_pv)
                        rl_row = self.RW[0:1, 2560:3072]
                        self.use(self.V, "RW6")
                        vn = self.V.inc(nc.vector.reciprocal(rl_row, ps_o[64:65, :]))
                        self.use(self.T, "P6")
                        self.T.wait(self.V, vn)
                        t_bc = self.T.inc(nc.tensor.matmul(
                            self.P6[0:64, :], self.RW[0:1, 0:64], rl_row,
                            start=True, stop=True))
                        self.done("RW6", self.T, t_bc)
                        self.use(self.S, "SCRt2")
                        self.S.wait(self.T, t_bc)
                        sn = self.S.inc(nc.scalar.copy(
                            out=self.scr(off_t2, 0, 512, 0, 64), in_=ps_o[0:64, :]))
                        self.done(("P4", "P5")[h2], self.S, sn)
                        self.V.wait(self.S, sn)
                        self.use(self.V, "SCRao")
                        vn = self.V.inc(nc.vector.tensor_tensor(
                            self.scr(off_ao, qt * 512, (qt + 1) * 512, pbase, pbase + 64),
                            self.scr(off_t2, 0, 512, 0, 64), self.P6[0:64, :], ALU.mult))
                        self.done("P6", self.V, vn)
                        self.done("SCRt2", self.V, vn)
                self.D.wait(self.V, self.V.n)
                dn = self.st_ao.dma(attb[b, hp], self.scr(off_ao, 0, T))
                self.done("SCRao", self.st_ao, dn)
                self.done("SCRq", self.st_ao, dn)
                self.done("SCRk", self.st_ao, dn)

            self.done("WCOL", self.V)
            self.swait(self.D, self.st_ao)
            self.use(self.D, "SCRal")
            self.D.wait(self.T, self.T.n)
            al = 0
            for cc in range(4):
                al = self.st_al.dma(
                    self.SCR[:, off_al + cc * T: off_al + (cc + 1) * T], attb[b, cc])
            self.use(self.D, "WCOL", "WCa", "WCb")
            wsts = (self.st_wa, self.st_wb)
            wlds = [0, 0]
            wlds[0] = self.load_wcol2(self.stgw["b_wo" + str(l)], 0,
                                      self.WCOL[:, 0:512], wsts[0])
            for oc in range(NCC):
                buf = oc % 2
                wsl = self.WCOL[:, buf * 512: buf * 512 + 512]
                if oc + 1 < NCC:
                    nb = 1 - buf
                    self.use(self.D, "WC" + "ab"[nb])
                    wlds[nb] = self.load_wcol2(
                        self.stgw["b_wo" + str(l)], oc + 1,
                        self.WCOL[:, nb * 512: nb * 512 + 512], wsts[nb])
                wld = wlds[buf]
                for tt in range(4):
                    pst = (self.P0, self.P1)[tt % 2]
                    self.use(self.T, "P" + str(tt % 2))
                    self.swait(self.T, wsts[buf], wld)
                    self.swait(self.T, self.st_al, al)
                    tn = 0
                    for cc in range(4):
                        mm = nc.tensor.matmul(
                            pst[:], wsl[:, cc * P:(cc + 1) * P],
                            self.SCR[:, off_al + cc * T + tt * 512:
                                     off_al + cc * T + (tt + 1) * 512],
                            start=(cc == 0), stop=(cc == 3))
                        if cc == 3:
                            tn = self.T.inc(mm)
                    self.V.wait(self.T, tn)
                    vn = self.V.inc(nc.vector.tensor_tensor(
                        self.xmid(oc, tt * 512, (tt + 1) * 512), pst[:],
                        self.xmid(oc, tt * 512, (tt + 1) * 512), ALU.add))
                    self.done("P" + str(tt % 2), self.V, vn)
                self.done("WC" + "ab"[buf], self.V)
            self.done("WCOL", self.V)
            self.done("SCRal", self.V)
            self.done("SCRv", self.V)
            self.dump_dbg("xa%d" % l, b)

            self.use(self.D, "BCOLg")
            for cc in range(NCC):
                self.ss_bcol.dma(self.BCOL[:, 8 + cc:9 + cc],
                                 ln_g[l, cc * P:(cc + 1) * P].unsqueeze(1))
                self.ss_bcol.dma(self.BCOL2[:, cc:cc + 1],
                                 ln_b[l, cc * P:(cc + 1) * P].unsqueeze(1))
            g_d = self.ss_bcol.n
            self.swait(self.S, self.ss_bcol, g_d)
            self.barrier()
            self.emit_ln(srcf=xshm, dstf=None,
                         dstb=self.xn,
                         gains=(lambda cc: self.BCOL[:, 8 + cc:9 + cc],
                                lambda cc: self.BCOL2[:, cc:cc + 1]))
            self.done("BCOLg", self.S)
            self.barrier()
            self.dump_dbg("hn%d" % l, b, src="xn")

            b1a_c = self.BCOL[:, 0:1]
            b1g_c = self.BCOL[:, 1:2]
            b2_c = self.BCOL[:, 2:3]
            for tt in range(4):
                lo, hi = tt * 512, (tt + 1) * 512
                self.use(self.D, "WCOL", "WCa", "WCb")
                wsts = (self.st_wa, self.st_wb)
                wlds = [0, 0]
                self.load_wcol2(self.stgw["b_w1a" + str(l)], 0,
                                self.WCOL[:, 0:1024], wsts[0])
                wlds[0] = self.load_wcol2(self.stgw["b_w1g" + str(l)], 0,
                                          self.WCOL[:, 1024:2048], wsts[0])
                for j in range(NJ):
                    w = min(P, HID - j * P)
                    buf = j % 2
                    wa = self.WCOL[:, buf * 2048: buf * 2048 + 1024]
                    wg = self.WCOL[:, buf * 2048 + 1024: buf * 2048 + 2048]
                    if j + 1 < NJ:
                        nb = 1 - buf
                        self.use(self.D, "WC" + "ab"[nb])
                        self.load_wcol2(self.stgw["b_w1a" + str(l)], j + 1,
                                        self.WCOL[:, nb * 2048: nb * 2048 + 1024],
                                        wsts[nb])
                        wlds[nb] = self.load_wcol2(
                            self.stgw["b_w1g" + str(l)], j + 1,
                            self.WCOL[:, nb * 2048 + 1024: nb * 2048 + 2048],
                            wsts[nb])
                    wld = wlds[buf]
                    self.use(self.D, "BCOL01")
                    self.ss_bcol.dma(b1a_c[0:w, :], b1[l, j * P: j * P + w].unsqueeze(1))
                    self.ss_bcol.dma(b1g_c[0:w, :],
                                     b1[l, HID + j * P: HID + j * P + w].unsqueeze(1))
                    bd = self.ss_bcol.n
                    for (wc, bcol, half) in ((wa, b1a_c, 0), (wg, b1g_c, 1)):
                        pst = (self.P0, self.P1)[half]
                        self.use(self.T, "P" + str(half))
                        self.swait(self.T, wsts[buf], wld)
                        tn = 0
                        for cc in range(NCC):
                            mm = nc.tensor.matmul(
                                pst[0:w, :], wc[:, cc * P: cc * P + w],
                                self.xn(cc, lo, hi),
                                start=(cc == 0), stop=(cc == NCC - 1))
                            if cc == NCC - 1:
                                tn = self.T.inc(mm)
                        self.S.wait(self.T, tn)
                        self.swait(self.S, self.ss_bcol, bd)
                        self.use(self.S, "X2" + str(half))
                        sn = self.S.inc(nc.scalar.activation(
                            out=self.X2[0:w, half * 512:half * 512 + 512],
                            in_=pst[0:w, :],
                            func=(AF.Identity if half == 0 else AF.Gelu),
                            bias=bcol[0:w, :], scale=1.0))
                        self.done("P" + str(half), self.S, sn)
                    s_ag = self.S.n
                    self.done("WC" + "ab"[buf], self.S, s_ag)
                    self.done("BCOL01", self.S, s_ag)
                    self.V.wait(self.S, s_ag)
                    self.use(self.V, "SCRhf")
                    vn = self.V.inc(nc.vector.tensor_tensor(
                        self.scr(off_hf, j * 512, j * 512 + 512, 0, w),
                        self.X2[0:w, 0:512], self.X2[0:w, 512:1024], ALU.mult))
                    self.done("X20", self.V, vn)
                    self.done("X21", self.V, vn)
                hf_v = self.V.n
                HW2 = 11 * P
                self.use(self.D, "WCOL", "WCa", "WCb")
                wsts = (self.st_wa, self.st_wb)
                wlds = [0, 0]
                wlds[0] = self.load_wcol2_part(
                    self.stgw["b_w2" + str(l)], 0, 0, HW2,
                    self.WCOL[:, 0:HW2], wsts[0])
                for oc in range(NCC):
                    self.use(self.D, "BCOL2")
                    self.ss_b2.dma(b2_c, b2[l, oc * P:(oc + 1) * P].unsqueeze(1))
                    b2d = self.ss_b2.n
                    pst = (self.P0, self.P1)[oc % 2]
                    self.use(self.T, "P" + str(oc % 2))
                    self.T.wait(self.V, hf_v)
                    tn = 0
                    for hv in range(2):
                        idx = oc * 2 + hv
                        buf = idx % 2
                        if idx + 1 < 16:
                            nb = 1 - buf
                            noc, nhv = divmod(idx + 1, 2)
                            self.use(self.D, "WC" + "ab"[nb])
                            wlds[nb] = self.load_wcol2_part(
                                self.stgw["b_w2" + str(l)], noc, nhv * HW2, HW2,
                                self.WCOL[:, nb * HW2: nb * HW2 + HW2], wsts[nb])
                        wsl = self.WCOL[:, buf * HW2: buf * HW2 + HW2]
                        self.swait(self.T, wsts[buf], wlds[buf])
                        tg = 0
                        for jl in range(11):
                            j = hv * 11 + jl
                            w = min(P, HID - j * P)
                            mm = nc.tensor.matmul(
                                pst[:], wsl[0:w, jl * P:(jl + 1) * P],
                                self.scr(off_hf, j * 512, j * 512 + 512, 0, w),
                                start=(j == 0), stop=(j == NJ - 1))
                            if jl == 10:
                                tg = self.T.inc(mm)
                        self.done("WC" + "ab"[buf], self.T, tg)
                        if hv == 1:
                            tn = tg
                    self.V.wait(self.T, tn)
                    self.swait(self.V, self.ss_b2, b2d)
                    vn = self.V.inc(nc.vector.scalar_tensor_tensor(
                        out=self.xmid(oc, lo, hi), in0=pst[:], scalar=b2_c,
                        in1=self.xmid(oc, lo, hi), op0=ALU.add, op1=ALU.add))
                    self.done("P" + str(oc % 2), self.V, vn)
                    self.done("BCOL2", self.V)
                self.done("WCOL", self.T)
                self.done("SCRhf", self.T)
            self.barrier()
            vn = 0
            for cc in range(NCC):
                vn = self.V.inc(nc.vector.tensor_copy(out=self.xn(cc), in_=self.xmid(cc)))
            self.done("XN", self.V, vn)
            self.done("SCRtab", self.V, vn)
            self.done("BCOL45", self.V, vn)
            self.barrier()
            self.dump_dbg("x%d" % l, b)
        return

    def emit_l1_ffn_prep(self, fc_logits, pj_logits, fc_s1, fc_s2, pj_s1,
                         pj_s2, fc_mix, pj_mix, iota8, ssel, mixt, mixtp):
        nc = self.nc
        import concourse.mybir as mybir_
        r = lambda i, w=8: self.TMP[0:1, 64 + i * 16: 64 + i * 16 + w]
        self.barrier()
        fl_i32 = self.sbuf("FLI", [1, 320], I32)
        self.FLI = fl_i32
        pos = 0
        self.swait(self.D, self.st_setup)
        iot = self.TMP[0:1, 48:56]
        self.st_tmp.dma(iot, iota8)
        eqrows = {}
        for (lg, tens) in (("fc", fc_logits), ("pj", pj_logits)):
            lrow = r(0)
            self.D.wait(self.V, self.V.n)
            self.D.wait(self.T, self.T.n)
            self.st_tmp.dma(lrow, tens.unsqueeze(0))
            self.swait(self.V, self.st_tmp)
            v = nc.vector
            m1 = r(1, 1)
            v.tensor_reduce(m1, lrow, mybir_.AxisListType.X, ALU.max)
            v.drain()
            eq1 = r(2) if lg == "fc" else r(16)
            v.tensor_scalar(eq1, lrow, m1, None, ALU.is_equal)
            v.drain()
            msk = r(3)
            v.scalar_tensor_tensor(msk, eq1, -1e30, lrow, ALU.mult, ALU.add)
            v.drain()
            m2 = r(4, 1)
            v.tensor_reduce(m2, msk, mybir_.AxisListType.X, ALU.max)
            v.drain()
            eq2 = r(5) if lg == "fc" else r(17)
            v.tensor_scalar(eq2, msk, m2, None, ALU.is_equal)
            v.drain()
            eqrows[lg] = (eq1, eq2)
            d_ = r(6, 1)
            vn = self.V.inc(v.tensor_tensor(d_, m2, m1, ALU.subtract))
            v.drain()
            self.S.wait(self.V, vn)
            e_ = r(8, 1)
            sn = self.S.inc(nc.scalar.activation(out=e_, in_=d_, func=AF.Exp))
            self.S.drain()
            self.V.wait(self.S, sn)
            se = r(9, 1)
            v.tensor_scalar(se, e_, 1.0, None, ALU.add)
            v.drain()
            c1_ = r(10, 1)
            v.reciprocal(c1_, se)
            v.drain()
            c2_ = r(11, 1)
            vn = self.V.inc(v.tensor_scalar(c2_, c1_, -1.0, 1.0, ALU.mult, ALU.add))
            v.drain()
            self.T.wait(self.V, vn)
            self.use(self.T, "P2")
            col0 = 8 if lg == "fc" else 10
            nc.tensor.matmul(self.P2[:, 0:1], self.RW[0:1, 0:P], c1_,
                             start=True, stop=True)
            tn = self.T.inc(nc.tensor.matmul(self.P2[:, 1:2], self.RW[0:1, 0:P],
                                             c2_, start=True, stop=True))
            self.V.wait(self.T, tn)
            v.tensor_copy(out=self.BCOL2[:, col0:col0 + 1], in_=self.P2[:, 0:1])
            v.drain()
            vn = self.V.inc(v.tensor_copy(out=self.BCOL2[:, col0 + 1:col0 + 2],
                                          in_=self.P2[:, 1:2]))
            self.done("P2", self.V, vn)
            pos += 160
        for ti, (lg, ki) in enumerate((("fc", 0), ("fc", 1), ("pj", 0), ("pj", 1))):
            eq = eqrows[lg][ki]
            for sh in range(8):
                self.use(self.T, "P2")
                tn = self.T.inc(nc.tensor.matmul(
                    self.P2[:, 2:3], self.RW[0:1, 0:P], eq[0:1, sh:sh + 1],
                    start=True, stop=True))
                self.V.wait(self.T, tn)
                vn = self.V.inc(nc.vector.tensor_copy(
                    out=self.BCOL2[:, 12 + ti * 8 + sh: 13 + ti * 8 + sh],
                    in_=self.P2[:, 2:3]))
                nc.vector.drain()
                self.done("P2", self.V, vn)
        for ti in range(4):
            shc = self.BCOL2[:, 44 + ti: 45 + ti]
            nc.vector.tensor_scalar(
                shc, self.BCOL2[:, 13 + ti * 8: 14 + ti * 8], 1.0, None,
                ALU.mult)
            nc.vector.drain()
            for bb in range(2, 8):
                op = nc.vector.scalar_tensor_tensor(
                    shc, self.BCOL2[:, 12 + ti * 8 + bb: 13 + ti * 8 + bb],
                    float(bb), shc, ALU.mult, ALU.add)
                if bb == 7:
                    self.V.inc(op)
                nc.vector.drain()
        self.barrier()

        pcol_h = self.din("hc_pcols", [P, 8])
        ocol_h = self.din("hc_ocols", [P, 64])
        self.IDX = self.sbuf("IDXS", [P, 96], I32)
        self.IDI = self.sbuf("IDI", [P, 32], I32)
        self.PCt = self.sbuf("PCt", [P, 8], F32)
        self.OCt = self.sbuf("OCt", [P, 64], F32)
        self.st_setup.dma(self.PCt[:], pcol_h)
        self.st_setup.dma(self.OCt[:], ocol_h)
        self.swait(self.V, self.st_setup)
        v = nc.vector
        iota = self.PCt[:, 0:1]
        fcs = self.TMP[:, 512:513]
        for (ci, ti, R) in ((0, 0, 1024), (1, 1, 1024), (2, 0, 4096),
                            (3, 1, 4096), (4, 2, 4096), (5, 3, 4096),
                            (6, 2, 1024), (7, 3, 1024)):
            shc = self.BCOL2[:, 44 + ti: 45 + ti]
            v.tensor_scalar(fcs, shc, float(R), None, ALU.mult)
            v.drain()
            v.tensor_tensor(fcs, fcs, iota, ALU.add)
            v.drain()
            self.V.inc(v.tensor_copy(out=self.IDX[:, ci:ci + 1], in_=fcs))
            v.drain()
        for (ti_base, n, ocn, OCd, cb, kk, pmc, pdc) in (
                (0, 16, 8, 8, 8, 8.0, 2, 3), (2, 64, 32, 32, 24, 2.0, 4, 1)):
            for ki in range(2):
                ti = ti_base + ki
                shc = self.BCOL2[:, 44 + ti: 45 + ti]
                Ac = self.TMP[:, 512:513]
                Bc = self.TMP[:, 513:514]
                pmod = self.PCt[:, pmc:pmc + 1]
                pdiv = self.PCt[:, pdc:pdc + 1]
                v.tensor_scalar(Ac, pmod, shc, float(n), ALU.subtract,
                                ALU.add)
                v.drain()
                v.tensor_scalar(Bc, Ac, 1.0 / n, -(n - 1) / (2.0 * n),
                                ALU.mult, ALU.add)
                v.drain()
                v.tensor_copy(out=self.IDI[:, 0:1], in_=Bc)
                v.drain()
                v.tensor_copy(out=Bc, in_=self.IDI[:, 0:1])
                v.drain()
                v.scalar_tensor_tensor(Ac, Bc, -float(n), Ac, ALU.mult,
                                       ALU.add)
                v.drain()
                v.tensor_scalar(Ac, Ac, 64.0, None, ALU.mult)
                v.drain()
                v.tensor_scalar(Ac, Ac, pdiv, None, ALU.add)
                v.drain()
                A = self.TMP[:, 576:576 + ocn]
                B = self.TMP[:, 640:640 + ocn]
                v.tensor_scalar(A, self.OCt[:, 0:ocn], kk / 2.0, None,
                                ALU.mult)
                v.drain()
                v.tensor_scalar(A, A, Ac, None, ALU.add)
                v.drain()
                v.tensor_scalar(B, A, 1.0 / 128, -127.0 / 256.0,
                                ALU.mult, ALU.add)
                v.drain()
                v.tensor_copy(out=self.IDI[:, 0:ocn], in_=B)
                v.drain()
                v.tensor_copy(out=B, in_=self.IDI[:, 0:ocn])
                v.drain()
                v.scalar_tensor_tensor(A, B, -128.0, A, ALU.mult, ALU.add)
                v.drain()
                v.tensor_scalar(A, A, float(OCd), None, ALU.mult)
                v.drain()
                v.tensor_tensor(A, A, B, ALU.add)
                v.drain()
                self.V.inc(v.tensor_copy(out=self.IDX[:, cb + ki * ocn:
                                                      cb + ki * ocn + ocn],
                                         in_=A))
                v.drain()
        self.barrier()

        IOA = lambda col: bass.IndirectOffsetOnAxis(ap=col, axis=0)
        sgq = self.sg_w
        for (key, tens, R, ti_base, scaled, colR) in (
                ("s1f", fc_s1, 1024, 0, False, 0),
                ("s2f", fc_s2, 4096, 0, True, 2),
                ("s1p", pj_s1, 4096, 2, False, 4),
                ("s2p", pj_s2, 1024, 2, True, 6)):
            srcv = APc(tens.tensor, tens.offset, [[64, 8 * R], [1, 64]])
            for ki in range(2):
                icol = self.IDX[:, colR + ki: colR + ki + 1]
                ccol = self.BCOL2[:, (8 if ti_base == 0 else 10) + ki:
                                  (9 if ti_base == 0 else 11) + ki]
                for cc in range(R // P):
                    buf = cc % 2
                    xsl = self.XMID[:, buf * 64: buf * 64 + 64]
                    bsl = self.SCR[:, buf * 64: buf * 64 + 64]
                    self.use(self.G, "PXA" + str(buf))
                    g = nc.gpsimd.indirect_dma_start(
                        out=xsl, out_offset=None, in_=srcv,
                        in_offset=IOA(icol), element_offset=cc * P * 64)
                    g.then_inc(sgq.sem, 16)
                    sgq.n += 16
                    self.swait(self.V, sgq)
                    self.use(self.V, "PSB" + str(buf))
                    if scaled:
                        vn = self.V.inc(v.tensor_scalar(bsl, xsl, ccol, None,
                                                        ALU.mult))
                    else:
                        vn = self.V.inc(v.tensor_copy(out=bsl, in_=xsl))
                    self.done("PXA" + str(buf), self.V, vn)
                    self.D.wait(self.V, vn)
                    st = self.st_x.dma(ssel[key][ki, cc * P:(cc + 1) * P, :],
                                       bsl)
                    self.done("PSB" + str(buf), self.st_x, st)
        self.barrier()

        for (lg, mix, NR, W, grp, OCs, scol, cb) in (
                ("fc", fc_mix, 4096, 1024, 4, 8, 2, 8),
                ("pj", pj_mix, 1024, 4096, 1, 32, 6, 24)):
            srcv = APc(mix.tensor, mix.offset, [[W, 8 * NR], [1, W]])
            rowW = grp * P
            ngrp = NR // P // grp
            for ki in range(2):
                mtp = mixtp[lg + str(ki)]
                nrows = NR * W // rowW
                outv = APc(mtp, 0, [[rowW, nrows], [1, rowW]])
                gi = self.IDX[:, scol + ki: scol + ki + 1]

                def _slot(mg_, u_):
                    return (mg_ % 2) * grp + u_

                def _gather(mg_):
                    if mg_ >= ngrp:
                        return
                    for u in range(grp):
                        mcc = mg_ * grp + u
                        slot = _slot(mg_, u)
                        xsl = self.XMID[:, slot * W: (slot + 1) * W]
                        self.use(self.G, "MXA%d" % slot)
                        g = nc.gpsimd.indirect_dma_start(
                            out=xsl, out_offset=None, in_=srcv,
                            in_offset=IOA(gi), element_offset=mcc * P * W)
                        g.then_inc(sgq.sem, 16)
                        sgq.n += 16

                _gather(0)
                gn_prev = sgq.n
                for mg in range(ngrp):
                    _gather(mg + 1)
                    cvn = 0
                    for u in range(grp):
                        slot = _slot(mg, u)
                        xsl = self.XMID[:, slot * W: (slot + 1) * W]
                        csl = self.SCR[:, slot * W: (slot + 1) * W]
                        self.swait(self.V, sgq, gn_prev)
                        self.use(self.V, "MCB%d" % slot)
                        cvn = self.V.inc(v.tensor_copy(out=csl, in_=xsl))
                        self.done("MXA%d" % slot, self.V, cvn)
                    gn_prev = sgq.n
                    for o_ in range(OCs):
                        ps = (self.P2, self.P3)[o_ % 2]
                        self.use(self.T, "P" + str(2 + o_ % 2))
                        self.T.wait(self.V, cvn)
                        tn = 0
                        for u in range(grp):
                            slot = _slot(mg, u)
                            csl = self.SCR[:, slot * W: (slot + 1) * W]
                            mm = nc.tensor.matmul(
                                ps[:, u * P:(u + 1) * P],
                                csl[:, o_ * P:(o_ + 1) * P], self.IDNB[:],
                                start=True, stop=True)
                            if u == grp - 1:
                                tn = self.T.inc(mm)
                        self.V.wait(self.T, tn)
                        sslot = o_ % 4
                        stg = self.SCR[:, 16384 + sslot * 512:
                                       16384 + sslot * 512 + rowW]
                        self.use(self.V, "MST%d" % sslot)
                        svn = self.V.inc(v.tensor_copy(out=stg,
                                                       in_=ps[:, 0:rowW]))
                        self.done("P" + str(2 + o_ % 2), self.V, svn)
                        self.G.wait(self.V, svn)
                        sc = nc.gpsimd.indirect_dma_start(
                            out=outv,
                            out_offset=IOA(self.IDX[:, cb + ki * OCs + o_:
                                                    cb + ki * OCs + o_ + 1]),
                            in_=stg, in_offset=None,
                            element_offset=mg * P * OCs * rowW)
                        sc.then_inc(sgq.sem, 16)
                        sgq.n += 16
                        self.done("MST%d" % sslot, sgq)
                    for u in range(grp):
                        slot = _slot(mg, u)
                        self.done("MCB%d" % slot, self.T, tn)
        self.barrier()
        self.V.inc(nc.vector.memset(self.XMID[:], 0.0))
        self.barrier()
        return

    def load_flags(self, tag):
        import concourse.mybir as mybir__
        gidx = 0 if tag.startswith("fc") else 1
        ki = int(tag[-1])
        base = gidx * 160
        vals = [self.nc.values_load(
            self.FLI[0:1, base + ki * 8 + bb: base + ki * 8 + bb + 1],
            min_val=0, max_val=1, skip_runtime_bounds_check=True)
            for bb in range(8)]
        return vals

    def load_idx(self, tag):
        gidx = 0 if tag.startswith("fc") else 1
        ki = int(tag[-1])
        base = gidx * 160
        return self.nc.values_load(self.FLI[0:1, base + 16 + ki: base + 17 + ki],
                                   min_val=0, max_val=7,
                                   skip_runtime_bounds_check=True)

    def free_flags(self, vals):
        for v in vals:
            for h in v.val.handles:
                en = str(h.engine).split(".")[-1]
                eng = {"Pool": self.nc.gpsimd, "SP": self.nc.sync,
                       "PE": self.nc.tensor, "DVE": self.nc.vector,
                       "Activation": self.nc.scalar}.get(en)
                if eng is None:
                    continue
                try:
                    eng.free_register(h)
                except Exception:
                    pass

    def emit_rms(self, eps=1e-6):
        nc = self.nc
        ps_sq = self.P4[64:65, :]
        r_row = self.row(2)
        for tt in range(T // 512):
            lo, hi = tt * 512, (tt + 1) * 512
            self.use(self.T, "P4")
            for cc in range(NCC):
                xh = self.X2[:, (cc % 2) * 512:(cc % 2) * 512 + 512]
                self.use(self.S, "X2" + str(cc % 2))
                sn = self.S.inc(nc.scalar.activation(
                    out=xh, in_=self.xmid(cc, lo, hi), func=AF.Square))
                self.T.wait(self.S, sn)
                tn = self.T.inc(nc.tensor.matmul(
                    ps_sq, self.OCOLB[:, 0:1], xh,
                    start=(cc == 0), stop=(cc == NCC - 1)))
                self.done("X2" + str(cc % 2), self.T, tn)
            self.V.wait(self.T, self.T.n)
            self.use(self.V, "LNrows")
            v = nc.vector
            vn = self.V.inc(v.tensor_scalar(r_row, ps_sq, 1.0 / D, float(eps),
                                            ALU.mult, ALU.add))
            self.done("P4", self.V, vn)
            self.S.wait(self.V, vn)
            nc.scalar.activation(out=r_row, in_=r_row, func=AF.Ln)
            self.S.drain()
            sn = self.S.inc(nc.scalar.activation(out=r_row, in_=r_row,
                                                 func=AF.Exp, scale=-0.5))
            self.use(self.T, "P0")
            self.T.wait(self.S, sn)
            t_bc = self.T.inc(nc.tensor.matmul(
                self.P0[:], self.RW[0:1, 0:P], r_row, start=True, stop=True))
            self.done("LNrows", self.T, t_bc)
            self.V.wait(self.T, t_bc)
            vn = 0
            for cc in range(NCC):
                vn = self.V.inc(v.tensor_tensor(
                    self.xn(cc, lo, hi), self.xmid(cc, lo, hi), self.P0[:],
                    ALU.mult))
            self.done("P0", self.V, vn)
        self.done("XN", self.V)
        return

    def _l1f_normrope(self, offd, gcol, eps=1e-6):
        nc = self.nc
        v = nc.vector
        off_sw = self._l1f_offsw
        for tt in range(4):
            lo, hi = tt * 512, (tt + 1) * 512
            self.use(self.S, "X20")
            sn = self.S.inc(nc.scalar.activation(
                out=self.X2[:, 0:512], in_=self.scr(offd, lo, hi), func=AF.Square))
            self.S.drain()
            self.T.wait(self.S, sn)
            self.use(self.T, "P6")
            tn = self.T.inc(nc.tensor.matmul(
                self.P6[0:2, :], self.BLK2[:, 0:2], self.X2[:, 0:512],
                start=True, stop=True))
            self.done("X20", self.T, tn)
            self.V.wait(self.T, tn)
            self.use(self.V, "RW6")
            vn = self.V.inc(v.tensor_scalar(self.TMP[0:2, 0:512], self.P6[0:2, :],
                                            1.0 / 64, float(eps), ALU.mult,
                                            ALU.add))
            self.S.wait(self.V, vn)
            nc.scalar.activation(out=self.TMP[0:2, 0:512], in_=self.TMP[0:2, 0:512],
                                 func=AF.Ln)
            self.S.drain()
            sn = self.S.inc(nc.scalar.activation(
                out=self.TMP[0:2, 0:512], in_=self.TMP[0:2, 0:512],
                func=AF.Exp, scale=-0.5))
            self.V.wait(self.S, sn)
            vn = self.V.inc(v.tensor_scalar(self.TMP[0:2, 0:512],
                                            self.TMP[0:2, 0:512], gcol, None,
                                            ALU.mult))
            self.use(self.T, "P2")
            self.T.wait(self.V, vn)
            t_bc = self.T.inc(nc.tensor.matmul(
                self.P2[:], self.BLK2T[0:2, :], self.TMP[0:2, 0:512],
                start=True, stop=True))
            self.done("RW6", self.T, t_bc)
            self.done("P6", self.T, t_bc)
            self.V.wait(self.T, t_bc)
            vn = self.V.inc(v.tensor_tensor(
                self.scr(offd, lo, hi), self.scr(offd, lo, hi),
                self.P2[:], ALU.mult))
            self.done("P2", self.V, vn)
        self.swait(self.V, self.ss_tab, getattr(self, "tab_d", 0))
        self.use(self.V, "SCRsw2")
        for (d0, s0) in ((0, 32), (32, 0), (64, 96), (96, 64)):
            v.tensor_copy(out=self.scr(off_sw, 0, T, d0, d0 + 32),
                          in_=self.scr(offd, 0, T, s0, s0 + 32))
        v.drain()
        v.tensor_tensor(self.scr(offd, 0, T), self.scr(offd, 0, T),
                        self.TAB[:, 0:T], ALU.mult)
        v.tensor_tensor(self.scr(off_sw, 0, T), self.scr(off_sw, 0, T),
                        self.TAB[:, T:2 * T], ALU.mult)
        v.drain()
        vn = self.V.inc(v.tensor_tensor(
            self.scr(offd, 0, T), self.scr(offd, 0, T),
            self.scr(off_sw, 0, T), ALU.add))
        v.drain()
        self.done("SCRsw2", self.V, vn)
        return self.V.n

    def emit_l1_ffn(self, b, wq_bf, wk_bf, wv_bf, wo_bf, q_gain, k_gain,
                    ssel, mixtp, attb, hfc_d):
        nc = self.nc
        off_v = 0
        off_k = 4160
        off_kd = 8256
        off_q = 12352
        off_sw = 14400
        off_t2 = 16448
        off_pt = 18496
        off_ao = 26688
        self._l1f_offsw = off_sw
        self._l1f_offt2 = off_t2
        EXPSC = 0.125

        self.emit_rms()
        self.barrier()

        gq_c = self.BCOL[0:2, 12:13]
        for ck in range(2):
            wsl = self.WCOL[:, 0:1024]
            self.use(self.D, "WCOL")
            self.D.wait(self.T, self.T.n)
            for st_ in self.sg_band:
                self.swait(self.D, st_)
            for cc in range(NCC):
                wld = self.ss_wt.dma(
                    wsl[:, cc * P:(cc + 1) * P],
                    wk_bf[ck * P:(ck + 1) * P, cc * P:(cc + 1) * P],
                    transpose=True)
            for tt in range(4):
                pst = (self.P0, self.P1)[tt % 2]
                self.use(self.T, "P" + str(tt % 2))
                self.swait(self.T, self.ss_wt, wld)
                self.swait(self.T, self.sg_xn, self.x_g)
                tn = 0
                for cc in range(NCC):
                    mm = nc.tensor.matmul(
                        pst[:], wsl[:, cc * P:(cc + 1) * P],
                        self.xn(cc, tt * 512, (tt + 1) * 512),
                        start=(cc == 0), stop=(cc == NCC - 1))
                    if cc == NCC - 1:
                        tn = self.T.inc(mm)
                self.S.wait(self.T, tn)
                self.use(self.S, "SCRk")
                sn = self.S.inc(nc.scalar.copy(
                    out=self.scr(off_k + ck * T, tt * 512, (tt + 1) * 512),
                    in_=pst[:]))
                self.S.drain()
                self.done("P" + str(tt % 2), self.S, sn)
            self.use(self.D, "BCOLg2")
            gl = self.ss_bcol.dma(gq_c, k_gain[2 * ck: 2 * ck + 2].unsqueeze(1))
            self.swait(self.V, self.ss_bcol, gl)
            self._l1f_normrope(off_k + ck * T, gq_c)
            self.done("BCOLg2", self.V)
            self.done("SCRk", self.V)
        self.done("WCOL", self.T)
        self.G.wait(self.V, self.V.n)
        kd = 0
        for ck in range(2):
            for (d0, s0) in ((0, 64), (64, 0)):
                kd = self.sg_w.dma(
                    self.scr(off_kd + ck * T, 0, T, d0, d0 + 64),
                    self.scr(off_k + ck * T, 0, T, s0, s0 + 64))
        self.kd_n = kd
        self.use(self.V, "SCRv")
        v_ones = self.V.inc(nc.vector.memset(self.SCR[:, off_v:off_v + 4160], 1.0))
        wsl = self.WCOL[:, 1024 + 0:1024 + 2048]
        self.use(self.D, "WCOLv")
        self.D.wait(self.T, self.T.n)
        for cc in range(NCC):
            wld = self.ss_wt.dma(
                wsl[:, cc * 256:(cc + 1) * 256],
                wv_bf[0:256, cc * P:(cc + 1) * P], transpose=True)
        for tk in range(T // P):
            pst = (self.P0, self.P1)[tk % 2]
            self.use(self.T, "P" + str(tk % 2))
            self.swait(self.T, self.ss_wt, wld)
            tn = 0
            for cc in range(NCC):
                mm = nc.tensor.matmul(
                    pst[:, 0:256], self.xn(cc, tk * P, (tk + 1) * P),
                    wsl[:, cc * 256:(cc + 1) * 256],
                    start=(cc == 0), stop=(cc == NCC - 1))
                if cc == NCC - 1:
                    tn = self.T.inc(mm)
            self.V.wait(self.T, tn)
            if tk == 0:
                self.V.wait(self.V, v_ones)
            vn = 0
            for kvh in range(4):
                slot = off_v + tk * 260 + kvh * 65
                cp = nc.vector.tensor_copy(out=self.SCR[:, slot:slot + 64],
                                           in_=pst[:, kvh * 64:(kvh + 1) * 64])
                if kvh == 3:
                    vn = self.V.inc(cp)
            self.done("P" + str(tk % 2), self.V, vn)
        self.done("WCOLv", self.T)
        v_ready = self.V.n

        wq_bufs = (self.WCOL[:, 0:1024], self.WCOL[:, 3072:4096])
        wq_lds = [0, 0]

        def _load_wq(hp_):
            if hp_ >= NCC:
                return
            buf_ = hp_ % 2
            wsl_ = wq_bufs[buf_]
            self.use(self.D, "WQT" + str(buf_))
            if hp_ == 0:
                self.use(self.D, "WCOL")
            for cc in range(NCC):
                wq_lds[buf_] = self.ss_wt.dma(
                    wsl_[:, cc * P:(cc + 1) * P],
                    wq_bf[hp_ * P:(hp_ + 1) * P, cc * P:(cc + 1) * P],
                    transpose=True)

        _load_wq(0)
        for hp in range(NCC):
            buf = hp % 2
            wsl = wq_bufs[buf]
            tn = 0
            for tt in range(4):
                pst = (self.P0, self.P1)[tt % 2]
                self.use(self.T, "P" + str(tt % 2))
                self.swait(self.T, self.ss_wt, wq_lds[buf])
                for cc in range(NCC):
                    mm = nc.tensor.matmul(
                        pst[:], wsl[:, cc * P:(cc + 1) * P],
                        self.xn(cc, tt * 512, (tt + 1) * 512),
                        start=(cc == 0), stop=(cc == NCC - 1))
                    if cc == NCC - 1:
                        tn = self.T.inc(mm)
                self.S.wait(self.T, tn)
                self.use(self.S, "SCRq")
                sn = self.S.inc(nc.scalar.copy(
                    out=self.scr(off_q, tt * 512, (tt + 1) * 512), in_=pst[:]))
                self.S.drain()
                self.done("P" + str(tt % 2), self.S, sn)
            self.done("WQT" + str(buf), self.T, tn)
            _load_wq(hp + 1)
            self.use(self.D, "BCOLg2")
            gl = self.ss_bcol.dma(gq_c, q_gain[2 * hp: 2 * hp + 2].unsqueeze(1))
            self.swait(self.V, self.ss_bcol, gl)
            rope_v = self._l1f_normrope(off_q, gq_c)
            self.done("BCOLg2", self.V)

            for h2 in range(2):
                h = hp * 2 + h2
                pbase = h2 * 64
                kvh = h // 4
                ktile = off_k if (kvh % 2) == (h % 2) else off_kd
                kcol = ktile + (kvh // 2) * T
                ps_o = (self.P4, self.P5)[h2]
                for qt in range(4):
                    kmax = 4 * qt + 4
                    for kc in range(kmax):
                        dlt = 128 * kc - 512 * qt
                        lo = max(0, dlt)
                        pss = (self.P0, self.P1)[kc % 2]
                        self.use(self.T, "P" + str(kc % 2))
                        self.T.wait(self.V, rope_v)
                        self.swait(self.T, self.sg_w, self.kd_n)
                        tn = self.T.inc(nc.tensor.matmul(
                            pss[:, lo:512],
                            self.scr(kcol, kc * P, (kc + 1) * P, pbase, pbase + 64),
                            self.scr(off_q, qt * 512 + lo, (qt + 1) * 512,
                                     pbase, pbase + 64),
                            start=True, stop=True))
                        self.S.wait(self.T, tn)
                        sn = self.S.inc(nc.scalar.activation(
                            out=self.scr(off_pt, kc * 512 + lo, (kc + 1) * 512),
                            in_=pss[:, lo:512], func=AF.Exp, scale=EXPSC))
                        self.done("P" + str(kc % 2), self.S, sn)
                        if dlt >= 0:
                            self.V.wait(self.S, sn)
                            self.V.inc(nc.vector.tensor_tensor(
                                self.scr(off_pt, kc * 512 + dlt, kc * 512 + dlt + P),
                                self.scr(off_pt, kc * 512 + dlt, kc * 512 + dlt + P),
                                self.MSK[:, 0:P], ALU.mult))
                    s_exp = self.S.n
                    v_msk = self.V.n
                    self.T.wait(self.S, s_exp)
                    self.T.wait(self.V, v_msk)
                    self.use(self.T, ("P4", "P5")[h2])
                    if hp == 0 and h2 == 0 and qt == 0:
                        self.T.wait(self.V, v_ready)
                    t_pv = 0
                    for kc in range(kmax):
                        lo = max(0, 128 * kc - 512 * qt)
                        slot = off_v + kc * 260 + kvh * 65
                        mm = nc.tensor.matmul(
                            ps_o[0:65, lo:512], self.SCR[:, slot:slot + 65],
                            self.scr(off_pt, kc * 512 + lo, (kc + 1) * 512),
                            start=(kc == 0), stop=(kc == kmax - 1))
                        if kc == kmax - 1:
                            t_pv = self.T.inc(mm)
                    self.done("SCRpt", self.T, t_pv)
                    self.V.wait(self.T, t_pv)
                    rl_row = self.row(5)
                    self.use(self.V, "RW6")
                    vn = self.V.inc(nc.vector.reciprocal(rl_row, ps_o[64:65, :]))
                    self.use(self.T, "P6")
                    self.T.wait(self.V, vn)
                    t_bc = self.T.inc(nc.tensor.matmul(
                        self.P6[0:64, :], self.RW[0:1, 0:64], rl_row,
                        start=True, stop=True))
                    self.done("RW6", self.T, t_bc)
                    self.use(self.S, "SCRt22")
                    self.S.wait(self.T, t_bc)
                    sn = self.S.inc(nc.scalar.copy(
                        out=self.scr(off_t2, 0, 512, 0, 64), in_=ps_o[0:64, :]))
                    self.done(("P4", "P5")[h2], self.S, sn)
                    self.V.wait(self.S, sn)
                    self.use(self.V, "SCRao")
                    vn = self.V.inc(nc.vector.tensor_tensor(
                        self.scr(off_ao, qt * 512, (qt + 1) * 512, pbase, pbase + 64),
                        self.scr(off_t2, 0, 512, 0, 64), self.P6[0:64, :], ALU.mult))
                    self.done("P6", self.V, vn)
                    self.done("SCRt22", self.V, vn)
            self.D.wait(self.V, self.V.n)
            dn = self.st_ao.dma(attb[b, hp], self.scr(off_ao, 0, T))
            self.done("SCRao", self.st_ao, dn)
            self.done("SCRq", self.st_ao, dn)

        self.swait(self.D, self.st_ao)
        self.use(self.D, "SCRal")
        self.D.wait(self.T, self.T.n)
        al = 0
        for cc in range(NCC):
            al = self.st_al.dma(self.SCR[:, 4160 + cc * T: 4160 + (cc + 1) * T],
                                attb[b, cc])
        wo_bufs = (self.WCOL[:, 0:1024], self.WCOL[:, 1024:2048])
        wo_lds = [0, 0]

        def _load_wo(oc_):
            if oc_ >= NCC:
                return
            buf_ = oc_ % 2
            wsl_ = wo_bufs[buf_]
            self.use(self.D, "WOT" + str(buf_))
            if oc_ == 0:
                self.use(self.D, "WQT0")
            if oc_ == 1:
                self.use(self.D, "WCOLv")
            for cc in range(NCC):
                wo_lds[buf_] = self.ss_wt.dma(
                    wsl_[:, cc * P:(cc + 1) * P],
                    wo_bf[oc_ * P:(oc_ + 1) * P, cc * P:(cc + 1) * P],
                    transpose=True)

        _load_wo(0)
        _load_wo(1)
        for oc in range(NCC):
            buf = oc % 2
            wsl = wo_bufs[buf]
            tn = 0
            for tt in range(4):
                pst = (self.P0, self.P1)[tt % 2]
                self.use(self.T, "P" + str(tt % 2))
                self.swait(self.T, self.ss_wt, wo_lds[buf])
                self.swait(self.T, self.st_al, al)
                for cc in range(NCC):
                    mm = nc.tensor.matmul(
                        pst[:], wsl[:, cc * P:(cc + 1) * P],
                        self.SCR[:, 4160 + cc * T + tt * 512:
                                 4160 + cc * T + (tt + 1) * 512],
                        start=(cc == 0), stop=(cc == NCC - 1))
                    if cc == NCC - 1:
                        tn = self.T.inc(mm)
                self.V.wait(self.T, tn)
                vn = self.V.inc(nc.vector.tensor_tensor(
                    self.xmid(oc, tt * 512, (tt + 1) * 512), pst[:],
                    self.xmid(oc, tt * 512, (tt + 1) * 512), ALU.add))
                self.done("P" + str(tt % 2), self.V, vn)
            self.done("WOT" + str(buf), self.T, tn)
            _load_wo(oc + 2)
        self.done("SCRal", self.V)
        self.done("SCRk", self.V)
        self.done("SCRv", self.V)
        self.barrier()
        self.dump_dbg("l1fxa", b)

        self.emit_rms()
        self.barrier()

        o_s1f = 0
        o_s2f = 1024
        o_s1p = 5120
        o_s2p = 9216
        o_hb1 = 10240
        o_hm = 26688
        o_hst = 27200
        ldn = 0
        for (key, off0, ncc_) in (("s1f", o_s1f, 8), ("s2f", o_s2f, 32),
                                  ("s1p", o_s1p, 32), ("s2p", o_s2p, 8)):
            for ki in range(2):
                base = off0 + ki * ncc_ * 64
                for cc in range(ncc_):
                    ldn = self.sg_w.dma(
                        self.SCR[:, base + cc * 64: base + (cc + 1) * 64],
                        ssel[key][ki, cc * P:(cc + 1) * P, :])
        self.G.wait(self.V, 0)
        bd_v = self.V.n
        self.swait(self.T, self.sg_w, ldn)

        for tt in range(4):
            lo, hi = tt * 512, (tt + 1) * 512
            for ki in range(2):
                for p_ in range(NCC):
                    pst = (self.P0, self.P1)[p_ % 2]
                    self.use(self.T, "P" + str(p_ % 2))
                    self.T.wait(self.V, bd_v)
                    base = o_s1f + ki * 512 + p_ * 64
                    nc.tensor.matmul(
                        pst[0:64, :], self.SCR[0:64, base: base + 64],
                        self.xn(p_, lo, hi, )[0:64, :],
                        start=True, stop=True, tile_position=(0, 0))
                    tn = self.T.inc(nc.tensor.matmul(
                        pst[64:128, :], self.SCR[64:128, base: base + 64],
                        self.xn(p_, lo, hi)[64:128, :],
                        start=True, stop=True, tile_position=(64, 64)))
                    self.S.wait(self.T, tn)
                    self.use(self.S, "SCRhb")
                    sn = self.S.inc(nc.scalar.copy(
                        out=self.scr(o_hb1 + ki * 4096, p_ * 512, (p_ + 1) * 512),
                        in_=pst[:]))
                    self.done("P" + str(p_ % 2), self.S, sn)
            hb_s = self.S.n
            wlds_fc = [0, 0]
            for m in range(32):
                g4, mi = divmod(m, 4)
                for ki in range(2):
                    psm = (self.P2, self.P3)[ki]
                    wsl = (self.WCOL[:, 0:4096],
                           self.SCR[:, 18432:22528])[ki]
                    stm = (self.st_al, self.st_mx)[ki]
                    if mi == 0:
                        self.use(self.D, "WCOLm" + str(ki))
                        wlds_fc[ki] = stm.dma(
                            wsl, APc(mixtp["fc" + str(ki)], g4 * P * 8 * 512,
                                     [[8 * 512, P], [1, 8 * 512]]))
                    self.use(self.T, "P" + str(2 + ki))
                    self.swait(self.T, stm, wlds_fc[ki])
                    self.T.wait(self.S, hb_s)
                    tn = 0
                    for o_ in range(NCC):
                        mm = nc.tensor.matmul(
                            psm[:], wsl[:, o_ * 512 + mi * P:
                                         o_ * 512 + (mi + 1) * P],
                            self.scr(o_hb1 + ki * 4096, o_ * 512, (o_ + 1) * 512),
                            start=(o_ == 0), stop=(o_ == NCC - 1))
                        if o_ == NCC - 1:
                            tn = self.T.inc(mm)
                    if mi == 3:
                        self.done("WCOLm" + str(ki), self.T, tn)
                    self.V.wait(self.T, tn)
                    self.use(self.V, "X2" + str(ki))
                    vn = self.V.inc(nc.vector.tensor_copy(
                        out=self.X2[:, ki * 512:(ki + 1) * 512], in_=psm[:]))
                    self.done("P" + str(2 + ki), self.V, vn)
                hm_v = self.V.n
                self.use(self.T, "P0")
                self.T.wait(self.V, hm_v)
                tn = 0
                for ki in range(2):
                    base = o_s2f + ki * 2048 + m * 64
                    nc.tensor.matmul(
                        self.P0[0:64, :], self.SCR[0:64, base: base + 64],
                        self.X2[0:64, ki * 512:(ki + 1) * 512],
                        start=(ki == 0), stop=False, tile_position=(0, 0),
                        skip_group_check=True)
                    tn = self.T.inc(nc.tensor.matmul(
                        self.P0[64:128, :], self.SCR[64:128, base: base + 64],
                        self.X2[64:128, ki * 512:(ki + 1) * 512],
                        start=(ki == 0), stop=(ki == 1), tile_position=(64, 64),
                        skip_group_check=True))
                self.done("X20", self.T, tn)
                self.done("X21", self.T, tn)
                import os as _os3
                if _os3.environ.get("DBGBD2"):
                    self.S.wait(self.T, tn)
                    self.use(self.S, "SCRhm")
                    sn = self.S.inc(nc.scalar.copy(
                        out=self.scr(o_hm, 0, 512), in_=self.P0[:]))
                    self.S.drain()
                    vn = self.V.n
                    self.V.wait(self.S, sn)
                    vn = self.V.tick()
                    self.done("P0", self.V, vn)
                    self.D.wait(self.S, sn)
                    hn_ = self.st_x.dma(hfc_d[tt, m], self.scr(o_hm, 0, 512))
                    self.done("SCRhm", self.st_x, hn_)
                    continue
                if _os3.environ.get("DBGHM"):
                    self.S.wait(self.T, tn)
                    self.use(self.S, "SCRhm")
                    _half = 512 if _os3.environ.get("DBGHM2") else 0
                    sn = self.S.inc(nc.scalar.copy(
                        out=self.scr(o_hm, 0, 512), in_=self.X2[:, _half:_half + 512]))
                    self.S.drain()
                    self.V.wait(self.S, sn)
                    vn = self.V.tick()
                    self.done("P0", self.V, vn)
                    self.D.wait(self.S, sn)
                    hn_ = self.st_x.dma(hfc_d[tt, m], self.scr(o_hm, 0, 512))
                    self.done("SCRhm", self.st_x, hn_)
                    continue
                self.S.wait(self.T, tn)
                self.use(self.S, "SCRhm")
                sn = self.S.inc(nc.scalar.activation(
                    out=self.scr(o_hm, 0, 512), in_=self.P0[:], func=AF.Relu))
                self.V.wait(self.S, sn)
                vn = self.V.inc(nc.vector.tensor_tensor(
                    self.scr(o_hm, 0, 512), self.P0[:], self.scr(o_hm, 0, 512),
                    ALU.mult))
                self.done("P0", self.V, vn)
                self.D.wait(self.V, vn)
                hn_ = self.st_x.dma(hfc_d[tt, m], self.scr(o_hm, 0, 512))
                self.done("SCRhm", self.st_x, hn_)
            self.done("SCRhb", self.T)
        hfc_done = self.st_x.n
        self.barrier()

        import os as _os2
        for ki in range(0 if not _os2.environ.get("SKIP_PJ") else 2, 2):
            for tt in range(4):
                lo, hi = tt * 512, (tt + 1) * 512
                for p_ in range(32):
                    hslot = o_hst + (p_ % 2) * 512
                    self.use(self.D, "HST" + str(p_ % 2))
                    hld = (self.st_tmp, self.st_tmp2)[p_ % 2].dma(
                        self.scr(hslot, 0, 512), hfc_d[tt, p_])
                    pst = (self.P0, self.P1)[p_ % 2]
                    self.use(self.T, "P" + str(p_ % 2))
                    self.swait(self.T, (self.st_tmp, self.st_tmp2)[p_ % 2], hld)
                    self.T.wait(self.V, bd_v)
                    base = o_s1p + ki * 2048 + p_ * 64
                    nc.tensor.matmul(
                        pst[0:64, :], self.SCR[0:64, base: base + 64],
                        self.scr(hslot, 0, 512, 0, 64),
                        start=True, stop=True, tile_position=(0, 0))
                    tn = self.T.inc(nc.tensor.matmul(
                        pst[64:128, :], self.SCR[64:128, base: base + 64],
                        self.scr(hslot, 0, 512, 64, 128),
                        start=True, stop=True, tile_position=(64, 64)))
                    self.done("HST" + str(p_ % 2), self.T, tn)
                    self.S.wait(self.T, tn)
                    self.use(self.S, "SCRhb")
                    sn = self.S.inc(nc.scalar.copy(
                        out=self.scr(o_hb1, p_ * 512, (p_ + 1) * 512), in_=pst[:]))
                    self.done("P" + str(p_ % 2), self.S, sn)
                hb_s = self.S.n
                for m in range(NCC):
                    psm = self.P2
                    wsl = self.WCOL[:, 0:4096]
                    self.use(self.D, "WCOLm0")
                    wld = 0
                    for half in range(2):
                        wld = self.st_al.dma(
                            wsl[:, half * 2048: half * 2048 + 2048],
                            APc(mixtp["pj" + str(ki)],
                                m * P * 32 * P + half * 16 * P,
                                [[32 * P, P], [1, 16 * P]]))
                    self.use(self.T, "P2")
                    self.swait(self.T, self.st_al, wld)
                    self.T.wait(self.S, hb_s)
                    tn = 0
                    for o_ in range(32):
                        mm = nc.tensor.matmul(
                            psm[:], wsl[:, o_ * P:(o_ + 1) * P],
                            self.scr(o_hb1, o_ * 512, (o_ + 1) * 512),
                            start=(o_ == 0), stop=(o_ == 31))
                        if o_ == 31:
                            tn = self.T.inc(mm)
                    self.done("WCOLm0", self.T, tn)
                    self.V.wait(self.T, tn)
                    self.use(self.V, "X20")
                    vn = self.V.inc(nc.vector.tensor_copy(
                        out=self.X2[:, 0:512], in_=psm[:]))
                    self.done("P2", self.V, vn)
                    self.use(self.T, "P3")
                    self.T.wait(self.V, vn)
                    base = o_s2p + ki * 512 + m * 64
                    nc.tensor.matmul(
                        self.P3[0:64, :], self.SCR[0:64, base: base + 64],
                        self.X2[0:64, 0:512],
                        start=True, stop=True, tile_position=(0, 0))
                    tn = self.T.inc(nc.tensor.matmul(
                        self.P3[64:128, :], self.SCR[64:128, base: base + 64],
                        self.X2[64:128, 0:512],
                        start=True, stop=True, tile_position=(64, 64)))
                    self.done("X20", self.T, tn)
                    self.V.wait(self.T, tn)
                    vn = self.V.inc(nc.vector.tensor_tensor(
                        self.xmid(m, lo, hi), self.P3[:],
                        self.xmid(m, lo, hi), ALU.add))
                    self.done("P3", self.V, vn)
                self.done("SCRhb", self.T)
        self.done("XN", self.T)
        self.barrier()
        return


def make_host_consts():
    c = {}
    c["hc_ident"] = np.eye(P, dtype=np.float32)
    i = np.arange(P)[:, None]
    j = np.arange(P)[None, :]
    c["hc_tri_ge"] = (j >= i).astype(ml_dtypes.bfloat16)
    c["hc_tri_le"] = (j <= i).astype(ml_dtypes.bfloat16)
    c["hc_identb"] = np.eye(P).astype(ml_dtypes.bfloat16)
    HS = 64
    half = HS // 2
    inv_freq = 1.0 / 10000.0 ** (np.arange(half, dtype=np.float64) * 2.0 / HS)
    t = np.arange(T, dtype=np.float64)
    ang = t[:, None] * inv_freq[None, :]
    cos = np.cos(ang).T
    sin = np.sin(ang).T
    C = np.concatenate([cos, cos], 0)
    Sp = np.concatenate([-sin, sin], 0)
    c["hc_l0C"] = np.tile(C, (2, 1)).astype(ml_dtypes.bfloat16)
    c["hc_l0S"] = np.tile(Sp, (2, 1)).astype(ml_dtypes.bfloat16)

    HD = 64
    half_idx = np.arange(0, HD, 2, dtype=np.float64)
    inv_freq = 1.0 / 10000.0 ** (half_idx / HD)
    tt = np.arange(T, dtype=np.float64)
    fr = tt[:, None] * inv_freq[None, :]
    base = (half_idx + 0.4 * HD) / (1.4 * HD)
    power = (tt - T // 2) / float(512 // 2)
    scl = base[None, :] ** power[:, None]
    cosf = np.cos(fr)
    sinf = np.sin(fr)
    Cq = (cosf * scl).T
    Sq = (sinf * scl).T
    Ck = (cosf / scl).T
    Sk = (sinf / scl).T
    def half2(a_first, a_second):
        blk = np.concatenate([a_first, a_second], 0)
        return np.tile(blk, (2, 1)).astype(ml_dtypes.bfloat16)
    c["hc_xCq"] = half2(Cq, Cq)
    c["hc_xSq"] = half2(-Sq, Sq)
    c["hc_xCk"] = half2(Ck, Ck)
    c["hc_xSk"] = half2(-Sk, Sk)
    blk2t = np.zeros((2, P), dtype=np.float32)
    blk2t[0, 0:64] = 1.0
    blk2t[1, 64:128] = 1.0
    c["hc_blk2t"] = blk2t
    blk2 = np.zeros((P, 2), dtype=ml_dtypes.bfloat16)
    blk2[0:64, 0] = 1.0
    blk2[64:128, 1] = 1.0
    c["hc_blk2"] = blk2
    c["hc_iota8"] = np.arange(8, dtype=np.float32)[None, :]
    pp = np.arange(P)
    pc = np.zeros((P, 8), dtype=np.float32)
    pc[:, 0] = pp
    pc[:, 1] = pp // 64
    pc[:, 2] = pp % 16
    pc[:, 3] = pp // 16
    pc[:, 4] = pp % 64
    c["hc_pcols"] = pc
    ocm = np.zeros((P, 64), dtype=np.float32)
    ocm[:, 0:32] = (2.0 * np.arange(32, dtype=np.float32))[None, :]
    c["hc_ocols"] = ocm
    inv1 = 1.0 / 10000.0 ** (np.arange(0, 64, 2, dtype=np.float64) / 64)
    fr1 = np.arange(T, dtype=np.float64)[:, None] * inv1[None, :]
    c1 = np.cos(fr1).T
    s1_ = np.sin(fr1).T
    C1 = np.concatenate([c1, c1], 0)
    S1 = np.concatenate([s1_, -s1_], 0)
    c["hc_l1C"] = np.tile(C1, (2, 1)).astype(ml_dtypes.bfloat16)
    c["hc_l1S"] = np.tile(S1, (2, 1)).astype(ml_dtypes.bfloat16)
    return c
_PROGRAMS = {}


def _build_a():
    m = MK()
    xin = m.din("xin", [NB, T, D])
    wq = m.din("L0_attn_Wq", [D, D]); bq = m.din("L0_attn_bq", [D])
    wk = m.din("L0_attn_Wk", [D, D]); bk = m.din("L0_attn_bk", [D])
    wv = m.din("L0_attn_Wv", [D, D]); bv = m.din("L0_attn_bv", [D])
    wp = m.din("L0_attn_Wp", [D, D]); bp = m.din("L0_attn_bp", [D])
    out = m.dout("out", [NB, T, D])
    xfm = m.dint("xfm", [NB, NCC, P, T])
    attb = m.dint("attb", [NB, NCC, P, T], BF16)
    m.alloc()
    m.load_consts()
    t1 = m.din("hc_l0C", [P, T], BF16)
    t2 = m.din("hc_l0S", [P, T], BF16)
    m.load_tab(t1, t2)
    m.emit_ingest(xin, xfm)
    m.barrier()
    for b in range(NB):
        m.load_x(xfm, b)
        m.barrier()
        m.emit_l0_attn(b, wq, bq, wk, bk, wv, bv, wp, bp, attb)
        m.emit_ln(dstf=m.xmid, dstb=m.xn)
        m.barrier()
        m.emit_egress(b, out)
        m.done("XMID", m.T)
        m.barrier()
    m.nc.sync.wait_ge(m.st_out.sem, m.st_out.n)
    return m


def _build_b():
    m = MK()
    xin = m.din("xin", [NB, T, D])
    wqkv = m.din("L0_ffn_wqkv", [2, D, 1536])
    qs = m.din("L0_ffn_q_scale", [2, 64]); ks = m.din("L0_ffn_k_scale", [2, 64])
    wo = m.din("L0_ffn_wo", [2, 512, D])
    lg = m.din("L0_ffn_ln_g", [2, D]); lb = m.din("L0_ffn_ln_b", [2, D])
    w1 = m.din("L0_ffn_w1", [2, D, 5460]); b1 = m.din("L0_ffn_b1", [2, 5460])
    w2 = m.din("L0_ffn_w2", [2, 2730, D]); b2 = m.din("L0_ffn_b2", [2, D])
    out = m.dout("out", [NB, T, D])
    xfm = m.dint("xfm", [NB, NCC, P, T])
    attb = m.dint("attb", [NB, NCC, P, T], BF16)
    m.alloc()
    m.load_consts()
    xtabs = [m.din("hc_xCq", [P, T], BF16), m.din("hc_xSq", [P, T], BF16),
             m.din("hc_xCk", [P, T], BF16), m.din("hc_xSk", [P, T], BF16)]
    m.emit_ingest(xin, xfm)
    m.barrier()
    for b in range(NB):
        m.load_x(xfm, b)
        m.barrier()
        m.emit_l0_ffn(b, wqkv, qs, ks, wo, lg, lb, w1, b1, w2, b2, attb,
                      xtabs=xtabs)
        m.emit_addx(xfm, b)
        m.emit_ln(dstf=m.xmid, dstb=m.xn)
        m.barrier()
        m.emit_egress(b, out)
        m.done("XMID", m.T)
        m.barrier()
    m.nc.sync.wait_ge(m.st_out.sem, m.st_out.n)
    return m


def _build_c():
    m = MK()
    xin = m.din("xin", [NB, T, D])
    wq = m.din("L1_attn_W_q", [D, D])
    wo = m.din("L1_attn_W_o", [D, D])
    emb = m.din("L1_attn_rel_k_emb", [65, 64])
    out = m.dout("out", [NB, T, D])
    xfm = m.dint("xfm", [NB, NCC, P, T])
    attb = m.dint("attb", [NB, NCC, P, T], BF16)
    wq_bf = m.dint("wq_bf", [D, D], BF16)
    wo_bf = m.dint("wo_bf", [D, D], BF16)
    epad_h = m.nc.dram_tensor("epad", [NB * 16 * T * 576], BF16, kind="Internal")
    m.alloc()
    m.load_consts()
    m.emit_stage_bf16([(wq, wq_bf), (wo, wo_bf)])
    m.emit_l1_emb(emb)
    m.emit_ingest(xin, xfm)
    m.barrier()
    for b in range(NB):
        m.load_x(xfm, b)
        m.barrier()
        m.emit_l1_attn(b, wq_bf, wo_bf, epad_h, attb)
        m.emit_ln(dstf=m.xmid, dstb=m.xn)
        m.barrier()
        m.emit_egress(b, out)
        m.done("XMID", m.T)
        m.barrier()
    m.nc.sync.wait_ge(m.st_out.sem, m.st_out.n)
    return m


def _build_d():
    m = MK()
    xin = m.din("xin", [NB, T, D])
    wq = m.din("L1_ffn_wq", [D, D])
    wk = m.din("L1_ffn_wk", [256, D])
    wv = m.din("L1_ffn_wv", [256, D])
    wo = m.din("L1_ffn_wo", [D, D])
    qg = m.din("L1_ffn_q_gain", [16]); kg = m.din("L1_ffn_k_gain", [4])
    fc_s1 = m.din("L1_ffn_fc_s1", [8, D, 64])
    fc_s2 = m.din("L1_ffn_fc_s2", [8, 4096, 64])
    fc_mix = m.din("L1_ffn_fc_mix", [8, 4096, D])
    fc_logits = m.din("L1_ffn_fc_logits", [8])
    pj_s1 = m.din("L1_ffn_pj_s1", [8, 4096, 64])
    pj_s2 = m.din("L1_ffn_pj_s2", [8, D, 64])
    pj_mix = m.din("L1_ffn_pj_mix", [8, D, 4096])
    pj_logits = m.din("L1_ffn_pj_logits", [8])
    iota8 = m.din("hc_iota8", [1, 8])
    out = m.dout("out", [NB, T, D])
    xfm = m.dint("xfm", [NB, NCC, P, T])
    attb = m.dint("attb", [NB, NCC, P, T], BF16)
    wq_bf = m.dint("wq_bf", [D, D], BF16)
    wk_bf = m.dint("wk_bf", [256, D], BF16)
    wv_bf = m.dint("wv_bf", [256, D], BF16)
    wo_bf = m.dint("wo_bf", [D, D], BF16)
    hfc_d = m.dint("hfc_d", [4, 32, P, 512], BF16)
    mixt, mixtp = {}, {}
    for lg_, Md, Od in (("fc", 4096, 1024), ("pj", 1024, 4096)):
        for ki in range(2):
            tag = lg_ + str(ki)
            mixt[tag] = m.nc.dram_tensor("mt_" + tag, [Od * Md], BF16, kind="Internal")
            mixtp[tag] = m.nc.dram_tensor("mtp_" + tag, [Od * Md], BF16, kind="Internal")
    ssel = {"s1f": m.dint("ssel_s1f", [2, D, 64], BF16),
            "s2f": m.dint("ssel_s2f", [2, 4096, 64], BF16),
            "s1p": m.dint("ssel_s1p", [2, 4096, 64], BF16),
            "s2p": m.dint("ssel_s2p", [2, D, 64], BF16)}
    m.alloc()
    m.load_consts()
    t1 = m.din("hc_l1C", [P, T], BF16)
    t2 = m.din("hc_l1S", [P, T], BF16)
    m.load_tab(t1, t2)
    m.emit_stage_bf16([(wq, wq_bf), (wk, wk_bf), (wv, wv_bf), (wo, wo_bf)])
    m.emit_l1_ffn_prep(fc_logits, pj_logits, fc_s1, fc_s2, pj_s1, pj_s2,
                       fc_mix, pj_mix, iota8, ssel, mixt, mixtp)
    m.emit_ingest(xin, xfm)
    m.barrier()
    for b in range(NB):
        m.load_x(xfm, b)
        m.barrier()
        m.emit_l1_ffn(b, wq_bf, wk_bf, wv_bf, wo_bf, qg, kg,
                      ssel, mixtp, attb, hfc_d)
        m.emit_addx(xfm, b)
        m.emit_ln(dstf=m.xmid, dstb=m.xn)
        m.barrier()
        m.emit_egress(b, out)
        m.done("XMID", m.T)
        m.barrier()
    m.nc.sync.wait_ge(m.st_out.sem, m.st_out.n)
    return m


_WN = {
    "a": ["L0_attn_Wq", "L0_attn_bq", "L0_attn_Wk", "L0_attn_bk",
          "L0_attn_Wv", "L0_attn_bv", "L0_attn_Wp", "L0_attn_bp"],
    "b": ["L0_ffn_wqkv", "L0_ffn_q_scale", "L0_ffn_k_scale", "L0_ffn_wo",
          "L0_ffn_ln_g", "L0_ffn_ln_b", "L0_ffn_w1", "L0_ffn_b1",
          "L0_ffn_w2", "L0_ffn_b2"],
    "c": ["L1_attn_W_q", "L1_attn_W_o", "L1_attn_rel_k_emb"],
    "d": ["L1_ffn_wq", "L1_ffn_wk", "L1_ffn_wv", "L1_ffn_wo",
          "L1_ffn_q_gain", "L1_ffn_k_gain", "L1_ffn_fc_s1", "L1_ffn_fc_s2",
          "L1_ffn_fc_mix", "L1_ffn_fc_logits", "L1_ffn_pj_s1", "L1_ffn_pj_s2",
          "L1_ffn_pj_mix", "L1_ffn_pj_logits"],
}


def _input_names(m):
    return {a.memorylocations[0].name for a in m.nc.m.functions[0].allocations
            if hasattr(a, "kind") and a.kind == "ExternalInput"}


def kernel(**inputs):
    import numpy as _np
    hc = make_host_consts()
    if not _PROGRAMS:
        _PROGRAMS["a"] = _build_a()
        _PROGRAMS["b"] = _build_b()
        _PROGRAMS["c"] = _build_c()
        _PROGRAMS["d"] = _build_d()
    x = _np.ascontiguousarray(_np.asarray(inputs["L0_attn_x"], dtype=_np.float32))
    for stage in ("a", "b", "c", "d"):
        m = _PROGRAMS[stage]
        names = _input_names(m)
        in_maps = []
        for c in range(8):
            dmap = {"xin": _np.ascontiguousarray(x[c * NB:(c + 1) * NB])}
            if stage == "a":
                dmap["xin"] = dmap.pop("xin")
            for w in _WN[stage]:
                dmap[w] = _np.ascontiguousarray(_np.asarray(inputs[w],
                                                            dtype=_np.float32))
            dmap.update({k: v for k, v in hc.items() if k in names})
            in_maps.append(dmap)
        res = run_bass_kernel_spmd(m.nc, in_maps, core_ids=list(range(8)))
        x = _np.concatenate([_np.asarray(r["out"]) for r in res.results], 0)
    return x.astype(_np.float32)


def _build_fused():
    m = MK()
    xin = m.din("L0_attn_x", [NB, T, D])
    a_wq = m.din("L0_attn_Wq", [D, D]); a_bq = m.din("L0_attn_bq", [D])
    a_wk = m.din("L0_attn_Wk", [D, D]); a_bk = m.din("L0_attn_bk", [D])
    a_wv = m.din("L0_attn_Wv", [D, D]); a_bv = m.din("L0_attn_bv", [D])
    a_wp = m.din("L0_attn_Wp", [D, D]); a_bp = m.din("L0_attn_bp", [D])
    wqkv = m.din("L0_ffn_wqkv", [2, D, 1536])
    qs = m.din("L0_ffn_q_scale", [2, 64]); ks = m.din("L0_ffn_k_scale", [2, 64])
    b_wo = m.din("L0_ffn_wo", [2, 512, D])
    lg = m.din("L0_ffn_ln_g", [2, D]); lb = m.din("L0_ffn_ln_b", [2, D])
    w1 = m.din("L0_ffn_w1", [2, D, 5460]); b1 = m.din("L0_ffn_b1", [2, 5460])
    w2 = m.din("L0_ffn_w2", [2, 2730, D]); b2 = m.din("L0_ffn_b2", [2, D])
    c_wq = m.din("L1_attn_W_q", [D, D])
    c_wo = m.din("L1_attn_W_o", [D, D])
    emb = m.din("L1_attn_rel_k_emb", [65, 64])
    d_wq = m.din("L1_ffn_wq", [D, D])
    d_wk = m.din("L1_ffn_wk", [256, D])
    d_wv = m.din("L1_ffn_wv", [256, D])
    d_wo = m.din("L1_ffn_wo", [D, D])
    qg = m.din("L1_ffn_q_gain", [16]); kg = m.din("L1_ffn_k_gain", [4])
    fc_s1 = m.din("L1_ffn_fc_s1", [8, D, 64])
    fc_s2 = m.din("L1_ffn_fc_s2", [8, 4096, 64])
    fc_mix = m.din("L1_ffn_fc_mix", [8, 4096, D])
    fc_logits = m.din("L1_ffn_fc_logits", [8])
    pj_s1 = m.din("L1_ffn_pj_s1", [8, 4096, 64])
    pj_s2 = m.din("L1_ffn_pj_s2", [8, D, 64])
    pj_mix = m.din("L1_ffn_pj_mix", [8, D, 4096])
    pj_logits = m.din("L1_ffn_pj_logits", [8])
    iota8 = m.din("hc_iota8", [1, 8])
    out = m.dout("out", [NB, T, D])
    xfm = m.dint("xfm", [NB, NCC, P, T])
    xf1 = m.dint("xf1", [NB, NCC, P, T])
    xf3 = m.dint("xf3", [NB, NCC, P, T])
    attb = m.dint("attb", [NB, NCC, P, T], BF16)
    cwq_bf = m.dint("cwq_bf", [D, D], BF16)
    cwo_bf = m.dint("cwo_bf", [D, D], BF16)
    dwq_bf = m.dint("dwq_bf", [D, D], BF16)
    dwk_bf = m.dint("dwk_bf", [256, D], BF16)
    dwv_bf = m.dint("dwv_bf", [256, D], BF16)
    dwo_bf = m.dint("dwo_bf", [D, D], BF16)
    hfc_d = m.dint("hfc_d", [4, 32, P, 512], BF16)
    epad_h = m.nc.dram_tensor("epad", [NB * 16 * T * 576], BF16, kind="Internal")
    mixt, mixtp = {}, {}
    for lg_, Md, Od in (("fc", 4096, 1024), ("pj", 1024, 4096)):
        for ki in range(2):
            tag = lg_ + str(ki)
            mixt[tag] = m.nc.dram_tensor("mt_" + tag, [Od * Md], BF16, kind="Internal")
            mixtp[tag] = m.nc.dram_tensor("mtp_" + tag, [Od * Md], BF16, kind="Internal")
    ssel = {"s1f": m.dint("ssel_s1f", [2, D, 64], BF16),
            "s2f": m.dint("ssel_s2f", [2, 4096, 64], BF16),
            "s1p": m.dint("ssel_s1p", [2, 4096, 64], BF16),
            "s2p": m.dint("ssel_s2p", [2, D, 64], BF16)}
    m.alloc()
    m.load_consts()
    t_l0C = m.din("hc_l0C", [P, T], BF16)
    t_l0S = m.din("hc_l0S", [P, T], BF16)
    t_l1C = m.din("hc_l1C", [P, T], BF16)
    t_l1S = m.din("hc_l1S", [P, T], BF16)
    xtabs = [m.din("hc_xCq", [P, T], BF16), m.din("hc_xSq", [P, T], BF16),
             m.din("hc_xCk", [P, T], BF16), m.din("hc_xSk", [P, T], BF16)]
    m.emit_stage_bf16([(c_wq, cwq_bf), (c_wo, cwo_bf), (d_wq, dwq_bf),
                       (d_wk, dwk_bf), (d_wv, dwv_bf), (d_wo, dwo_bf)])
    m.emit_stage_l0a(a_wq, a_wk, a_wv, a_wp)
    m.emit_stage_l0f(wqkv, b_wo, w1, w2)
    m.emit_stage_deint()
    m.emit_l1_ffn_prep(fc_logits, pj_logits, fc_s1, fc_s2, pj_s1, pj_s2,
                       fc_mix, pj_mix, iota8, ssel, mixt, mixtp)
    m.emit_ingest(xin, xfm)
    m.barrier()
    for b in range(NB):
        m.load_x(xfm, b)
        m.barrier()
        m.load_tab(t_l0C, t_l0S)
        m.emit_l0_attn(b, a_wq, a_bq, a_wk, a_bk, a_wv, a_bv, a_wp, a_bp, attb)
        m.emit_ln(dstf=m.xmid, dstb=m.xn, dram_out=xf1, b=b)
        m.barrier()
        m.emit_l0_ffn(b, wqkv, qs, ks, b_wo, lg, lb, w1, b1, w2, b2, attb,
                      xtabs=xtabs)
        m.emit_addx(xf1, b)
        m.emit_ln(dstf=m.xmid, dstb=m.xn)
        m.barrier()
        m.emit_l1_emb(emb)
        m.barrier()
        m.emit_l1_attn(b, cwq_bf, cwo_bf, epad_h, attb)
        m.emit_ln(dstf=m.xmid, dstb=m.xn, dram_out=xf3, b=b)
        m.barrier()
        m.load_tab(t_l1C, t_l1S)
        m.emit_l1_ffn(b, dwq_bf, dwk_bf, dwv_bf, dwo_bf, qg, kg,
                      ssel, mixtp, attb, hfc_d)
        m.emit_addx(xf3, b)
        m.emit_ln(dstf=m.xmid, dstb=m.xn)
        m.barrier()
        m.emit_egress(b, out)
        m.done("XMID", m.T)
        m.done("XN", m.T)
        m.barrier()
    m.nc.sync.wait_ge(m.st_out.sem, m.st_out.n)
    return m


def kernel(**inputs):
    import numpy as _np
    hc = make_host_consts()
    if "fused" not in _PROGRAMS:
        _PROGRAMS["fused"] = _build_fused()
    m = _PROGRAMS["fused"]
    names = _input_names(m)
    x = _np.ascontiguousarray(_np.asarray(inputs["L0_attn_x"], dtype=_np.float32))
    wn = [k for k in inputs.keys() if k != "L0_attn_x"]
    in_maps = []
    for c in range(8):
        dmap = {"L0_attn_x": _np.ascontiguousarray(x[c * NB:(c + 1) * NB])}
        for w in wn:
            dmap[w] = _np.ascontiguousarray(_np.asarray(inputs[w], dtype=_np.float32))
        dmap.update({k: v for k, v in hc.items() if k in names})
        in_maps.append(dmap)
    res = run_bass_kernel_spmd(m.nc, in_maps, core_ids=list(range(8)))
    return _np.concatenate([_np.asarray(r["out"]) for r in res.results],
                           0).astype(_np.float32)


# version 09
# speedup vs baseline: 1.0401x; 1.0215x over previous
import contextlib
import numpy as np
import ml_dtypes
import concourse.bass as bass
import concourse.mybir as mybir
from concourse.bass_utils import run_bass_kernel_spmd
from concourse.ap import AP as APc

F32 = mybir.dt.float32
BF16 = mybir.dt.bfloat16
I32 = mybir.dt.int32
AF = mybir.ActivationFunctionType
ALU = mybir.AluOpType

P = 128
D = 1024
T = 2048
NB = 2
NCC = 8
CS = T + 4
XOFF = 4


class Q:
    def __init__(self, ctx, eng, name):
        self.ctx = ctx
        self.eng = eng
        self.sem = ctx.stack.enter_context(ctx.nc.semaphore(name))
        self.n = 0

    def inc(self, inst, k=1):
        inst.then_inc(self.sem, k)
        self.n += k
        return self.n

    def tick(self, k=1):
        return self.inc(self.eng.nop(), k)

    def dma(self, out, in_, **kw):
        return self.inc(self.eng.dma_start(out=out, in_=in_, **kw), 16)

    def wait(self, other, n):
        if n > 0:
            self.eng.wait_ge(other.sem, n)

    def drain(self):
        self.eng.drain()


class Stream:

    def __init__(self, ctx, eng, name):
        self.ctx = ctx
        self.eng = eng
        self.sem = ctx.stack.enter_context(ctx.nc.semaphore(name))
        self.n = 0

    def dma(self, out, in_, **kw):
        self.eng.dma_start(out=out, in_=in_, **kw).then_inc(self.sem, 16)
        self.n += 16
        return self.n


class MK:
    def __init__(self):
        self.nc = bass.Bass("TRN2", target_bir_lowering=False)
        self.stack = contextlib.ExitStack()
        nc = self.nc
        self.T = Q(self, nc.tensor, "sT")
        self.V = Q(self, nc.vector, "sV")
        self.S = Q(self, nc.scalar, "sS")
        self.G = Q(self, nc.gpsimd, "sG")
        self.D = Q(self, nc.sync, "sD")
        self.st_setup = Stream(self, nc.sync, "st_setup")
        self.ss_tab = Stream(self, nc.sync, "ss_tab")
        self.ss_bcol = Stream(self, nc.sync, "ss_bcol")
        self.ss_rw = Stream(self, nc.sync, "ss_rw")
        self.ss_b2 = Stream(self, nc.sync, "ss_b2")
        self.st_al = Stream(self, nc.sync, "st_al")
        self.st_mx = Stream(self, nc.sync, "st_mx")
        self.st_wa = Stream(self, nc.sync, "st_wa")
        self.st_wb = Stream(self, nc.sync, "st_wb")
        self.st_x = Stream(self, nc.sync, "st_x")
        self.st_tmp = Stream(self, nc.sync, "st_tmp")
        self.st_tmp2 = Stream(self, nc.sync, "st_tmp2")
        self.st_ao = Stream(self, nc.sync, "st_ao")
        self.st_out = Stream(self, nc.sync, "st_out")
        self.sg_xn = Stream(self, nc.gpsimd, "sg_xn")
        self.sg_w = Stream(self, nc.gpsimd, "sg_w")
        self.ss_wt = Stream(self, nc.sync, "ss_wt")
        self.sg_band = [Stream(self, nc.sync, "sg_band%d" % i) for i in range(8)]
        self.ss_ep = Stream(self, nc.sync, "ss_ep")
        self.sg_stg = Stream(self, nc.gpsimd, "sg_stg")
        self.guard = {}
        self.stgw = {}
        self.dbg_ap = None

    def swait(self, q, stream, n=None):
        n = stream.n if n is None else n
        if n > 0:
            q.eng.wait_ge(stream.sem, n)

    def din(self, name, shape, dtype=F32):
        return self.nc.dram_tensor(name, list(shape), dtype, kind="ExternalInput").ap()

    def dout(self, name, shape, dtype=F32):
        return self.nc.dram_tensor(name, list(shape), dtype, kind="ExternalOutput").ap()

    def dint(self, name, shape, dtype=F32):
        return self.nc.dram_tensor(name, list(shape), dtype, kind="Internal").ap()

    def sbuf(self, name, shape, dtype):
        return self.stack.enter_context(self.nc.sbuf_tensor(name, shape, dtype))

    def psumt(self, name, shape, dtype=F32):
        return self.stack.enter_context(self.nc.psum_tensor(name, shape, dtype))

    def alloc(self):
        self.XMID = self.sbuf("XMID", [P, NCC * CS], F32)
        self.XN = self.sbuf("XN", [P, NCC * CS], BF16)
        self.SCR = self.sbuf("SCR", [P, 35072], BF16)
        self.WCOL = self.sbuf("WCOL", [P, 4096], BF16)
        self.TAB = self.sbuf("TAB", [P, 2 * T], BF16)
        self.RW = self.sbuf("RW", [1, 4096], F32)
        self.BCOL = self.sbuf("BCOL", [P, 16], F32)
        self.TMP = self.sbuf("TMP", [P, 1024], F32)
        self.X2 = self.sbuf("X2", [P, 1024], BF16)
        self.IDN = self.sbuf("IDN", [P, P], F32)
        self.IDNB = self.sbuf("IDNB", [P, P], BF16)
        self.MSK = self.sbuf("MSK", [P, 2 * P], BF16)
        self.OCOL = self.sbuf("OCOL", [P, 2], F32)
        self.ONER = self.sbuf("ONER", [P, 256], BF16)
        self.MT = self.sbuf("MT", [P, 80], BF16)
        self.BLK2 = self.sbuf("BLK2", [P, 2], BF16)
        self.BLK2T = self.sbuf("BLK2T", [2, P], F32)
        self.BCOL2 = self.sbuf("BCOL2", [P, 96], F32)
        self.OCOLB = self.sbuf("OCOLB", [P, 2], BF16)
        self.P0 = self.psumt("P0", [P, 512])
        self.P1 = self.psumt("P1", [P, 512])
        self.P2 = self.psumt("P2", [P, 512])
        self.P3 = self.psumt("P3", [P, 512])
        self.P4 = self.psumt("P4", [66, 512])
        self.P5 = self.psumt("P5", [66, 512])
        self.P6 = self.psumt("P6", [66, 512])

    def xmid(self, cc, lo=0, hi=T):
        return self.XMID[:, cc * CS + XOFF + lo: cc * CS + XOFF + hi]

    def xn(self, cc, lo=0, hi=T):
        return self.XN[:, cc * CS + XOFF + lo: cc * CS + XOFF + hi]

    def scr(self, off, lo, hi, p0=0, p1=P):
        return self.SCR[p0:p1, off + lo: off + hi]

    def row(self, i, w=512):
        return self.RW[0:1, i * 512: i * 512 + w]

    def use(self, waiter, *names):
        for name in names:
            if name in self.guard:
                q, n = self.guard[name]
                waiter.wait(q, n)

    def done(self, name, q, n=None):
        self.guard[name] = (q, q.n if n is None else n)

    def barrier(self):
        qs = [self.T, self.V, self.S, self.G, self.D]
        sts = [self.st_setup, self.ss_tab, self.ss_bcol, self.ss_rw, self.ss_b2,
               self.st_al, self.st_mx, self.st_wa, self.st_wb,
               self.st_x, self.st_tmp, self.st_tmp2, self.st_ao, self.st_out,
               self.sg_xn, self.sg_w, self.ss_wt, self.ss_ep,
               self.sg_stg] + self.sg_band
        for a in qs:
            for b_ in qs:
                if a is not b_:
                    a.wait(b_, b_.n)
            for st in sts:
                self.swait(a, st)

    def load_consts(self, hc_shapes=True):
        self.hc_ident = self.din("hc_ident", [P, P], F32)
        self.hc_tri_ge = self.din("hc_tri_ge", [P, P], BF16)
        self.hc_tri_le = self.din("hc_tri_le", [P, P], BF16)
        self.hc_identb = self.din("hc_identb", [P, P], BF16)
        self.hc_blk2 = self.din("hc_blk2", [P, 2], BF16)
        self.hc_blk2t = self.din("hc_blk2t", [2, P], F32)
        self.st_setup.dma(self.IDN[:], self.hc_ident)
        self.st_setup.dma(self.MSK[:, 0:P], self.hc_tri_ge)
        self.st_setup.dma(self.MSK[:, P:2 * P], self.hc_tri_le)
        self.st_setup.dma(self.IDNB[:], self.hc_identb)
        self.st_setup.dma(self.BLK2[:], self.hc_blk2)
        self.st_setup.dma(self.BLK2T[:], self.hc_blk2t)
        nc = self.nc
        nc.vector.memset(self.RW[0:1, 0:1024], 1.0)
        nc.vector.memset(self.OCOL[:], 1.0)
        nc.vector.memset(self.OCOLB[:], 1.0)
        nc.vector.memset(self.XMID[:], 0.0)
        nc.vector.memset(self.ONER[:, 0:128], 1.0)
        nc.vector.memset(self.ONER[:, 128:256], -1.0)
        self.V.inc(nc.vector.memset(self.XN[:], 0.0))
        self.consts_v = self.V.n
        self.swait(self.T, self.st_setup)
        self.T.wait(self.V, self.consts_v)
        self.S.wait(self.V, self.consts_v)
        self.G.wait(self.V, self.consts_v)
        self.D.wait(self.V, self.consts_v)

    @property
    def ones_row(self):
        return self.RW[0:1, 0:512]

    def emit_addx(self, xfm, b):
        nc = self.nc
        for cc in range(NCC):
            for tt in range(4):
                half = (cc * 4 + tt) % 2
                th = self.TMP[:, half * 512:(half + 1) * 512]
                st = (self.st_tmp, self.st_tmp2)[half]
                self.use(self.D, "TMP" + str(half))
                ld = st.dma(th, xfm[b, cc, :, tt * 512:(tt + 1) * 512])
                self.swait(self.V, st, ld)
                self.use(self.V, "TMP")
                vn = self.V.inc(nc.vector.tensor_tensor(
                    self.xmid(cc, tt * 512, (tt + 1) * 512), th,
                    self.xmid(cc, tt * 512, (tt + 1) * 512), ALU.add))
                self.done("TMP" + str(half), self.V, vn)
        self.barrier()
        return

    def dump_dbg(self, tag, b, src="xmid"):
        import os
        if os.environ.get("DBGPHASE") != tag or b != 0 or self.dbg_ap is None:
            return
        self.barrier()
        for cc in range(NCC):
            if src == "xmid":
                self.st_out.dma(self.dbg_ap[cc], self.xmid(cc))
            else:
                self.st_out.dma(self.dbg_ap[cc].bitcast(BF16)[:, 0:T], self.xn(cc))
        self.barrier()

    def load_tab(self, *aps):
        off = 0
        for ap in aps:
            self.use(self.D, "TAB")
            self.ss_tab.dma(self.TAB[:, off:off + ap.shape[-1]], ap)
            off += ap.shape[-1]
        self.tab_d = self.ss_tab.n
        return self.tab_d

    def emit_ingest(self, xin, xfm):
        nc = self.nc
        for b in range(NB):
            self.swait(self.V, self.st_x)
            for tc in range(T // P):
                self.use(self.D, "TMP")
                ld = self.st_tmp.dma(self.TMP[:, 0:1024], xin[b, tc * P:(tc + 1) * P, :])
                self.swait(self.T, self.st_tmp, ld)
                for half in range(2):
                    pst = (self.P0, self.P1)[half]
                    self.use(self.T, "P" + str(half))
                    tn = 0
                    for q in range(4):
                        cc = half * 4 + q
                        mm = nc.tensor.matmul(pst[:, q * P:(q + 1) * P],
                                              self.TMP[:, cc * P:(cc + 1) * P],
                                              self.IDN[:], start=True, stop=True)
                        if q == 3:
                            tn = self.T.inc(mm)
                    self.V.wait(self.T, tn)
                    vn = 0
                    for q in range(4):
                        cc = half * 4 + q
                        cp = nc.vector.tensor_copy(
                            out=self.xmid(cc, tc * P, (tc + 1) * P),
                            in_=pst[:, q * P:(q + 1) * P])
                        if q == 3:
                            vn = self.V.inc(cp)
                    self.done("P" + str(half), self.V, vn)
                self.done("TMP", self.V)
            self.D.wait(self.V, self.V.n)
            for cc in range(NCC):
                self.st_x.dma(xfm[b, cc], self.xmid(cc))
        self.done("XMID", self.st_x)
        return self.st_x.n

    def emit_egress(self, b, xout):
        nc = self.nc
        for tc in range(T // P):
            for half in range(2):
                pst = (self.P0, self.P1)[half]
                self.use(self.T, "P" + str(half))
                tn = 0
                for q in range(4):
                    cc = half * 4 + q
                    mm = nc.tensor.matmul(pst[:, q * P:(q + 1) * P],
                                          self.xmid(cc, tc * P, (tc + 1) * P),
                                          self.IDN[:], start=True, stop=True)
                    if q == 3:
                        tn = self.T.inc(mm)
                self.V.wait(self.T, tn)
                self.use(self.V, "TMP")
                vn = self.V.inc(nc.vector.tensor_copy(
                    out=self.TMP[:, half * 512:(half + 1) * 512], in_=pst[:]))
                self.done("P" + str(half), self.V, vn)
            self.D.wait(self.V, self.V.n)
            self.st_out.dma(xout[b, tc * P:(tc + 1) * P, :], self.TMP[:, 0:1024])
            self.done("TMP", self.st_out)
        return self.st_out.n

    def load_x(self, xfm, b):
        self.use(self.D, "XMID")
        self.use(self.G, "XN")
        self.swait(self.D, self.st_x)
        for cc in range(NCC):
            self.st_x.dma(self.xmid(cc), xfm[b, cc])
            self.sg_xn.dma(self.xn(cc), xfm[b, cc])
        self.x_d = self.st_x.n
        self.x_g = self.sg_xn.n
        return self.x_d, self.x_g

    def emit_stage_wcol(self, name, wap, ncc, ow, n_oc, col0=0, rows=None,
                        w_last=None):
        rows = ncc * P if rows is None else rows
        colw = ncc * ow
        h = self.nc.dram_tensor(name, [n_oc * P * colw], BF16, kind="Internal")
        base = wap.offset
        pitch = wap.ap[0][0]
        for oc in range(n_oc):
            w_oc = ow if (w_last is None or oc < n_oc - 1) else w_last
            c_full = min(ncc, rows // P)
            if c_full:
                src = APc(wap.tensor, base + col0 + oc * ow,
                          [[pitch, P], [P * pitch, c_full], [1, w_oc]])
                dst = APc(h, oc * P * colw, [[colw, P], [ow, c_full], [1, w_oc]])
                self.sg_stg.dma(dst, src)
            hh = rows - c_full * P
            if hh > 0:
                src = APc(wap.tensor, base + c_full * P * pitch + col0 + oc * ow,
                          [[pitch, hh], [1, w_oc]])
                dst = APc(h, oc * P * colw + c_full * ow, [[colw, hh], [1, w_oc]])
                self.sg_stg.dma(dst, src)
        return h, colw

    def load_wcol2(self, hcw, oc, dst, stream):
        h, colw = hcw
        return stream.dma(dst[:, 0:colw],
                          APc(h, oc * P * colw, [[colw, P], [1, colw]]))

    def load_wcol2_part(self, hcw, oc, coff, wid, dst, stream):
        h, colw = hcw
        return stream.dma(dst[:, 0:wid],
                          APc(h, oc * P * colw + coff, [[colw, P], [1, wid]]))

    def emit_stage_l0a(self, wq, wk, wv, wp):
        self.stgw["a_wv"] = self.emit_stage_wcol("stg_a_wv", wv, 8, 512, 2)
        self.stgw["a_wp"] = self.emit_stage_wcol("stg_a_wp", wp, 8, 128, 8)
        self.stgw["a_wq"] = self.emit_stage_wcol("stg_a_wq", wq, 8, 128, 8)
        self.stgw["a_wk"] = self.emit_stage_wcol("stg_a_wk", wk, 8, 128, 8)

    def emit_stage_l0f(self, wqkv, wo, w1, w2):
        for l in range(2):
            t = str(l)
            self.stgw["b_wqk" + t] = self.emit_stage_wcol(
                "stg_b_wqk" + t, wqkv[l], 8, 128, 8)
            self.stgw["b_wv" + t] = self.emit_stage_wcol(
                "stg_b_wv" + t, wqkv[l], 8, 512, 1, col0=1024)
            self.stgw["b_wo" + t] = self.emit_stage_wcol(
                "stg_b_wo" + t, wo[l], 4, 128, 8, rows=512)
            self.stgw["b_w1a" + t] = self.emit_stage_wcol(
                "stg_b_w1a" + t, w1[l], 8, 128, 22, w_last=42)
            self.stgw["b_w1g" + t] = self.emit_stage_wcol(
                "stg_b_w1g" + t, w1[l], 8, 128, 22, col0=2730, w_last=42)
            self.stgw["b_w2" + t] = self.emit_stage_wcol(
                "stg_b_w2" + t, w2[l], 22, 128, 8, rows=2730)

    def emit_stage_deint(self):
        nc = self.nc
        vv = nc.vector
        self.swait(self.D, self.sg_stg)
        for key in ("a_wq", "a_wk"):
            h = nc.dram_tensor("stg_" + key + "d", [8 * P * 1024], BF16,
                               kind="Internal")
            for oc in range(8):
                buf = oc % 2
                st = (self.st_wa, self.st_wb)[buf]
                wn = self.WCOL[:, buf * 2048: buf * 2048 + 1024]
                wd = self.WCOL[:, buf * 2048 + 1024: buf * 2048 + 2048]
                self.use(self.D, "WC" + str(buf))
                wld = self.load_wcol2(self.stgw[key], oc, wn, st)
                self.swait(self.V, st, wld)
                self.use(self.V, "WCd" + str(buf))
                dei = 0
                for h2 in range(2):
                    for eo in range(2):
                        src = wn.rearrange("p (c x) -> p c x", c=8)[
                            :, :, h2 * 64 + eo: h2 * 64 + 64: 2]
                        dst = wd.rearrange("p (c x) -> p c x", c=8)[
                            :, :, h2 * 64 + eo * 32: h2 * 64 + eo * 32 + 32]
                        dei = self.V.inc(vv.tensor_copy(out=dst, in_=src))
                vv.drain()
                self.done("WC" + str(buf), self.V, dei)
                self.D.wait(self.V, dei)
                sst = (self.st_tmp, self.st_tmp2)[buf]
                sn = sst.dma(APc(h, oc * P * 1024, [[1024, P], [1, 1024]]), wd)
                self.done("WCd" + str(buf), sst, sn)
            self.stgw[key + "d"] = (h, 1024)
        self.barrier()

    def load_wcol(self, wdram, oc, ncc, ow, dst, w=None, col0=None, rows=None):
        w = w or ow
        c0 = oc * ow if col0 is None else col0
        n = 0
        nrow = ncc * P if rows is None else rows
        for cc in range(ncc):
            h = min(P, nrow - cc * P)
            if h <= 0:
                break
            src = wdram[cc * P: cc * P + h, c0: c0 + w]
            n = self.sg_w.dma(dst[0:h, cc * ow: cc * ow + w], src)
        return n

    def _ln_rows(self, ps_sum, ps_sq, eps, nfeat):
        nc = self.nc
        v = nc.vector
        m_row = self.row(1)
        r_row = self.row(2)
        v_row = self.row(3)
        mr_row = self.row(4)
        v.tensor_scalar(m_row, ps_sum, 1.0 / nfeat, None, ALU.mult)
        v.drain()
        v.tensor_scalar(v_row, ps_sq, 1.0 / nfeat, None, ALU.mult)
        v.drain()
        v.tensor_tensor(mr_row, m_row, m_row, ALU.mult)
        v.drain()
        vn = self.V.inc(v.scalar_tensor_tensor(
            v_row, v_row, float(eps), mr_row, ALU.add, ALU.subtract))
        self.S.wait(self.V, vn)
        nc.scalar.activation(out=r_row, in_=v_row, func=AF.Ln)
        self.S.drain()
        sn = self.S.inc(nc.scalar.activation(out=r_row, in_=r_row, func=AF.Exp,
                                             scale=-0.5))
        self.V.wait(self.S, sn)
        return self.V.inc(v.tensor_tensor(mr_row, m_row, r_row, ALU.mult))

    def emit_ln(self, srcf=None, dstf=None, dstb=None, gains=None, eps=1e-5,
                dram_out=None, b=None):
        nc = self.nc
        if srcf is None:
            srcf = self.xmid
        ps_sum = self.P6[64:65, :]
        ps_sq = self.P4[64:65, :]
        r_row = self.row(2)
        mr_row = self.row(4)
        for tt in range(T // 512):
            lo, hi = tt * 512, (tt + 1) * 512
            self.use(self.T, "P6", "P4")
            for cc in range(NCC):
                mm = nc.tensor.matmul(ps_sum, self.OCOL[:, 0:1], srcf(cc, lo, hi),
                                      start=(cc == 0), stop=(cc == NCC - 1))
                if cc == NCC - 1:
                    self.T.inc(mm)
            for cc in range(NCC):
                xh = self.X2[:, (cc % 2) * 512:(cc % 2) * 512 + 512]
                self.use(self.S, "X2" + str(cc % 2))
                sn = self.S.inc(nc.scalar.activation(
                    out=xh, in_=srcf(cc, lo, hi), func=AF.Square))
                self.T.wait(self.S, sn)
                tn = self.T.inc(nc.tensor.matmul(
                    ps_sq, self.OCOLB[:, 0:1], xh,
                    start=(cc == 0), stop=(cc == NCC - 1)))
                self.done("X2" + str(cc % 2), self.T, tn)
            self.V.wait(self.T, self.T.n)
            self.use(self.V, "LNrows")
            vrows = self._ln_rows(ps_sum, ps_sq, eps, D)
            self.done("P6", self.V, vrows)
            self.done("P4", self.V, vrows)
            self.use(self.T, "P0", "P1")
            self.T.wait(self.V, vrows)
            nc.tensor.matmul(self.P0[:], self.RW[0:1, 0:P], r_row,
                             start=True, stop=True)
            t_bc = self.T.inc(nc.tensor.matmul(
                self.P1[:], self.RW[0:1, 0:P], mr_row, start=True, stop=True))
            self.done("LNrows", self.T, t_bc)
            self.V.wait(self.T, t_bc)
            for cc in range(NCC):
                v = nc.vector
                self.use(self.V, "TMP")
                v.tensor_tensor(self.TMP[:, 0:512], srcf(cc, lo, hi), self.P0[:],
                                ALU.mult)
                v.drain()
                if gains is None:
                    tgt = dstf(cc, lo, hi) if dstf is not None else dstb(cc, lo, hi)
                    vn = self.V.inc(v.tensor_tensor(tgt, self.TMP[:, 0:512],
                                                    self.P1[:], ALU.subtract))
                    v.drain()
                    if dstf is not None and dstb is not None:
                        vn = self.V.inc(v.tensor_copy(out=dstb(cc, lo, hi),
                                                      in_=dstf(cc, lo, hi)))
                        v.drain()
                    self.done("TMP", self.V, vn)
                else:
                    vn = self.V.inc(v.tensor_tensor(
                        self.TMP[:, 512:1024], self.TMP[:, 0:512],
                        self.P1[:], ALU.subtract))
                    v.drain()
                    self.done("TMP", self.V, vn)
                    gcol, bcol = gains
                    self.S.wait(self.V, vn)
                    sn = self.S.inc(nc.scalar.activation(
                        out=dstb(cc, lo, hi), in_=self.TMP[:, 512:1024],
                        func=AF.Identity, bias=bcol(cc), scale=gcol(cc)))
                    self.done("TMP", self.S, sn)
            self.done("P0", self.V if gains is None else self.S)
            self.done("P1", self.V if gains is None else self.S)
        if dram_out is not None:
            self.D.wait(self.V, self.V.n)
            self.D.wait(self.S, self.S.n)
            for cc in range(NCC):
                self.st_x.dma(dram_out[b, cc], self.xmid(cc))
            self.done("XMID", self.st_x)
        return

    def emit_l0_attn(self, b, wq, bq, wk, bk, wv, bv, wp, bp, attb):
        nc = self.nc
        off_v = 0
        off_q = 16640
        off_k = 18688
        off_sw = 20736
        off_t2 = 22784
        off_pt = 24832
        off_ao = 33024
        off_al = 0
        EXPSC = 0.125

        bvrow = self.row(6, 1024)
        self.use(self.D, "RW8")
        self.ss_rw.dma(bvrow, bv.unsqueeze(0))
        bv_d = self.ss_rw.n
        self.use(self.V, "SCRv")
        v_ones = self.V.inc(nc.vector.memset(self.SCR[:, off_v:off_v + 16640], 1.0))
        for half in range(2):
            wsl = self.WCOL[:, 0:4096]
            self.use(self.D, "WCOL")
            wld = self.load_wcol2(self.stgw["a_wv"], half, wsl, self.st_wa)
            self.swait(self.T, self.st_wa, wld)
            self.swait(self.T, self.sg_xn, self.x_g)
            self.swait(self.T, self.ss_rw, bv_d)
            for tk in range(T // P):
                pst = (self.P0, self.P1)[tk % 2]
                self.use(self.T, "P" + str(tk % 2))
                for cc in range(NCC):
                    nc.tensor.matmul(pst[:], self.xn(cc, tk * P, (tk + 1) * P),
                                     wsl[:, cc * 512:(cc + 1) * 512],
                                     start=(cc == 0), stop=False)
                tn = self.T.inc(nc.tensor.matmul(
                    pst[:], self.ones_row[0:1, 0:P],
                    bvrow[0:1, half * 512:(half + 1) * 512],
                    start=False, stop=True))
                self.V.wait(self.T, tn)
                if tk == 0 and half == 0:
                    self.V.wait(self.V, v_ones)
                vn = 0
                for j in range(8):
                    h = half * 8 + j
                    slot = off_v + tk * 1040 + h * 65
                    cp = nc.vector.tensor_copy(out=self.SCR[:, slot:slot + 64],
                                               in_=pst[:, j * 64:(j + 1) * 64])
                    if j == 7:
                        vn = self.V.inc(cp)
                self.done("P" + str(tk % 2), self.V, vn)
            self.done("WCOL", self.T)
        v_ready = self.V.n

        self.use(self.D, "WCOL", "WCa", "WCb")
        wsts = (self.st_wa, self.st_wb)
        wlds = [0, 0]
        self.load_wcol2(self.stgw["a_wqd"], 0, self.WCOL[:, 0:1024], wsts[0])
        wlds[0] = self.load_wcol2(self.stgw["a_wkd"], 0,
                                  self.WCOL[:, 1024:2048], wsts[0])
        for hp in range(NCC):
            buf = hp % 2
            wq_d = self.WCOL[:, buf * 2048: buf * 2048 + 1024]
            wk_d = self.WCOL[:, buf * 2048 + 1024: buf * 2048 + 2048]
            if hp + 1 < NCC:
                nb = 1 - buf
                self.use(self.D, "WC" + "ab"[nb])
                self.load_wcol2(self.stgw["a_wqd"], hp + 1,
                                self.WCOL[:, nb * 2048: nb * 2048 + 1024], wsts[nb])
                wlds[nb] = self.load_wcol2(
                    self.stgw["a_wkd"], hp + 1,
                    self.WCOL[:, nb * 2048 + 1024: nb * 2048 + 2048], wsts[nb])
            bq_c = self.BCOL[:, 0:1]
            bk_c = self.BCOL[:, 1:2]
            self.use(self.D, "BCOL01")
            for (bsrc, bdst) in ((bq, bq_c), (bk, bk_c)):
                for h2 in range(2):
                    for eo in range(2):
                        src = APc(bsrc.tensor, bsrc.offset + 128 * hp + 64 * h2 + eo,
                                  [[2, 32], [1, 1]])
                        self.ss_bcol.dma(bdst[h2 * 64 + eo * 32: h2 * 64 + eo * 32 + 32, :],
                                         src, allow_slow_non_contiguous=True)
            bias_d = self.ss_bcol.n

            for (wdsl, bcol, offd) in ((wq_d, bq_c, off_q), (wk_d, bk_c, off_k)):
                for tt in range(4):
                    pst = (self.P0, self.P1)[tt % 2]
                    self.use(self.T, "P" + str(tt % 2))
                    self.swait(self.T, wsts[buf], wlds[buf])
                    tn = 0
                    for cc in range(NCC):
                        mm = nc.tensor.matmul(
                            pst[:], wdsl[:, cc * 128:(cc + 1) * 128],
                            self.xn(cc, tt * 512, (tt + 1) * 512),
                            start=(cc == 0), stop=(cc == NCC - 1))
                        if cc == NCC - 1:
                            tn = self.T.inc(mm)
                    self.S.wait(self.T, tn)
                    self.swait(self.S, self.ss_bcol, bias_d)
                    self.use(self.S, "SCRq")
                    sn = self.S.inc(nc.scalar.activation(
                        out=self.scr(offd, tt * 512, (tt + 1) * 512),
                        in_=pst[:], func=AF.Identity, bias=bcol, scale=1.0))
                    self.done("P" + str(tt % 2), self.S, sn)
            s_qk = self.S.n
            self.done("WC" + "ab"[buf], self.S, s_qk)
            self.done("BCOL01", self.S, s_qk)

            self.V.wait(self.S, s_qk)
            self.swait(self.V, self.ss_tab, getattr(self, "tab_d", 0))
            for offd in (off_q, off_k):
                vv = nc.vector
                self.use(self.V, "SCRsw", "SCRt2")
                for (d0, s0) in ((0, 32), (32, 0), (64, 96), (96, 64)):
                    vv.tensor_copy(out=self.scr(off_sw, 0, T, d0, d0 + 32),
                                   in_=self.scr(offd, 0, T, s0, s0 + 32))
                vv.drain()
                vv.tensor_tensor(self.scr(off_t2, 0, T), self.scr(offd, 0, T),
                                 self.TAB[:, 0:T], ALU.mult)
                vv.drain()
                vv.tensor_tensor(self.scr(off_sw, 0, T), self.scr(off_sw, 0, T),
                                 self.TAB[:, T:2 * T], ALU.mult)
                vv.drain()
                rope_v = self.V.inc(vv.tensor_tensor(
                    self.scr(offd, 0, T), self.scr(off_t2, 0, T),
                    self.scr(off_sw, 0, T), ALU.add))
                vv.drain()
            self.done("SCRsw", self.V, rope_v)
            self.done("SCRt2", self.V, rope_v)
            self.done("SCRq", self.V, rope_v)

            for h2 in range(2):
                h = hp * 2 + h2
                pbase = h2 * 64
                ps_o = (self.P4, self.P5)[h2]
                for qt in range(4):
                    kmax = 4 * qt + 4
                    sn_by = {}
                    vm_by = {}
                    for kc in range(kmax):
                        dlt = 128 * kc - 512 * qt
                        lo = max(0, dlt)
                        pss = (self.P0, self.P1)[kc % 2]
                        self.use(self.T, "P" + str(kc % 2), "SCRpt")
                        self.T.wait(self.V, rope_v)
                        tn = self.T.inc(nc.tensor.matmul(
                            pss[:, lo:512],
                            self.scr(off_k, kc * P, (kc + 1) * P, pbase, pbase + 64),
                            self.scr(off_q, qt * 512 + lo, (qt + 1) * 512,
                                     pbase, pbase + 64),
                            start=True, stop=True))
                        self.S.wait(self.T, tn)
                        sn = self.S.inc(nc.scalar.activation(
                            out=self.scr(off_pt, kc * 512 + lo, (kc + 1) * 512),
                            in_=pss[:, lo:512], func=AF.Exp, scale=EXPSC))
                        self.done("P" + str(kc % 2), self.S, sn)
                        sn_by[kc] = sn
                        vm_by[kc] = 0
                        if dlt >= 0:
                            self.V.wait(self.S, sn)
                            vv = nc.vector
                            vm_by[kc] = self.V.inc(vv.tensor_tensor(
                                self.scr(off_pt, kc * 512 + dlt, kc * 512 + dlt + P),
                                self.scr(off_pt, kc * 512 + dlt, kc * 512 + dlt + P),
                                self.MSK[:, 0:P], ALU.mult))
                    self.use(self.T, ("P4", "P5")[h2])
                    if hp == 0 and h2 == 0 and qt == 0:
                        self.T.wait(self.V, v_ready)
                    t_pv = 0
                    for kc in range(kmax):
                        lo = max(0, 128 * kc - 512 * qt)
                        slot = off_v + kc * 1040 + h * 65
                        self.T.wait(self.S, sn_by[kc])
                        if vm_by[kc]:
                            self.T.wait(self.V, vm_by[kc])
                        mm = nc.tensor.matmul(
                            ps_o[0:65, lo:512], self.SCR[:, slot:slot + 65],
                            self.scr(off_pt, kc * 512 + lo, (kc + 1) * 512),
                            start=(kc == 0), stop=(kc == kmax - 1))
                        if kc == kmax - 1:
                            t_pv = self.T.inc(mm)
                    self.done("SCRpt", self.T, t_pv)
                    self.V.wait(self.T, t_pv)
                    rl_row = self.row(5)
                    self.use(self.V, "RW6")
                    vn = self.V.inc(nc.vector.reciprocal(rl_row, ps_o[64:65, :]))
                    self.use(self.T, "P6")
                    self.T.wait(self.V, vn)
                    t_bc = self.T.inc(nc.tensor.matmul(
                        self.P6[0:64, :], self.RW[0:1, 0:64], rl_row,
                        start=True, stop=True))
                    self.done("RW6", self.T, t_bc)
                    self.use(self.S, "SCRt2")
                    self.S.wait(self.T, t_bc)
                    sn = self.S.inc(nc.scalar.copy(
                        out=self.scr(off_t2, 0, 512, 0, 64), in_=ps_o[0:64, :]))
                    self.done(("P4", "P5")[h2], self.S, sn)
                    self.V.wait(self.S, sn)
                    self.use(self.V, "SCRao")
                    vv = nc.vector
                    vn = self.V.inc(vv.tensor_tensor(
                        self.scr(off_ao, qt * 512, (qt + 1) * 512, pbase, pbase + 64),
                        self.scr(off_t2, 0, 512, 0, 64), self.P6[0:64, :], ALU.mult))
                    self.done("P6", self.V, vn)
                    self.done("SCRt2", self.V, vn)
            self.D.wait(self.V, self.V.n)
            dn = self.st_ao.dma(attb[b, hp], self.scr(off_ao, 0, T))
            self.done("SCRao", self.st_ao, dn)
            self.done("SCRq", self.st_ao, dn)

        self.use(self.D, "SCRv_w")
        t_after_pv = self.T.n
        self.D.wait(self.T, t_after_pv)
        self.swait(self.D, self.st_ao)
        al = 0
        for cc in range(NCC):
            al = self.st_al.dma(self.SCR[:, off_al + cc * T: off_al + (cc + 1) * T],
                                attb[b, cc])
        bp_c = self.BCOL[:, 2:3]
        self.use(self.D, "WCOL", "WCa", "WCb")
        wsts = (self.st_wa, self.st_wb)
        wlds = [0, 0]
        wlds[0] = self.load_wcol2(self.stgw["a_wp"], 0, self.WCOL[:, 0:1024],
                                  wsts[0])
        for oc in range(NCC):
            buf = oc % 2
            wsl = self.WCOL[:, buf * 1024: buf * 1024 + 1024]
            if oc + 1 < NCC:
                nb = 1 - buf
                self.use(self.D, "WC" + "ab"[nb])
                wlds[nb] = self.load_wcol2(
                    self.stgw["a_wp"], oc + 1,
                    self.WCOL[:, nb * 1024: nb * 1024 + 1024], wsts[nb])
            wld = wlds[buf]
            self.use(self.D, "BCOL2")
            self.ss_b2.dma(bp_c, bp[oc * P:(oc + 1) * P].unsqueeze(1))
            bp_d = self.ss_b2.n
            for tt in range(4):
                pst = (self.P0, self.P1)[tt % 2]
                self.use(self.T, "P" + str(tt % 2))
                self.swait(self.T, wsts[buf], wld)
                self.swait(self.T, self.st_al, al)
                tn = 0
                for cc in range(NCC):
                    mm = nc.tensor.matmul(
                        pst[:], wsl[:, cc * P:(cc + 1) * P],
                        self.SCR[:, off_al + cc * T + tt * 512:
                                 off_al + cc * T + (tt + 1) * 512],
                        start=(cc == 0), stop=(cc == NCC - 1))
                    if cc == NCC - 1:
                        tn = self.T.inc(mm)
                self.V.wait(self.T, tn)
                self.swait(self.V, self.ss_b2, bp_d)
                vv = nc.vector
                vn = self.V.inc(vv.scalar_tensor_tensor(
                    out=self.xmid(oc, tt * 512, (tt + 1) * 512),
                    in0=pst[:], scalar=bp_c,
                    in1=self.xmid(oc, tt * 512, (tt + 1) * 512),
                    op0=ALU.add, op1=ALU.add))
                self.done("P" + str(tt % 2), self.V, vn)
            self.done("WC" + "ab"[buf], self.V)
            self.done("BCOL2", self.V)
        self.done("WCOL", self.V)
        self.done("SCRv", self.V)
        self.done("SCRv_w", self.V)
        self.done("RW8", self.T)
        self.done("XN", self.T)
        self.barrier()
        return

    def emit_stage_bf16(self, pairs):
        n = 0
        for (src, dst) in pairs:
            n = self.sg_stg.dma(dst, src)
        self.stg_n = n
        return n

    def emit_l1_emb(self, emb):
        nc = self.nc
        for r in range(65):
            self.sg_stg.dma(self.X2[r:r + 1, 0:64],
                            emb[64 - r].unsqueeze(0))
        self.swait(self.T, self.sg_stg)
        nc.vector.tensor_copy(out=self.MT[0:65, 0:65], in_=self.IDNB[0:65, 0:65])
        nc.vector.drain()
        nc.vector.memset(self.MT[64:65, 0:64], -1.0)
        nc.vector.drain()
        mtv = self.V.inc(nc.vector.memset(self.MT[64:65, 64:65], 0.0))
        self.T.wait(self.V, mtv)
        self.use(self.T, "P2")
        mm = nc.tensor.matmul(self.P2[0:64, 0:65],
                              self.X2[0:65, 0:64],
                              self.MT[0:65, 0:65], start=True, stop=True)
        tn = self.T.inc(mm)
        self.V.wait(self.T, tn)
        nc.vector.tensor_copy(out=self.TAB[0:64, 0:65], in_=self.P2[0:64, 0:65])
        nc.vector.drain()
        vn = self.V.inc(nc.vector.tensor_copy(out=self.TAB[64:128, 0:65],
                                              in_=self.P2[0:64, 0:65]))
        self.done("P2", self.V, vn)
        self.done("X20", self.V, vn)
        self.emb_v = vn
        return vn

    def emit_l1_attn(self, b, wq_bf, wo_bf, epad_h, attb):
        import os
        noband = bool(os.environ.get("NOBAND"))
        self._bandcap = int(os.environ.get("BANDCAP", "999999"))
        noep = bool(os.environ.get("NOEP"))
        nc = self.nc
        off_v = 0
        off_q = 16640
        off_e = 18688
        off_ep = 20736
        off_bt = 21312
        off_t2 = 22336
        off_pt = 24832
        off_ao = 33024
        off_al = 0
        EXPSC = 0.125
        EPW = 576

        self.use(self.V, "SCRv")
        v_ones = self.V.inc(nc.vector.memset(self.SCR[:, off_v:off_v + 16640], 1.0))
        for half in range(2):
            wsl = self.WCOL[:, 0:4096]
            self.use(self.D, "WCOL")
            for cc in range(NCC):
                wld = self.ss_wt.dma(
                    wsl[:, cc * 512:(cc + 1) * 512],
                    wq_bf[half * 512:(half + 1) * 512, cc * P:(cc + 1) * P],
                    transpose=True)
            self.swait(self.T, self.ss_wt, wld)
            self.swait(self.T, self.sg_xn, self.x_g)
            for tk in range(T // P):
                pst = (self.P0, self.P1)[tk % 2]
                self.use(self.T, "P" + str(tk % 2))
                tn = 0
                for cc in range(NCC):
                    mm = nc.tensor.matmul(pst[:], self.xn(cc, tk * P, (tk + 1) * P),
                                          wsl[:, cc * 512:(cc + 1) * 512],
                                          start=(cc == 0), stop=(cc == NCC - 1))
                    if cc == NCC - 1:
                        tn = self.T.inc(mm)
                self.V.wait(self.T, tn)
                if tk == 0 and half == 0:
                    self.V.wait(self.V, v_ones)
                vn = 0
                for j in range(8):
                    h = half * 8 + j
                    slot = off_v + tk * 1040 + h * 65
                    cp = nc.vector.tensor_copy(out=self.SCR[:, slot:slot + 64],
                                               in_=pst[:, j * 64:(j + 1) * 64])
                    if j == 7:
                        vn = self.V.inc(cp)
                self.done("P" + str(tk % 2), self.V, vn)
            self.done("WCOL", self.T)
        v_ready = self.V.n

        if not hasattr(self, "_ep_zeroed"):
            self._ep_zeroed = True
            self.use(self.V, "SCRep")
            self.V.inc(nc.vector.memset(self.scr(off_ep, 321, 544), 0.0))

        for hp in range(NCC):
            wsl = self.WCOL[:, 0:1024]
            self.use(self.D, "WCOL", "WCOLd")
            self.D.wait(self.T, self.T.n)
            for st_ in self.sg_band:
                self.swait(self.D, st_)
            for cc in range(NCC):
                wld = self.ss_wt.dma(
                    wsl[:, cc * P:(cc + 1) * P],
                    wq_bf[hp * P:(hp + 1) * P, cc * P:(cc + 1) * P],
                    transpose=True)
            for tt in range(4):
                pst = (self.P0, self.P1)[tt % 2]
                self.use(self.T, "P" + str(tt % 2))
                self.swait(self.T, self.ss_wt, wld)
                tn = 0
                for cc in range(NCC):
                    mm = nc.tensor.matmul(
                        pst[:], wsl[:, cc * P:(cc + 1) * P],
                        self.xn(cc, tt * 512, (tt + 1) * 512),
                        start=(cc == 0), stop=(cc == NCC - 1))
                    if cc == NCC - 1:
                        tn = self.T.inc(mm)
                self.S.wait(self.T, tn)
                self.use(self.S, "SCRq")
                sn = self.S.inc(nc.scalar.copy(
                    out=self.scr(off_q, tt * 512, (tt + 1) * 512), in_=pst[:]))
                self.done("P" + str(tt % 2), self.S, sn)
            s_q = self.S.n
            self.done("WCOLd", self.S, s_q)

            for h2 in range(2):
                h = hp * 2 + h2
                pbase = h2 * 64
                ps_o = (self.P4, self.P5)[h2]
                self.T.wait(self.V, self.emb_v)
                self.T.wait(self.S, s_q)
                for tt in range(4):
                    self.use(self.T, "P6")
                    tn = self.T.inc(nc.tensor.matmul(
                        self.P6[0:65, :], self.TAB[pbase:pbase + 64, 0:65],
                        self.scr(off_q, tt * 512, (tt + 1) * 512, pbase, pbase + 64),
                        start=True, stop=True))
                    self.V.wait(self.T, tn)
                    self.use(self.V, "SCRe")
                    vn = self.V.inc(nc.vector.tensor_copy(
                        out=self.scr(off_e, tt * 512, (tt + 1) * 512, 0, 65),
                        in_=self.P6[0:65, :]))
                    self.done("P6", self.V, vn)
                e_v = self.V.n
                for tk in range(0 if not noep else T // P, T // P):
                    self.use(self.T, "P2")
                    self.T.wait(self.V, e_v)
                    tn = self.T.inc(nc.tensor.matmul(
                        self.P2[:, 0:65], self.scr(off_e, tk * P, (tk + 1) * P, 0, 65),
                        self.IDNB[0:65, 0:65], start=True, stop=True))
                    self.V.wait(self.T, tn)
                    self.use(self.V, "SCRep")
                    vn = self.V.inc(nc.vector.tensor_copy(
                        out=self.scr(off_ep, 256, 321), in_=self.P2[:, 0:65]))
                    self.done("P2", self.V, vn)
                    self.S.wait(self.V, vn)
                    sn = self.S.inc(nc.scalar.activation(
                        out=self.scr(off_ep, 33, 256), in_=self.scr(off_ep, 33, 256),
                        func=AF.Identity, bias=self.scr(off_ep, 256, 257),
                        scale=0.0))
                    self.D.wait(self.S, sn)
                    base = ((b * 16 + h) * T + tk * P) * EPW
                    dst = APc(epad_h, base + 33, [[EPW, P], [1, 511]])
                    en = self.ss_ep.dma(dst, self.scr(off_ep, 33, 544))
                    self.done("SCRep", self.ss_ep, en)
                ep_done = self.ss_ep.n

                for qt in range(4):
                    pend = {}

                    def _issue(kci):
                        bl = [it for it in range(4 * qt, 4 * qt + 4)
                              if abs(it - kci) <= 1] if (not noband
                                                         and kci < 16) else []
                        if self._bandcap <= 0:
                            bl = []
                        self._bandcap -= len(bl)
                        lst = []
                        for bi, it in enumerate(bl):
                            si = (kci % 2) * 4 + bi
                            slot = off_bt + si * P
                            self.use(self.D, "BT" + str(si))
                            self.swait(self.D, self.ss_ep, ep_done)
                            base = ((b * 16 + h) * T + it * P) * EPW
                            step = EPW if os.environ.get("SKEW576") else EPW - 1
                            src = APc(epad_h, base + 288 - 128 * (it - kci),
                                      [[step, P], [1, P]])
                            bn = self.sg_band[si].dma(
                                self.SCR[:, slot:slot + P], src)
                            lst.append((it, si, bn))
                        pend[kci] = lst

                    _issue(0)
                    sn_by = {}
                    for kc in range(16):
                        _issue(kc + 1)
                        pss = (self.P0, self.P1)[kc % 2]
                        self.use(self.T, "P" + str(kc % 2))
                        bands = pend.pop(kc)
                        locol = max(0, (kc + 2) * P - 512 * qt) if not noband else 512
                        n_extra = len(bands) + (1 if locol < 512 else 0)
                        ei = 0
                        tn = self.T.inc(nc.tensor.matmul(
                            pss[:],
                            self.scr(off_q, kc * P, (kc + 1) * P, pbase, pbase + 64),
                            self.scr(off_q, qt * 512, (qt + 1) * 512, pbase, pbase + 64),
                            start=True, stop=(n_extra == 0),
                            skip_group_check=(n_extra > 0)))
                        for (it, si, bn) in bands:
                            slot = off_bt + si * P
                            self.swait(self.T, self.sg_band[si], bn)
                            ei += 1
                            tn = self.T.inc(nc.tensor.matmul(
                                pss[:, (it - 4 * qt) * P:(it - 4 * qt + 1) * P],
                                self.SCR[:, slot:slot + P], self.IDNB[:],
                                start=False, stop=(ei == n_extra),
                                skip_group_check=True))
                            self.done("BT" + str(si), self.T, tn)
                        if locol < 512:
                            ei += 1
                            tn = self.T.inc(nc.tensor.matmul(
                                pss[:, locol:512], self.ONER[0:1, 0:128],
                                self.scr(off_e, qt * 512 + locol, (qt + 1) * 512, 0, 1),
                                start=False, stop=(ei == n_extra),
                                skip_group_check=True))
                        self.S.wait(self.T, tn)
                        sn = self.S.inc(nc.scalar.activation(
                            out=self.scr(off_pt, kc * 512, (kc + 1) * 512),
                            in_=pss[:], func=AF.Exp, scale=EXPSC))
                        self.done("P" + str(kc % 2), self.S, sn)
                        sn_by[kc] = sn
                    self.use(self.T, ("P4", "P5")[h2])
                    if hp == 0 and h2 == 0 and qt == 0:
                        self.T.wait(self.V, v_ready)
                    t_pv = 0
                    for kc in range(16):
                        slot = off_v + kc * 1040 + h * 65
                        self.T.wait(self.S, sn_by[kc])
                        mm = nc.tensor.matmul(
                            ps_o[0:65, :], self.SCR[:, slot:slot + 65],
                            self.scr(off_pt, kc * 512, (kc + 1) * 512),
                            start=(kc == 0), stop=(kc == 15))
                        if kc == 15:
                            t_pv = self.T.inc(mm)
                    self.done("SCRpt", self.T, t_pv)
                    self.V.wait(self.T, t_pv)
                    rl_row = self.row(5)
                    self.use(self.V, "RW6")
                    vn = self.V.inc(nc.vector.reciprocal(rl_row, ps_o[64:65, :]))
                    self.use(self.T, "P6")
                    self.T.wait(self.V, vn)
                    t_bc = self.T.inc(nc.tensor.matmul(
                        self.P6[0:64, :], self.RW[0:1, 0:64], rl_row,
                        start=True, stop=True))
                    self.done("RW6", self.T, t_bc)
                    self.use(self.S, "SCRt2")
                    self.S.wait(self.T, t_bc)
                    sn = self.S.inc(nc.scalar.copy(
                        out=self.scr(off_t2, 0, 512, 0, 64), in_=ps_o[0:64, :]))
                    self.done(("P4", "P5")[h2], self.S, sn)
                    self.V.wait(self.S, sn)
                    self.use(self.V, "SCRao")
                    vn = self.V.inc(nc.vector.tensor_tensor(
                        self.scr(off_ao, qt * 512, (qt + 1) * 512, pbase, pbase + 64),
                        self.scr(off_t2, 0, 512, 0, 64), self.P6[0:64, :], ALU.mult))
                    self.done("P6", self.V, vn)
                    self.done("SCRt2", self.V, vn)
                self.done("SCRe", self.T)
            self.D.wait(self.V, self.V.n)
            dn = self.st_ao.dma(attb[b, hp], self.scr(off_ao, 0, T))
            self.done("SCRao", self.st_ao, dn)
            self.done("SCRq", self.st_ao, dn)

        self.swait(self.D, self.st_ao)
        self.use(self.D, "SCRv_w")
        self.D.wait(self.T, self.T.n)
        al = 0
        for cc in range(NCC):
            al = self.st_al.dma(self.SCR[:, off_al + cc * T: off_al + (cc + 1) * T],
                                attb[b, cc])
        for oc in range(NCC):
            wsl = self.WCOL[:, 0:1024]
            self.use(self.D, "WCOL", "WCOLd")
            for cc in range(NCC):
                wld = self.ss_wt.dma(
                    wsl[:, cc * P:(cc + 1) * P],
                    wo_bf[oc * P:(oc + 1) * P, cc * P:(cc + 1) * P],
                    transpose=True)
            for tt in range(4):
                pst = (self.P0, self.P1)[tt % 2]
                self.use(self.T, "P" + str(tt % 2))
                self.swait(self.T, self.ss_wt, wld)
                self.swait(self.T, self.st_al, al)
                tn = 0
                for cc in range(NCC):
                    mm = nc.tensor.matmul(
                        pst[:], wsl[:, cc * P:(cc + 1) * P],
                        self.SCR[:, off_al + cc * T + tt * 512:
                                 off_al + cc * T + (tt + 1) * 512],
                        start=(cc == 0), stop=(cc == NCC - 1))
                    if cc == NCC - 1:
                        tn = self.T.inc(mm)
                self.V.wait(self.T, tn)
                vn = self.V.inc(nc.vector.tensor_tensor(
                    self.xmid(oc, tt * 512, (tt + 1) * 512), pst[:],
                    self.xmid(oc, tt * 512, (tt + 1) * 512), ALU.add))
                self.done("P" + str(tt % 2), self.V, vn)
            self.done("WCOL", self.V)
            self.done("WCOLd", self.V)
        self.done("SCRv", self.V)
        self.done("SCRv_w", self.V)
        self.done("XN", self.T)
        self.barrier()
        return

    def emit_l0_ffn(self, b, wqkv, q_scale, k_scale, wo, ln_g, ln_b, w1, b1,
                    w2, b2, attb, xtabs=None):
        self.xtabs = xtabs or self.xtabs
        nc = self.nc
        HID = 2730
        NJ = 22
        off_v = 0
        off_q = 8320
        off_k = 10368
        off_sw = 12416
        off_t2 = 14464
        off_pt = 16512
        off_ao = 20608
        off_tab = 22656
        off_hf = 0
        off_al = 12416

        def xsh(cc, lo, hi, base=XOFF):
            d = 1 if cc >= 4 else 0
            return self.XN[:, cc * CS + base + lo - d: cc * CS + base + hi - d]

        def xshm(cc, lo, hi):
            d = 1 if cc >= 4 else 0
            return self.XMID[:, cc * CS + XOFF + lo - d: cc * CS + XOFF + hi - d]

        for l in range(2):
            wqkv_l = wqkv[l]
            wo_l = wo[l]
            w1_l = w1[l]
            w2_l = w2[l]
            qs_c = self.BCOL[:, 4:5]
            ks_c = self.BCOL[:, 5:6]
            self.use(self.D, "BCOL45")
            for (src, dstc) in ((q_scale, qs_c), (k_scale, ks_c)):
                self.ss_bcol.dma(dstc[0:64, :], src[l].unsqueeze(1))
                self.ss_bcol.dma(dstc[64:128, :], src[l].unsqueeze(1))
            qs_s = self.BCOL[:, 6:7]
            ks_s = self.BCOL[:, 7:8]
            for (src, dstc) in ((q_scale, qs_s), (k_scale, ks_s)):
                sl = src[l]
                self.ss_bcol.dma(dstc[0:32, :], sl[32:64].unsqueeze(1))
                self.ss_bcol.dma(dstc[32:64, :], sl[0:32].unsqueeze(1))
                self.ss_bcol.dma(dstc[64:96, :], sl[32:64].unsqueeze(1))
                self.ss_bcol.dma(dstc[96:128, :], sl[0:32].unsqueeze(1))
            bc_d = self.ss_bcol.n
            self.use(self.D, "SCRtab")
            tl = 0
            for (ti, hcap) in enumerate(self.xtabs):
                tl = self.ss_tab.dma(self.scr(off_tab + ti * T, 0, T), hcap)
            self.swait(self.V, self.ss_tab, tl)
            self.swait(self.V, self.ss_bcol, bc_d)
            vv = nc.vector
            vn = 0
            for (ti, col) in enumerate((qs_c, qs_s, ks_c, ks_s)):
                vn = self.V.inc(vv.tensor_scalar(
                    self.scr(off_tab + ti * T, 0, T),
                    self.scr(off_tab + ti * T, 0, T),
                    col, None, ALU.mult))
            tab_v = vn

            self.use(self.V, "SCRv")
            v_ones = self.V.inc(nc.vector.memset(self.SCR[:, off_v:off_v + 8320], 1.0))
            wsl = self.WCOL[:, 0:4096]
            self.use(self.D, "WCOL")
            wld = self.load_wcol2(self.stgw["b_wv" + str(l)], 0, wsl, self.st_wa)
            self.swait(self.T, self.st_wa, wld)
            self.swait(self.T, self.sg_xn, self.x_g)
            for tk in range(T // P):
                pst = (self.P0, self.P1)[tk % 2]
                self.use(self.T, "P" + str(tk % 2))
                tn = 0
                for cc in range(NCC):
                    mm = nc.tensor.matmul(pst[:], xsh(cc, tk * P, (tk + 1) * P),
                                          wsl[:, cc * 512:(cc + 1) * 512],
                                          start=(cc == 0), stop=(cc == NCC - 1))
                    if cc == NCC - 1:
                        tn = self.T.inc(mm)
                self.V.wait(self.T, tn)
                if tk == 0:
                    self.V.wait(self.V, v_ones)
                vn = 0
                for h in range(8):
                    slot = off_v + tk * 520 + h * 65
                    cp = nc.vector.tensor_copy(out=self.SCR[:, slot:slot + 64],
                                               in_=pst[:, h * 64:(h + 1) * 64])
                    if h == 7:
                        vn = self.V.inc(cp)
                self.done("P" + str(tk % 2), self.V, vn)
            self.done("WCOL", self.T)
            v_ready = self.V.n

            self.use(self.D, "WCOL", "WCa", "WCb")
            wsts = (self.st_wa, self.st_wb)
            wlds = [0, 0]
            self.load_wcol2(self.stgw["b_wqk" + str(l)], 0,
                            self.WCOL[:, 0:1024], wsts[0])
            wlds[0] = self.load_wcol2(self.stgw["b_wqk" + str(l)], 4,
                                      self.WCOL[:, 1024:2048], wsts[0])
            for hp in range(4):
                buf = hp % 2
                wq_c = self.WCOL[:, buf * 2048: buf * 2048 + 1024]
                wk_c = self.WCOL[:, buf * 2048 + 1024: buf * 2048 + 2048]
                if hp + 1 < 4:
                    nb = 1 - buf
                    self.use(self.D, "WC" + "ab"[nb])
                    self.load_wcol2(self.stgw["b_wqk" + str(l)], hp + 1,
                                    self.WCOL[:, nb * 2048: nb * 2048 + 1024],
                                    wsts[nb])
                    wlds[nb] = self.load_wcol2(
                        self.stgw["b_wqk" + str(l)], 5 + hp,
                        self.WCOL[:, nb * 2048 + 1024: nb * 2048 + 2048],
                        wsts[nb])
                wld = wlds[buf]
                vv = nc.vector
                for (wc, offd, colc, cols, tabC, tabS) in (
                        (wq_c, off_q, qs_c, qs_s, 0, 1),
                        (wk_c, off_k, ks_c, ks_s, 2, 3)):
                    for tt in range(4):
                        lo, hi = tt * 512, (tt + 1) * 512
                        pst = (self.P0, self.P1)[tt % 2]
                        self.use(self.T, "P" + str(tt % 2))
                        self.swait(self.T, wsts[buf], wld)
                        tn = 0
                        for cc in range(NCC):
                            mm = nc.tensor.matmul(
                                pst[:], wc[:, cc * P:(cc + 1) * P],
                                xsh(cc, lo, hi),
                                start=(cc == 0), stop=(cc == NCC - 1))
                            if cc == NCC - 1:
                                tn = self.T.inc(mm)
                        self.S.wait(self.T, tn)
                        self.use(self.S, "X2" + str(tt % 2))
                        x2h = self.X2[:, (tt % 2) * 512:(tt % 2) * 512 + 512]
                        sn = self.S.inc(nc.scalar.activation(
                            out=x2h, in_=pst[:], func=AF.Square))
                        self.use(self.S, "SCRq" if offd == off_q else "SCRk")
                        sn2 = self.S.inc(nc.scalar.copy(
                            out=self.scr(offd, lo, hi), in_=pst[:]))
                        self.done("P" + str(tt % 2), self.S, sn2)
                        self.T.wait(self.S, sn)
                        self.use(self.T, "P6")
                        tn = self.T.inc(nc.tensor.matmul(
                            self.P6[0:2, :], self.BLK2[:, 0:2], x2h,
                            start=True, stop=True))
                        self.done("X2" + str(tt % 2), self.T, tn)
                        self.V.wait(self.T, tn)
                        self.use(self.V, "RW6")
                        r2 = self.P6[0:2, :]
                        vn = self.V.inc(nc.vector.tensor_scalar(
                            self.TMP[0:2, 0:512], r2, 1e-24, None, ALU.max))
                        self.S.wait(self.V, vn)
                        nc.scalar.activation(out=self.TMP[0:2, 0:512],
                                             in_=self.TMP[0:2, 0:512], func=AF.Ln)
                        self.S.drain()
                        sn = self.S.inc(nc.scalar.activation(
                            out=self.TMP[0:2, 0:512], in_=self.TMP[0:2, 0:512],
                            func=AF.Exp, scale=-0.5))
                        self.T.wait(self.S, sn)
                        self.use(self.T, "P2")
                        t_bc = self.T.inc(nc.tensor.matmul(
                            self.P2[:], self.BLK2T[0:2, :],
                            self.TMP[0:2, 0:512], start=True, stop=True))
                        self.done("RW6", self.T, t_bc)
                        self.V.wait(self.T, t_bc)
                        vn = self.V.inc(vv.tensor_tensor(
                            self.scr(offd, lo, hi), self.scr(offd, lo, hi),
                            self.P2[:], ALU.mult))
                        self.done("P6", self.V, vn)
                        self.done("P2", self.V, vn)
                    self.V.wait(self.V, tab_v)
                    self.use(self.V, "SCRsw")
                    for (d0, s0) in ((0, 32), (32, 0), (64, 96), (96, 64)):
                        vv.tensor_copy(out=self.scr(off_sw, 0, T, d0, d0 + 32),
                                       in_=self.scr(offd, 0, T, s0, s0 + 32))
                    vv.drain()
                    vv.tensor_tensor(self.scr(offd, 0, T), self.scr(offd, 0, T),
                                     self.scr(off_tab + tabC * T, 0, T),
                                     ALU.mult)
                    vv.tensor_tensor(self.scr(off_sw, 0, T),
                                     self.scr(off_sw, 0, T),
                                     self.scr(off_tab + tabS * T, 0, T),
                                     ALU.mult)
                    vv.drain()
                    vn = self.V.inc(vv.tensor_tensor(
                        self.scr(offd, 0, T), self.scr(offd, 0, T),
                        self.scr(off_sw, 0, T), ALU.add))
                    vv.drain()
                    self.done("SCRsw", self.V, vn)
                rope_v = self.V.n
                self.done("WC" + "ab"[buf], self.V, rope_v)

                for h2 in range(2):
                    h = hp * 2 + h2
                    pbase = h2 * 64
                    ps_o = (self.P4, self.P5)[h2]
                    for qt in range(4):
                        kcs = [kc for kc in range(max(0, 4 * (qt - 1)), 4 * qt + 4)]
                        sn_by = {}
                        vm_by = {}
                        for ki, kc in enumerate(kcs):
                            dlt = 128 * kc - 512 * qt
                            lo = max(0, dlt)
                            hi = min(512, dlt + 640)
                            pss = (self.P0, self.P1)[ki % 2]
                            self.use(self.T, "P" + str(ki % 2))
                            self.T.wait(self.V, rope_v)
                            tn = self.T.inc(nc.tensor.matmul(
                                pss[:, lo:hi],
                                self.scr(off_k, kc * P, (kc + 1) * P, pbase, pbase + 64),
                                self.scr(off_q, qt * 512 + lo, qt * 512 + hi,
                                         pbase, pbase + 64),
                                start=True, stop=True))
                            self.S.wait(self.T, tn)
                            sn = self.S.inc(nc.scalar.activation(
                                out=self.scr(off_pt, ki * 512 + lo, ki * 512 + hi),
                                in_=pss[:, lo:hi], func=AF.Exp, scale=8.0))
                            self.done("P" + str(ki % 2), self.S, sn)
                            self.V.wait(self.S, sn)
                            vv = nc.vector
                            if dlt >= 0:
                                vm_by[ki] = self.V.inc(vv.tensor_tensor(
                                    self.scr(off_pt, ki * 512 + dlt, ki * 512 + dlt + P),
                                    self.scr(off_pt, ki * 512 + dlt, ki * 512 + dlt + P),
                                    self.MSK[:, 0:P], ALU.mult))
                            else:
                                d2 = dlt + 512
                                vm_by[ki] = self.V.inc(vv.tensor_tensor(
                                    self.scr(off_pt, ki * 512 + d2, ki * 512 + d2 + P),
                                    self.scr(off_pt, ki * 512 + d2, ki * 512 + d2 + P),
                                    self.MSK[:, P:2 * P], ALU.mult))
                            sn_by[ki] = sn
                        self.use(self.T, ("P4", "P5")[h2])
                        if hp == 0 and h2 == 0 and qt == 0:
                            self.T.wait(self.V, v_ready)
                        t_pv = 0
                        kio = sorted(range(len(kcs)),
                                     key=lambda ki_: (not (
                                         max(0, 128 * kcs[ki_] - 512 * qt) == 0
                                         and min(512, 128 * kcs[ki_] - 512 * qt + 640) == 512), ki_))
                        for oi, ki in enumerate(kio):
                            kc = kcs[ki]
                            dlt = 128 * kc - 512 * qt
                            lo = max(0, dlt)
                            hi = min(512, dlt + 640)
                            slot = off_v + kc * 520 + h * 65
                            self.T.wait(self.S, sn_by[ki])
                            self.T.wait(self.V, vm_by[ki])
                            mm = nc.tensor.matmul(
                                ps_o[0:65, lo:hi], self.SCR[:, slot:slot + 65],
                                self.scr(off_pt, ki * 512 + lo, ki * 512 + hi),
                                start=(oi == 0), stop=(oi == len(kcs) - 1),
                                skip_group_check=True)
                            if oi == len(kcs) - 1:
                                t_pv = self.T.inc(mm)
                        self.done("SCRpt", self.T, t_pv)
                        self.V.wait(self.T, t_pv)
                        rl_row = self.RW[0:1, 2560:3072]
                        self.use(self.V, "RW6")
                        vn = self.V.inc(nc.vector.reciprocal(rl_row, ps_o[64:65, :]))
                        self.use(self.T, "P6")
                        self.T.wait(self.V, vn)
                        t_bc = self.T.inc(nc.tensor.matmul(
                            self.P6[0:64, :], self.RW[0:1, 0:64], rl_row,
                            start=True, stop=True))
                        self.done("RW6", self.T, t_bc)
                        self.use(self.S, "SCRt2")
                        self.S.wait(self.T, t_bc)
                        sn = self.S.inc(nc.scalar.copy(
                            out=self.scr(off_t2, 0, 512, 0, 64), in_=ps_o[0:64, :]))
                        self.done(("P4", "P5")[h2], self.S, sn)
                        self.V.wait(self.S, sn)
                        self.use(self.V, "SCRao")
                        vn = self.V.inc(nc.vector.tensor_tensor(
                            self.scr(off_ao, qt * 512, (qt + 1) * 512, pbase, pbase + 64),
                            self.scr(off_t2, 0, 512, 0, 64), self.P6[0:64, :], ALU.mult))
                        self.done("P6", self.V, vn)
                        self.done("SCRt2", self.V, vn)
                self.D.wait(self.V, self.V.n)
                dn = self.st_ao.dma(attb[b, hp], self.scr(off_ao, 0, T))
                self.done("SCRao", self.st_ao, dn)
                self.done("SCRq", self.st_ao, dn)
                self.done("SCRk", self.st_ao, dn)

            self.done("WCOL", self.V)
            self.swait(self.D, self.st_ao)
            self.use(self.D, "SCRal")
            self.D.wait(self.T, self.T.n)
            al = 0
            for cc in range(4):
                al = self.st_al.dma(
                    self.SCR[:, off_al + cc * T: off_al + (cc + 1) * T], attb[b, cc])
            self.use(self.D, "WCOL", "WCa", "WCb")
            wsts = (self.st_wa, self.st_wb)
            wlds = [0, 0]
            wlds[0] = self.load_wcol2(self.stgw["b_wo" + str(l)], 0,
                                      self.WCOL[:, 0:512], wsts[0])
            for oc in range(NCC):
                buf = oc % 2
                wsl = self.WCOL[:, buf * 512: buf * 512 + 512]
                if oc + 1 < NCC:
                    nb = 1 - buf
                    self.use(self.D, "WC" + "ab"[nb])
                    wlds[nb] = self.load_wcol2(
                        self.stgw["b_wo" + str(l)], oc + 1,
                        self.WCOL[:, nb * 512: nb * 512 + 512], wsts[nb])
                wld = wlds[buf]
                for tt in range(4):
                    pst = (self.P0, self.P1)[tt % 2]
                    self.use(self.T, "P" + str(tt % 2))
                    self.swait(self.T, wsts[buf], wld)
                    self.swait(self.T, self.st_al, al)
                    tn = 0
                    for cc in range(4):
                        mm = nc.tensor.matmul(
                            pst[:], wsl[:, cc * P:(cc + 1) * P],
                            self.SCR[:, off_al + cc * T + tt * 512:
                                     off_al + cc * T + (tt + 1) * 512],
                            start=(cc == 0), stop=(cc == 3))
                        if cc == 3:
                            tn = self.T.inc(mm)
                    self.V.wait(self.T, tn)
                    vn = self.V.inc(nc.vector.tensor_tensor(
                        self.xmid(oc, tt * 512, (tt + 1) * 512), pst[:],
                        self.xmid(oc, tt * 512, (tt + 1) * 512), ALU.add))
                    self.done("P" + str(tt % 2), self.V, vn)
                self.done("WC" + "ab"[buf], self.V)
            self.done("WCOL", self.V)
            self.done("SCRal", self.V)
            self.done("SCRv", self.V)
            self.dump_dbg("xa%d" % l, b)

            self.use(self.D, "BCOLg")
            for cc in range(NCC):
                self.ss_bcol.dma(self.BCOL[:, 8 + cc:9 + cc],
                                 ln_g[l, cc * P:(cc + 1) * P].unsqueeze(1))
                self.ss_bcol.dma(self.BCOL2[:, cc:cc + 1],
                                 ln_b[l, cc * P:(cc + 1) * P].unsqueeze(1))
            g_d = self.ss_bcol.n
            self.swait(self.S, self.ss_bcol, g_d)
            self.barrier()
            self.emit_ln(srcf=xshm, dstf=None,
                         dstb=self.xn,
                         gains=(lambda cc: self.BCOL[:, 8 + cc:9 + cc],
                                lambda cc: self.BCOL2[:, cc:cc + 1]))
            self.done("BCOLg", self.S)
            self.barrier()
            self.dump_dbg("hn%d" % l, b, src="xn")

            b1a_c = self.BCOL[:, 0:1]
            b1g_c = self.BCOL[:, 1:2]
            b2_c = self.BCOL[:, 2:3]
            for tt in range(4):
                lo, hi = tt * 512, (tt + 1) * 512
                self.use(self.D, "WCOL", "WCa", "WCb")
                wsts = (self.st_wa, self.st_wb)
                wlds = [0, 0]
                self.load_wcol2(self.stgw["b_w1a" + str(l)], 0,
                                self.WCOL[:, 0:1024], wsts[0])
                wlds[0] = self.load_wcol2(self.stgw["b_w1g" + str(l)], 0,
                                          self.WCOL[:, 1024:2048], wsts[0])
                for j in range(NJ):
                    w = min(P, HID - j * P)
                    buf = j % 2
                    wa = self.WCOL[:, buf * 2048: buf * 2048 + 1024]
                    wg = self.WCOL[:, buf * 2048 + 1024: buf * 2048 + 2048]
                    if j + 1 < NJ:
                        nb = 1 - buf
                        self.use(self.D, "WC" + "ab"[nb])
                        self.load_wcol2(self.stgw["b_w1a" + str(l)], j + 1,
                                        self.WCOL[:, nb * 2048: nb * 2048 + 1024],
                                        wsts[nb])
                        wlds[nb] = self.load_wcol2(
                            self.stgw["b_w1g" + str(l)], j + 1,
                            self.WCOL[:, nb * 2048 + 1024: nb * 2048 + 2048],
                            wsts[nb])
                    wld = wlds[buf]
                    self.use(self.D, "BCOL01")
                    self.ss_bcol.dma(b1a_c[0:w, :], b1[l, j * P: j * P + w].unsqueeze(1))
                    self.ss_bcol.dma(b1g_c[0:w, :],
                                     b1[l, HID + j * P: HID + j * P + w].unsqueeze(1))
                    bd = self.ss_bcol.n
                    for (wc, bcol, half) in ((wa, b1a_c, 0), (wg, b1g_c, 1)):
                        pst = (self.P0, self.P1)[half]
                        self.use(self.T, "P" + str(half))
                        self.swait(self.T, wsts[buf], wld)
                        tn = 0
                        for cc in range(NCC):
                            mm = nc.tensor.matmul(
                                pst[0:w, :], wc[:, cc * P: cc * P + w],
                                self.xn(cc, lo, hi),
                                start=(cc == 0), stop=(cc == NCC - 1))
                            if cc == NCC - 1:
                                tn = self.T.inc(mm)
                        self.S.wait(self.T, tn)
                        self.swait(self.S, self.ss_bcol, bd)
                        self.use(self.S, "X2" + str(half))
                        sn = self.S.inc(nc.scalar.activation(
                            out=self.X2[0:w, half * 512:half * 512 + 512],
                            in_=pst[0:w, :],
                            func=(AF.Identity if half == 0 else AF.Gelu),
                            bias=bcol[0:w, :], scale=1.0))
                        self.done("P" + str(half), self.S, sn)
                    s_ag = self.S.n
                    self.done("WC" + "ab"[buf], self.S, s_ag)
                    self.done("BCOL01", self.S, s_ag)
                    self.V.wait(self.S, s_ag)
                    self.use(self.V, "SCRhf")
                    vn = self.V.inc(nc.vector.tensor_tensor(
                        self.scr(off_hf, j * 512, j * 512 + 512, 0, w),
                        self.X2[0:w, 0:512], self.X2[0:w, 512:1024], ALU.mult))
                    self.done("X20", self.V, vn)
                    self.done("X21", self.V, vn)
                hf_v = self.V.n
                HW2 = 11 * P
                self.use(self.D, "WCOL", "WCa", "WCb")
                wsts = (self.st_wa, self.st_wb)
                wlds = [0, 0]
                wlds[0] = self.load_wcol2_part(
                    self.stgw["b_w2" + str(l)], 0, 0, HW2,
                    self.WCOL[:, 0:HW2], wsts[0])
                for oc in range(NCC):
                    self.use(self.D, "BCOL2")
                    self.ss_b2.dma(b2_c, b2[l, oc * P:(oc + 1) * P].unsqueeze(1))
                    b2d = self.ss_b2.n
                    pst = (self.P0, self.P1)[oc % 2]
                    self.use(self.T, "P" + str(oc % 2))
                    self.T.wait(self.V, hf_v)
                    tn = 0
                    for hv in range(2):
                        idx = oc * 2 + hv
                        buf = idx % 2
                        if idx + 1 < 16:
                            nb = 1 - buf
                            noc, nhv = divmod(idx + 1, 2)
                            self.use(self.D, "WC" + "ab"[nb])
                            wlds[nb] = self.load_wcol2_part(
                                self.stgw["b_w2" + str(l)], noc, nhv * HW2, HW2,
                                self.WCOL[:, nb * HW2: nb * HW2 + HW2], wsts[nb])
                        wsl = self.WCOL[:, buf * HW2: buf * HW2 + HW2]
                        self.swait(self.T, wsts[buf], wlds[buf])
                        tg = 0
                        for jl in range(11):
                            j = hv * 11 + jl
                            w = min(P, HID - j * P)
                            mm = nc.tensor.matmul(
                                pst[:], wsl[0:w, jl * P:(jl + 1) * P],
                                self.scr(off_hf, j * 512, j * 512 + 512, 0, w),
                                start=(j == 0), stop=(j == NJ - 1))
                            if jl == 10:
                                tg = self.T.inc(mm)
                        self.done("WC" + "ab"[buf], self.T, tg)
                        if hv == 1:
                            tn = tg
                    self.V.wait(self.T, tn)
                    self.swait(self.V, self.ss_b2, b2d)
                    vn = self.V.inc(nc.vector.scalar_tensor_tensor(
                        out=self.xmid(oc, lo, hi), in0=pst[:], scalar=b2_c,
                        in1=self.xmid(oc, lo, hi), op0=ALU.add, op1=ALU.add))
                    self.done("P" + str(oc % 2), self.V, vn)
                    self.done("BCOL2", self.V)
                self.done("WCOL", self.T)
                self.done("SCRhf", self.T)
            self.barrier()
            vn = 0
            for cc in range(NCC):
                vn = self.V.inc(nc.vector.tensor_copy(out=self.xn(cc), in_=self.xmid(cc)))
            self.done("XN", self.V, vn)
            self.done("SCRtab", self.V, vn)
            self.done("BCOL45", self.V, vn)
            self.barrier()
            self.dump_dbg("x%d" % l, b)
        return

    def emit_l1_ffn_prep(self, fc_logits, pj_logits, fc_s1, fc_s2, pj_s1,
                         pj_s2, fc_mix, pj_mix, iota8, ssel, mixt, mixtp):
        nc = self.nc
        import concourse.mybir as mybir_
        r = lambda i, w=8: self.TMP[0:1, 64 + i * 16: 64 + i * 16 + w]
        self.barrier()
        fl_i32 = self.sbuf("FLI", [1, 320], I32)
        self.FLI = fl_i32
        pos = 0
        self.swait(self.D, self.st_setup)
        iot = self.TMP[0:1, 48:56]
        self.st_tmp.dma(iot, iota8)
        eqrows = {}
        for (lg, tens) in (("fc", fc_logits), ("pj", pj_logits)):
            lrow = r(0)
            self.D.wait(self.V, self.V.n)
            self.D.wait(self.T, self.T.n)
            self.st_tmp.dma(lrow, tens.unsqueeze(0))
            self.swait(self.V, self.st_tmp)
            v = nc.vector
            m1 = r(1, 1)
            v.tensor_reduce(m1, lrow, mybir_.AxisListType.X, ALU.max)
            v.drain()
            eq1 = r(2) if lg == "fc" else r(16)
            v.tensor_scalar(eq1, lrow, m1, None, ALU.is_equal)
            v.drain()
            msk = r(3)
            v.scalar_tensor_tensor(msk, eq1, -1e30, lrow, ALU.mult, ALU.add)
            v.drain()
            m2 = r(4, 1)
            v.tensor_reduce(m2, msk, mybir_.AxisListType.X, ALU.max)
            v.drain()
            eq2 = r(5) if lg == "fc" else r(17)
            v.tensor_scalar(eq2, msk, m2, None, ALU.is_equal)
            v.drain()
            eqrows[lg] = (eq1, eq2)
            d_ = r(6, 1)
            vn = self.V.inc(v.tensor_tensor(d_, m2, m1, ALU.subtract))
            v.drain()
            self.S.wait(self.V, vn)
            e_ = r(8, 1)
            sn = self.S.inc(nc.scalar.activation(out=e_, in_=d_, func=AF.Exp))
            self.S.drain()
            self.V.wait(self.S, sn)
            se = r(9, 1)
            v.tensor_scalar(se, e_, 1.0, None, ALU.add)
            v.drain()
            c1_ = r(10, 1)
            v.reciprocal(c1_, se)
            v.drain()
            c2_ = r(11, 1)
            vn = self.V.inc(v.tensor_scalar(c2_, c1_, -1.0, 1.0, ALU.mult, ALU.add))
            v.drain()
            self.T.wait(self.V, vn)
            self.use(self.T, "P2")
            col0 = 8 if lg == "fc" else 10
            nc.tensor.matmul(self.P2[:, 0:1], self.RW[0:1, 0:P], c1_,
                             start=True, stop=True)
            tn = self.T.inc(nc.tensor.matmul(self.P2[:, 1:2], self.RW[0:1, 0:P],
                                             c2_, start=True, stop=True))
            self.V.wait(self.T, tn)
            v.tensor_copy(out=self.BCOL2[:, col0:col0 + 1], in_=self.P2[:, 0:1])
            v.drain()
            vn = self.V.inc(v.tensor_copy(out=self.BCOL2[:, col0 + 1:col0 + 2],
                                          in_=self.P2[:, 1:2]))
            self.done("P2", self.V, vn)
            pos += 160
        for ti, (lg, ki) in enumerate((("fc", 0), ("fc", 1), ("pj", 0), ("pj", 1))):
            eq = eqrows[lg][ki]
            for sh in range(8):
                self.use(self.T, "P2")
                tn = self.T.inc(nc.tensor.matmul(
                    self.P2[:, 2:3], self.RW[0:1, 0:P], eq[0:1, sh:sh + 1],
                    start=True, stop=True))
                self.V.wait(self.T, tn)
                vn = self.V.inc(nc.vector.tensor_copy(
                    out=self.BCOL2[:, 12 + ti * 8 + sh: 13 + ti * 8 + sh],
                    in_=self.P2[:, 2:3]))
                nc.vector.drain()
                self.done("P2", self.V, vn)
        for ti in range(4):
            shc = self.BCOL2[:, 44 + ti: 45 + ti]
            nc.vector.tensor_scalar(
                shc, self.BCOL2[:, 13 + ti * 8: 14 + ti * 8], 1.0, None,
                ALU.mult)
            nc.vector.drain()
            for bb in range(2, 8):
                op = nc.vector.scalar_tensor_tensor(
                    shc, self.BCOL2[:, 12 + ti * 8 + bb: 13 + ti * 8 + bb],
                    float(bb), shc, ALU.mult, ALU.add)
                if bb == 7:
                    self.V.inc(op)
                nc.vector.drain()
        self.barrier()

        pcol_h = self.din("hc_pcols", [P, 8])
        ocol_h = self.din("hc_ocols", [P, 64])
        self.IDX = self.sbuf("IDXS", [P, 96], I32)
        self.IDI = self.sbuf("IDI", [P, 32], I32)
        self.PCt = self.sbuf("PCt", [P, 8], F32)
        self.OCt = self.sbuf("OCt", [P, 64], F32)
        self.st_setup.dma(self.PCt[:], pcol_h)
        self.st_setup.dma(self.OCt[:], ocol_h)
        self.swait(self.V, self.st_setup)
        v = nc.vector
        iota = self.PCt[:, 0:1]
        fcs = self.TMP[:, 512:513]
        for (ci, ti, R) in ((0, 0, 1024), (1, 1, 1024), (2, 0, 4096),
                            (3, 1, 4096), (4, 2, 4096), (5, 3, 4096),
                            (6, 2, 1024), (7, 3, 1024)):
            shc = self.BCOL2[:, 44 + ti: 45 + ti]
            v.tensor_scalar(fcs, shc, float(R), None, ALU.mult)
            v.drain()
            v.tensor_tensor(fcs, fcs, iota, ALU.add)
            v.drain()
            self.V.inc(v.tensor_copy(out=self.IDX[:, ci:ci + 1], in_=fcs))
            v.drain()
        for (ti_base, n, ocn, OCd, cb, kk, pmc, pdc) in (
                (0, 16, 8, 8, 8, 8.0, 2, 3), (2, 64, 32, 32, 24, 2.0, 4, 1)):
            for ki in range(2):
                ti = ti_base + ki
                shc = self.BCOL2[:, 44 + ti: 45 + ti]
                Ac = self.TMP[:, 512:513]
                Bc = self.TMP[:, 513:514]
                pmod = self.PCt[:, pmc:pmc + 1]
                pdiv = self.PCt[:, pdc:pdc + 1]
                v.tensor_scalar(Ac, pmod, shc, float(n), ALU.subtract,
                                ALU.add)
                v.drain()
                v.tensor_scalar(Bc, Ac, 1.0 / n, -(n - 1) / (2.0 * n),
                                ALU.mult, ALU.add)
                v.drain()
                v.tensor_copy(out=self.IDI[:, 0:1], in_=Bc)
                v.drain()
                v.tensor_copy(out=Bc, in_=self.IDI[:, 0:1])
                v.drain()
                v.scalar_tensor_tensor(Ac, Bc, -float(n), Ac, ALU.mult,
                                       ALU.add)
                v.drain()
                v.tensor_scalar(Ac, Ac, 64.0, None, ALU.mult)
                v.drain()
                v.tensor_scalar(Ac, Ac, pdiv, None, ALU.add)
                v.drain()
                A = self.TMP[:, 576:576 + ocn]
                B = self.TMP[:, 640:640 + ocn]
                v.tensor_scalar(A, self.OCt[:, 0:ocn], kk / 2.0, None,
                                ALU.mult)
                v.drain()
                v.tensor_scalar(A, A, Ac, None, ALU.add)
                v.drain()
                v.tensor_scalar(B, A, 1.0 / 128, -127.0 / 256.0,
                                ALU.mult, ALU.add)
                v.drain()
                v.tensor_copy(out=self.IDI[:, 0:ocn], in_=B)
                v.drain()
                v.tensor_copy(out=B, in_=self.IDI[:, 0:ocn])
                v.drain()
                v.scalar_tensor_tensor(A, B, -128.0, A, ALU.mult, ALU.add)
                v.drain()
                v.tensor_scalar(A, A, float(OCd), None, ALU.mult)
                v.drain()
                v.tensor_tensor(A, A, B, ALU.add)
                v.drain()
                self.V.inc(v.tensor_copy(out=self.IDX[:, cb + ki * ocn:
                                                      cb + ki * ocn + ocn],
                                         in_=A))
                v.drain()
        self.barrier()

        IOA = lambda col: bass.IndirectOffsetOnAxis(ap=col, axis=0)
        sgq = self.sg_w
        for (key, tens, R, ti_base, scaled, colR) in (
                ("s1f", fc_s1, 1024, 0, False, 0),
                ("s2f", fc_s2, 4096, 0, True, 2),
                ("s1p", pj_s1, 4096, 2, False, 4),
                ("s2p", pj_s2, 1024, 2, True, 6)):
            srcv = APc(tens.tensor, tens.offset, [[64, 8 * R], [1, 64]])
            for ki in range(2):
                icol = self.IDX[:, colR + ki: colR + ki + 1]
                ccol = self.BCOL2[:, (8 if ti_base == 0 else 10) + ki:
                                  (9 if ti_base == 0 else 11) + ki]
                for cc in range(R // P):
                    buf = cc % 2
                    xsl = self.XMID[:, buf * 64: buf * 64 + 64]
                    bsl = self.SCR[:, buf * 64: buf * 64 + 64]
                    self.use(self.G, "PXA" + str(buf))
                    g = nc.gpsimd.indirect_dma_start(
                        out=xsl, out_offset=None, in_=srcv,
                        in_offset=IOA(icol), element_offset=cc * P * 64)
                    g.then_inc(sgq.sem, 16)
                    sgq.n += 16
                    self.swait(self.V, sgq)
                    self.use(self.V, "PSB" + str(buf))
                    if scaled:
                        vn = self.V.inc(v.tensor_scalar(bsl, xsl, ccol, None,
                                                        ALU.mult))
                    else:
                        vn = self.V.inc(v.tensor_copy(out=bsl, in_=xsl))
                    self.done("PXA" + str(buf), self.V, vn)
                    self.D.wait(self.V, vn)
                    st = self.st_x.dma(ssel[key][ki, cc * P:(cc + 1) * P, :],
                                       bsl)
                    self.done("PSB" + str(buf), self.st_x, st)
        self.barrier()

        for (lg, mix, NR, W, grp, OCs, scol, cb) in (
                ("fc", fc_mix, 4096, 1024, 4, 8, 2, 8),
                ("pj", pj_mix, 1024, 4096, 1, 32, 6, 24)):
            srcv = APc(mix.tensor, mix.offset, [[W, 8 * NR], [1, W]])
            rowW = grp * P
            ngrp = NR // P // grp
            for ki in range(2):
                mtp = mixtp[lg + str(ki)]
                nrows = NR * W // rowW
                outv = APc(mtp, 0, [[rowW, nrows], [1, rowW]])
                gi = self.IDX[:, scol + ki: scol + ki + 1]

                def _slot(mg_, u_):
                    return (mg_ % 2) * grp + u_

                def _gather(mg_):
                    if mg_ >= ngrp:
                        return
                    for u in range(grp):
                        mcc = mg_ * grp + u
                        slot = _slot(mg_, u)
                        xsl = self.XMID[:, slot * W: (slot + 1) * W]
                        self.use(self.G, "MXA%d" % slot)
                        g = nc.gpsimd.indirect_dma_start(
                            out=xsl, out_offset=None, in_=srcv,
                            in_offset=IOA(gi), element_offset=mcc * P * W)
                        g.then_inc(sgq.sem, 16)
                        sgq.n += 16

                _gather(0)
                gn_prev = sgq.n
                for mg in range(ngrp):
                    _gather(mg + 1)
                    cvn = 0
                    for u in range(grp):
                        slot = _slot(mg, u)
                        xsl = self.XMID[:, slot * W: (slot + 1) * W]
                        csl = self.SCR[:, slot * W: (slot + 1) * W]
                        self.swait(self.V, sgq, gn_prev)
                        self.use(self.V, "MCB%d" % slot)
                        cvn = self.V.inc(v.tensor_copy(out=csl, in_=xsl))
                        self.done("MXA%d" % slot, self.V, cvn)
                    gn_prev = sgq.n
                    for o_ in range(OCs):
                        ps = (self.P2, self.P3)[o_ % 2]
                        self.use(self.T, "P" + str(2 + o_ % 2))
                        self.T.wait(self.V, cvn)
                        tn = 0
                        for u in range(grp):
                            slot = _slot(mg, u)
                            csl = self.SCR[:, slot * W: (slot + 1) * W]
                            mm = nc.tensor.matmul(
                                ps[:, u * P:(u + 1) * P],
                                csl[:, o_ * P:(o_ + 1) * P], self.IDNB[:],
                                start=True, stop=True)
                            if u == grp - 1:
                                tn = self.T.inc(mm)
                        self.V.wait(self.T, tn)
                        sslot = o_ % 4
                        stg = self.SCR[:, 16384 + sslot * 512:
                                       16384 + sslot * 512 + rowW]
                        self.use(self.V, "MST%d" % sslot)
                        svn = self.V.inc(v.tensor_copy(out=stg,
                                                       in_=ps[:, 0:rowW]))
                        self.done("P" + str(2 + o_ % 2), self.V, svn)
                        self.G.wait(self.V, svn)
                        sc = nc.gpsimd.indirect_dma_start(
                            out=outv,
                            out_offset=IOA(self.IDX[:, cb + ki * OCs + o_:
                                                    cb + ki * OCs + o_ + 1]),
                            in_=stg, in_offset=None,
                            element_offset=mg * P * OCs * rowW)
                        sc.then_inc(sgq.sem, 16)
                        sgq.n += 16
                        self.done("MST%d" % sslot, sgq)
                    for u in range(grp):
                        slot = _slot(mg, u)
                        self.done("MCB%d" % slot, self.T, tn)
        self.barrier()
        self.V.inc(nc.vector.memset(self.XMID[:], 0.0))
        self.barrier()
        return

    def load_flags(self, tag):
        import concourse.mybir as mybir__
        gidx = 0 if tag.startswith("fc") else 1
        ki = int(tag[-1])
        base = gidx * 160
        vals = [self.nc.values_load(
            self.FLI[0:1, base + ki * 8 + bb: base + ki * 8 + bb + 1],
            min_val=0, max_val=1, skip_runtime_bounds_check=True)
            for bb in range(8)]
        return vals

    def load_idx(self, tag):
        gidx = 0 if tag.startswith("fc") else 1
        ki = int(tag[-1])
        base = gidx * 160
        return self.nc.values_load(self.FLI[0:1, base + 16 + ki: base + 17 + ki],
                                   min_val=0, max_val=7,
                                   skip_runtime_bounds_check=True)

    def free_flags(self, vals):
        for v in vals:
            for h in v.val.handles:
                en = str(h.engine).split(".")[-1]
                eng = {"Pool": self.nc.gpsimd, "SP": self.nc.sync,
                       "PE": self.nc.tensor, "DVE": self.nc.vector,
                       "Activation": self.nc.scalar}.get(en)
                if eng is None:
                    continue
                try:
                    eng.free_register(h)
                except Exception:
                    pass

    def emit_rms(self, eps=1e-6):
        nc = self.nc
        ps_sq = self.P4[64:65, :]
        r_row = self.row(2)
        for tt in range(T // 512):
            lo, hi = tt * 512, (tt + 1) * 512
            self.use(self.T, "P4")
            for cc in range(NCC):
                xh = self.X2[:, (cc % 2) * 512:(cc % 2) * 512 + 512]
                self.use(self.S, "X2" + str(cc % 2))
                sn = self.S.inc(nc.scalar.activation(
                    out=xh, in_=self.xmid(cc, lo, hi), func=AF.Square))
                self.T.wait(self.S, sn)
                tn = self.T.inc(nc.tensor.matmul(
                    ps_sq, self.OCOLB[:, 0:1], xh,
                    start=(cc == 0), stop=(cc == NCC - 1)))
                self.done("X2" + str(cc % 2), self.T, tn)
            self.V.wait(self.T, self.T.n)
            self.use(self.V, "LNrows")
            v = nc.vector
            vn = self.V.inc(v.tensor_scalar(r_row, ps_sq, 1.0 / D, float(eps),
                                            ALU.mult, ALU.add))
            self.done("P4", self.V, vn)
            self.S.wait(self.V, vn)
            nc.scalar.activation(out=r_row, in_=r_row, func=AF.Ln)
            self.S.drain()
            sn = self.S.inc(nc.scalar.activation(out=r_row, in_=r_row,
                                                 func=AF.Exp, scale=-0.5))
            self.use(self.T, "P0")
            self.T.wait(self.S, sn)
            t_bc = self.T.inc(nc.tensor.matmul(
                self.P0[:], self.RW[0:1, 0:P], r_row, start=True, stop=True))
            self.done("LNrows", self.T, t_bc)
            self.V.wait(self.T, t_bc)
            vn = 0
            for cc in range(NCC):
                vn = self.V.inc(v.tensor_tensor(
                    self.xn(cc, lo, hi), self.xmid(cc, lo, hi), self.P0[:],
                    ALU.mult))
            self.done("P0", self.V, vn)
        self.done("XN", self.V)
        return

    def _l1f_normrope(self, offd, gcol, eps=1e-6):
        nc = self.nc
        v = nc.vector
        off_sw = self._l1f_offsw
        for tt in range(4):
            lo, hi = tt * 512, (tt + 1) * 512
            self.use(self.S, "X20")
            sn = self.S.inc(nc.scalar.activation(
                out=self.X2[:, 0:512], in_=self.scr(offd, lo, hi), func=AF.Square))
            self.S.drain()
            self.T.wait(self.S, sn)
            self.use(self.T, "P6")
            tn = self.T.inc(nc.tensor.matmul(
                self.P6[0:2, :], self.BLK2[:, 0:2], self.X2[:, 0:512],
                start=True, stop=True))
            self.done("X20", self.T, tn)
            self.V.wait(self.T, tn)
            self.use(self.V, "RW6")
            vn = self.V.inc(v.tensor_scalar(self.TMP[0:2, 0:512], self.P6[0:2, :],
                                            1.0 / 64, float(eps), ALU.mult,
                                            ALU.add))
            self.S.wait(self.V, vn)
            nc.scalar.activation(out=self.TMP[0:2, 0:512], in_=self.TMP[0:2, 0:512],
                                 func=AF.Ln)
            self.S.drain()
            sn = self.S.inc(nc.scalar.activation(
                out=self.TMP[0:2, 0:512], in_=self.TMP[0:2, 0:512],
                func=AF.Exp, scale=-0.5))
            self.V.wait(self.S, sn)
            vn = self.V.inc(v.tensor_scalar(self.TMP[0:2, 0:512],
                                            self.TMP[0:2, 0:512], gcol, None,
                                            ALU.mult))
            self.use(self.T, "P2")
            self.T.wait(self.V, vn)
            t_bc = self.T.inc(nc.tensor.matmul(
                self.P2[:], self.BLK2T[0:2, :], self.TMP[0:2, 0:512],
                start=True, stop=True))
            self.done("RW6", self.T, t_bc)
            self.done("P6", self.T, t_bc)
            self.V.wait(self.T, t_bc)
            vn = self.V.inc(v.tensor_tensor(
                self.scr(offd, lo, hi), self.scr(offd, lo, hi),
                self.P2[:], ALU.mult))
            self.done("P2", self.V, vn)
        self.swait(self.V, self.ss_tab, getattr(self, "tab_d", 0))
        self.use(self.V, "SCRsw2")
        for (d0, s0) in ((0, 32), (32, 0), (64, 96), (96, 64)):
            v.tensor_copy(out=self.scr(off_sw, 0, T, d0, d0 + 32),
                          in_=self.scr(offd, 0, T, s0, s0 + 32))
        v.drain()
        v.tensor_tensor(self.scr(offd, 0, T), self.scr(offd, 0, T),
                        self.TAB[:, 0:T], ALU.mult)
        v.tensor_tensor(self.scr(off_sw, 0, T), self.scr(off_sw, 0, T),
                        self.TAB[:, T:2 * T], ALU.mult)
        v.drain()
        vn = self.V.inc(v.tensor_tensor(
            self.scr(offd, 0, T), self.scr(offd, 0, T),
            self.scr(off_sw, 0, T), ALU.add))
        v.drain()
        self.done("SCRsw2", self.V, vn)
        return self.V.n

    def emit_l1_ffn(self, b, wq_bf, wk_bf, wv_bf, wo_bf, q_gain, k_gain,
                    ssel, mixtp, attb, hfc_d):
        nc = self.nc
        off_v = 0
        off_k = 4160
        off_kd = 8256
        off_q = 12352
        off_sw = 14400
        off_t2 = 16448
        off_pt = 18496
        off_ao = 26688
        self._l1f_offsw = off_sw
        self._l1f_offt2 = off_t2
        EXPSC = 0.125

        self.emit_rms()
        self.barrier()

        gq_c = self.BCOL[0:2, 12:13]
        for ck in range(2):
            wsl = self.WCOL[:, 0:1024]
            self.use(self.D, "WCOL")
            self.D.wait(self.T, self.T.n)
            for st_ in self.sg_band:
                self.swait(self.D, st_)
            for cc in range(NCC):
                wld = self.ss_wt.dma(
                    wsl[:, cc * P:(cc + 1) * P],
                    wk_bf[ck * P:(ck + 1) * P, cc * P:(cc + 1) * P],
                    transpose=True)
            for tt in range(4):
                pst = (self.P0, self.P1)[tt % 2]
                self.use(self.T, "P" + str(tt % 2))
                self.swait(self.T, self.ss_wt, wld)
                self.swait(self.T, self.sg_xn, self.x_g)
                tn = 0
                for cc in range(NCC):
                    mm = nc.tensor.matmul(
                        pst[:], wsl[:, cc * P:(cc + 1) * P],
                        self.xn(cc, tt * 512, (tt + 1) * 512),
                        start=(cc == 0), stop=(cc == NCC - 1))
                    if cc == NCC - 1:
                        tn = self.T.inc(mm)
                self.S.wait(self.T, tn)
                self.use(self.S, "SCRk")
                sn = self.S.inc(nc.scalar.copy(
                    out=self.scr(off_k + ck * T, tt * 512, (tt + 1) * 512),
                    in_=pst[:]))
                self.S.drain()
                self.done("P" + str(tt % 2), self.S, sn)
            self.use(self.D, "BCOLg2")
            gl = self.ss_bcol.dma(gq_c, k_gain[2 * ck: 2 * ck + 2].unsqueeze(1))
            self.swait(self.V, self.ss_bcol, gl)
            self._l1f_normrope(off_k + ck * T, gq_c)
            self.done("BCOLg2", self.V)
            self.done("SCRk", self.V)
        self.done("WCOL", self.T)
        self.G.wait(self.V, self.V.n)
        kd = 0
        for ck in range(2):
            for (d0, s0) in ((0, 64), (64, 0)):
                kd = self.sg_w.dma(
                    self.scr(off_kd + ck * T, 0, T, d0, d0 + 64),
                    self.scr(off_k + ck * T, 0, T, s0, s0 + 64))
        self.kd_n = kd
        self.use(self.V, "SCRv")
        v_ones = self.V.inc(nc.vector.memset(self.SCR[:, off_v:off_v + 4160], 1.0))
        wsl = self.WCOL[:, 1024 + 0:1024 + 2048]
        self.use(self.D, "WCOLv")
        self.D.wait(self.T, self.T.n)
        for cc in range(NCC):
            wld = self.ss_wt.dma(
                wsl[:, cc * 256:(cc + 1) * 256],
                wv_bf[0:256, cc * P:(cc + 1) * P], transpose=True)
        for tk in range(T // P):
            pst = (self.P0, self.P1)[tk % 2]
            self.use(self.T, "P" + str(tk % 2))
            self.swait(self.T, self.ss_wt, wld)
            tn = 0
            for cc in range(NCC):
                mm = nc.tensor.matmul(
                    pst[:, 0:256], self.xn(cc, tk * P, (tk + 1) * P),
                    wsl[:, cc * 256:(cc + 1) * 256],
                    start=(cc == 0), stop=(cc == NCC - 1))
                if cc == NCC - 1:
                    tn = self.T.inc(mm)
            self.V.wait(self.T, tn)
            if tk == 0:
                self.V.wait(self.V, v_ones)
            vn = 0
            for kvh in range(4):
                slot = off_v + tk * 260 + kvh * 65
                cp = nc.vector.tensor_copy(out=self.SCR[:, slot:slot + 64],
                                           in_=pst[:, kvh * 64:(kvh + 1) * 64])
                if kvh == 3:
                    vn = self.V.inc(cp)
            self.done("P" + str(tk % 2), self.V, vn)
        self.done("WCOLv", self.T)
        v_ready = self.V.n

        wq_bufs = (self.WCOL[:, 0:1024], self.WCOL[:, 3072:4096])
        wq_lds = [0, 0]

        def _load_wq(hp_):
            if hp_ >= NCC:
                return
            buf_ = hp_ % 2
            wsl_ = wq_bufs[buf_]
            self.use(self.D, "WQT" + str(buf_))
            if hp_ == 0:
                self.use(self.D, "WCOL")
            for cc in range(NCC):
                wq_lds[buf_] = self.ss_wt.dma(
                    wsl_[:, cc * P:(cc + 1) * P],
                    wq_bf[hp_ * P:(hp_ + 1) * P, cc * P:(cc + 1) * P],
                    transpose=True)

        _load_wq(0)
        for hp in range(NCC):
            buf = hp % 2
            wsl = wq_bufs[buf]
            tn = 0
            for tt in range(4):
                pst = (self.P0, self.P1)[tt % 2]
                self.use(self.T, "P" + str(tt % 2))
                self.swait(self.T, self.ss_wt, wq_lds[buf])
                for cc in range(NCC):
                    mm = nc.tensor.matmul(
                        pst[:], wsl[:, cc * P:(cc + 1) * P],
                        self.xn(cc, tt * 512, (tt + 1) * 512),
                        start=(cc == 0), stop=(cc == NCC - 1))
                    if cc == NCC - 1:
                        tn = self.T.inc(mm)
                self.S.wait(self.T, tn)
                self.use(self.S, "SCRq")
                sn = self.S.inc(nc.scalar.copy(
                    out=self.scr(off_q, tt * 512, (tt + 1) * 512), in_=pst[:]))
                self.S.drain()
                self.done("P" + str(tt % 2), self.S, sn)
            self.done("WQT" + str(buf), self.T, tn)
            _load_wq(hp + 1)
            self.use(self.D, "BCOLg2")
            gl = self.ss_bcol.dma(gq_c, q_gain[2 * hp: 2 * hp + 2].unsqueeze(1))
            self.swait(self.V, self.ss_bcol, gl)
            rope_v = self._l1f_normrope(off_q, gq_c)
            self.done("BCOLg2", self.V)

            for h2 in range(2):
                h = hp * 2 + h2
                pbase = h2 * 64
                kvh = h // 4
                ktile = off_k if (kvh % 2) == (h % 2) else off_kd
                kcol = ktile + (kvh // 2) * T
                ps_o = (self.P4, self.P5)[h2]
                for qt in range(4):
                    kmax = 4 * qt + 4
                    sn_by = {}
                    vm_by = {}
                    for kc in range(kmax):
                        dlt = 128 * kc - 512 * qt
                        lo = max(0, dlt)
                        pss = (self.P0, self.P1)[kc % 2]
                        self.use(self.T, "P" + str(kc % 2))
                        self.T.wait(self.V, rope_v)
                        self.swait(self.T, self.sg_w, self.kd_n)
                        tn = self.T.inc(nc.tensor.matmul(
                            pss[:, lo:512],
                            self.scr(kcol, kc * P, (kc + 1) * P, pbase, pbase + 64),
                            self.scr(off_q, qt * 512 + lo, (qt + 1) * 512,
                                     pbase, pbase + 64),
                            start=True, stop=True))
                        self.S.wait(self.T, tn)
                        sn = self.S.inc(nc.scalar.activation(
                            out=self.scr(off_pt, kc * 512 + lo, (kc + 1) * 512),
                            in_=pss[:, lo:512], func=AF.Exp, scale=EXPSC))
                        self.done("P" + str(kc % 2), self.S, sn)
                        sn_by[kc] = sn
                        vm_by[kc] = 0
                        if dlt >= 0:
                            self.V.wait(self.S, sn)
                            vm_by[kc] = self.V.inc(nc.vector.tensor_tensor(
                                self.scr(off_pt, kc * 512 + dlt, kc * 512 + dlt + P),
                                self.scr(off_pt, kc * 512 + dlt, kc * 512 + dlt + P),
                                self.MSK[:, 0:P], ALU.mult))
                    self.use(self.T, ("P4", "P5")[h2])
                    if hp == 0 and h2 == 0 and qt == 0:
                        self.T.wait(self.V, v_ready)
                    t_pv = 0
                    for kc in range(kmax):
                        lo = max(0, 128 * kc - 512 * qt)
                        slot = off_v + kc * 260 + kvh * 65
                        self.T.wait(self.S, sn_by[kc])
                        if vm_by[kc]:
                            self.T.wait(self.V, vm_by[kc])
                        mm = nc.tensor.matmul(
                            ps_o[0:65, lo:512], self.SCR[:, slot:slot + 65],
                            self.scr(off_pt, kc * 512 + lo, (kc + 1) * 512),
                            start=(kc == 0), stop=(kc == kmax - 1))
                        if kc == kmax - 1:
                            t_pv = self.T.inc(mm)
                    self.done("SCRpt", self.T, t_pv)
                    self.V.wait(self.T, t_pv)
                    rl_row = self.row(5)
                    self.use(self.V, "RW6")
                    vn = self.V.inc(nc.vector.reciprocal(rl_row, ps_o[64:65, :]))
                    self.use(self.T, "P6")
                    self.T.wait(self.V, vn)
                    t_bc = self.T.inc(nc.tensor.matmul(
                        self.P6[0:64, :], self.RW[0:1, 0:64], rl_row,
                        start=True, stop=True))
                    self.done("RW6", self.T, t_bc)
                    self.use(self.S, "SCRt22")
                    self.S.wait(self.T, t_bc)
                    sn = self.S.inc(nc.scalar.copy(
                        out=self.scr(off_t2, 0, 512, 0, 64), in_=ps_o[0:64, :]))
                    self.done(("P4", "P5")[h2], self.S, sn)
                    self.V.wait(self.S, sn)
                    self.use(self.V, "SCRao")
                    vn = self.V.inc(nc.vector.tensor_tensor(
                        self.scr(off_ao, qt * 512, (qt + 1) * 512, pbase, pbase + 64),
                        self.scr(off_t2, 0, 512, 0, 64), self.P6[0:64, :], ALU.mult))
                    self.done("P6", self.V, vn)
                    self.done("SCRt22", self.V, vn)
            self.D.wait(self.V, self.V.n)
            dn = self.st_ao.dma(attb[b, hp], self.scr(off_ao, 0, T))
            self.done("SCRao", self.st_ao, dn)
            self.done("SCRq", self.st_ao, dn)

        self.swait(self.D, self.st_ao)
        self.use(self.D, "SCRal")
        self.D.wait(self.T, self.T.n)
        al = 0
        for cc in range(NCC):
            al = self.st_al.dma(self.SCR[:, 4160 + cc * T: 4160 + (cc + 1) * T],
                                attb[b, cc])
        wo_bufs = (self.WCOL[:, 0:1024], self.WCOL[:, 1024:2048])
        wo_lds = [0, 0]

        def _load_wo(oc_):
            if oc_ >= NCC:
                return
            buf_ = oc_ % 2
            wsl_ = wo_bufs[buf_]
            self.use(self.D, "WOT" + str(buf_))
            if oc_ == 0:
                self.use(self.D, "WQT0")
            if oc_ == 1:
                self.use(self.D, "WCOLv")
            for cc in range(NCC):
                wo_lds[buf_] = self.ss_wt.dma(
                    wsl_[:, cc * P:(cc + 1) * P],
                    wo_bf[oc_ * P:(oc_ + 1) * P, cc * P:(cc + 1) * P],
                    transpose=True)

        _load_wo(0)
        _load_wo(1)
        for oc in range(NCC):
            buf = oc % 2
            wsl = wo_bufs[buf]
            tn = 0
            for tt in range(4):
                pst = (self.P0, self.P1)[tt % 2]
                self.use(self.T, "P" + str(tt % 2))
                self.swait(self.T, self.ss_wt, wo_lds[buf])
                self.swait(self.T, self.st_al, al)
                for cc in range(NCC):
                    mm = nc.tensor.matmul(
                        pst[:], wsl[:, cc * P:(cc + 1) * P],
                        self.SCR[:, 4160 + cc * T + tt * 512:
                                 4160 + cc * T + (tt + 1) * 512],
                        start=(cc == 0), stop=(cc == NCC - 1))
                    if cc == NCC - 1:
                        tn = self.T.inc(mm)
                self.V.wait(self.T, tn)
                vn = self.V.inc(nc.vector.tensor_tensor(
                    self.xmid(oc, tt * 512, (tt + 1) * 512), pst[:],
                    self.xmid(oc, tt * 512, (tt + 1) * 512), ALU.add))
                self.done("P" + str(tt % 2), self.V, vn)
            self.done("WOT" + str(buf), self.T, tn)
            _load_wo(oc + 2)
        self.done("SCRal", self.V)
        self.done("SCRk", self.V)
        self.done("SCRv", self.V)
        self.barrier()
        self.dump_dbg("l1fxa", b)

        self.emit_rms()
        self.barrier()

        o_s1f = 0
        o_s2f = 1024
        o_s1p = 5120
        o_s2p = 9216
        o_hb1 = 10240
        o_hm = 26688
        o_hst = 27200
        ldn = 0
        for (key, off0, ncc_) in (("s1f", o_s1f, 8), ("s2f", o_s2f, 32),
                                  ("s1p", o_s1p, 32), ("s2p", o_s2p, 8)):
            for ki in range(2):
                base = off0 + ki * ncc_ * 64
                for cc in range(ncc_):
                    ldn = self.sg_w.dma(
                        self.SCR[:, base + cc * 64: base + (cc + 1) * 64],
                        ssel[key][ki, cc * P:(cc + 1) * P, :])
        self.G.wait(self.V, 0)
        bd_v = self.V.n
        self.swait(self.T, self.sg_w, ldn)

        for tt in range(4):
            lo, hi = tt * 512, (tt + 1) * 512
            for ki in range(2):
                for p_ in range(NCC):
                    pst = (self.P0, self.P1)[p_ % 2]
                    self.use(self.T, "P" + str(p_ % 2))
                    self.T.wait(self.V, bd_v)
                    base = o_s1f + ki * 512 + p_ * 64
                    nc.tensor.matmul(
                        pst[0:64, :], self.SCR[0:64, base: base + 64],
                        self.xn(p_, lo, hi, )[0:64, :],
                        start=True, stop=True, tile_position=(0, 0))
                    tn = self.T.inc(nc.tensor.matmul(
                        pst[64:128, :], self.SCR[64:128, base: base + 64],
                        self.xn(p_, lo, hi)[64:128, :],
                        start=True, stop=True, tile_position=(64, 64)))
                    self.S.wait(self.T, tn)
                    self.use(self.S, "SCRhb")
                    sn = self.S.inc(nc.scalar.copy(
                        out=self.scr(o_hb1 + ki * 4096, p_ * 512, (p_ + 1) * 512),
                        in_=pst[:]))
                    self.done("P" + str(p_ % 2), self.S, sn)
            hb_s = self.S.n
            wlds_fc = [0, 0]
            for m in range(32):
                g4, mi = divmod(m, 4)
                for ki in range(2):
                    psm = (self.P2, self.P3)[ki]
                    wsl = (self.WCOL[:, 0:4096],
                           self.SCR[:, 18432:22528])[ki]
                    stm = (self.st_al, self.st_mx)[ki]
                    if mi == 0:
                        self.use(self.D, "WCOLm" + str(ki))
                        wlds_fc[ki] = stm.dma(
                            wsl, APc(mixtp["fc" + str(ki)], g4 * P * 8 * 512,
                                     [[8 * 512, P], [1, 8 * 512]]))
                    self.use(self.T, "P" + str(2 + ki))
                    self.swait(self.T, stm, wlds_fc[ki])
                    self.T.wait(self.S, hb_s)
                    tn = 0
                    for o_ in range(NCC):
                        mm = nc.tensor.matmul(
                            psm[:], wsl[:, o_ * 512 + mi * P:
                                         o_ * 512 + (mi + 1) * P],
                            self.scr(o_hb1 + ki * 4096, o_ * 512, (o_ + 1) * 512),
                            start=(o_ == 0), stop=(o_ == NCC - 1))
                        if o_ == NCC - 1:
                            tn = self.T.inc(mm)
                    if mi == 3:
                        self.done("WCOLm" + str(ki), self.T, tn)
                    self.V.wait(self.T, tn)
                    self.use(self.V, "X2" + str(ki))
                    vn = self.V.inc(nc.vector.tensor_copy(
                        out=self.X2[:, ki * 512:(ki + 1) * 512], in_=psm[:]))
                    self.done("P" + str(2 + ki), self.V, vn)
                hm_v = self.V.n
                self.use(self.T, "P0")
                self.T.wait(self.V, hm_v)
                tn = 0
                for ki in range(2):
                    base = o_s2f + ki * 2048 + m * 64
                    nc.tensor.matmul(
                        self.P0[0:64, :], self.SCR[0:64, base: base + 64],
                        self.X2[0:64, ki * 512:(ki + 1) * 512],
                        start=(ki == 0), stop=False, tile_position=(0, 0),
                        skip_group_check=True)
                    tn = self.T.inc(nc.tensor.matmul(
                        self.P0[64:128, :], self.SCR[64:128, base: base + 64],
                        self.X2[64:128, ki * 512:(ki + 1) * 512],
                        start=(ki == 0), stop=(ki == 1), tile_position=(64, 64),
                        skip_group_check=True))
                self.done("X20", self.T, tn)
                self.done("X21", self.T, tn)
                import os as _os3
                if _os3.environ.get("DBGBD2"):
                    self.S.wait(self.T, tn)
                    self.use(self.S, "SCRhm")
                    sn = self.S.inc(nc.scalar.copy(
                        out=self.scr(o_hm, 0, 512), in_=self.P0[:]))
                    self.S.drain()
                    vn = self.V.n
                    self.V.wait(self.S, sn)
                    vn = self.V.tick()
                    self.done("P0", self.V, vn)
                    self.D.wait(self.S, sn)
                    hn_ = self.st_x.dma(hfc_d[tt, m], self.scr(o_hm, 0, 512))
                    self.done("SCRhm", self.st_x, hn_)
                    continue
                if _os3.environ.get("DBGHM"):
                    self.S.wait(self.T, tn)
                    self.use(self.S, "SCRhm")
                    _half = 512 if _os3.environ.get("DBGHM2") else 0
                    sn = self.S.inc(nc.scalar.copy(
                        out=self.scr(o_hm, 0, 512), in_=self.X2[:, _half:_half + 512]))
                    self.S.drain()
                    self.V.wait(self.S, sn)
                    vn = self.V.tick()
                    self.done("P0", self.V, vn)
                    self.D.wait(self.S, sn)
                    hn_ = self.st_x.dma(hfc_d[tt, m], self.scr(o_hm, 0, 512))
                    self.done("SCRhm", self.st_x, hn_)
                    continue
                self.S.wait(self.T, tn)
                self.use(self.S, "SCRhm")
                sn = self.S.inc(nc.scalar.activation(
                    out=self.scr(o_hm, 0, 512), in_=self.P0[:], func=AF.Relu))
                self.V.wait(self.S, sn)
                vn = self.V.inc(nc.vector.tensor_tensor(
                    self.scr(o_hm, 0, 512), self.P0[:], self.scr(o_hm, 0, 512),
                    ALU.mult))
                self.done("P0", self.V, vn)
                self.D.wait(self.V, vn)
                hn_ = self.st_x.dma(hfc_d[tt, m], self.scr(o_hm, 0, 512))
                self.done("SCRhm", self.st_x, hn_)
            self.done("SCRhb", self.T)
        hfc_done = self.st_x.n
        self.barrier()

        import os as _os2
        for ki in range(0 if not _os2.environ.get("SKIP_PJ") else 2, 2):
            for tt in range(4):
                lo, hi = tt * 512, (tt + 1) * 512
                for p_ in range(32):
                    hslot = o_hst + (p_ % 2) * 512
                    self.use(self.D, "HST" + str(p_ % 2))
                    hld = (self.st_tmp, self.st_tmp2)[p_ % 2].dma(
                        self.scr(hslot, 0, 512), hfc_d[tt, p_])
                    pst = (self.P0, self.P1)[p_ % 2]
                    self.use(self.T, "P" + str(p_ % 2))
                    self.swait(self.T, (self.st_tmp, self.st_tmp2)[p_ % 2], hld)
                    self.T.wait(self.V, bd_v)
                    base = o_s1p + ki * 2048 + p_ * 64
                    nc.tensor.matmul(
                        pst[0:64, :], self.SCR[0:64, base: base + 64],
                        self.scr(hslot, 0, 512, 0, 64),
                        start=True, stop=True, tile_position=(0, 0))
                    tn = self.T.inc(nc.tensor.matmul(
                        pst[64:128, :], self.SCR[64:128, base: base + 64],
                        self.scr(hslot, 0, 512, 64, 128),
                        start=True, stop=True, tile_position=(64, 64)))
                    self.done("HST" + str(p_ % 2), self.T, tn)
                    self.S.wait(self.T, tn)
                    self.use(self.S, "SCRhb")
                    sn = self.S.inc(nc.scalar.copy(
                        out=self.scr(o_hb1, p_ * 512, (p_ + 1) * 512), in_=pst[:]))
                    self.done("P" + str(p_ % 2), self.S, sn)
                hb_s = self.S.n
                for m in range(NCC):
                    psm = self.P2
                    wsl = self.WCOL[:, 0:4096]
                    self.use(self.D, "WCOLm0")
                    wld = 0
                    for half in range(2):
                        wld = self.st_al.dma(
                            wsl[:, half * 2048: half * 2048 + 2048],
                            APc(mixtp["pj" + str(ki)],
                                m * P * 32 * P + half * 16 * P,
                                [[32 * P, P], [1, 16 * P]]))
                    self.use(self.T, "P2")
                    self.swait(self.T, self.st_al, wld)
                    self.T.wait(self.S, hb_s)
                    tn = 0
                    for o_ in range(32):
                        mm = nc.tensor.matmul(
                            psm[:], wsl[:, o_ * P:(o_ + 1) * P],
                            self.scr(o_hb1, o_ * 512, (o_ + 1) * 512),
                            start=(o_ == 0), stop=(o_ == 31))
                        if o_ == 31:
                            tn = self.T.inc(mm)
                    self.done("WCOLm0", self.T, tn)
                    self.V.wait(self.T, tn)
                    self.use(self.V, "X20")
                    vn = self.V.inc(nc.vector.tensor_copy(
                        out=self.X2[:, 0:512], in_=psm[:]))
                    self.done("P2", self.V, vn)
                    self.use(self.T, "P3")
                    self.T.wait(self.V, vn)
                    base = o_s2p + ki * 512 + m * 64
                    nc.tensor.matmul(
                        self.P3[0:64, :], self.SCR[0:64, base: base + 64],
                        self.X2[0:64, 0:512],
                        start=True, stop=True, tile_position=(0, 0))
                    tn = self.T.inc(nc.tensor.matmul(
                        self.P3[64:128, :], self.SCR[64:128, base: base + 64],
                        self.X2[64:128, 0:512],
                        start=True, stop=True, tile_position=(64, 64)))
                    self.done("X20", self.T, tn)
                    self.V.wait(self.T, tn)
                    vn = self.V.inc(nc.vector.tensor_tensor(
                        self.xmid(m, lo, hi), self.P3[:],
                        self.xmid(m, lo, hi), ALU.add))
                    self.done("P3", self.V, vn)
                self.done("SCRhb", self.T)
        self.done("XN", self.T)
        self.barrier()
        return


def make_host_consts():
    c = {}
    c["hc_ident"] = np.eye(P, dtype=np.float32)
    i = np.arange(P)[:, None]
    j = np.arange(P)[None, :]
    c["hc_tri_ge"] = (j >= i).astype(ml_dtypes.bfloat16)
    c["hc_tri_le"] = (j <= i).astype(ml_dtypes.bfloat16)
    c["hc_identb"] = np.eye(P).astype(ml_dtypes.bfloat16)
    HS = 64
    half = HS // 2
    inv_freq = 1.0 / 10000.0 ** (np.arange(half, dtype=np.float64) * 2.0 / HS)
    t = np.arange(T, dtype=np.float64)
    ang = t[:, None] * inv_freq[None, :]
    cos = np.cos(ang).T
    sin = np.sin(ang).T
    C = np.concatenate([cos, cos], 0)
    Sp = np.concatenate([-sin, sin], 0)
    c["hc_l0C"] = np.tile(C, (2, 1)).astype(ml_dtypes.bfloat16)
    c["hc_l0S"] = np.tile(Sp, (2, 1)).astype(ml_dtypes.bfloat16)

    HD = 64
    half_idx = np.arange(0, HD, 2, dtype=np.float64)
    inv_freq = 1.0 / 10000.0 ** (half_idx / HD)
    tt = np.arange(T, dtype=np.float64)
    fr = tt[:, None] * inv_freq[None, :]
    base = (half_idx + 0.4 * HD) / (1.4 * HD)
    power = (tt - T // 2) / float(512 // 2)
    scl = base[None, :] ** power[:, None]
    cosf = np.cos(fr)
    sinf = np.sin(fr)
    Cq = (cosf * scl).T
    Sq = (sinf * scl).T
    Ck = (cosf / scl).T
    Sk = (sinf / scl).T
    def half2(a_first, a_second):
        blk = np.concatenate([a_first, a_second], 0)
        return np.tile(blk, (2, 1)).astype(ml_dtypes.bfloat16)
    c["hc_xCq"] = half2(Cq, Cq)
    c["hc_xSq"] = half2(-Sq, Sq)
    c["hc_xCk"] = half2(Ck, Ck)
    c["hc_xSk"] = half2(-Sk, Sk)
    blk2t = np.zeros((2, P), dtype=np.float32)
    blk2t[0, 0:64] = 1.0
    blk2t[1, 64:128] = 1.0
    c["hc_blk2t"] = blk2t
    blk2 = np.zeros((P, 2), dtype=ml_dtypes.bfloat16)
    blk2[0:64, 0] = 1.0
    blk2[64:128, 1] = 1.0
    c["hc_blk2"] = blk2
    c["hc_iota8"] = np.arange(8, dtype=np.float32)[None, :]
    pp = np.arange(P)
    pc = np.zeros((P, 8), dtype=np.float32)
    pc[:, 0] = pp
    pc[:, 1] = pp // 64
    pc[:, 2] = pp % 16
    pc[:, 3] = pp // 16
    pc[:, 4] = pp % 64
    c["hc_pcols"] = pc
    ocm = np.zeros((P, 64), dtype=np.float32)
    ocm[:, 0:32] = (2.0 * np.arange(32, dtype=np.float32))[None, :]
    c["hc_ocols"] = ocm
    inv1 = 1.0 / 10000.0 ** (np.arange(0, 64, 2, dtype=np.float64) / 64)
    fr1 = np.arange(T, dtype=np.float64)[:, None] * inv1[None, :]
    c1 = np.cos(fr1).T
    s1_ = np.sin(fr1).T
    C1 = np.concatenate([c1, c1], 0)
    S1 = np.concatenate([s1_, -s1_], 0)
    c["hc_l1C"] = np.tile(C1, (2, 1)).astype(ml_dtypes.bfloat16)
    c["hc_l1S"] = np.tile(S1, (2, 1)).astype(ml_dtypes.bfloat16)
    return c
_PROGRAMS = {}


def _build_a():
    m = MK()
    xin = m.din("xin", [NB, T, D])
    wq = m.din("L0_attn_Wq", [D, D]); bq = m.din("L0_attn_bq", [D])
    wk = m.din("L0_attn_Wk", [D, D]); bk = m.din("L0_attn_bk", [D])
    wv = m.din("L0_attn_Wv", [D, D]); bv = m.din("L0_attn_bv", [D])
    wp = m.din("L0_attn_Wp", [D, D]); bp = m.din("L0_attn_bp", [D])
    out = m.dout("out", [NB, T, D])
    xfm = m.dint("xfm", [NB, NCC, P, T])
    attb = m.dint("attb", [NB, NCC, P, T], BF16)
    m.alloc()
    m.load_consts()
    t1 = m.din("hc_l0C", [P, T], BF16)
    t2 = m.din("hc_l0S", [P, T], BF16)
    m.load_tab(t1, t2)
    m.emit_ingest(xin, xfm)
    m.barrier()
    for b in range(NB):
        m.load_x(xfm, b)
        m.barrier()
        m.emit_l0_attn(b, wq, bq, wk, bk, wv, bv, wp, bp, attb)
        m.emit_ln(dstf=m.xmid, dstb=m.xn)
        m.barrier()
        m.emit_egress(b, out)
        m.done("XMID", m.T)
        m.barrier()
    m.nc.sync.wait_ge(m.st_out.sem, m.st_out.n)
    return m


def _build_b():
    m = MK()
    xin = m.din("xin", [NB, T, D])
    wqkv = m.din("L0_ffn_wqkv", [2, D, 1536])
    qs = m.din("L0_ffn_q_scale", [2, 64]); ks = m.din("L0_ffn_k_scale", [2, 64])
    wo = m.din("L0_ffn_wo", [2, 512, D])
    lg = m.din("L0_ffn_ln_g", [2, D]); lb = m.din("L0_ffn_ln_b", [2, D])
    w1 = m.din("L0_ffn_w1", [2, D, 5460]); b1 = m.din("L0_ffn_b1", [2, 5460])
    w2 = m.din("L0_ffn_w2", [2, 2730, D]); b2 = m.din("L0_ffn_b2", [2, D])
    out = m.dout("out", [NB, T, D])
    xfm = m.dint("xfm", [NB, NCC, P, T])
    attb = m.dint("attb", [NB, NCC, P, T], BF16)
    m.alloc()
    m.load_consts()
    xtabs = [m.din("hc_xCq", [P, T], BF16), m.din("hc_xSq", [P, T], BF16),
             m.din("hc_xCk", [P, T], BF16), m.din("hc_xSk", [P, T], BF16)]
    m.emit_ingest(xin, xfm)
    m.barrier()
    for b in range(NB):
        m.load_x(xfm, b)
        m.barrier()
        m.emit_l0_ffn(b, wqkv, qs, ks, wo, lg, lb, w1, b1, w2, b2, attb,
                      xtabs=xtabs)
        m.emit_addx(xfm, b)
        m.emit_ln(dstf=m.xmid, dstb=m.xn)
        m.barrier()
        m.emit_egress(b, out)
        m.done("XMID", m.T)
        m.barrier()
    m.nc.sync.wait_ge(m.st_out.sem, m.st_out.n)
    return m


def _build_c():
    m = MK()
    xin = m.din("xin", [NB, T, D])
    wq = m.din("L1_attn_W_q", [D, D])
    wo = m.din("L1_attn_W_o", [D, D])
    emb = m.din("L1_attn_rel_k_emb", [65, 64])
    out = m.dout("out", [NB, T, D])
    xfm = m.dint("xfm", [NB, NCC, P, T])
    attb = m.dint("attb", [NB, NCC, P, T], BF16)
    wq_bf = m.dint("wq_bf", [D, D], BF16)
    wo_bf = m.dint("wo_bf", [D, D], BF16)
    epad_h = m.nc.dram_tensor("epad", [NB * 16 * T * 576], BF16, kind="Internal")
    m.alloc()
    m.load_consts()
    m.emit_stage_bf16([(wq, wq_bf), (wo, wo_bf)])
    m.emit_l1_emb(emb)
    m.emit_ingest(xin, xfm)
    m.barrier()
    for b in range(NB):
        m.load_x(xfm, b)
        m.barrier()
        m.emit_l1_attn(b, wq_bf, wo_bf, epad_h, attb)
        m.emit_ln(dstf=m.xmid, dstb=m.xn)
        m.barrier()
        m.emit_egress(b, out)
        m.done("XMID", m.T)
        m.barrier()
    m.nc.sync.wait_ge(m.st_out.sem, m.st_out.n)
    return m


def _build_d():
    m = MK()
    xin = m.din("xin", [NB, T, D])
    wq = m.din("L1_ffn_wq", [D, D])
    wk = m.din("L1_ffn_wk", [256, D])
    wv = m.din("L1_ffn_wv", [256, D])
    wo = m.din("L1_ffn_wo", [D, D])
    qg = m.din("L1_ffn_q_gain", [16]); kg = m.din("L1_ffn_k_gain", [4])
    fc_s1 = m.din("L1_ffn_fc_s1", [8, D, 64])
    fc_s2 = m.din("L1_ffn_fc_s2", [8, 4096, 64])
    fc_mix = m.din("L1_ffn_fc_mix", [8, 4096, D])
    fc_logits = m.din("L1_ffn_fc_logits", [8])
    pj_s1 = m.din("L1_ffn_pj_s1", [8, 4096, 64])
    pj_s2 = m.din("L1_ffn_pj_s2", [8, D, 64])
    pj_mix = m.din("L1_ffn_pj_mix", [8, D, 4096])
    pj_logits = m.din("L1_ffn_pj_logits", [8])
    iota8 = m.din("hc_iota8", [1, 8])
    out = m.dout("out", [NB, T, D])
    xfm = m.dint("xfm", [NB, NCC, P, T])
    attb = m.dint("attb", [NB, NCC, P, T], BF16)
    wq_bf = m.dint("wq_bf", [D, D], BF16)
    wk_bf = m.dint("wk_bf", [256, D], BF16)
    wv_bf = m.dint("wv_bf", [256, D], BF16)
    wo_bf = m.dint("wo_bf", [D, D], BF16)
    hfc_d = m.dint("hfc_d", [4, 32, P, 512], BF16)
    mixt, mixtp = {}, {}
    for lg_, Md, Od in (("fc", 4096, 1024), ("pj", 1024, 4096)):
        for ki in range(2):
            tag = lg_ + str(ki)
            mixt[tag] = m.nc.dram_tensor("mt_" + tag, [Od * Md], BF16, kind="Internal")
            mixtp[tag] = m.nc.dram_tensor("mtp_" + tag, [Od * Md], BF16, kind="Internal")
    ssel = {"s1f": m.dint("ssel_s1f", [2, D, 64], BF16),
            "s2f": m.dint("ssel_s2f", [2, 4096, 64], BF16),
            "s1p": m.dint("ssel_s1p", [2, 4096, 64], BF16),
            "s2p": m.dint("ssel_s2p", [2, D, 64], BF16)}
    m.alloc()
    m.load_consts()
    t1 = m.din("hc_l1C", [P, T], BF16)
    t2 = m.din("hc_l1S", [P, T], BF16)
    m.load_tab(t1, t2)
    m.emit_stage_bf16([(wq, wq_bf), (wk, wk_bf), (wv, wv_bf), (wo, wo_bf)])
    m.emit_l1_ffn_prep(fc_logits, pj_logits, fc_s1, fc_s2, pj_s1, pj_s2,
                       fc_mix, pj_mix, iota8, ssel, mixt, mixtp)
    m.emit_ingest(xin, xfm)
    m.barrier()
    for b in range(NB):
        m.load_x(xfm, b)
        m.barrier()
        m.emit_l1_ffn(b, wq_bf, wk_bf, wv_bf, wo_bf, qg, kg,
                      ssel, mixtp, attb, hfc_d)
        m.emit_addx(xfm, b)
        m.emit_ln(dstf=m.xmid, dstb=m.xn)
        m.barrier()
        m.emit_egress(b, out)
        m.done("XMID", m.T)
        m.barrier()
    m.nc.sync.wait_ge(m.st_out.sem, m.st_out.n)
    return m


_WN = {
    "a": ["L0_attn_Wq", "L0_attn_bq", "L0_attn_Wk", "L0_attn_bk",
          "L0_attn_Wv", "L0_attn_bv", "L0_attn_Wp", "L0_attn_bp"],
    "b": ["L0_ffn_wqkv", "L0_ffn_q_scale", "L0_ffn_k_scale", "L0_ffn_wo",
          "L0_ffn_ln_g", "L0_ffn_ln_b", "L0_ffn_w1", "L0_ffn_b1",
          "L0_ffn_w2", "L0_ffn_b2"],
    "c": ["L1_attn_W_q", "L1_attn_W_o", "L1_attn_rel_k_emb"],
    "d": ["L1_ffn_wq", "L1_ffn_wk", "L1_ffn_wv", "L1_ffn_wo",
          "L1_ffn_q_gain", "L1_ffn_k_gain", "L1_ffn_fc_s1", "L1_ffn_fc_s2",
          "L1_ffn_fc_mix", "L1_ffn_fc_logits", "L1_ffn_pj_s1", "L1_ffn_pj_s2",
          "L1_ffn_pj_mix", "L1_ffn_pj_logits"],
}


def _input_names(m):
    return {a.memorylocations[0].name for a in m.nc.m.functions[0].allocations
            if hasattr(a, "kind") and a.kind == "ExternalInput"}


def kernel(**inputs):
    import numpy as _np
    hc = make_host_consts()
    if not _PROGRAMS:
        _PROGRAMS["a"] = _build_a()
        _PROGRAMS["b"] = _build_b()
        _PROGRAMS["c"] = _build_c()
        _PROGRAMS["d"] = _build_d()
    x = _np.ascontiguousarray(_np.asarray(inputs["L0_attn_x"], dtype=_np.float32))
    for stage in ("a", "b", "c", "d"):
        m = _PROGRAMS[stage]
        names = _input_names(m)
        in_maps = []
        for c in range(8):
            dmap = {"xin": _np.ascontiguousarray(x[c * NB:(c + 1) * NB])}
            if stage == "a":
                dmap["xin"] = dmap.pop("xin")
            for w in _WN[stage]:
                dmap[w] = _np.ascontiguousarray(_np.asarray(inputs[w],
                                                            dtype=_np.float32))
            dmap.update({k: v for k, v in hc.items() if k in names})
            in_maps.append(dmap)
        res = run_bass_kernel_spmd(m.nc, in_maps, core_ids=list(range(8)))
        x = _np.concatenate([_np.asarray(r["out"]) for r in res.results], 0)
    return x.astype(_np.float32)


def _build_fused():
    m = MK()
    xin = m.din("L0_attn_x", [NB, T, D])
    a_wq = m.din("L0_attn_Wq", [D, D]); a_bq = m.din("L0_attn_bq", [D])
    a_wk = m.din("L0_attn_Wk", [D, D]); a_bk = m.din("L0_attn_bk", [D])
    a_wv = m.din("L0_attn_Wv", [D, D]); a_bv = m.din("L0_attn_bv", [D])
    a_wp = m.din("L0_attn_Wp", [D, D]); a_bp = m.din("L0_attn_bp", [D])
    wqkv = m.din("L0_ffn_wqkv", [2, D, 1536])
    qs = m.din("L0_ffn_q_scale", [2, 64]); ks = m.din("L0_ffn_k_scale", [2, 64])
    b_wo = m.din("L0_ffn_wo", [2, 512, D])
    lg = m.din("L0_ffn_ln_g", [2, D]); lb = m.din("L0_ffn_ln_b", [2, D])
    w1 = m.din("L0_ffn_w1", [2, D, 5460]); b1 = m.din("L0_ffn_b1", [2, 5460])
    w2 = m.din("L0_ffn_w2", [2, 2730, D]); b2 = m.din("L0_ffn_b2", [2, D])
    c_wq = m.din("L1_attn_W_q", [D, D])
    c_wo = m.din("L1_attn_W_o", [D, D])
    emb = m.din("L1_attn_rel_k_emb", [65, 64])
    d_wq = m.din("L1_ffn_wq", [D, D])
    d_wk = m.din("L1_ffn_wk", [256, D])
    d_wv = m.din("L1_ffn_wv", [256, D])
    d_wo = m.din("L1_ffn_wo", [D, D])
    qg = m.din("L1_ffn_q_gain", [16]); kg = m.din("L1_ffn_k_gain", [4])
    fc_s1 = m.din("L1_ffn_fc_s1", [8, D, 64])
    fc_s2 = m.din("L1_ffn_fc_s2", [8, 4096, 64])
    fc_mix = m.din("L1_ffn_fc_mix", [8, 4096, D])
    fc_logits = m.din("L1_ffn_fc_logits", [8])
    pj_s1 = m.din("L1_ffn_pj_s1", [8, 4096, 64])
    pj_s2 = m.din("L1_ffn_pj_s2", [8, D, 64])
    pj_mix = m.din("L1_ffn_pj_mix", [8, D, 4096])
    pj_logits = m.din("L1_ffn_pj_logits", [8])
    iota8 = m.din("hc_iota8", [1, 8])
    out = m.dout("out", [NB, T, D])
    xfm = m.dint("xfm", [NB, NCC, P, T])
    xf1 = m.dint("xf1", [NB, NCC, P, T])
    xf3 = m.dint("xf3", [NB, NCC, P, T])
    attb = m.dint("attb", [NB, NCC, P, T], BF16)
    cwq_bf = m.dint("cwq_bf", [D, D], BF16)
    cwo_bf = m.dint("cwo_bf", [D, D], BF16)
    dwq_bf = m.dint("dwq_bf", [D, D], BF16)
    dwk_bf = m.dint("dwk_bf", [256, D], BF16)
    dwv_bf = m.dint("dwv_bf", [256, D], BF16)
    dwo_bf = m.dint("dwo_bf", [D, D], BF16)
    hfc_d = m.dint("hfc_d", [4, 32, P, 512], BF16)
    epad_h = m.nc.dram_tensor("epad", [NB * 16 * T * 576], BF16, kind="Internal")
    mixt, mixtp = {}, {}
    for lg_, Md, Od in (("fc", 4096, 1024), ("pj", 1024, 4096)):
        for ki in range(2):
            tag = lg_ + str(ki)
            mixt[tag] = m.nc.dram_tensor("mt_" + tag, [Od * Md], BF16, kind="Internal")
            mixtp[tag] = m.nc.dram_tensor("mtp_" + tag, [Od * Md], BF16, kind="Internal")
    ssel = {"s1f": m.dint("ssel_s1f", [2, D, 64], BF16),
            "s2f": m.dint("ssel_s2f", [2, 4096, 64], BF16),
            "s1p": m.dint("ssel_s1p", [2, 4096, 64], BF16),
            "s2p": m.dint("ssel_s2p", [2, D, 64], BF16)}
    m.alloc()
    m.load_consts()
    t_l0C = m.din("hc_l0C", [P, T], BF16)
    t_l0S = m.din("hc_l0S", [P, T], BF16)
    t_l1C = m.din("hc_l1C", [P, T], BF16)
    t_l1S = m.din("hc_l1S", [P, T], BF16)
    xtabs = [m.din("hc_xCq", [P, T], BF16), m.din("hc_xSq", [P, T], BF16),
             m.din("hc_xCk", [P, T], BF16), m.din("hc_xSk", [P, T], BF16)]
    m.emit_stage_bf16([(c_wq, cwq_bf), (c_wo, cwo_bf), (d_wq, dwq_bf),
                       (d_wk, dwk_bf), (d_wv, dwv_bf), (d_wo, dwo_bf)])
    m.emit_stage_l0a(a_wq, a_wk, a_wv, a_wp)
    m.emit_stage_l0f(wqkv, b_wo, w1, w2)
    m.emit_stage_deint()
    m.emit_l1_ffn_prep(fc_logits, pj_logits, fc_s1, fc_s2, pj_s1, pj_s2,
                       fc_mix, pj_mix, iota8, ssel, mixt, mixtp)
    m.emit_ingest(xin, xfm)
    m.barrier()
    for b in range(NB):
        m.load_x(xfm, b)
        m.barrier()
        m.load_tab(t_l0C, t_l0S)
        m.emit_l0_attn(b, a_wq, a_bq, a_wk, a_bk, a_wv, a_bv, a_wp, a_bp, attb)
        m.emit_ln(dstf=m.xmid, dstb=m.xn, dram_out=xf1, b=b)
        m.barrier()
        m.emit_l0_ffn(b, wqkv, qs, ks, b_wo, lg, lb, w1, b1, w2, b2, attb,
                      xtabs=xtabs)
        m.emit_addx(xf1, b)
        m.emit_ln(dstf=m.xmid, dstb=m.xn)
        m.barrier()
        m.emit_l1_emb(emb)
        m.barrier()
        m.emit_l1_attn(b, cwq_bf, cwo_bf, epad_h, attb)
        m.emit_ln(dstf=m.xmid, dstb=m.xn, dram_out=xf3, b=b)
        m.barrier()
        m.load_tab(t_l1C, t_l1S)
        m.emit_l1_ffn(b, dwq_bf, dwk_bf, dwv_bf, dwo_bf, qg, kg,
                      ssel, mixtp, attb, hfc_d)
        m.emit_addx(xf3, b)
        m.emit_ln(dstf=m.xmid, dstb=m.xn)
        m.barrier()
        m.emit_egress(b, out)
        m.done("XMID", m.T)
        m.done("XN", m.T)
        m.barrier()
    m.nc.sync.wait_ge(m.st_out.sem, m.st_out.n)
    return m


def kernel(**inputs):
    import numpy as _np
    hc = make_host_consts()
    if "fused" not in _PROGRAMS:
        _PROGRAMS["fused"] = _build_fused()
    m = _PROGRAMS["fused"]
    names = _input_names(m)
    x = _np.ascontiguousarray(_np.asarray(inputs["L0_attn_x"], dtype=_np.float32))
    wn = [k for k in inputs.keys() if k != "L0_attn_x"]
    in_maps = []
    for c in range(8):
        dmap = {"L0_attn_x": _np.ascontiguousarray(x[c * NB:(c + 1) * NB])}
        for w in wn:
            dmap[w] = _np.ascontiguousarray(_np.asarray(inputs[w], dtype=_np.float32))
        dmap.update({k: v for k, v in hc.items() if k in names})
        in_maps.append(dmap)
    res = run_bass_kernel_spmd(m.nc, in_maps, core_ids=list(range(8)))
    return _np.concatenate([_np.asarray(r["out"]) for r in res.results],
                           0).astype(_np.float32)
